# version 04
# baseline (speedup 1.0000x reference)
import math
import numpy as np
import ml_dtypes
from contextlib import ExitStack

import concourse.bass as bass
import concourse.mybir as mybir
from concourse.bass_utils import run_bass_kernel_spmd

F32 = mybir.dt.float32
BF16 = mybir.dt.bfloat16
AF = mybir.ActivationFunctionType
ALU = mybir.AluOpType
AX = mybir.AxisListType

T = 2048
D = 1024
NT = T // 128
ND = D // 128
NCORES = 8


class Sem:
    def __init__(self, ctx, name):
        self.h = ctx.stack.enter_context(ctx.nc.semaphore(name))
        self.v = 0

    def inc(self, instr, n=1):
        instr.then_inc(self.h, n)
        self.v += n
        return instr

    def wait(self, eng, v=None):
        eng.wait_ge(self.h, self.v if v is None else v)


class Multi:
    def __init__(self, toks):
        self.toks = toks

    def wait(self, eng):
        for t in self.toks:
            t.wait(eng)


class Tok:

    def __init__(self, sem_h, v):
        self.h, self.v = sem_h, v

    def wait(self, eng):
        eng.wait_ge(self.h, self.v)


class Ring:

    def __init__(self, tiles):
        self.tiles = tiles
        self.toks = [None] * len(tiles)
        self.i = 0
        self.n = len(tiles)

    def acquire(self, eng):
        t = self.toks[self.i % self.n]
        if t is not None:
            t.wait(eng)
        tile = self.tiles[self.i % self.n]
        self.i += 1
        return tile

    def release(self, tok):
        self.toks[(self.i - 1) % self.n] = tok


class Ctx:
    def __init__(self, nc, stack):
        self.nc = nc
        self.stack = stack
        self._c = 0
        self.cstate = {"tile": None, "used": 0, "sem": None, "map": {}}
        self.cstate["tile"] = self.stack.enter_context(
            self.nc.sbuf_tensor("constc", [128, 48], F32))
        self.cstate["sem"] = self.sem("s_const")

    def constcol(self, val, n=48):
        cs = self.cstate
        val = float(val)
        if val in cs["map"]:
            return cs["map"][val]
        j = cs["used"]
        cs["used"] += 1
        ap = cs["tile"][:, j:j + 1]
        cs["sem"].inc(self.nc.vector.memset(ap, val))
        cs["map"][val] = ap
        return ap

    def wait_const(self, eng):
        self.cstate["sem"].wait(eng)

    def _n(self, p):
        self._c += 1
        return f"{p}{self._c}"

    def _streams(self):
        if not hasattr(self, "est"):
            self.est = {n: self.sem("st_" + n)
                        for n in ("tensor", "vector", "scalar", "pool")}
        return self.est

    def mark(self, instr, eng):
        st = self._streams()[eng]
        instr._wait_ge(st.h, st.v)
        st.inc(instr, 1)
        return Tok(st.h, st.v)

    def v(self, instr):
        return self.mark(instr, "vector")

    def s(self, instr):
        return self.mark(instr, "scalar")

    def g(self, instr):
        return self.mark(instr, "pool")

    def tip(self, eng):
        st = self._streams()[eng]
        return Tok(st.h, st.v)

    def sem(self, name=None):
        return Sem(self, name or self._n("s"))

    def dsem(self, tag):
        if not hasattr(self, "_dsems"):
            self._dsems = {}
        if tag not in self._dsems:
            self._dsems[tag] = self.sem("d_" + tag)
        return self._dsems[tag]

    def sb(self, shape, dtype=F32, name=None):
        return self.stack.enter_context(
            self.nc.sbuf_tensor(name or self._n("zb"), shape, dtype))

    def ps(self, shape, dtype=F32, name=None):
        return self.stack.enter_context(
            self.nc.psum_tensor(name or self._n("zp"), shape, dtype))

    def dram(self, shape, dtype=F32, name=None):
        return self.nc.dram_tensor(self._n("zd") if name is None else name,
                                   shape, dtype, kind="Internal").ap()

    def sub(self):
        parent = self

        class _SubCM:
            def __enter__(self_inner):
                st = ExitStack()
                child = Ctx.__new__(Ctx)
                child.nc = parent.nc
                child.stack = st
                child._c = 0
                child.cstate = parent.cstate
                child.sem = parent.sem
                child.dsem = parent.dsem
                child.mark = parent.mark
                child.tip = parent.tip
                child.constcol = parent.constcol
                if hasattr(parent, "est"):
                    child.est = parent.est
                child._n = lambda p: parent._n(p)
                child.exit_waits = []
                self_inner.child = child
                self_inner.st = st
                return child

            def __exit__(self_inner, *a):
                nc = parent.nc
                sems = list(parent._streams().values())
                waits = [(s_.h, s_.v) for s_ in sems]
                waits += [(s_.h, s_.v) for s_ in self_inner.child.exit_waits]
                for eng in (nc.tensor, nc.vector, nc.scalar, nc.gpsimd,
                            nc.sync):
                    for h, v in waits:
                        if v > 0:
                            eng.wait_ge(h, v)
                self_inner.st.close()
                return False

        return _SubCM()

    def ring(self, n, shape, dtype=F32):
        return Ring([self.ps(shape, dtype) for _ in range(n)])

    def sbring(self, n, shape, dtype):
        return Ring([self.sb(shape, dtype) for _ in range(n)])


def mm_group(nc, out_ap, pairs):
    n = len(pairs)
    mm = None
    for j, (l, r) in enumerate(pairs):
        mm = nc.tensor.matmul(out_ap, l, r, start=(j == 0), stop=(j == n - 1))
    return mm



def emit_load_x(c, x_hbm, xres, after=()):
    ld = c.sem()
    for s_ in after:
        s_.wait(c.nc.sync)
    src = x_hbm.rearrange("(t p) f -> p t f", p=128)
    dst = xres[:, :].rearrange("p (t f) -> p t f", t=NT)
    ld.inc(c.nc.sync.dma_start(dst, src), 16)
    return ld


def emit_store_x(c, xres, out_hbm, wait_sems):
    st = c.sem()
    for s in wait_sems:
        s.wait(c.nc.sync)
    dst = out_hbm.rearrange("(t p) f -> p t f", p=128)
    src = xres[:, :].rearrange("p (t f) -> p t f", t=NT)
    st.inc(c.nc.sync.dma_start(dst, src), 16)
    st.wait(c.nc.sync)
    return st


class WH:

    def __init__(self, tile, din, dout, sem):
        self.t, self.din, self.dout, self.sem = tile, din, dout, sem
        self.nki = din // 128

    def lhsT(self, ki, o0, o1):
        return self.t[:, ki * self.dout + o0: ki * self.dout + o1]


def emit_load_weight(c, w_hbm_slice, din, dout, queue=None, tile=None, sem=None):
    nc = c.nc
    nki = din // 128
    tile = tile if tile is not None else c.sb([128, nki * dout], BF16)
    s = sem or c.sem()
    src = w_hbm_slice.rearrange("(ki p) o -> p ki o", p=128)
    dst = tile[:, 0:nki * dout].rearrange("p (ki o) -> p ki o", ki=nki)
    s.inc((queue or nc.gpsimd).dma_start(dst, src), 16)
    return WH(tile, din, dout, s)


def emit_load_bf16_row(c, v_hbm, n, queue=None):
    tile = c.sb([1, n], BF16)
    s = c.sem()
    s.inc((queue or c.nc.gpsimd).dma_start(tile[0:1, :], v_hbm[None, :]), 16)
    return tile, s


def emit_load_f32(c, v_hbm_2d, p, n, queue=None):
    tile = c.sb([p, n], F32)
    s = c.sem()
    s.inc((queue or c.nc.gpsimd).dma_start(tile[:, :], v_hbm_2d[:, :]), 16)
    return tile, s


def emit_ones(c, n=128):
    ones = c.sb([1, n], BF16)
    tok = c.v(c.nc.vector.memset(ones[0:1, :], 1.0))
    return ones, tok


def emit_bcast_rows(c, cs, jobs, onesb=None):
    nc = c.nc
    if onesb is None:
        onesb, t_ob = emit_ones(cs)
    else:
        onesb, t_ob = onesb
    maxn = max(j[0].shape[1] for j in jobs)
    hi = cs.sb([1, maxn], BF16)
    lo = cs.sb([1, maxn], BF16)
    ps = cs.ps([128, 512])
    t_last = None
    for row, dst, rdy in jobs:
        n = row.shape[1]
        rdy.wait(nc.vector)
        t_hi = c.v(nc.vector.tensor_copy(hi[0:1, 0:n], row))
        t_lo = c.v(nc.vector.tensor_sub(lo[0:1, 0:n], row, hi[0:1, 0:n]))
        for o0 in range(0, n, 512):
            o1 = min(o0 + 512, n)
            t_lo.wait(nc.tensor)
            t_ob.wait(nc.tensor)
            if t_last is not None:
                t_last.wait(nc.tensor)
            nc.tensor.matmul(ps[:, 0:o1 - o0], onesb[0:1, :], hi[0:1, o0:o1],
                             start=True, stop=False)
            mm = nc.tensor.matmul(ps[:, 0:o1 - o0], onesb[0:1, :],
                                  lo[0:1, o0:o1], start=False, stop=True)
            t_bc = c.mark(mm, "tensor")
            t_bc.wait(nc.vector)
            t_last = c.v(nc.vector.tensor_copy(dst[:, o0:o1],
                                               ps[:, 0:o1 - o0]))
    return t_last



class LNOut:
    def __init__(self, lnT, sem, lnx, sem_tok):
        self.lnT, self.sem, self.lnx, self.sem_tok = lnT, sem, lnx, sem_tok

    def kT(self, ki, t0=0, t1=T):
        return self.lnT[:, ki * T + t0: ki * T + t1]


def emit_ln_transpose(c, src_sb, ready_sem, eps=1e-5, lnT=None,
                      rms_compose=False):
    nc = c.nc
    ssum = c.sb([128, NT])
    ssq = c.sb([128, NT])
    sq_scr = c.sb([128, D])
    mcol = c.sb([128, NT])
    msq = c.sb([128, NT])
    rstd = c.sb([128, NT])
    nmr = c.sb([128, NT])
    stg = [c.sb([128, D], BF16) for _ in range(2)]
    stg_sems = [c.sem(), c.sem()]
    ln_hbm = c.dram([T, D], BF16)
    lnT = lnT if lnT is not None else c.sb([128, ND * T], BF16)

    s_store = c.sem()
    s_tp = c.sem()

    ready_sem.wait(nc.vector)
    ready_sem.wait(nc.scalar)
    t_sq = t_red = None
    for t in range(NT):
        t_red = c.v(nc.vector.reduce_sum(
            ssum[:, t:t + 1], src_sb[:, t * D:(t + 1) * D], axis=AX.X))
        t_sq = c.s(nc.scalar.activation(
            sq_scr[:, :], src_sb[:, t * D:(t + 1) * D], AF.Square,
            accum_out=ssq[:, t:t + 1]))
    t_mcol = c.v(nc.vector.tensor_scalar_mul(mcol[:, :], ssum[:, :], 1.0 / D))
    t_sq.wait(nc.vector)
    c.v(nc.vector.tensor_scalar_mul(ssq[:, :], ssq[:, :], 1.0 / D))
    t_mcol.wait(nc.scalar)
    t_msq = c.s(nc.scalar.activation(msq[:, :], mcol[:, :], AF.Square))
    t_msq.wait(nc.vector)
    t_var = c.v(nc.vector.tensor_sub(ssq[:, :], ssq[:, :], msq[:, :]))
    epsc = c.constcol(eps)
    halfneg = c.constcol(-0.5)
    halfpos = c.constcol(0.5)
    t_var.wait(nc.scalar)
    c.wait_const(nc.scalar)
    c.s(nc.scalar.activation(rstd[:, :], ssq[:, :], AF.Ln, bias=epsc))
    t_rstd = c.s(nc.scalar.activation(rstd[:, :], rstd[:, :], AF.Exp,
                                      scale=halfneg))
    if rms_compose:
        sv = c.sb([128, NT])
        c.s(nc.scalar.activation(sv[:, :], ssq[:, :], AF.Ln))
        t_sv = c.s(nc.scalar.activation(sv[:, :], sv[:, :], AF.Exp,
                                        scale=halfpos))
        t_sv.wait(nc.vector)
        t_rstd.wait(nc.vector)
        c.v(nc.vector.tensor_mul(sv[:, :], sv[:, :], rstd[:, :]))
        c.v(nc.vector.tensor_scalar_add(sv[:, :], sv[:, :], 1e-8))
        t_sc = c.v(nc.vector.reciprocal(sv[:, :], sv[:, :]))
        t_rstd = c.v(nc.vector.tensor_mul(rstd[:, :], rstd[:, :], sv[:, :]))
    t_rstd.wait(nc.vector)
    t_nmr = c.v(nc.vector.scalar_tensor_tensor(
        nmr[:, :], mcol[:, :], -1.0, rstd[:, :], ALU.mult, ALU.mult))
    t_nmr.wait(nc.scalar)
    dst3 = ln_hbm.rearrange("(t p) f -> t p f", p=128)
    for t in range(NT):
        if t >= 2:
            stg_sems[t % 2].wait(nc.scalar)
        tok = c.s(nc.scalar.activation(
            stg[t % 2][:, :], src_sb[:, t * D:(t + 1) * D], AF.Identity,
            scale=rstd[:, t:t + 1], bias=nmr[:, t:t + 1]))
        tok.wait(nc.sync)
        st = nc.sync.dma_start(dst3[t, :, :], stg[t % 2][:, :])
        stg_sems[t % 2].inc(st, 16)
        s_store.v += 16
    stg_sems[0].wait(nc.sync)
    stg_sems[1].wait(nc.sync)
    for ki in range(ND):
        s_tp.inc(nc.sync.dma_start_transpose(
            lnT[:, ki * T:(ki + 1) * T], ln_hbm[:, ki * 128:(ki + 1) * 128]), 16)
    out = LNOut(lnT, s_tp, None, None)
    out.rstd, out.nmr, out.var = rstd, nmr, ssq
    out.t_affine = t_nmr
    return out



class FfnShared:

    def __init__(self, c, dff, dout=D, tb=512, h_rms=False):
        self.dff, self.tb = dff, tb
        self.nh = 2
        self.fh = dff // 2
        self.kh = self.fh // 128
        self.wtileA = c.sb([128, ND * self.fh], BF16)
        self.wtileB = c.sb([128, self.kh * dout], BF16)
        self.tokA = None
        self.hTr = c.sbring(1 if h_rms else 2, [128, self.kh * tb], BF16)
        self.hps = c.ring(2, [128, 512])
        self.yps = c.ring(2, [128, dout])
        self.ones, self.s_ones = emit_ones(c)
        if h_rms:
            self.onesc = c.sb([128, 1], BF16)
            self.s_onesc = c.v(c.nc.vector.memset(self.onesc[:, 0:1], 1.0))
            self.h2sc = c.sbring(2, [128, tb], BF16)
            self.l2ps = c.ring(2, [1, tb])
            self.l2a = c.sb([1, T])
            self.l2row = c.sb([1, tb])
            self.l2col = c.sb([128, NT])
            self.sc2 = c.sb([128, NT])
            self.l2_hbm = c.dram([T])
            self.y_hbm = c.dram([T, D])
            self.ystg = c.sbring(2, [128, D], F32)
            self.s_yst = c.sem()


def emit_ffn_expert(c, xres, s_x, ln, sh, W1sl, W2sl, *, act=AF.Relu,
                    b1col=None, s_b1=None, b2row=None, s_b2=None,
                    wcol=None, s_wcol=None, h_rms=False, w_bf16=False):
    nc = c.nc
    dff, TB = sh.dff, sh.tb
    FH, KH = sh.fh, sh.kh
    NTB = T // TB
    tok_add = None
    for half in range(2):
        if sh.tokA is not None:
            sh.tokA.wait(nc.gpsimd)
        w1 = emit_load_weight(c, W1sl[:, half * FH:(half + 1) * FH], D, FH,
                              tile=sh.wtileA)
        w2 = emit_load_weight(c, W2sl[half * FH:(half + 1) * FH, :], FH, D,
                              tile=sh.wtileB)
        for tb in range(NTB):
            t0 = tb * TB
            hT = sh.hTr.acquire(nc.scalar)
            w1.sem.wait(nc.tensor)
            ln.sem.wait(nc.tensor)
            tok_ev = None
            if h_rms:
                lp = sh.l2ps.acquire(nc.tensor)
            for ko in range(KH):
                pt = sh.hps.acquire(nc.tensor)
                mm = mm_group(nc, pt[:, 0:TB],
                              [(w1.lhsT(ki, ko * 128, (ko + 1) * 128),
                                ln.kT(ki, t0, t0 + TB)) for ki in range(ND)])
                tok_mm = c.mark(mm, "tensor")
                tok_mm.wait(nc.scalar)
                if s_b1 is not None:
                    s_b1.wait(nc.scalar)
                kw = {}
                if b1col is not None:
                    kw["bias"] = b1col[:, half * KH + ko: half * KH + ko + 1]
                ev = nc.scalar.activation(
                    hT[:, ko * TB:(ko + 1) * TB], pt[:, 0:TB], act, **kw)
                tok_ev = c.mark(ev, "scalar")
                sh.hps.release(tok_ev)
                if h_rms:
                    h2 = sh.h2sc.acquire(nc.scalar)
                    t_h2 = c.mark(nc.scalar.activation(
                        h2[:, 0:TB], hT[:, ko * TB:(ko + 1) * TB], AF.Square),
                        "scalar")
                    t_h2.wait(nc.tensor)
                    sh.s_onesc.wait(nc.tensor)
                    mm2 = nc.tensor.matmul(lp[0:1, 0:TB], sh.onesc[:, 0:1],
                                           h2[:, 0:TB], start=(ko == 0),
                                           stop=(ko == KH - 1))
                    t_mm2 = c.mark(mm2, "tensor")
                    sh.h2sc.release(t_mm2)
            if h_rms:
                t_mm2.wait(nc.vector)
                if half == 0:
                    t_l2 = c.mark(nc.vector.tensor_copy(
                        sh.l2a[0:1, t0:t0 + TB], lp[0:1, 0:TB]), "vector")
                    sh.l2ps.release(t_l2)
                else:
                    t_l2 = c.mark(nc.vector.tensor_add(
                        sh.l2row[0:1, 0:TB], sh.l2a[0:1, t0:t0 + TB],
                        lp[0:1, 0:TB]), "vector")
                    sh.l2ps.release(t_l2)
                    t_l2.wait(nc.sync)
                    s_st = c.dsem("ffl2st")
                    s_st.inc(nc.sync.dma_start(
                        sh.l2_hbm[t0:t0 + TB][None, :], sh.l2row[0:1, 0:TB]), 16)
                    s_st.wait(nc.sync)
                    s_ld = c.dsem("ffl2ld")
                    ntb_t = TB // 128
                    src = sh.l2_hbm[t0:t0 + TB].rearrange("(t p) -> p t", p=128)
                    s_ld.inc(nc.sync.dma_start(
                        sh.l2col[:, tb * ntb_t:(tb + 1) * ntb_t], src,
                        allow_slow_non_contiguous=True), 16)
                    cdff = c.constcol(1.0 / dff)
                    chalf = c.constcol(0.5)
                    s_ld.wait(nc.scalar)
                    c.wait_const(nc.scalar)
                    sl = slice(tb * ntb_t, (tb + 1) * ntb_t)
                    c.s(nc.scalar.activation(sh.sc2[:, sl], sh.l2col[:, sl],
                                             AF.Ln, scale=cdff))
                    t_s1 = c.mark(nc.scalar.activation(
                        sh.sc2[:, sl], sh.sc2[:, sl], AF.Exp,
                        scale=chalf), "scalar")
                    t_s1.wait(nc.vector)
                    c.v(nc.vector.tensor_scalar_add(sh.sc2[:, sl],
                                                    sh.sc2[:, sl], 1e-8))
                    t_sc2 = c.mark(nc.vector.reciprocal(sh.sc2[:, sl],
                                                        sh.sc2[:, sl]), "vector")
            w2.sem.wait(nc.tensor)
            sh.s_ones.wait(nc.tensor)
            if s_b2 is not None:
                s_b2.wait(nc.tensor)
            tok_ev.wait(nc.tensor)
            for st in range(TB // 128):
                t = (t0 + st * 128) // 128
                pt = sh.yps.acquire(nc.tensor)
                for b in range(2):
                    o0, o1 = b * 512, (b + 1) * 512
                    pairs = [(hT[:, ko * TB + st * 128: ko * TB + (st + 1) * 128],
                              w2.lhsT(ko, o0, o1)) for ko in range(KH)]
                    if half == 0 and b2row is not None:
                        pairs.append((sh.ones[0:1, 0:128], b2row[0:1, o0:o1]))
                    mm = mm_group(nc, pt[:, o0:o1], pairs)
                    tok_mm2 = c.mark(mm, "tensor")
                if h_rms and half == 0:
                    tok_mm2.wait(nc.vector)
                    ystg = sh.ystg.acquire(nc.vector)
                    tok_add = c.mark(nc.vector.tensor_copy(
                        ystg[:, :], pt[:, 0:D]), "vector")
                    sh.yps.release(tok_add)
                    tok_add.wait(nc.sync)
                    yd = sh.y_hbm.rearrange("(t p) f -> t p f", p=128)
                    styd = nc.sync.dma_start(yd[t, :, :], ystg[:, :])
                    sh.s_yst.inc(styd, 16)
                    sh.ystg.release(Tok(sh.s_yst.h, sh.s_yst.v))
                else:
                    tok_mm2.wait(nc.vector)
                    s_x.wait(nc.vector)
                    if h_rms:
                        ystg = sh.ystg.acquire(nc.sync)
                        sh.s_yst.wait(nc.sync)
                        yd = sh.y_hbm.rearrange("(t p) f -> t p f", p=128)
                        s_yld = c.dsem("ffyld")
                        s_yld.inc(nc.sync.dma_start(ystg[:, :], yd[t, :, :]), 16)
                        s_yld.wait(nc.vector)
                        c.v(nc.vector.tensor_add(
                            ystg[:, :], pt[:, 0:D], ystg[:, :]))
                        tok_add = c.v(nc.vector.scalar_tensor_tensor(
                            xres[:, t * D:(t + 1) * D], ystg[:, :],
                            sh.sc2[:, t:t + 1], xres[:, t * D:(t + 1) * D],
                            ALU.mult, ALU.add))
                        sh.ystg.release(tok_add)
                    elif wcol is not None:
                        if s_wcol is not None:
                            s_wcol.wait(nc.vector)
                        tok_add = c.mark(nc.vector.scalar_tensor_tensor(
                            xres[:, t * D:(t + 1) * D], pt[:, 0:D],
                            wcol(t), xres[:, t * D:(t + 1) * D],
                            ALU.mult, ALU.add), "vector")
                    else:
                        tok_add = c.mark(nc.vector.tensor_add(
                            xres[:, t * D:(t + 1) * D],
                            xres[:, t * D:(t + 1) * D], pt[:, 0:D]), "vector")
                    sh.yps.release(tok_add)
            sh.hTr.release(tok_mm2)
        sh.tokA = tok_mm2
    return tok_add



def emit_ternarize(c, W_h, din, dout, Wq_hbm, after=()):
    nc = c.nc
    nki = din // 128
    CH = max(1, 4096 // dout)
    nblk = nki // CH
    wf = c.sbring(2, [128, CH * dout], F32)
    m1 = c.sbring(2, [128, CH * dout], BF16)
    m2 = c.sbring(2, [128, CH * dout], BF16)
    red = c.sb([128, 2])
    alps = c.ps([1, 1])
    albc = c.ps([128, 1])
    onef = c.sb([128, 1])
    onef2 = c.sb([1, 128])
    thr = c.sb([128, 2])
    c.v(nc.vector.memset(onef[:, 0:1], 1.0))
    s_onef = c.v(nc.vector.memset(onef2[0:1, :], 1.0))
    c.v(nc.vector.memset(red[:, 0:1], 0.0))
    src3 = W_h.rearrange("(blk q) o -> blk q o", q=128 * CH)
    src4 = W_h.rearrange("(blk ki p) o -> blk p ki o", p=128, ki=CH)
    for s_ in after:
        s_.wait(c.nc.gpsimd)
    s_ld = c.sem()
    toks = []
    for blk in range(nblk):
        w = wf.acquire(nc.gpsimd)
        s_ld.inc(nc.gpsimd.dma_start(
            w[:, :].rearrange("p (ki o) -> p ki o", ki=CH), src4[blk]), 16)
        s_ld.wait(nc.vector)
        t = c.v(nc.vector.reduce_sum(red[:, 1:2], w[:, :], axis=AX.X,
                                     apply_absolute_value=True))
        t = c.v(nc.vector.tensor_add(red[:, 0:1], red[:, 0:1], red[:, 1:2]))
        wf.release(t)
    t_red = t
    t_red.wait(nc.tensor)
    s_onef.wait(nc.tensor)
    t_al = c.mark(nc.tensor.matmul(alps[0:1, 0:1], onef[:, 0:1], red[:, 0:1],
                                   start=True, stop=True), "tensor")
    t_al.wait(nc.vector)
    t_cp = c.v(nc.vector.tensor_copy(thr[0:1, 0:1], alps[0:1, 0:1]))
    t_cp.wait(nc.tensor)
    t_bc = c.mark(nc.tensor.matmul(albc[:, 0:1], onef2[0:1, :], thr[0:1, 0:1],
                                   start=True, stop=True), "tensor")
    halfn = c.constcol(0.5 / (din * dout))
    neg1 = c.constcol(-1.0)
    t_bc.wait(nc.scalar)
    c.wait_const(nc.scalar)
    c.s(nc.scalar.activation(thr[:, 0:1], albc[:, 0:1], AF.Identity,
                             scale=halfn))
    t_thr = c.s(nc.scalar.activation(thr[:, 1:2], thr[:, 0:1], AF.Identity,
                                     scale=neg1))
    t_thr.wait(nc.vector)
    s_store = c.sem()
    dst = Wq_hbm.rearrange("(blk ki p) o -> blk p ki o", p=128, ki=CH)
    for blk in range(nblk):
        w = wf.acquire(nc.gpsimd)
        s_ld.inc(nc.gpsimd.dma_start(
            w[:, :].rearrange("p (ki o) -> p ki o", ki=CH), src4[blk]), 16)
        s_ld.wait(nc.vector)
        a = m1.acquire(nc.vector)
        c.v(nc.vector.tensor_single_scalar(a[:, :], w[:, :], thr[:, 0:1],
                                           ALU.is_gt))
        b = m2.acquire(nc.vector)
        c.v(nc.vector.tensor_single_scalar(b[:, :], w[:, :], thr[:, 1:2],
                                           ALU.is_lt))
        m2.release(c.tip("vector"))
        t_q = c.v(nc.vector.tensor_sub(a[:, :], a[:, :], b[:, :]))
        wf.release(t_q)
        t_q.wait(nc.sync)
        st = nc.sync.dma_start(
            dst[blk], a[:, :].rearrange("p (ki o) -> p ki o", ki=CH))
        s_store.inc(st, 16)
        m1.release(Tok(s_store.h, s_store.v))
    return s_store



def col_layout(v):
    n = v.shape[0]
    return np.ascontiguousarray(v.reshape(n // 128, 128).T).astype(np.float32)


def build_l2ffn_program():
    nc = bass.Bass("TRN2", target_bir_lowering=False)
    stack = ExitStack()
    c = Ctx(nc, stack)
    x_h = nc.dram_tensor("x", [T, D], F32, kind="ExternalInput").ap()
    W1_h = nc.dram_tensor("W1", [D, 4096], F32, kind="ExternalInput").ap()
    b1c_h = nc.dram_tensor("b1c", [128, 32], F32, kind="ExternalInput").ap()
    W2_h = nc.dram_tensor("W2", [4096, D], F32, kind="ExternalInput").ap()
    b2_h = nc.dram_tensor("b2", [D], F32, kind="ExternalInput").ap()
    out_h = nc.dram_tensor("out", [T, D], F32, kind="ExternalOutput").ap()
    with stack:
        xres = c.sb([128, NT * D], F32)
        s_x = emit_load_x(c, x_h, xres)
        ln = emit_ln_transpose(c, xres, s_x)
        b2row, s_b2 = emit_load_bf16_row(c, b2_h, D)
        b1col, s_b1 = emit_load_f32(c, b1c_h, 128, 32)
        sh = FfnShared(c, 4096)
        tok = emit_ffn_expert(c, xres, s_x, ln, sh, W1_h, W2_h, act=AF.Relu,
                              b1col=b1col, s_b1=s_b1, b2row=b2row, s_b2=s_b2)
        emit_store_x(c, xres, out_h, [tok])
    return nc


def build_l1ffn_program():
    nc = bass.Bass("TRN2", target_bir_lowering=False)
    stack = ExitStack()
    c = Ctx(nc, stack)
    x_h = nc.dram_tensor("x", [T, D], F32, kind="ExternalInput").ap()
    Wu_h = nc.dram_tensor("Wu", [D, 4096], F32, kind="ExternalInput").ap()
    Wd_h = nc.dram_tensor("Wd", [4096, D], F32, kind="ExternalInput").ap()
    out_h = nc.dram_tensor("out", [T, D], F32, kind="ExternalOutput").ap()
    with stack:
        Wq1 = c.dram([D, 4096], BF16)
        Wq2 = c.dram([4096, D], BF16)
        with c.sub() as ct:
            s_t1 = emit_ternarize(ct, Wu_h, D, 4096, Wq1)
        with c.sub() as ct:
            s_t2 = emit_ternarize(ct, Wd_h, 4096, D, Wq2, after=[s_t1])
        xres = c.sb([128, NT * D], F32)
        s_x = emit_load_x(c, x_h, xres, after=[s_t1, s_t2])
        ln = emit_ln_transpose(c, xres, s_x, rms_compose=True)
        sh = FfnShared(c, 4096, h_rms=True, tb=256)
        s_t1.wait(nc.gpsimd)
        s_t2.wait(nc.gpsimd)
        tok = emit_ffn_expert(c, xres, s_x, ln, sh, Wq1, Wq2, act=AF.Gelu,
                              h_rms=True, w_bf16=True)
        emit_store_x(c, xres, out_h, [tok])
    return nc



def emit_l3_gating(c, ln, Wgt_h, bgt_h):
    nc = c.nc
    w = emit_load_weight(c, Wgt_h, D, 9)
    bgt, s_b = emit_load_bf16_row(c, bgt_h, 9)
    ones, t_ones = emit_ones(c)
    wgts = c.sb([128, NT * 8])
    gps = c.ring(2, [128, 9])
    mx = c.sb([128, 4])
    gat = c.sb([128, 8])
    tok = None
    for t in range(NT):
        pt = gps.acquire(nc.tensor)
        w.sem.wait(nc.tensor)
        ln.sem.wait(nc.tensor)
        s_b.wait(nc.tensor)
        t_ones.wait(nc.tensor)
        pairs = [(ln.lnT[:, ki * T + t * 128: ki * T + (t + 1) * 128],
                  w.lhsT(ki, 0, 9)) for ki in range(ND)]
        pairs.append((ones[0:1, 0:128], bgt[0:1, 0:9]))
        mm = mm_group(nc, pt[:, 0:9], pairs)
        t_mm = c.mark(mm, "tensor")
        t_mm.wait(nc.vector)
        t_mx = c.v(nc.vector.reduce_max(mx[:, 0:1], pt[:, 0:8], axis=AX.X,
                                        negate=True))
        t_mx.wait(nc.scalar)
        t_e8 = c.s(nc.scalar.activation(gat[:, 0:8], pt[:, 0:8], AF.Exp,
                                        bias=mx[:, 0:1],
                                        accum_out=mx[:, 1:2]))
        t_th = c.s(nc.scalar.activation(mx[:, 2:3], pt[:, 8:9], AF.Sigmoid))
        gps.release(t_th)
        t_e8.wait(nc.vector)
        t_th.wait(nc.vector)
        c.v(nc.vector.reciprocal(mx[:, 1:2], mx[:, 1:2]))
        c.v(nc.vector.tensor_scalar_mul(gat[:, 0:8], gat[:, 0:8], mx[:, 1:2]))
        c.v(nc.vector.tensor_scalar_mul(mx[:, 2:3], mx[:, 2:3], 0.125))
        c.v(nc.vector.tensor_scalar(gat[:, 0:8], gat[:, 0:8], mx[:, 2:3], 0.0,
                                    ALU.subtract, ALU.max))
        c.v(nc.vector.reduce_sum(mx[:, 3:4], gat[:, 0:8], axis=AX.X))
        c.v(nc.vector.tensor_single_scalar(mx[:, 1:2], mx[:, 3:4], 0.0,
                                           ALU.is_equal))
        c.v(nc.vector.tensor_add(mx[:, 3:4], mx[:, 3:4], mx[:, 1:2]))
        c.v(nc.vector.reciprocal(mx[:, 3:4], mx[:, 3:4]))
        tok = c.v(nc.vector.tensor_scalar_mul(
            wgts[:, t * 8:(t + 1) * 8], gat[:, 0:8], mx[:, 3:4]))
    return wgts, tok


def build_l3ffn_program():
    nc = bass.Bass("TRN2", target_bir_lowering=False)
    stack = ExitStack()
    c = Ctx(nc, stack)
    x_h = nc.dram_tensor("x", [T, D], F32, kind="ExternalInput").ap()
    Wgt_h = nc.dram_tensor("Wgt", [D, 9], F32, kind="ExternalInput").ap()
    bgt_h = nc.dram_tensor("bgt", [9], F32, kind="ExternalInput").ap()
    W1_h = nc.dram_tensor("W1", [8, D, 4096], F32, kind="ExternalInput").ap()
    b1c_h = nc.dram_tensor("b1c", [128, 8 * 32], F32, kind="ExternalInput").ap()
    W2_h = nc.dram_tensor("W2", [8, 4096, D], F32, kind="ExternalInput").ap()
    b2r_h = nc.dram_tensor("b2r", [8 * D], F32, kind="ExternalInput").ap()
    out_h = nc.dram_tensor("out", [T, D], F32, kind="ExternalOutput").ap()
    with stack:
        xres = c.sb([128, NT * D], F32)
        s_x = emit_load_x(c, x_h, xres)
        ln = emit_ln_transpose(c, xres, s_x)
        wgts, t_w = emit_l3_gating(c, ln, Wgt_h, bgt_h)
        b1col, s_b1 = emit_load_f32(c, b1c_h, 128, 8 * 32)
        b2stg = Ring([c.sb([1, D], BF16) for _ in range(2)])
        sh = FfnShared(c, 4096)
        tok = None
        for e in range(8):
            b2t = b2stg.acquire(nc.gpsimd)
            s_b2 = c.dsem("l3b2")
            s_b2.inc(nc.gpsimd.dma_start(b2t[0:1, :],
                                         b2r_h[e * D:(e + 1) * D][None, :]), 16)
            tok = emit_ffn_expert(
                c, xres, s_x, ln, sh, W1_h[e], W2_h[e],
                act=AF.Gelu_apprx_tanh,
                b1col=b1col[:, e * 32:(e + 1) * 32], s_b1=s_b1,
                b2row=b2t[0:1, :], s_b2=s_b2,
                wcol=lambda t, e=e: wgts[:, t * 8 + e: t * 8 + e + 1],
                s_wcol=None)
            b2stg.release(sh.tokA)
        emit_store_x(c, xres, out_h, [tok])
    return nc



def emit_l0_router(c, ln, Wr_h, br_h):
    nc = c.nc
    w = emit_load_weight(c, Wr_h, D, 16)
    br, s_b = emit_load_bf16_row(c, br_h, 16)
    ones, t_ones = emit_ones(c)
    comb = c.sb([128, NT * 16])
    rps = c.ring(2, [128, 16])
    mx = c.sb([128, 6])
    prb = c.sb([128, 16])
    msk = c.sb([128, 16])
    tok = None
    for t in range(NT):
        pt = rps.acquire(nc.tensor)
        w.sem.wait(nc.tensor)
        ln.sem.wait(nc.tensor)
        s_b.wait(nc.tensor)
        t_ones.wait(nc.tensor)
        pairs = [(ln.lnT[:, ki * T + t * 128: ki * T + (t + 1) * 128],
                  w.lhsT(ki, 0, 16)) for ki in range(ND)]
        pairs.append((ones[0:1, 0:128], br[0:1, 0:16]))
        t_mm = c.mark(mm_group(nc, pt[:, 0:16], pairs), "tensor")
        t_mm.wait(nc.vector)
        t_mx = c.v(nc.vector.reduce_max(mx[:, 0:1], pt[:, 0:16], axis=AX.X,
                                        negate=True))
        t_mx.wait(nc.scalar)
        t_e = c.s(nc.scalar.activation(prb[:, 0:16], pt[:, 0:16], AF.Exp,
                                       bias=mx[:, 0:1], accum_out=mx[:, 1:2]))
        rps.release(t_e)
        t_e.wait(nc.vector)
        c.v(nc.vector.reciprocal(mx[:, 1:2], mx[:, 1:2]))
        c.v(nc.vector.tensor_scalar_mul(prb[:, 0:16], prb[:, 0:16],
                                        mx[:, 1:2]))
        c.v(nc.vector.reduce_max(mx[:, 2:3], prb[:, 0:16], axis=AX.X))
        c.v(nc.vector.tensor_single_scalar(msk[:, 0:16], prb[:, 0:16],
                                           mx[:, 2:3], ALU.is_ge))
        c.v(nc.vector.scalar_tensor_tensor(msk[:, 0:16], msk[:, 0:16], -2.0,
                                           prb[:, 0:16], ALU.mult, ALU.add))
        c.v(nc.vector.reduce_max(mx[:, 3:4], msk[:, 0:16], axis=AX.X))
        c.v(nc.vector.tensor_single_scalar(msk[:, 0:16], prb[:, 0:16],
                                           mx[:, 3:4], ALU.is_ge))
        c.v(nc.vector.tensor_add(mx[:, 4:5], mx[:, 2:3], mx[:, 3:4]))
        c.v(nc.vector.tensor_scalar_add(mx[:, 4:5], mx[:, 4:5], 1e-6))
        c.v(nc.vector.reciprocal(mx[:, 4:5], mx[:, 4:5]))
        c.v(nc.vector.tensor_mul(msk[:, 0:16], msk[:, 0:16], prb[:, 0:16]))
        tok = c.v(nc.vector.tensor_scalar_mul(
            comb[:, t * 16:(t + 1) * 16], msk[:, 0:16], mx[:, 4:5]))
    return comb, tok


def emit_l0_gated(c, xres, s_x, ln, Wg_sl, Wu_sl, Wd_sl, *, conv_taps=None,
                  wcol=None, tiles=None, I=512):
    nc = c.nc
    KH = I // 128
    TB = 512
    NTB = T // TB
    wgt, wut, wdt, tok_prev = tiles
    if tok_prev is not None:
        tok_prev.wait(nc.gpsimd)
    if conv_taps is not None:
        ws = [emit_load_weight(c, W_h, D, I, tile=tl)
              for (W_h, sh), tl in zip(conv_taps, (wgt, wut, wdt))]
        wd = emit_load_weight(c, Wd_sl, I, D, tile=c.l0_wd2)
        wu = emit_load_weight(c, Wu_sl, D, I, tile=c.l0_wu2)
    else:
        wg = emit_load_weight(c, Wg_sl, D, I, tile=wgt)
        wu = emit_load_weight(c, Wu_sl, D, I, tile=wut)
        wd = emit_load_weight(c, Wd_sl, I, D, tile=wdt)
    hT = c.l0_hT
    gs = c.l0_gscr
    tok_mm2 = None
    for tb in range(NTB):
        t0 = tb * TB
        if c.l0_htok is not None:
            c.l0_htok.wait(nc.vector)
        ln.sem.wait(nc.tensor)
        wu.sem.wait(nc.tensor)
        wd.sem.wait(nc.tensor)
        tok_ev = None
        for ko in range(KH):
            pg = c.l0_gps.acquire(nc.tensor)
            if conv_taps is not None:
                for wi in ws:
                    wi.sem.wait(nc.tensor)
                order = sorted(range(3), key=lambda k: conv_taps[k][1] != 0)
                triples = []
                for k in order:
                    shf = conv_taps[k][1]
                    lo = max(t0 + shf, 0)
                    hi = t0 + TB + shf
                    off = lo - (t0 + shf)
                    for ki in range(ND):
                        triples.append((pg[:, off:TB],
                                        ws[k].lhsT(ki, ko * 128,
                                                   (ko + 1) * 128),
                                        ln.lnT[:, ki * T + lo: ki * T + hi]))
                for j, (o_, l_, r_) in enumerate(triples):
                    mm = nc.tensor.matmul(o_, l_, r_, start=(j == 0),
                                          stop=(j == len(triples) - 1))
                t_g = c.mark(mm, "tensor")
            else:
                wg.sem.wait(nc.tensor)
                t_g = c.mark(mm_group(nc, pg[:, 0:TB],
                             [(wg.lhsT(ki, ko * 128, (ko + 1) * 128),
                               ln.kT(ki, t0, t0 + TB)) for ki in range(ND)]),
                             "tensor")
            pu = c.l0_ups.acquire(nc.tensor)
            t_u = c.mark(mm_group(nc, pu[:, 0:TB],
                         [(wu.lhsT(ki, ko * 128, (ko + 1) * 128),
                           ln.kT(ki, t0, t0 + TB)) for ki in range(ND)]),
                         "tensor")
            t_g.wait(nc.scalar)
            gt = gs.acquire(nc.scalar)
            t_sg = c.s(nc.scalar.activation(gt[:, 0:TB], pg[:, 0:TB], AF.Silu))
            c.l0_gps.release(t_sg)
            t_u.wait(nc.vector)
            t_sg.wait(nc.vector)
            tok_ev = c.v(nc.vector.tensor_mul(
                hT[:, ko * TB:(ko + 1) * TB], gt[:, 0:TB], pu[:, 0:TB]))
            gs.release(tok_ev)
            c.l0_ups.release(tok_ev)
        tok_ev.wait(nc.tensor)
        for st in range(TB // 128):
            t = (t0 + st * 128) // 128
            pt = c.l0_yps.acquire(nc.tensor)
            for b in range(2):
                o0, o1 = b * 512, (b + 1) * 512
                pairs = [(hT[:, ko * TB + st * 128: ko * TB + (st + 1) * 128],
                          wd.lhsT(ko, o0, o1)) for ko in range(KH)]
                mm = mm_group(nc, pt[:, o0:o1], pairs)
                tok_mm2 = c.mark(mm, "tensor")
            tok_mm2.wait(nc.vector)
            s_x.wait(nc.vector)
            if wcol is not None:
                tok_add = c.v(nc.vector.scalar_tensor_tensor(
                    xres[:, t * D:(t + 1) * D], pt[:, 0:D], wcol(t),
                    xres[:, t * D:(t + 1) * D], ALU.mult, ALU.add))
            else:
                tok_add = c.v(nc.vector.tensor_add(
                    xres[:, t * D:(t + 1) * D], xres[:, t * D:(t + 1) * D],
                    pt[:, 0:D]))
            c.l0_yps.release(tok_add)
        c.l0_htok = tok_mm2
    return tok_mm2, tok_add


def emit_l0_ffn(c, xres, s_x, ln, io):
    nc = c.nc
    comb, t_comb = emit_l0_router(c, ln, io["router_w"], io["router_bias"])
    I = 512
    KH = I // 128
    TB = 512
    c.l0_hT = c.sb([128, KH * TB], BF16)
    c.l0_gscr = Ring([c.sb([128, TB], BF16) for _ in range(2)])
    c.l0_gps = c.ring(2, [128, TB])
    c.l0_ups = c.ring(2, [128, TB])
    c.l0_yps = c.ring(1, [128, D])
    c.l0_htok = None
    wgt = c.sb([128, ND * I], BF16)
    wut = c.sb([128, ND * I], BF16)
    wdt = c.sb([128, KH * D], BF16)
    c.l0_wd2 = c.sb([128, KH * D], BF16)
    c.l0_wu2 = c.sb([128, ND * I], BF16)
    tok_prev = None
    tok_add = None
    for e in range(15):
        tok_prev, tok_add = emit_l0_gated(
            c, xres, s_x, ln, io["gate_w"][e], io["up_w"][e], io["down_w"][e],
            wcol=lambda t, e=e: comb[:, t * 16 + e: t * 16 + e + 1],
            tiles=(wgt, wut, wdt, tok_prev))
    lnscr = c.sb([128, D], BF16)
    for t in range(NT):
        ln.t_affine.wait(nc.scalar)
        t_l = c.s(nc.scalar.activation(
            lnscr[:, :], xres[:, t * D:(t + 1) * D], AF.Identity,
            scale=ln.rstd[:, t:t + 1], bias=ln.nmr[:, t:t + 1]))
        t_l.wait(nc.vector)
        tok_add = c.v(nc.vector.scalar_tensor_tensor(
            xres[:, t * D:(t + 1) * D], lnscr[:, :],
            comb[:, t * 16 + 15: t * 16 + 16],
            xres[:, t * D:(t + 1) * D], ALU.mult, ALU.add))
        tok_add.wait(nc.scalar)
    tok_prev, tok_add = emit_l0_gated(
        c, xres, s_x, ln, None, io["shared_up"], io["shared_down"],
        conv_taps=[(io["conv_w0"], -2), (io["conv_w1"], -1),
                   (io["conv_w2"], 0)],
        wcol=None, tiles=(wgt, wut, wdt, tok_prev))
    return tok_add



def emit_bounceT(c, src_sb, src_toks, dout, outT=None, name=None):
    nc = c.nc
    hbm = c.dram([T, dout], BF16)
    nko = dout // 128
    outT = outT if outT is not None else c.sb([128, nko * T], BF16)
    s_st = c.sem()
    s_tp = c.sem()
    dst3 = hbm.rearrange("(t p) f -> t p f", p=128)
    src3 = src_sb[:, :].rearrange("p (t f) -> p t f", t=NT)
    toks = src_toks if isinstance(src_toks, list) else [src_toks] * NT
    for t in range(NT):
        toks[t].wait(nc.sync)
        s_st.inc(nc.sync.dma_start(dst3[t, :, :], src3[:, t, :]), 16)
    s_st.wait(nc.sync)
    for ko in range(nko):
        s_tp.inc(nc.sync.dma_start_transpose(
            outT[:, ko * T:(ko + 1) * T], hbm[:, ko * 128:(ko + 1) * 128]), 16)
    return outT, s_tp



class MhaShared:
    def __init__(self, c, H, dh, dv_total, after=()):
        self.H, self.dh = H, dh
        self.dvt = dv_total
        for s_ in after:
            s_.wait(c.nc.gpsimd)
            s_.wait(c.nc.vector)
        self.sps = c.ring(1, [128, T])
        self.tps = c.ring(1, [128, 128], BF16)
        self.ops_ = c.ring(1, [128, dv_total])
        self.P = c.sb([128, T], BF16)
        self.PT = c.sb([128, T], BF16)
        self.mxl = c.sb([128, 4])
        self.ident = c.sb([128, 128], BF16)
        self.iscr = c.sb([128, 128])
        t_i = c.g(c.nc.gpsimd.iota(self.iscr[:, :], pattern=[[1, 128]], base=0,
                                   channel_multiplier=-1,
                                   allow_small_or_imprecise_dtypes=True))
        t_i.wait(c.nc.vector)
        self.t_ident = c.v(c.nc.vector.tensor_single_scalar(
            self.ident[:, :], self.iscr[:, :], 0.0, ALU.is_equal))
        self.maskU = c.sb([128, 128])
        t_m = c.v(c.nc.vector.tensor_single_scalar(
            self.maskU[:, :], self.iscr[:, :], 0.0, ALU.is_gt))
        self.t_maskU = c.v(c.nc.vector.tensor_scalar_mul(
            self.maskU[:, :], self.maskU[:, :], -1e30))


def emit_mha(c, sh, qT, kT, v_tok, out_tok, ti, s_qkv, *, causal,
             pre_hook=None, scale=1.0, scale_ap=None, q_tile_mode=False,
             pieces_fn=None):
    nc = c.nc
    H, dh, dvt = sh.H, sh.dh, sh.dvt
    klen = (ti + 1) * 128 if causal else T
    ntk = klen // 128
    op = sh.ops_.acquire(nc.tensor)
    prev_ttp = None
    prev_av = None
    t_av = None
    for h in range(H):
        sp = sh.sps.acquire(nc.tensor)
        s_qkv.wait(nc.tensor)
        if pieces_fn is not None:
            pieces = pieces_fn(h)
        else:
            step = min(dh, 128)
            pieces = [((h * dh + r) // 128, (h * dh + r) % 128, step)
                      for r in range(0, dh, step)]
        for n0 in range(0, klen, 512):
            n1 = min(n0 + 512, klen)
            for j, (ki, base, nr) in enumerate(pieces):
                rows = slice(base, base + nr)
                if q_tile_mode:
                    q_ap = qT[:, ki * 128:(ki + 1) * 128][rows, :]
                else:
                    q_ap = qT[:, ki * T + ti * 128:
                              ki * T + (ti + 1) * 128][rows, :]
                kw2 = {}
                if pieces_fn is not None:
                    kw2["tile_position"] = (base, 0)
                mm = nc.tensor.matmul(
                    sp[:, n0:n1], q_ap,
                    kT[:, ki * T + n0: ki * T + n1][rows, :],
                    start=(j == 0), stop=(j == len(pieces) - 1), **kw2)
        t_s = c.mark(mm, "tensor")
        t_pre = None
        if pre_hook is not None:
            t_pre = pre_hook(h, ti, klen, sp, t_s)
        if causal:
            sh.t_maskU.wait(nc.vector)
            if t_pre is None:
                t_s.wait(nc.vector)
            t_pre = c.v(nc.vector.tensor_add(
                sp[:, klen - 128:klen], sp[:, klen - 128:klen],
                sh.maskU[:, :]))
        (t_pre or t_s).wait(nc.vector)
        t_mx = c.v(nc.vector.reduce_max(sh.mxl[:, 0:1], sp[:, 0:klen],
                                        axis=AX.X, negate=True))
        if scale_ap is not None:
            t_mx = c.v(nc.vector.tensor_scalar_mul(sh.mxl[:, 1:2],
                                                   sh.mxl[:, 0:1], scale_ap))
            bias_ap = sh.mxl[:, 1:2]
        elif scale != 1.0:
            t_mx = c.v(nc.vector.tensor_scalar_mul(sh.mxl[:, 1:2],
                                                   sh.mxl[:, 0:1], scale))
            bias_ap = sh.mxl[:, 1:2]
        else:
            bias_ap = sh.mxl[:, 0:1]
        t_mx.wait(nc.scalar)
        if prev_ttp is not None:
            prev_ttp.wait(nc.scalar)
        kw = {}
        if scale_ap is not None:
            kw["scale"] = scale_ap
        elif scale != 1.0:
            c.wait_const(nc.scalar)
            kw["scale"] = c.constcol(scale)
        t_exp = c.s(nc.scalar.activation(
            sh.P[:, 0:klen], sp[:, 0:klen], AF.Exp, bias=bias_ap,
            accum_out=sh.mxl[:, 2:3], **kw))
        sh.sps.release(t_exp)
        t_exp.wait(nc.vector)
        c.v(nc.vector.reciprocal(sh.mxl[:, 3:4], sh.mxl[:, 2:3]))
        t_pl = c.v(nc.vector.tensor_scalar_mul(
            sh.P[:, 0:klen], sh.P[:, 0:klen], sh.mxl[:, 3:4]))
        if getattr(c, "dbg_dump_P", None) is not None and h == 0 and ti == 0:
            dst_h, s_d = c.dbg_dump_P
            t_pl.wait(nc.gpsimd)
            s_d.inc(nc.gpsimd.dma_start(dst_h[0:128, 0:klen], sh.P[:, 0:klen]),
                    16)
        t_ptc = None
        for tk in range(ntk):
            tp = sh.tps.acquire(nc.tensor)
            t_pl.wait(nc.tensor)
            sh.t_ident.wait(nc.tensor)
            t_tp = c.mark(nc.tensor.transpose(
                tp[:, :], sh.P[:, tk * 128:(tk + 1) * 128], sh.ident[:, :]),
                "tensor")
            t_tp.wait(nc.scalar)
            if prev_av is not None:
                prev_av.wait(nc.scalar)
                prev_av = None
            t_ptc = c.s(nc.scalar.copy(sh.PT[:, tk * 128:(tk + 1) * 128],
                                       tp[:, :]))
            sh.tps.release(t_ptc)
        prev_ttp = t_tp
        t_ptc.wait(nc.tensor)
        if getattr(c, "dbg_dump_PT", None) is not None and h == 0 and ti == 0:
            dst_h, s_d = c.dbg_dump_PT
            t_ptc.wait(nc.gpsimd)
            s_d.inc(nc.gpsimd.dma_start(dst_h[0:128, 0:T], sh.PT[:, 0:T]), 16)
        for tk in range(ntk):
            for b0 in range(0, dh, 512):
                b1 = min(b0 + 512, dh)
                mm = nc.tensor.matmul(
                    op[:, h * dh + b0: h * dh + b1],
                    sh.PT[:, tk * 128:(tk + 1) * 128],
                    v_tok[:, tk * dvt + h * dh + b0: tk * dvt + h * dh + b1],
                    start=(tk == 0), stop=(tk == ntk - 1))
        t_av = c.mark(mm, "tensor")
        prev_av = t_av
        if getattr(c, "dbg_dump_OP", None) is not None and ti == 0 and \
                h == c.dbg_dump_OP[2]:
            dst_h, s_d, _hh = c.dbg_dump_OP
            scr_d = c.dbg_dump_scr
            t_av.wait(nc.vector)
            t_cpd = c.v(nc.vector.tensor_copy(scr_d[:, 0:dvt], op[:, 0:dvt]))
            t_cpd.wait(nc.gpsimd)
            s_d.inc(nc.gpsimd.dma_start(dst_h[0:128, 0:dvt], scr_d[:, 0:dvt]),
                    16)
    t_av.wait(nc.vector)
    if out_tok is None:
        t_out = sh.out_cb(op)
    else:
        t_out = c.v(nc.vector.tensor_copy(out_tok, op[:, 0:dvt]))
    sh.ops_.release(t_out)
    return t_out


def ceil_div(a, b):
    return (a + b - 1) // b



def build_l1attnb_program():
    nc = bass.Bass("TRN2", target_bir_lowering=False)
    stack = ExitStack()
    c = Ctx(nc, stack)
    x_h = nc.dram_tensor("x", [T, D], F32, kind="ExternalInput").ap()
    W_h = nc.dram_tensor("W", [D, 2 * D], F32, kind="ExternalInput").ap()
    out_h = nc.dram_tensor("out", [T, D], F32, kind="ExternalOutput").ap()
    with stack:
        xres = c.sb([128, NT * D], F32)
        s_x = emit_load_x(c, x_h, xres)
        ln = emit_ln_transpose(c, xres, s_x)
        tok = emit_l1_attn_b(c, xres, s_x, ln, W_h)
        emit_store_x(c, xres, out_h, [tok])
    return nc


def emit_l1_attn_b(c, xres, s_x, ln, W_h):
    nc = c.nc
    kT = c.sb([128, ND * T], BF16)
    v_tok = c.sb([128, NT * D], BF16)
    sub = c.sub()
    cs = sub.__enter__()
    wk = emit_load_weight(cs, W_h[:, 0:D], D, D)
    wv = emit_load_weight(cs, W_h[:, D:2 * D], D, D)
    kps = cs.ring(2, [128, 512])
    t_ev = None
    for ko in range(ND):
        for n0 in range(0, T, 512):
            pt = kps.acquire(nc.tensor)
            wk.sem.wait(nc.tensor)
            ln.sem.wait(nc.tensor)
            t_mm = c.mark(mm_group(nc, pt[:, 0:512],
                          [(wk.lhsT(ki, ko * 128, (ko + 1) * 128),
                            ln.kT(ki, n0, n0 + 512)) for ki in range(ND)]),
                          "tensor")
            t_mm.wait(nc.scalar)
            t_ev = c.s(nc.scalar.copy(kT[:, ko * T + n0: ko * T + n0 + 512],
                                      pt[:, 0:512]))
            kps.release(t_ev)
    t_kT = t_ev
    vps = cs.ring(2, [128, D])
    for t in range(NT):
        pt = vps.acquire(nc.tensor)
        wv.sem.wait(nc.tensor)
        ln.sem.wait(nc.tensor)
        for b in range(2):
            o0, o1 = b * 512, (b + 1) * 512
            mm = mm_group(nc, pt[:, o0:o1],
                          [(ln.lnT[:, ki * T + t * 128: ki * T + (t + 1) * 128],
                            wv.lhsT(ki, o0, o1)) for ki in range(ND)])
        t_mm = c.mark(mm, "tensor")
        t_mm.wait(nc.scalar)
        t_ev = c.s(nc.scalar.copy(v_tok[:, t * D:(t + 1) * D], pt[:, 0:D]))
        vps.release(t_ev)
    t_vt = t_ev
    sub.__exit__(None, None, None)
    sh = MhaShared(c, 1, D, D, after=[t_kT, t_vt])
    scr = c.sb([128, D])
    tok_add = None

    for ti in range(NT):
        def out_cb(op, ti=ti):
            t = ti
            s_x.wait(nc.vector)
            ln.t_affine.wait(nc.vector)
            c.v(nc.vector.scalar_tensor_tensor(
                scr[:, :], xres[:, t * D:(t + 1) * D], ln.rstd[:, t:t + 1],
                op[:, 0:D], ALU.mult, ALU.add))
            c.v(nc.vector.tensor_scalar_add(scr[:, :], scr[:, :],
                                            ln.nmr[:, t:t + 1]))
            return c.v(nc.vector.tensor_add(
                xres[:, t * D:(t + 1) * D], xres[:, t * D:(t + 1) * D],
                scr[:, :]))
        sh.out_cb = out_cb
        t_kT.wait(nc.tensor)
        t_vt.wait(nc.tensor)
        tok_add = emit_mha(c, sh, ln.lnT, kT, v_tok, None, ti, s_x,
                           causal=True)
    return tok_add



def emit_proj_fm(c, rhsT, s_rhs, w, outT, *, bias_col=None, s_bias=None,
                 act=None, psr=None):
    nc = c.nc
    psr = psr or c.ring(2, [128, 512])
    nko = w.dout // 128
    t_ev = None
    for ko in range(nko):
        for n0 in range(0, T, 512):
            pt = psr.acquire(nc.tensor)
            w.sem.wait(nc.tensor)
            s_rhs.wait(nc.tensor)
            t_mm = c.mark(mm_group(nc, pt[:, 0:512],
                          [(w.lhsT(ki, ko * 128, (ko + 1) * 128),
                            rhsT[:, ki * T + n0: ki * T + n0 + 512])
                           for ki in range(w.nki)]), "tensor")
            t_mm.wait(nc.scalar)
            if s_bias is not None:
                s_bias.wait(nc.scalar)
            kw = {}
            if bias_col is not None:
                kw["bias"] = bias_col[:, ko:ko + 1]
            t_ev = c.s(nc.scalar.activation(
                outT[:, ko * T + n0: ko * T + n0 + 512], pt[:, 0:512],
                act or AF.Identity, **kw))
            psr.release(t_ev)
    return t_ev


def emit_proj_tm(c, lhsT_fn, s_lhs, w, out_tok, *, bias_row=None, s_bias=None,
                 ones=None, psr=None, act=None):
    nc = c.nc
    dout = w.dout
    psr = psr or c.ring(2, [128, dout])
    t_ev = None
    for t in range(NT):
        pt = psr.acquire(nc.tensor)
        w.sem.wait(nc.tensor)
        s_lhs.wait(nc.tensor)
        if s_bias is not None:
            s_bias.wait(nc.tensor)
        for o0 in range(0, dout, 512):
            o1 = min(o0 + 512, dout)
            pairs = [(lhsT_fn(ki, t), w.lhsT(ki, o0, o1))
                     for ki in range(w.nki)]
            if bias_row is not None:
                pairs.append((ones[0:1, 0:128], bias_row[0:1, o0:o1]))
            mm = mm_group(nc, pt[:, o0:o1], pairs)
        t_mm = c.mark(mm, "tensor")
        t_mm.wait(nc.scalar)
        t_ev = c.s(nc.scalar.activation(out_tok[:, t * dout:(t + 1) * dout],
                                        pt[:, 0:dout], act or AF.Identity))
        psr.release(t_ev)
    return t_ev



class DStream:

    def __init__(self, c, dout, hbm=None):
        self.c = c
        self.dout = dout
        self.nki = dout // 128
        self.hbm = hbm if hbm is not None else c.dram([T, dout], BF16)
        self.s_store = c.sem()
        self.h3 = self.hbm.rearrange("(t p) f -> t p f", p=128)

    def put(self, t, tok, src_ap):
        tok.wait(self.c.nc.sync)
        self.s_store.inc(self.c.nc.sync.dma_start(self.h3[t, :, :], src_ap),
                         16)

    def load_fm(self, dst, n0, n1, s_out, full=True):
        nc = self.c.nc
        w = n1 - n0
        if full:
            self.s_store.wait(nc.sync)
        for ki in range(self.nki):
            s_out.inc(nc.sync.dma_start_transpose(
                dst[:, ki * w:(ki + 1) * w],
                self.hbm[n0:n1, ki * 128:(ki + 1) * 128]), 16)
        return s_out


def emit_ln_stream(c, x_hbm, *, pe_fn=None, stats_out=None):
    nc = c.nc
    x3 = x_hbm.rearrange("(t p) f -> t p f", p=128)
    ssum = c.sb([128, NT])
    ssq = c.sb([128, NT])
    mcol = c.sb([128, NT])
    msq = c.sb([128, NT])
    rstd = c.sb([128, NT])
    nmr = c.sb([128, NT])
    sq_scr = c.sb([128, D])
    xr = c.sbring(3, [128, D], F32)
    stg = c.sbring(2, [128, D], BF16)
    ds = DStream(c, D)
    t_red = t_sq = None
    xtoks = []
    for t in range(NT):
        xt = xr.acquire(nc.sync)
        s_ld = c.dsem("lnx")
        s_ld.inc(nc.sync.dma_start(xt[:, :], x3[t, :, :]), 16)
        s_ld.wait(nc.vector)
        t_red = c.v(nc.vector.reduce_sum(ssum[:, t:t + 1], xt[:, :],
                                         axis=AX.X))
        s_ld.wait(nc.scalar)
        t_sq = c.s(nc.scalar.activation(sq_scr[:, :], xt[:, :], AF.Square,
                                        accum_out=ssq[:, t:t + 1]))
        xr.release(Multi([t_red, t_sq]))
    t_sq.wait(nc.vector)
    c.v(nc.vector.tensor_scalar_mul(mcol[:, :], ssum[:, :], 1.0 / D))
    t_m = c.v(nc.vector.tensor_scalar_mul(ssq[:, :], ssq[:, :], 1.0 / D))
    t_m.wait(nc.scalar)
    t_msq = c.s(nc.scalar.activation(msq[:, :], mcol[:, :], AF.Square))
    t_msq.wait(nc.vector)
    t_var = c.v(nc.vector.tensor_sub(ssq[:, :], ssq[:, :], msq[:, :]))
    epsc = c.constcol(1e-5)
    halfneg = c.constcol(-0.5)
    t_var.wait(nc.scalar)
    c.wait_const(nc.scalar)
    c.s(nc.scalar.activation(rstd[:, :], ssq[:, :], AF.Ln, bias=epsc))
    t_rstd = c.s(nc.scalar.activation(rstd[:, :], rstd[:, :], AF.Exp,
                                      scale=halfneg))
    t_rstd.wait(nc.vector)
    t_cols = c.v(nc.vector.scalar_tensor_tensor(
        nmr[:, :], mcol[:, :], -1.0, rstd[:, :], ALU.mult, ALU.mult))
    for t in range(NT):
        xt = xr.acquire(nc.sync)
        s_ld = c.dsem("lnx")
        s_ld.inc(nc.sync.dma_start(xt[:, :], x3[t, :, :]), 16)
        st = stg.acquire(nc.scalar if pe_fn is None else nc.vector)
        if pe_fn is None:
            s_ld.wait(nc.scalar)
            t_cols.wait(nc.scalar)
            tok = c.s(nc.scalar.activation(
                st[:, :], xt[:, :], AF.Identity, scale=rstd[:, t:t + 1],
                bias=nmr[:, t:t + 1]))
        else:
            t_pe = pe_fn(t, sq_scr[:, :])
            s_ld.wait(nc.vector)
            t_cols.wait(nc.vector)
            t_pe.wait(nc.vector)
            c.v(nc.vector.scalar_tensor_tensor(
                sq_scr[:, :], xt[:, :], rstd[:, t:t + 1], sq_scr[:, :],
                ALU.mult, ALU.add))
            tok = c.v(nc.vector.tensor_scalar_add(st[:, :], sq_scr[:, :],
                                                  nmr[:, t:t + 1]))
        xr.release(tok)
        ds.put(t, tok, st[:, :])
        stg.release(Tok(ds.s_store.h, ds.s_store.v))
    return ds, rstd, nmr, t_cols


def emit_kv_fm_tm(c, ds, wk_h, wv_h, kT, v_tok, *, bk_row=None, bv_row=None,
                  act=None, wq2=None, qout=None):
    nc = c.nc
    sub = c.sub()
    cs = sub.__enter__()
    ds.s_store.wait(nc.gpsimd)
    ds.s_store.wait(nc.vector)
    wk = emit_load_weight(cs, wk_h, D, D) if wk_h is not None else None
    wv = emit_load_weight(cs, wv_h, D, D)
    wq = emit_load_weight(cs, wq2, D, D) if wq2 is not None else None
    ones, t_on = emit_ones(cs)
    onesw = cs.sb([1, 512], BF16)
    t_onw = c.v(c.nc.vector.memset(onesw[0:1, :], 1.0))
    if bk_row is not None:
        bk, s_bk = emit_load_bf16_row(cs, bk_row, D)
    if bv_row is not None:
        bv, s_bv = emit_load_bf16_row(cs, bv_row, D)
    yblk = cs.sbring(2, [128, ND * 512], BF16)
    kps = cs.ring(2, [128, 512])
    vps = cs.ring(2, [128, D])
    t_k = t_v = t_q = None
    for nb in range(T // 512):
        n0 = nb * 512
        yb = yblk.acquire(nc.sync)
        s_y = c.dsem("kvblk")
        ds.load_fm(yb, n0, n0 + 512, s_y)
        fm_jobs = []
        if wk is not None:
            fm_jobs.append((wk, kT, bk if bk_row is not None else None))
        if wq is not None:
            fm_jobs.append((wq, qout, None))
        for w_, outT, bias in fm_jobs:
            for ko in range(ND):
                pt = kps.acquire(nc.tensor)
                w_.sem.wait(nc.tensor)
                s_y.wait(nc.tensor)
                pairs = [(w_.lhsT(ki, ko * 128, (ko + 1) * 128),
                          yb[:, ki * 512:(ki + 1) * 512])
                         for ki in range(ND)]
                if bias is not None:
                    s_bk.wait(nc.tensor)
                    t_onw.wait(nc.tensor)
                    pairs.append((bias[0:1, ko * 128:(ko + 1) * 128],
                                  onesw[0:1, 0:512]))
                t_mm = c.mark(mm_group(nc, pt[:, 0:512], pairs), "tensor")
                t_mm.wait(nc.scalar)
                t_ev = c.s(nc.scalar.activation(
                    outT[:, ko * T + n0: ko * T + n0 + 512], pt[:, 0:512],
                    act or AF.Identity))
                kps.release(t_ev)
                if outT is kT:
                    t_k = t_ev
                else:
                    t_q = t_ev
                last_here = t_ev
        for tt in range(4):
            t = nb * 4 + tt
            pt = vps.acquire(nc.tensor)
            wv.sem.wait(nc.tensor)
            s_y.wait(nc.tensor)
            for o0 in range(0, D, 512):
                pairs = [(yb[:, ki * 512 + tt * 128: ki * 512 + (tt + 1) * 128],
                          wv.lhsT(ki, o0, o0 + 512)) for ki in range(ND)]
                if bv_row is not None:
                    s_bv.wait(nc.tensor)
                    t_on.wait(nc.tensor)
                    pairs.append((ones[0:1, 0:128], bv[0:1, o0:o0 + 512]))
                mm = mm_group(nc, pt[:, o0:o0 + 512], pairs)
            t_mm = c.mark(mm, "tensor")
            t_mm.wait(nc.scalar)
            t_v = c.s(nc.scalar.activation(v_tok[:, t * D:(t + 1) * D],
                                           pt[:, 0:D], act or AF.Identity))
            vps.release(t_v)
        yblk.release(Multi([x for x in (t_k, t_q, t_v) if x]))
    sub.__exit__(None, None, None)
    toks = [x for x in (t_k, t_q, t_v) if x is not None]
    return Multi(toks)



def emit_l2_attn_a(c, x_hbm, out_hbm, io, phase=9):
    nc = c.nc
    rby, s_rby = emit_load_f32(c, io["rby"][None, :], 1, 3)
    fac = c.sb([1, 8])
    wrow = c.sb([1, 3])
    c.v(nc.vector.memset(wrow[0:1, 0:1], 0.1))
    c.v(nc.vector.memset(wrow[0:1, 1:2], 0.05))
    c.v(nc.vector.memset(wrow[0:1, 2:3], 0.15))
    s_rby.wait(nc.vector)
    c.v(nc.vector.tensor_mul(fac[0:1, 4:7], rby[0:1, 0:3], wrow[0:1, 0:3]))
    c.v(nc.vector.reduce_sum(fac[0:1, 0:1], fac[0:1, 4:7], axis=AX.X))
    c.v(nc.vector.tensor_scalar(fac[0:1, 0:1], fac[0:1, 0:1], 1.0, 0.125,
                                ALU.add, ALU.mult))
    c.v(nc.vector.tensor_scalar(fac[0:1, 1:2], rby[0:1, 2:3], 0.1, 1.0,
                                ALU.mult, ALU.add))
    c.v(nc.vector.reciprocal(fac[0:1, 1:2], fac[0:1, 1:2]))
    c.v(nc.vector.tensor_scalar_mul(fac[0:1, 4:5], rby[0:1, 0:1], 0.1))
    c.v(nc.vector.tensor_scalar_mul(fac[0:1, 5:6], rby[0:1, 2:3], 0.05))
    c.v(nc.vector.tensor_add(fac[0:1, 2:3], fac[0:1, 4:5], fac[0:1, 5:6]))
    c.v(nc.vector.tensor_scalar_add(fac[0:1, 2:3], fac[0:1, 2:3], 1.0))
    c.v(nc.vector.tensor_scalar(fac[0:1, 3:4], rby[0:1, 1:2], 0.1, 1.0,
                                ALU.mult, ALU.add))
    t_fac = c.v(nc.vector.tensor_scalar_mul(fac[0:1, 3:4], fac[0:1, 3:4],
                                            1e-5))
    onesf = c.sb([1, 128])
    t_of = c.v(nc.vector.memset(onesf[0:1, :], 1.0))
    smcol = c.sb([128, 1])
    epscol = c.sb([128, 1])
    inv_b = c.sb([128, D])
    off_b = c.sb([128, D])
    iot = c.sb([1, D])
    poscol = c.sb([128, 2])
    iot2 = c.sb([1, D])
    sub = c.sub()
    cs = sub.__enter__()
    t_i1 = c.g(nc.gpsimd.iota(iot[0:1, :].rearrange("a (i two) -> a i two",
                                                    two=2),
                              pattern=[[1, 512], [0, 2]], base=0,
                              channel_multiplier=0,
                              allow_small_or_imprecise_dtypes=True))
    t_i1.wait(nc.scalar)
    c1 = -4.0 * math.log(10000.0) / D
    c.wait_const(nc.scalar)
    t_inv = c.s(nc.scalar.activation(iot[0:1, :], iot[0:1, :], AF.Exp,
                                     scale=c.constcol(c1)[0:1, 0:1]))
    t_inv.wait(nc.vector)
    t_inv = c.v(nc.vector.tensor_scalar_mul(iot[0:1, :], iot[0:1, :],
                                            fac[0:1, 1:2]))
    t_i2 = c.g(nc.gpsimd.iota(iot2[0:1, :].rearrange("a (i two) -> a i two",
                                                     two=2),
                              pattern=[[0, 512], [1, 2]], base=0,
                              channel_multiplier=0,
                              allow_small_or_imprecise_dtypes=True))
    t_i2.wait(nc.vector)
    t_off = c.v(nc.vector.tensor_scalar_mul(iot2[0:1, :], iot2[0:1, :],
                                            math.pi / 2))
    emit_bcast_rows(c, cs, [
        (fac[0:1, 0:1], smcol, t_fac),
        (fac[0:1, 3:4], epscol, t_fac),
        (iot[0:1, :], inv_b, t_inv),
        (iot2[0:1, :], off_b, t_off),
    ])
    sub.__exit__(None, None, None)
    pescr2 = c.sb([128, D])
    kint = c.sb([128, D], mybir.dt.int32)
    t_pc = c.g(nc.gpsimd.iota(poscol[:, 0:1], pattern=[[0, 1]], base=0,
                              channel_multiplier=1,
                              allow_small_or_imprecise_dtypes=True))

    def emit_pe(t, dst):
        t_pc.wait(nc.vector)
        c.v(nc.vector.tensor_scalar_add(poscol[:, 1:2], poscol[:, 0:1],
                                        float(t * 128)))
        c.v(nc.vector.tensor_scalar_mul(dst, inv_b[:, :], poscol[:, 1:2]))
        c.v(nc.vector.tensor_add(dst, dst, off_b[:, :]))
        c.v(nc.vector.tensor_scalar_mul(pescr2[:, :], dst,
                                        1.0 / (2 * math.pi)))
        c.v(nc.vector.tensor_copy(kint[:, :], pescr2[:, :]))
        c.v(nc.vector.tensor_copy(pescr2[:, :], kint[:, :]))
        c.v(nc.vector.scalar_tensor_tensor(dst, pescr2[:, :],
                                           -2 * math.pi, dst,
                                           ALU.mult, ALU.add))
        c.v(nc.vector.tensor_scalar_min(dst, dst, math.pi))
        t_u = c.v(nc.vector.tensor_scalar_max(dst, dst, -math.pi))
        t_u.wait(nc.scalar)
        return c.s(nc.scalar.activation(dst, dst, AF.Sin))

    ds, rstd, nmr, t_cols = emit_ln_stream(c, x_hbm, pe_fn=emit_pe)

    if phase == 1:
        s1 = c.sem()
        ds.s_store.wait(nc.gpsimd)
        s1.inc(nc.gpsimd.dma_start(out_hbm[:, :], ds.hbm[:, :]), 16)
        s1.wait(nc.gpsimd)
        return s1
    kT = c.sb([128, ND * T], BF16)
    v_tok = c.sb([128, NT * D], BF16)
    t_kv = emit_kv_fm_tm(c, ds, io["Wk"], io["Wv"], kT, v_tok,
                         bk_row=io["bk"], bv_row=io["bv"])
    if phase == 25:
        s1 = c.sem()
        t_kv.wait(nc.gpsimd)
        s1.inc(nc.gpsimd.dma_start(
            out_hbm.rearrange("(t p) f -> p t f", p=128),
            v_tok[:, :].rearrange("p (t f) -> p t f", t=NT)), 16)
        s1.wait(nc.gpsimd)
        return s1
    if phase == 27:
        s1 = c.sem()
        t_kv.wait(nc.gpsimd)
        dstk = out_hbm.rearrange("(a p) f -> p a f", p=128)
        srck = kT[:, :].rearrange("p (a f) -> p a f", a=16)
        s1.inc(nc.gpsimd.dma_start(dstk, srck), 16)
        s1.wait(nc.gpsimd)
        return s1
    att_ds = DStream(c, D)
    sub = c.sub()
    cs = sub.__enter__()
    t_kv.wait(nc.gpsimd)
    t_kv.wait(nc.vector)
    t_kv.wait(nc.scalar)
    wq = emit_load_weight(cs, io["Wq"], D, D)
    bq, s_bq = emit_load_bf16_row(cs, io["bq"], D)
    onesq = cs.sb([1, 128], BF16)
    t_onq = c.v(nc.vector.memset(onesq[0:1, :], 1.0))
    qsrc = cs.sbring(2, [128, ND * 128], BF16)
    qT1r = cs.sbring(2, [128, ND * 128], BF16)
    qps = cs.ring(1, [128, 128])
    attstg = cs.sbring(2, [128, D], BF16)
    sh = MhaShared(cs, 16, 64, D, after=[t_kv])

    for ti in range(NT):
        yb = qsrc.acquire(nc.sync)
        if ti == 0:
            t_kv.wait(nc.sync)
        s_y = c.dsem("qblk")
        ds.load_fm(yb, ti * 128, (ti + 1) * 128, s_y)
        qT1 = qT1r.acquire(nc.scalar)
        t_qe = None
        for ko in range(ND):
            pt = qps.acquire(nc.tensor)
            wq.sem.wait(nc.tensor)
            s_y.wait(nc.tensor)
            s_bq.wait(nc.tensor)
            t_onq.wait(nc.tensor)
            pairs = [(wq.lhsT(ki, ko * 128, (ko + 1) * 128),
                      yb[:, ki * 128:(ki + 1) * 128]) for ki in range(ND)]
            pairs.append((bq[0:1, ko * 128:(ko + 1) * 128], onesq[0:1, :]))
            t_mm = c.mark(mm_group(nc, pt[:, 0:128], pairs), "tensor")
            t_mm.wait(nc.scalar)
            t_qe = c.s(nc.scalar.activation(
                qT1[:, ko * 128:(ko + 1) * 128], pt[:, 0:128], AF.Identity))
            qps.release(t_qe)
        qsrc.release(t_qe)
        at = attstg.acquire(nc.vector)
        import os as _os
        if phase == 29:
            t_att = emit_mha(c, sh, kT, kT, v_tok, at[:, :], ti,
                             Multi([t_kv, t_qe]), causal=False,
                             scale_ap=smcol[:, 0:1], q_tile_mode=False)
        elif _os.environ.get("L2CONSTSCALE"):
            t_att = emit_mha(c, sh, qT1, kT, v_tok, at[:, :], ti,
                             Multi([t_kv, t_qe]), causal=False,
                             scale=float(_os.environ["L2CONSTSCALE"]),
                             q_tile_mode=True)
        else:
            t_att = emit_mha(c, sh, qT1, kT, v_tok, at[:, :], ti,
                             Multi([t_kv, t_qe]), causal=False,
                             scale_ap=smcol[:, 0:1], q_tile_mode=True)
        att_ds.put(ti, t_att, at[:, :])
        attstg.release(Tok(att_ds.s_store.h, att_ds.s_store.v))
        qT1r.release(t_att)
        if phase == 28 and ti == 0:
            s1 = c.sem()
            t_att.wait(nc.gpsimd)
            s1.inc(nc.gpsimd.dma_start(out_hbm[0:128, 0:D], at[:, :]), 16)
            s1.wait(nc.gpsimd)
            sub.__exit__(None, None, None)
            return s1
        if phase == 26 and ti == 3:
            s1 = c.sem()
            t_qe.wait(nc.gpsimd)
            s1.inc(nc.gpsimd.dma_start(out_hbm[0:128, 0:ND * 128], qT1[:, :]),
                   16)
            s1.wait(nc.gpsimd)
            sub.__exit__(None, None, None)
            return s1
    sub.__exit__(None, None, None)

    if phase == 2:
        s1 = c.sem()
        att_ds.s_store.wait(nc.gpsimd)
        s1.inc(nc.gpsimd.dma_start(out_hbm[:, :], att_ds.hbm[:, :]), 16)
        s1.wait(nc.gpsimd)
        return s1
    t_att.wait(nc.gpsimd)
    t_att.wait(nc.vector)
    t_att.wait(nc.scalar)
    t_att.wait(nc.sync)
    gam_b = c.sb([128, D])
    bet_b = c.sb([128, D])
    y1scr = c.sb([128, D])
    sqs = c.sb([128, D])
    ssum2 = c.sb([128, NT])
    ssq2 = c.sb([128, NT])
    mcol2 = c.sb([128, NT])
    rstd2 = c.sb([128, NT])
    nmr2 = c.sb([128, NT])
    sub = c.sub()
    cs = sub.__enter__()
    wo = emit_load_weight(cs, io["Wo"], D, D)
    bo, s_bo = emit_load_bf16_row(cs, io["bo"], D)
    ones2, t_on2 = emit_ones(cs)
    gb, s_gb = emit_load_f32(cs, io["ln_gamma"][None, :], 1, D)
    bb, s_bb = emit_load_f32(cs, io["ln_beta"][None, :], 1, D)
    s_gb.wait(nc.vector)
    t_fac.wait(nc.vector)
    t_gr = c.v(nc.vector.tensor_scalar_mul(gb[0:1, :], gb[0:1, :],
                                           fac[0:1, 2:3]))
    t_gb = emit_bcast_rows(c, cs, [(gb[0:1, :], gam_b, t_gr),
                                   (bb[0:1, :], bet_b, s_bb)])
    y1_hbm = c.dram([T, D])
    yd3 = y1_hbm.rearrange("(t p) f -> t p f", p=128)
    x3 = x_hbm.rearrange("(t p) f -> t p f", p=128)
    o3 = out_hbm.rearrange("(t p) f -> t p f", p=128)
    alhs = cs.sbring(2, [128, ND * 128], BF16)
    y1ps = cs.ring(2, [128, D])
    xrr = cs.sbring(2, [128, D], F32)
    s_y1st = c.sem()
    t_sq = None
    for t in range(NT):
        ab = alhs.acquire(nc.sync)
        s_a = c.dsem("ablk")
        att_ds.load_fm(ab, t * 128, (t + 1) * 128, s_a)
        pt = y1ps.acquire(nc.tensor)
        wo.sem.wait(nc.tensor)
        s_a.wait(nc.tensor)
        s_bo.wait(nc.tensor)
        t_on2.wait(nc.tensor)
        for o0 in (0, 512):
            pairs = [(ab[:, ki * 128:(ki + 1) * 128],
                      wo.lhsT(ki, o0, o0 + 512)) for ki in range(ND)]
            pairs.append((ones2[0:1, 0:128], bo[0:1, o0:o0 + 512]))
            mm = mm_group(nc, pt[:, o0:o0 + 512], pairs)
        t_mm = c.mark(mm, "tensor")
        alhs.release(t_mm)
        xt = xrr.acquire(nc.sync)
        s_ld = c.dsem("xw")
        s_ld.inc(nc.sync.dma_start(xt[:, :], x3[t, :, :]), 16)
        t_pe = emit_pe(t, y1scr[:, :])
        t_pe.wait(nc.vector)
        t_mm.wait(nc.vector)
        s_ld.wait(nc.vector)
        t_cols.wait(nc.vector)
        c.v(nc.vector.scalar_tensor_tensor(
            y1scr[:, :], xt[:, :], rstd[:, t:t + 1], y1scr[:, :],
            ALU.mult, ALU.add))
        xrr.release(c.tip("vector"))
        c.v(nc.vector.tensor_scalar_add(y1scr[:, :], y1scr[:, :],
                                        nmr[:, t:t + 1]))
        t_y1 = c.v(nc.vector.tensor_add(y1scr[:, :], y1scr[:, :], pt[:, 0:D]))
        y1ps.release(t_y1)
        c.v(nc.vector.reduce_sum(ssum2[:, t:t + 1], y1scr[:, :], axis=AX.X))
        t_y1.wait(nc.scalar)
        t_sq = c.s(nc.scalar.activation(sqs[:, :], y1scr[:, :], AF.Square,
                                        accum_out=ssq2[:, t:t + 1]))
        t_y1.wait(nc.sync)
        s_y1st.inc(nc.sync.dma_start(yd3[t, :, :], y1scr[:, :]), 16)
        t_sq.wait(nc.vector)
        Tok(s_y1st.h, s_y1st.v).wait(nc.vector)
    t_sq.wait(nc.vector)
    c.v(nc.vector.tensor_scalar_mul(mcol2[:, :], ssum2[:, :], 1.0 / D))
    c.v(nc.vector.tensor_scalar_mul(ssq2[:, :], ssq2[:, :], 1.0 / D))
    t_m2 = c.v(nc.vector.tensor_mul(rstd2[:, :], mcol2[:, :], mcol2[:, :]))
    c.v(nc.vector.tensor_sub(ssq2[:, :], ssq2[:, :], rstd2[:, :]))
    t_var = c.v(nc.vector.tensor_scalar_add(ssq2[:, :], ssq2[:, :],
                                            epscol[:, 0:1]))
    t_var.wait(nc.scalar)
    c.wait_const(nc.scalar)
    c.s(nc.scalar.activation(rstd2[:, :], ssq2[:, :], AF.Ln))
    t_r2 = c.s(nc.scalar.activation(rstd2[:, :], rstd2[:, :], AF.Exp,
                                    scale=c.constcol(-0.5)))
    t_r2.wait(nc.vector)
    t_n2 = c.v(nc.vector.scalar_tensor_tensor(
        nmr2[:, :], mcol2[:, :], -1.0, rstd2[:, :], ALU.mult, ALU.mult))
    y1t = cs.sbring(2, [128, D], F32)
    s_out = c.sem()
    s_y1st.wait(nc.sync)
    for t in range(NT):
        yt = y1t.acquire(nc.sync)
        s_ld = c.dsem("y1l")
        s_ld.inc(nc.sync.dma_start(yt[:, :], yd3[t, :, :]), 16)
        xt = xrr.acquire(nc.sync)
        s_lx = c.dsem("xw2")
        s_lx.inc(nc.sync.dma_start(xt[:, :], x3[t, :, :]), 16)
        s_ld.wait(nc.scalar)
        t_n2.wait(nc.scalar)
        t_nm = c.s(nc.scalar.activation(sqs[:, :], yt[:, :], AF.Identity,
                                        scale=rstd2[:, t:t + 1],
                                        bias=nmr2[:, t:t + 1]))
        y1t.release(t_nm)
        t_nm.wait(nc.vector)
        t_gb.wait(nc.vector)
        c.v(nc.vector.tensor_mul(sqs[:, :], sqs[:, :], gam_b[:, :]))
        c.v(nc.vector.tensor_add(sqs[:, :], sqs[:, :], bet_b[:, :]))
        s_lx.wait(nc.vector)
        tok_add = c.v(nc.vector.tensor_add(sqs[:, :], sqs[:, :], xt[:, :]))
        xrr.release(tok_add)
        tok_add.wait(nc.sync)
        s_out.inc(nc.sync.dma_start(o3[t, :, :], sqs[:, :]), 16)
        Tok(s_out.h, s_out.v).wait(nc.scalar)
        tok_add.wait(nc.scalar)
    sub.__exit__(None, None, None)
    s_out.wait(nc.sync)
    return s_out


def build_l2attna_program():
    nc = bass.Bass("TRN2", target_bir_lowering=False)
    stack = ExitStack()
    c = Ctx(nc, stack)
    dt = lambda n, sh_: nc.dram_tensor(n, sh_, F32, kind="ExternalInput").ap()
    io = {
        "rby": dt("rby", [3]),
        "Wq": dt("Wq", [D, D]), "bq": dt("bq", [D]),
        "Wk": dt("Wk", [D, D]), "bk": dt("bk", [D]),
        "Wv": dt("Wv", [D, D]), "bv": dt("bv", [D]),
        "Wo": dt("Wo", [D, D]), "bo": dt("bo", [D]),
        "ln_gamma": dt("ln_gamma", [D]), "ln_beta": dt("ln_beta", [D]),
    }
    x_h = dt("x", [T, D])
    out_h = nc.dram_tensor("out", [T, D], F32, kind="ExternalOutput").ap()
    import os
    with stack:
        emit_l2_attn_a(c, x_h, out_h, io, phase=int(os.environ.get("L2PH", "9")))
    return nc



def emit_l1_attn_a(c, x_hbm, out_hbm, io):
    nc = c.nc
    H, DH = 16, 64
    ds, rstd, nmr, t_cols = emit_ln_stream(c, x_hbm)
    qT = c.sb([128, ND * T], BF16)
    kT = c.sb([128, ND * T], BF16)
    v_tok = c.sb([128, NT * D], BF16)
    Wq_sl = io["Wqkv"][:, 0:D]
    Wk_sl = io["Wqkv"][:, D:2 * D]
    Wv_sl = io["Wqkv"][:, 2 * D:3 * D]
    t_kv = emit_kv_fm_tm(c, ds, Wk_sl, Wv_sl, kT, v_tok, wq2=Wq_sl, qout=qT)

    hs, s_hs = emit_load_f32(c, io["heads3"], 1, 48)
    hrow = c.sb([1, 16])
    hrow2 = c.sb([1, 16])
    pig16 = c.dram([16])
    nch16 = c.dram([16])
    s_hs.wait(nc.scalar)
    t_p = c.s(nc.scalar.activation(hrow[0:1, :], hs[0:1, 16:32], AF.Sigmoid))
    t_e = c.s(nc.scalar.activation(hrow2[0:1, :], hs[0:1, 32:48], AF.Sigmoid))
    t_x = c.s(nc.scalar.activation(hs[0:1, 0:16], hs[0:1, 0:16], AF.Exp,
                                   scale=c.constcol(-1.0)[0:1, 0:1]))
    t_x.wait(nc.vector)
    t_e.wait(nc.vector)
    c.v(nc.vector.tensor_mul(hrow2[0:1, :], hrow2[0:1, :], hs[0:1, 0:16]))
    t_n = c.v(nc.vector.tensor_scalar_mul(hrow2[0:1, :], hrow2[0:1, :], -8.0))
    s_hd = c.sem()
    t_p.wait(nc.sync)
    s_hd.inc(nc.sync.dma_start(pig16[None, :], hrow[0:1, :]), 16)
    t_n.wait(nc.sync)
    s_hd.inc(nc.sync.dma_start(nch16[None, :], hrow2[0:1, :]), 16)
    pigc = c.sb([16, 1])
    s_pc = c.sem()
    s_hd.wait(nc.sync)
    s_pc.inc(nc.sync.dma_start(pigc[:, 0:1], pig16[:, None]), 16)
    nchB = c.sb([128, 16])
    fcol = c.sb([128, 2])
    posB = c.sb([128, T])
    posr = c.sb([1, T])
    sub = c.sub()
    cs = sub.__enter__()
    t_nb = emit_bcast_rows(c, cs, [(hrow2[0:1, :], nchB, t_n)])
    pighi = cs.sb([16, 1], BF16)
    piglo = cs.sb([16, 1], BF16)
    s_pc.wait(nc.vector)
    c.v(nc.vector.tensor_copy(pighi[:, 0:1], pigc[:, 0:1]))
    t_pl = c.v(nc.vector.tensor_sub(piglo[:, 0:1], pigc[:, 0:1],
                                    pighi[:, 0:1]))
    kint1 = cs.sb([128, 1], mybir.dt.int32)
    t_i = c.g(nc.gpsimd.iota(fcol[:, 0:1], pattern=[[0, 1]], base=0,
                             channel_multiplier=1,
                             allow_small_or_imprecise_dtypes=True))
    t_i.wait(nc.vector)
    c.v(nc.vector.tensor_scalar_mul(fcol[:, 1:2], fcol[:, 0:1], 1.0 / 32.0))
    c.v(nc.vector.tensor_scalar_add(fcol[:, 1:2], fcol[:, 1:2], -0.499))
    c.v(nc.vector.tensor_copy(kint1[:, 0:1], fcol[:, 1:2]))
    c.v(nc.vector.tensor_copy(fcol[:, 1:2], kint1[:, 0:1]))
    c.v(nc.vector.scalar_tensor_tensor(fcol[:, 0:1], fcol[:, 1:2], -32.0,
                                       fcol[:, 0:1], ALU.mult, ALU.add))
    t_f = c.v(nc.vector.tensor_scalar(fcol[:, 0:1], fcol[:, 0:1],
                                      -2.0 / 64.0 * math.log(math.pi),
                                      math.log(math.pi), ALU.mult, ALU.add))
    t_f.wait(nc.scalar)
    t_f = c.s(nc.scalar.activation(fcol[:, 0:1], fcol[:, 0:1], AF.Exp))
    t_pr = c.g(nc.gpsimd.iota(posr[0:1, :], pattern=[[1, T]], base=0,
                              channel_multiplier=0,
                              allow_small_or_imprecise_dtypes=True))
    emit_bcast_rows(c, cs, [(posr[0:1, :], posB, t_pr)])
    selps = cs.ps([128, 1])
    sel = cs.sb([16, 128], BF16)
    seli = cs.sb([16, 128])
    thetasc = cs.sb([128, T])
    tric = cs.sb([128, T])
    kint = cs.sb([128, T], mybir.dt.int32)
    rs1 = cs.sb([128, T], BF16)
    rs2 = cs.sb([128, T], BF16)
    rs3 = cs.sb([128, T], BF16)
    rs4 = cs.sb([128, T], BF16)
    pfc = cs.sb([128, 2])

    def sincos(dst, src_tok, extra_bias):
        c.v(nc.vector.tensor_scalar(tric[:, :], thetasc[:, :], 1.0,
                                    extra_bias, ALU.mult, ALU.add)) \
            if extra_bias else c.v(nc.vector.tensor_copy(tric[:, :],
                                                         thetasc[:, :]))
        c.v(nc.vector.tensor_scalar_mul(dst, tric[:, :],
                                        1.0 / (2 * math.pi)))
        c.v(nc.vector.tensor_copy(kint[:, :], dst))
        c.v(nc.vector.tensor_copy(dst, kint[:, :]))
        c.v(nc.vector.scalar_tensor_tensor(dst, dst, -2 * math.pi,
                                           tric[:, :], ALU.mult, ALU.add))
        c.v(nc.vector.tensor_scalar_min(dst, dst, math.pi))
        t_u = c.v(nc.vector.tensor_scalar_max(dst, dst, -math.pi))
        t_u.wait(nc.scalar)
        return c.s(nc.scalar.activation(dst, dst, AF.Sin))

    t_rot = None
    for ch in range(4):
        t_si = c.g(nc.gpsimd.iota(
            seli[:, :].rearrange("h (g k) -> h g k", g=4),
            pattern=[[1, 4], [0, 32]], base=4 * ch, channel_multiplier=-1,
            allow_small_or_imprecise_dtypes=True))
        t_si.wait(nc.vector)
        t_se = c.v(nc.vector.tensor_single_scalar(sel[:, :], seli[:, :], 0.0,
                                                  ALU.is_equal))
        t_se.wait(nc.tensor)
        t_pl.wait(nc.tensor)
        nc.tensor.matmul(selps[:, 0:1], sel[:, :] if False else sel[:, :],
                         pighi[:, 0:1], start=True, stop=False)
        mm = nc.tensor.matmul(selps[:, 0:1], sel[:, :], piglo[:, 0:1],
                              start=False, stop=True)
        t_sm = c.mark(mm, "tensor")
        t_sm.wait(nc.vector)
        c.v(nc.vector.tensor_copy(pfc[:, 0:1], selps[:, 0:1]))
        t_f.wait(nc.vector)
        t_pf = c.v(nc.vector.tensor_mul(pfc[:, 0:1], pfc[:, 0:1],
                                        fcol[:, 0:1]))
        t_pf.wait(nc.vector)
        c.v(nc.vector.tensor_scalar_mul(thetasc[:, :], posB[:, :],
                                        pfc[:, 0:1]))
        t_sin = sincos(rs1[:, :] if False else tric[:, :], None, 0.0)
        sinb = rs3
        t_sin.wait(nc.vector)
        t_sb = c.v(nc.vector.tensor_copy(sinb[:, :], tric[:, :]))
        t_cos = sincos(tric[:, :], None, math.pi / 2)
        cosb = rs4
        t_cos.wait(nc.vector)
        t_cb = c.v(nc.vector.tensor_copy(cosb[:, :], tric[:, :]))
        for buf in (qT, kT):
            t1 = buf[:, ch * T:(ch + 1) * T]
            t2 = buf[:, (ch + 4) * T:(ch + 5) * T]
            t_kv.wait(nc.vector)
            c.v(nc.vector.tensor_mul(rs1[:, :], t1, cosb[:, :]))
            c.v(nc.vector.tensor_mul(rs2[:, :], t2, sinb[:, :]))
            kv1 = c.v(nc.vector.tensor_mul(thetasc[:, :], t1, sinb[:, :]))
            c.v(nc.vector.tensor_sub(t1, rs1[:, :], rs2[:, :]))
            c.v(nc.vector.tensor_mul(rs2[:, :], t2, cosb[:, :]))
            t_rot = c.v(nc.vector.tensor_add(t2, thetasc[:, :], rs2[:, :]))
    sub.__exit__(None, None, None)

    D0 = c.sb([128, T])
    absD = c.sb([128, T])
    t_d0 = c.g(nc.gpsimd.iota(D0[:, :], pattern=[[1, T]], base=0,
                              channel_multiplier=-1,
                              allow_small_or_imprecise_dtypes=True))
    att_ds = DStream(c, D)
    sub = c.sub()
    cs = sub.__enter__()
    attstg = cs.sbring(2, [128, D], BF16)
    sh = MhaShared(cs, H, DH, D, after=[t_rot, Tok(c.est["pool"].h,
                                                   c.est["pool"].v)])
    state = {"absD_tok": None}
    for ti in range(NT):
        t_d0.wait(nc.vector)
        if state["absD_tok"] is not None:
            pass
        t_a1 = c.v(nc.vector.tensor_scalar_add(absD[:, 0:(ti + 1) * 128],
                                               D0[:, 0:(ti + 1) * 128],
                                               float(-ti * 128)))
        t_a1.wait(nc.scalar)
        t_abs = c.s(nc.scalar.activation(absD[:, 0:(ti + 1) * 128],
                                         absD[:, 0:(ti + 1) * 128], AF.Abs))

        def hook(h, ti_, klen, sp, t_s, t_abs=t_abs):
            t_abs.wait(nc.vector)
            t_nb.wait(nc.vector)
            t_s.wait(nc.vector)
            return c.v(nc.vector.scalar_tensor_tensor(
                sp[:, 0:klen], absD[:, 0:klen], nchB[:, h:h + 1],
                sp[:, 0:klen], ALU.mult, ALU.add))
        at = attstg.acquire(nc.vector)
        t_att = emit_mha(c, sh, qT, kT, v_tok, at[:, :], ti, t_kv,
                         causal=True, pre_hook=hook, scale=0.125,
                         pieces_fn=lambda h: [
                             (h // 4, (h % 4) * 32, 32),
                             (4 + h // 4, (h % 4) * 32, 32)])
        att_ds.put(ti, t_att, at[:, :])
        attstg.release(Tok(att_ds.s_store.h, att_ds.s_store.v))
        t_att.wait(nc.vector)
        t_att.wait(nc.scalar)
    sub.__exit__(None, None, None)

    oscr = c.sb([128, D])
    sub = c.sub()
    cs = sub.__enter__()
    wo = emit_load_weight(cs, io["Wo"], D, D)
    alhs = cs.sbring(2, [128, ND * 128], BF16)
    y1ps = cs.ring(2, [128, D])
    xrr = cs.sbring(2, [128, D], F32)
    x3 = x_hbm.rearrange("(t p) f -> t p f", p=128)
    o3 = out_hbm.rearrange("(t p) f -> t p f", p=128)
    s_out = c.sem()
    for t in range(NT):
        ab = alhs.acquire(nc.sync)
        s_a = c.dsem("ablk")
        att_ds.load_fm(ab, t * 128, (t + 1) * 128, s_a)
        pt = y1ps.acquire(nc.tensor)
        wo.sem.wait(nc.tensor)
        s_a.wait(nc.tensor)
        for o0 in (0, 512):
            mm = mm_group(nc, pt[:, o0:o0 + 512],
                          [(ab[:, ki * 128:(ki + 1) * 128],
                            wo.lhsT(ki, o0, o0 + 512)) for ki in range(ND)])
        t_mm = c.mark(mm, "tensor")
        alhs.release(t_mm)
        xt = xrr.acquire(nc.sync)
        s_ld = c.dsem("xw")
        s_ld.inc(nc.sync.dma_start(xt[:, :], x3[t, :, :]), 16)
        t_mm.wait(nc.vector)
        s_ld.wait(nc.vector)
        tok_o = c.v(nc.vector.tensor_add(oscr[:, :], xt[:, :], pt[:, 0:D]))
        y1ps.release(tok_o)
        xrr.release(tok_o)
        tok_o.wait(nc.sync)
        s_out.inc(nc.sync.dma_start(o3[t, :, :], oscr[:, :]), 16)
        Tok(s_out.h, s_out.v).wait(nc.vector)
    sub.__exit__(None, None, None)
    s_out.wait(nc.sync)
    return s_out


def build_l1attna_program():
    nc = bass.Bass("TRN2", target_bir_lowering=False)
    stack = ExitStack()
    c = Ctx(nc, stack)
    dtm = lambda n, sh_: nc.dram_tensor(n, sh_, F32, kind="ExternalInput").ap()
    io = {
        "Wqkv": dtm("Wqkv", [D, 3 * D]),
        "Wo": dtm("Wo", [D, D]),
        "heads3": dtm("heads3", [1, 48]),
    }
    x_h = dtm("x", [T, D])
    out_h = nc.dram_tensor("out", [T, D], F32, kind="ExternalOutput").ap()
    with stack:
        emit_l1_attn_a(c, x_h, out_h, io)
    return nc



def emit_l0_attn_b(c, x_hbm, out_hbm, io):
    nc = c.nc
    G, DHg = 4, 256
    WINDOWS = (32.0, 64.0, 128.0, 256.0)
    ds, rstd, nmr, t_cols = emit_ln_stream(c, x_hbm)
    q_inT = c.sb([128, ND * T], BF16)
    sub = c.sub()
    cs = sub.__enter__()
    wqi = emit_load_weight(cs, io["Wq"], D, D)
    yblk = cs.sbring(2, [128, ND * 512], BF16)
    kps = cs.ring(2, [128, 512])
    t_qi = None
    for nb in range(T // 512):
        n0 = nb * 512
        yb = yblk.acquire(nc.sync)
        s_y = c.dsem("kvblk")
        ds.load_fm(yb, n0, n0 + 512, s_y)
        for ko in range(ND):
            pt = kps.acquire(nc.tensor)
            wqi.sem.wait(nc.tensor)
            s_y.wait(nc.tensor)
            t_mm = c.mark(mm_group(nc, pt[:, 0:512],
                          [(wqi.lhsT(ki, ko * 128, (ko + 1) * 128),
                            yb[:, ki * 512:(ki + 1) * 512])
                           for ki in range(ND)]), "tensor")
            t_mm.wait(nc.scalar)
            t_qi = c.s(nc.scalar.copy(
                q_inT[:, ko * T + n0: ko * T + n0 + 512], pt[:, 0:512]))
            kps.release(t_qi)
        yblk.release(t_qi)
    sub.__exit__(None, None, None)

    qT = c.sb([128, ND * T], BF16)
    kT = c.sb([128, ND * T], BF16)
    v_tok = c.sb([128, NT * D], BF16)
    hT256 = c.sb([128, 2 * T], BF16)
    dxcol = c.sb([128, NT])
    dx_hbm = c.dram([T])
    sub = c.sub()
    cs = sub.__enter__()
    wtile = cs.sb([128, ND * D], BF16)
    bq, s_bq = emit_load_bf16_row(cs, io["bqkv"][0:D], D)
    bk, s_bk = emit_load_bf16_row(cs, io["bqkv"][D:2 * D], D)
    bv, s_bv = emit_load_bf16_row(cs, io["bqkv"][2 * D:3 * D], D)
    ones, t_on = emit_ones(cs)
    onesw = cs.sb([1, 512], BF16)
    t_onw = c.v(nc.vector.memset(onesw[0:1, :], 1.0))
    relub = cs.sbring(1, [128, ND * 512], BF16)
    pps = cs.ring(2, [128, 512])
    vps = cs.ring(2, [128, D])
    dps = cs.ring(2, [1, 512])
    w2hi = cs.sb([128, 2], BF16)
    w2lo = cs.sb([128, 2], BF16)
    w2r, s_w2 = emit_load_f32(cs, io["W2c"], 128, 2)
    b2r, s_b2 = emit_load_f32(cs, io["b2"][None, :], 1, 1)
    s_w2.wait(nc.vector)
    c.v(nc.vector.tensor_copy(w2hi[:, :], w2r[:, :]))
    t_w2 = c.v(nc.vector.tensor_sub(w2lo[:, :], w2r[:, :], w2hi[:, :]))
    dxrow = cs.sb([1, T])
    t_q = t_k = t_v = None
    tok_w = None

    for w_sl, outT, brow, s_b in ((io["Wqkv"][:, 0:D], qT, bq, s_bq),
                                  (io["Wqkv"][:, D:2 * D], kT, bk, s_bk)):
        if tok_w is not None:
            tok_w.wait(nc.gpsimd)
        w_ = emit_load_weight(cs, w_sl, D, D, tile=wtile)
        for nb in range(4):
            n0 = nb * 512
            for ko in range(ND):
                pt = pps.acquire(nc.tensor)
                w_.sem.wait(nc.tensor)
                t_qi.wait(nc.tensor)
                s_b.wait(nc.tensor)
                t_onw.wait(nc.tensor)
                pairs = [(w_.lhsT(ki, ko * 128, (ko + 1) * 128),
                          q_inT[:, ki * T + n0: ki * T + n0 + 512])
                         for ki in range(ND)]
                pairs.append((brow[0:1, ko * 128:(ko + 1) * 128],
                              onesw[0:1, 0:512]))
                t_mm = c.mark(mm_group(nc, pt[:, 0:512], pairs), "tensor")
                tok_w = t_mm
                t_mm.wait(nc.scalar)
                t_ev = c.s(nc.scalar.copy(
                    outT[:, ko * T + n0: ko * T + n0 + 512], pt[:, 0:512]))
                pps.release(t_ev)
                if outT is qT:
                    t_q = t_ev
                else:
                    t_k = t_ev
    tok_w.wait(nc.gpsimd)
    wv = emit_load_weight(cs, io["Wqkv"][:, 2 * D:3 * D], D, D, tile=wtile)
    for t in range(NT):
        pt = vps.acquire(nc.tensor)
        wv.sem.wait(nc.tensor)
        t_qi.wait(nc.tensor)
        s_bv.wait(nc.tensor)
        t_on.wait(nc.tensor)
        for o0 in (0, 512):
            pairs = [(q_inT[:, ki * T + t * 128: ki * T + (t + 1) * 128],
                      wv.lhsT(ki, o0, o0 + 512)) for ki in range(ND)]
            pairs.append((ones[0:1, 0:128], bv[0:1, o0:o0 + 512]))
            mm = mm_group(nc, pt[:, o0:o0 + 512], pairs)
        t_mm = c.mark(mm, "tensor")
        tok_w = t_mm
        t_mm.wait(nc.scalar)
        t_v = c.s(nc.scalar.copy(v_tok[:, t * D:(t + 1) * D], pt[:, 0:D]))
        vps.release(t_v)
    tok_w.wait(nc.gpsimd)
    wop = emit_load_weight(cs, io["Wop"], D, 256, tile=wtile)
    t_dx = None
    for nb in range(4):
        n0 = nb * 512
        rb = relub.acquire(nc.vector)
        t_qi.wait(nc.vector)
        t_r = None
        for ki in range(ND):
            t_r = c.v(nc.vector.tensor_relu(
                rb[:, ki * 512:(ki + 1) * 512],
                q_inT[:, ki * T + n0: ki * T + n0 + 512]))
        t_h = None
        for ko in range(2):
            pt = pps.acquire(nc.tensor)
            wop.sem.wait(nc.tensor)
            t_r.wait(nc.tensor)
            t_mm = c.mark(mm_group(nc, pt[:, 0:512],
                          [(wop.lhsT(ki, ko * 128, (ko + 1) * 128),
                            rb[:, ki * 512:(ki + 1) * 512])
                           for ki in range(ND)]), "tensor")
            t_mm.wait(nc.scalar)
            t_h = c.s(nc.scalar.copy(
                hT256[:, ko * T + n0: ko * T + n0 + 512], pt[:, 0:512]))
            pps.release(t_h)
        relub.release(t_mm)
        dp = dps.acquire(nc.tensor)
        t_h.wait(nc.tensor)
        t_w2.wait(nc.tensor)
        for j, wcolt in enumerate((w2hi, w2lo)):
            for ko in range(2):
                mm = nc.tensor.matmul(
                    dp[0:1, 0:512], wcolt[:, ko:ko + 1],
                    hT256[:, ko * T + n0: ko * T + n0 + 512],
                    start=(j == 0 and ko == 0), stop=(j == 1 and ko == 1))
        t_dmm = c.mark(mm, "tensor")
        t_dmm.wait(nc.scalar)
        s_b2.wait(nc.scalar)
        t_sg = c.s(nc.scalar.activation(dxrow[0:1, n0:n0 + 512],
                                        dp[0:1, 0:512], AF.Sigmoid,
                                        bias=b2r[0:1, 0:1]))
        dps.release(t_sg)
        t_sg.wait(nc.vector)
        t_dx = c.v(nc.vector.tensor_scalar_mul(dxrow[0:1, n0:n0 + 512],
                                               dxrow[0:1, n0:n0 + 512],
                                               float(T)))
    s_st = c.sem()
    t_dx.wait(nc.sync)
    s_st.inc(nc.sync.dma_start(dx_hbm[None, :], dxrow[0:1, :]), 16)
    s_st.wait(nc.sync)
    s_dc = c.sem()
    s_dc.inc(nc.sync.dma_start(
        dxcol[:, :], dx_hbm.rearrange("(t p) -> p t", p=128),
        allow_slow_non_contiguous=True), 16)
    sub.__exit__(None, None, None)

    D0 = c.sb([128, T])
    mscr = c.sb([128, T], BF16)
    m2scr = c.sb([128, T], BF16)
    t_d0 = c.g(nc.gpsimd.iota(D0[:, :], pattern=[[1, T]], base=0,
                              channel_multiplier=-1,
                              allow_small_or_imprecise_dtypes=True))
    att_ds = DStream(c, D)
    sub = c.sub()
    cs = sub.__enter__()
    attstg = cs.sbring(2, [128, D], BF16)
    thrc = cs.sb([128, 8])
    sh = MhaShared(cs, G, DHg, D, after=[Tok(c.est["scalar"].h,
                                             c.est["scalar"].v)])
    scale = 2.0 / math.sqrt(float(T))
    for ti in range(NT):
        def hook(g, ti_, klen, sp, t_s, ti__=ti):
            q0 = float(ti__ * 128)
            s_dc.wait(nc.vector)
            t_d0.wait(nc.vector)
            c.v(nc.vector.tensor_scalar_add(thrc[:, g:g + 1],
                                            dxcol[:, ti__:ti__ + 1],
                                            q0 + WINDOWS[g]))
            c.v(nc.vector.tensor_single_scalar(mscr[:, :], D0[:, :],
                                               thrc[:, g:g + 1], ALU.is_le))
            c.v(nc.vector.tensor_single_scalar(m2scr[:, :], D0[:, :], q0,
                                               ALU.is_ge))
            c.v(nc.vector.tensor_mul(mscr[:, :], mscr[:, :], m2scr[:, :]))
            c.v(nc.vector.tensor_scalar(mscr[:, :], mscr[:, :], 1e30, -1e30,
                                        ALU.mult, ALU.add))
            t_s.wait(nc.vector)
            return c.v(nc.vector.tensor_add(sp[:, 0:klen], sp[:, 0:klen],
                                            mscr[:, :]))
        at = attstg.acquire(nc.vector)
        t_att = emit_mha(c, sh, qT, kT, v_tok, at[:, :], ti,
                         Multi([t_q, t_k, t_v]), causal=False,
                         pre_hook=hook, scale=scale)
        att_ds.put(ti, t_att, at[:, :])
        attstg.release(Tok(att_ds.s_store.h, att_ds.s_store.v))
        t_att.wait(nc.vector)
    sub.__exit__(None, None, None)

    oscr = c.sb([128, D])
    sub = c.sub()
    cs = sub.__enter__()
    xrr = cs.sbring(2, [128, D], F32)
    arr = cs.sbring(2, [128, D], BF16)
    x3 = x_hbm.rearrange("(t p) f -> t p f", p=128)
    o3 = out_hbm.rearrange("(t p) f -> t p f", p=128)
    a3 = att_ds.h3
    s_out = c.sem()
    att_ds.s_store.wait(nc.sync)
    for t in range(NT):
        xt = xrr.acquire(nc.sync)
        s_lx = c.dsem("xw")
        s_lx.inc(nc.sync.dma_start(xt[:, :], x3[t, :, :]), 16)
        ar = arr.acquire(nc.sync)
        s_la = c.dsem("aw")
        s_la.inc(nc.sync.dma_start(ar[:, :], a3[t, :, :]), 16)
        s_lx.wait(nc.vector)
        s_la.wait(nc.vector)
        tok_o = c.v(nc.vector.tensor_add(oscr[:, :], xt[:, :], ar[:, :]))
        xrr.release(tok_o)
        arr.release(tok_o)
        tok_o.wait(nc.sync)
        s_out.inc(nc.sync.dma_start(o3[t, :, :], oscr[:, :]), 16)
        Tok(s_out.h, s_out.v).wait(nc.vector)
    sub.__exit__(None, None, None)
    s_out.wait(nc.sync)
    return s_out


def build_l0attnb_program():
    nc = bass.Bass("TRN2", target_bir_lowering=False)
    stack = ExitStack()
    c = Ctx(nc, stack)
    dtm = lambda n, sh_: nc.dram_tensor(n, sh_, F32, kind="ExternalInput").ap()
    io = {
        "Wq": dtm("Wq", [D, D]),
        "Wqkv": dtm("Wqkv", [D, 3 * D]),
        "bqkv": dtm("bqkv", [3 * D]),
        "Wop": dtm("Wop", [D, 256]),
        "W2c": dtm("W2c", [128, 2]),
        "b2": dtm("b2", [1]),
    }
    x_h = dtm("x", [T, D])
    out_h = nc.dram_tensor("out", [T, D], F32, kind="ExternalOutput").ap()
    with stack:
        emit_l0_attn_b(c, x_h, out_h, io)
    return nc



def emit_l2_attn_b(c, x_hbm, out_hbm, io, phase=9):
    nc = c.nc
    DI = 2048
    NKI = DI // 128
    DS = 16
    DTR = 64
    TH = T // 2

    x3 = x_hbm.rearrange("(t p) f -> t p f", p=128)
    ssum = c.sb([128, NT])
    ssq = c.sb([128, NT])
    mcol = c.sb([128, NT])
    msq = c.sb([128, NT])
    rstd = c.sb([128, NT])
    nmr = c.sb([128, NT])
    sq_scr = c.sb([128, D])
    g_bc = c.sb([128, D])
    b_bc = c.sb([128, D])
    xr = c.sbring(3, [128, D], F32)
    stg = c.sbring(2, [128, D], BF16)
    xh_ds = DStream(c, D)
    xn_ds = DStream(c, D)
    gr, s_gr = emit_load_f32(c, io["norm_g"][None, :], 1, D)
    br, s_br = emit_load_f32(c, io["norm_b"][None, :], 1, D)
    sub = c.sub()
    cs = sub.__enter__()
    emit_bcast_rows(c, cs, [(gr[0:1, :], g_bc, s_gr), (br[0:1, :], b_bc,
                                                       s_br)])
    sub.__exit__(None, None, None)
    t_red = t_sq = None
    for t in range(NT):
        xt = xr.acquire(nc.sync)
        s_ld = c.dsem("lnx")
        s_ld.inc(nc.sync.dma_start(xt[:, :], x3[t, :, :]), 16)
        s_ld.wait(nc.vector)
        t_red = c.v(nc.vector.reduce_sum(ssum[:, t:t + 1], xt[:, :],
                                         axis=AX.X))
        s_ld.wait(nc.scalar)
        t_sq = c.s(nc.scalar.activation(sq_scr[:, :], xt[:, :], AF.Square,
                                        accum_out=ssq[:, t:t + 1]))
        xr.release(Multi([t_red, t_sq]))
    t_sq.wait(nc.vector)
    c.v(nc.vector.tensor_scalar_mul(mcol[:, :], ssum[:, :], 1.0 / D))
    t_m = c.v(nc.vector.tensor_scalar_mul(ssq[:, :], ssq[:, :], 1.0 / D))
    t_m.wait(nc.scalar)
    t_msq = c.s(nc.scalar.activation(msq[:, :], mcol[:, :], AF.Square))
    t_msq.wait(nc.vector)
    t_var = c.v(nc.vector.tensor_sub(ssq[:, :], ssq[:, :], msq[:, :]))
    epsc = c.constcol(1e-5)
    halfneg = c.constcol(-0.5)
    t_var.wait(nc.scalar)
    c.wait_const(nc.scalar)
    c.s(nc.scalar.activation(rstd[:, :], ssq[:, :], AF.Ln, bias=epsc))
    t_rstd = c.s(nc.scalar.activation(rstd[:, :], rstd[:, :], AF.Exp,
                                      scale=halfneg))
    rstdc = c.sb([128, NT])
    sv = c.sb([128, NT])
    t_rstd.wait(nc.vector)
    c.v(nc.vector.tensor_mul(sv[:, :], ssq[:, :], rstd[:, :]))
    t_sv = c.v(nc.vector.tensor_mul(sv[:, :], sv[:, :], rstd[:, :]))
    t_sv.wait(nc.scalar)
    c.s(nc.scalar.activation(sv[:, :], sv[:, :], AF.Ln, bias=epsc))
    t_r2 = c.s(nc.scalar.activation(sv[:, :], sv[:, :], AF.Exp,
                                    scale=halfneg))
    t_r2.wait(nc.vector)
    t_rc = c.v(nc.vector.tensor_mul(rstdc[:, :], rstd[:, :], sv[:, :]))
    t_nm = c.v(nc.vector.scalar_tensor_tensor(
        nmr[:, :], mcol[:, :], -1.0, rstdc[:, :], ALU.mult, ALU.mult))
    nmr1 = c.sb([128, NT])
    t_nm1 = c.v(nc.vector.scalar_tensor_tensor(
        nmr1[:, :], mcol[:, :], -1.0, rstd[:, :], ALU.mult, ALU.mult))
    for t in range(NT):
        xt = xr.acquire(nc.sync)
        s_ld = c.dsem("lnx")
        s_ld.inc(nc.sync.dma_start(xt[:, :], x3[t, :, :]), 16)
        st1 = stg.acquire(nc.scalar)
        s_ld.wait(nc.scalar)
        t_nm1.wait(nc.scalar)
        tok1 = c.s(nc.scalar.activation(st1[:, :], xt[:, :], AF.Identity,
                                        scale=rstd[:, t:t + 1],
                                        bias=nmr1[:, t:t + 1]))
        xh_ds.put(t, tok1, st1[:, :])
        stg.release(Tok(xh_ds.s_store.h, xh_ds.s_store.v))
        st2 = stg.acquire(nc.scalar)
        t_nm.wait(nc.scalar)
        tok2 = c.s(nc.scalar.activation(sq_scr[:, :], xt[:, :], AF.Identity,
                                        scale=rstdc[:, t:t + 1],
                                        bias=nmr[:, t:t + 1]))
        xr.release(tok2)
        tok2.wait(nc.vector)
        c.v(nc.vector.tensor_mul(sq_scr[:, :], sq_scr[:, :], g_bc[:, :]))
        tok3 = c.v(nc.vector.tensor_add(st2[:, :], sq_scr[:, :], b_bc[:, :]))
        xn_ds.put(t, tok3, st2[:, :])
        stg.release(Tok(xn_ds.s_store.h, xn_ds.s_store.v))
        tok3.wait(nc.scalar)

    if phase == 1:
        s1 = c.sem()
        xn_ds.s_store.wait(nc.gpsimd)
        s1.inc(nc.gpsimd.dma_start(out_hbm[:, :], xn_ds.hbm[:, :]), 16)
        s1.wait(nc.gpsimd)
        return s1
    xssm_hbm = c.dram([DI, T], BF16)
    z_hbm = c.dram([DI, T], BF16)
    s_xs = c.sem()
    s_zs = c.sem()
    sub = c.sub()
    cs = sub.__enter__()
    wtile = cs.sb([128, ND * DI], BF16)
    yblk = cs.sbring(2, [128, ND * 512], BF16)
    pps = cs.ring(2, [128, 512])
    pstg = cs.sbring(2, [128, 512], BF16)
    tok_w = None
    for half, (cols, hbm, s_st) in enumerate(
            (((0, DI), xssm_hbm, s_xs), ((DI, 2 * DI), z_hbm, s_zs))):
        if tok_w is not None:
            tok_w.wait(nc.gpsimd)
        w_ = emit_load_weight(cs, io["in_proj"][:, cols[0]:cols[1]], D, DI,
                              tile=wtile)
        for nb in range(4):
            n0 = nb * 512
            yb = yblk.acquire(nc.sync)
            s_y = c.dsem("kvblk")
            xn_ds.load_fm(yb, n0, n0 + 512, s_y)
            for ko in range(NKI):
                pt = pps.acquire(nc.tensor)
                w_.sem.wait(nc.tensor)
                s_y.wait(nc.tensor)
                t_mm = c.mark(mm_group(nc, pt[:, 0:512],
                              [(w_.lhsT(ki, ko * 128, (ko + 1) * 128),
                                yb[:, ki * 512:(ki + 1) * 512])
                               for ki in range(ND)]), "tensor")
                tok_w = t_mm
                ps_ = pstg.acquire(nc.scalar)
                t_mm.wait(nc.scalar)
                t_ev = c.s(nc.scalar.copy(ps_[:, :], pt[:, 0:512]))
                pps.release(t_ev)
                t_ev.wait(nc.sync)
                s_st.inc(nc.sync.dma_start(
                    hbm[ko * 128:(ko + 1) * 128, n0:n0 + 512], ps_[:, :]), 16)
                pstg.release(Tok(s_st.h, s_st.v))
            yblk.release(t_mm)
    sub.__exit__(None, None, None)

    if phase == 2:
        s1 = c.sem()
        s_xs.wait(nc.gpsimd)
        s1.inc(nc.gpsimd.dma_start(
            out_hbm[:, :].rearrange("(a p) t -> p a t", p=128),
            xssm_hbm[:, 1024:2048].rearrange("(a p) t -> p a t", p=128)), 16)
        s1.wait(nc.gpsimd)
        return s1
    conv_hbm = c.dram([DI, T], BF16)
    xpT = c.sb([128, T], BF16)
    s_cv = c.sem()
    sub = c.sub()
    cs = sub.__enter__()
    cwc, s_cw = emit_load_f32(cs, io["conv_wc"], 128, NKI * 4)
    cbc, s_cb = emit_load_f32(cs, io["conv_bc"], 128, NKI)
    wxp = emit_load_weight(cs, io["x_proj"], DI, 96)
    xin = cs.sbring(2, [128, T + 4], BF16)
    cscr = cs.sb([128, T])
    cbf = cs.sbring(2, [128, T], BF16)
    xpps = [cs.ps([128, 512]) for _ in range(4)]
    t_cvl = None
    for ko in range(NKI):
        xi = xin.acquire(nc.sync)
        s_xl = c.dsemm if False else c.dsem("cvx")
        s_xs.wait(nc.sync)
        s_xl.inc(nc.sync.dma_start(xi[:, 4:4 + T],
                                   xssm_hbm[ko * 128:(ko + 1) * 128, :]), 16)
        s_xl.wait(nc.vector)
        t_z = c.v(nc.vector.memset(xi[:, 0:4], 0.0))
        s_cw.wait(nc.vector)
        s_cb.wait(nc.vector)
        t_cv = None
        for k in range(4):
            if k == 0:
                c.v(nc.vector.tensor_scalar_mul(
                    cscr[:, :], xi[:, 1 + k: 1 + k + T],
                    cwc[:, ko * 4 + k: ko * 4 + k + 1]))
            else:
                c.v(nc.vector.scalar_tensor_tensor(
                    cscr[:, :], xi[:, 1 + k: 1 + k + T],
                    cwc[:, ko * 4 + k: ko * 4 + k + 1], cscr[:, :],
                    ALU.mult, ALU.add))
        t_cv = c.v(nc.vector.tensor_scalar_add(cscr[:, :], cscr[:, :],
                                               cbc[:, ko:ko + 1]))
        cb = cbf.acquire(nc.scalar)
        t_cv.wait(nc.scalar)
        t_si = c.s(nc.scalar.activation(cb[:, :], cscr[:, :], AF.Silu))
        t_si.wait(nc.sync)
        s_cv.inc(nc.sync.dma_start(conv_hbm[ko * 128:(ko + 1) * 128, :],
                                   cb[:, :]), 16)
        wxp.sem.wait(nc.tensor)
        t_si.wait(nc.tensor)
        t_xp = None
        for nb in range(4):
            mm = nc.tensor.matmul(xpps[nb][0:96, 0:512],
                                  wxp.lhsT(ko, 0, 96),
                                  cb[:, nb * 512:(nb + 1) * 512],
                                  start=(ko == 0), stop=(ko == NKI - 1))
            t_xp = c.mark(mm, "tensor")
        cbf.release(Multi([t_xp, Tok(s_cv.h, s_cv.v)]))
        xin.release(t_cv)
        t_cvl = t_xp
    t_xpe = None
    for nb in range(4):
        t_cvl.wait(nc.scalar)
        t_xpe = c.s(nc.scalar.copy(xpT[0:96, nb * 512:(nb + 1) * 512],
                                   xpps[nb][0:96, 0:512]))
    sub.__exit__(None, None, None)

    if phase == 3:
        s1 = c.sem()
        s_cv.wait(nc.gpsimd)
        t_xpe.wait(nc.gpsimd)
        s1.inc(nc.gpsimd.dma_start(
            out_hbm[0:1024, :].rearrange("(a p) t -> p a t", p=128),
            conv_hbm[0:1024, 1024:2048].rearrange("(a p) t -> p a t", p=128)), 16)
        s1.inc(nc.gpsimd.dma_start(
            out_hbm[1024:1280, :].rearrange("(a p) t -> p a t", p=128),
            xpT[:, :].rearrange("p (a t) -> p a t", a=2)), 16)
        s1.wait(nc.gpsimd)
        return s1
    bc_hbm = c.dram([32, T], BF16)
    s_bc = c.sem()
    t_xpe.wait(nc.sync)
    s_bc.inc(nc.sync.dma_start(bc_hbm[:, :], xpT[64:96, :]), 16)

    y_hbm = c.dram([DI, T], BF16)
    s_yh = c.sem()
    Acol = c.sb([128, DS])
    states = c.sb([128, NKI * DS])
    logc, s_lA = emit_load_f32(c, io["logA_c"], 128, NKI * DS)
    dpc, s_dp = emit_load_f32(c, io["Dp_c"], 128, NKI)
    dtbc, s_dtb = emit_load_bf16_row(c, io["dt_bias"], DI)
    t_st0 = c.v(nc.vector.memset(states[:, :], 0.0))
    sub = c.sub()
    cs = sub.__enter__()
    dtp = cs.sb([64, DI], BF16)
    s_dtp = c.sem()
    s_dtp.inc(nc.gpsimd.dma_start(dtp[:, :], io["dt_proj"][:, :]), 16)
    onesw = cs.sb([1, 512], BF16)
    t_onw = c.v(nc.vector.memset(onesw[0:1, :], 1.0))
    B_bc = [cs.sb([128, TH], BF16) for _ in range(DS)]
    C_bc = [cs.sb([128, TH], BF16) for _ in range(DS)]
    rowst = cs.sbring(2, [1, TH], BF16)
    bcps = cs.ring(2, [128, 512], F32)
    deltaC = cs.sb([128, TH])
    dxC = cs.sb([128, TH])
    dAs = cs.sb([128, TH])
    us = cs.sb([128, TH])
    sco = cs.sb([128, TH])
    ytile = cs.sb([128, TH])
    tmps = cs.sb([128, TH])
    convC = cs.sbring(2, [128, TH], BF16)
    zC = cs.sbring(2, [128, TH], BF16)
    ystg = cs.sbring(2, [128, TH], BF16)
    dps = cs.ring(2, [128, 512])
    t_sc = t_ya = t_u = t_ye = None
    halves = [0] if phase == 5 else range(2)
    for half in halves:
        h0 = half * TH
        if half == 1:
            c.tip("pool").wait(nc.vector)
        for s_i in range(2 * DS):
            rw = rowst.acquire(nc.sync)
            s_rl = c.dsem("bcrow")
            s_bc.wait(nc.sync)
            s_rl.inc(nc.sync.dma_start(rw[0:1, :],
                                       bc_hbm[s_i:s_i + 1, h0:h0 + TH]), 16)
            dst = (B_bc if s_i < DS else C_bc)[s_i % DS]
            t_b = None
            for o0 in range(0, TH, 512):
                pt = bcps.acquire(nc.tensor)
                s_rl.wait(nc.tensor)
                t_onw.wait(nc.tensor)
                mm = nc.tensor.matmul(pt[:, 0:512], onesw[0:1, 0:128],
                                      rw[0:1, o0:o0 + 512],
                                      start=True, stop=True)
                t_mmb = c.mark(mm, "tensor")
                t_mmb.wait(nc.vector)
                t_b = c.v(nc.vector.tensor_copy(dst[:, o0:o0 + 512],
                                                pt[:, 0:512]))
                bcps.release(t_b)
            rowst.release(t_b)
        t_bcast = t_b
        for ko in ([0] if phase == 5 else range(NKI)):
            cvt = convC.acquire(nc.sync)
            s_cl = c.dsem("cvl")
            s_cv.wait(nc.sync)
            s_cl.inc(nc.sync.dma_start(
                cvt[:, :], conv_hbm[ko * 128:(ko + 1) * 128, h0:h0 + TH]), 16)
            zt = zC.acquire(nc.sync)
            s_zl = c.dsem("zl")
            s_zs.wait(nc.sync)
            s_zl.inc(nc.sync.dma_start(
                zt[:, :], z_hbm[ko * 128:(ko + 1) * 128, h0:h0 + TH]), 16)
            t_dl = None
            for o0 in range(0, TH, 512):
                pt = dps.acquire(nc.tensor)
                s_dtp.wait(nc.tensor)
                t_xpe.wait(nc.tensor)
                s_dtb.wait(nc.tensor)
                t_onw.wait(nc.tensor)
                pairs = [(dtp[:, ko * 128:(ko + 1) * 128],
                          xpT[0:64, h0 + o0: h0 + o0 + 512])]
                pairs.append((dtbc[0:1, ko * 128:(ko + 1) * 128],
                              onesw[0:1, 0:512]))
                t_mmd = c.mark(mm_group(nc, pt[:, 0:512], pairs), "tensor")
                t_mmd.wait(nc.scalar)
                c.wait_const(nc.scalar)
                if t_ye is not None and o0 == 0:
                    t_ye.wait(nc.scalar)
                c.s(nc.scalar.activation(dAs[:, o0:o0 + 512], pt[:, 0:512],
                                         AF.Abs))
                c.s(nc.scalar.activation(dAs[:, o0:o0 + 512],
                                         dAs[:, o0:o0 + 512], AF.Exp,
                                         scale=c.constcol(-1.0)))
                c.s(nc.scalar.activation(dAs[:, o0:o0 + 512],
                                         dAs[:, o0:o0 + 512], AF.Ln,
                                         bias=c.constcol(1.0)))
                t_rl = c.s(nc.scalar.activation(deltaC[:, o0:o0 + 512],
                                                pt[:, 0:512], AF.Relu))
                t_rl.wait(nc.vector)
                t_dl = c.v(nc.vector.tensor_add(deltaC[:, o0:o0 + 512],
                                                deltaC[:, o0:o0 + 512],
                                                dAs[:, o0:o0 + 512]))
                dps.release(t_dl)
            s_cl.wait(nc.vector)
            if t_u is not None:
                t_u.wait(nc.vector)
            t_dx = c.v(nc.vector.tensor_mul(dxC[:, :], deltaC[:, :],
                                            cvt[:, :]))
            s_lA.wait(nc.scalar)
            t_A = c.s(nc.scalar.activation(
                Acol[:, :], logc[:, ko * DS:(ko + 1) * DS], AF.Exp))
            t_A.wait(nc.vector)
            t_An = c.v(nc.vector.tensor_scalar_mul(Acol[:, :], Acol[:, :],
                                                   -1.0))
            t_y0 = c.v(nc.vector.memset(ytile[:, :], 0.0))
            for s_i in range(DS):
                t_An.wait(nc.scalar)
                t_dl.wait(nc.scalar)
                if t_sc is not None:
                    t_sc.wait(nc.scalar)
                t_dA = c.s(nc.scalar.activation(dAs[:, :], deltaC[:, :],
                                                AF.Exp,
                                                scale=Acol[:, s_i:s_i + 1]))
                if t_sc is not None:
                    t_sc.wait(nc.gpsimd)
                t_dx.wait(nc.gpsimd)
                if s_i == 0:
                    t_bcast.wait(nc.gpsimd)
                t_u = c.g(nc.gpsimd.tensor_mul(us[:, :], dxC[:, :],
                                               B_bc[s_i][:, :]))
                t_dA.wait(nc.vector)
                t_u.wait(nc.vector)
                t_sc = c.v(nc.vector.tensor_tensor_scan(
                    sco[:, :], dAs[:, :], us[:, :],
                    states[:, ko * DS + s_i: ko * DS + s_i + 1],
                    ALU.mult, ALU.add))
                c.v(nc.vector.tensor_copy(
                    states[:, ko * DS + s_i: ko * DS + s_i + 1],
                    sco[:, TH - 1:TH]))
                if t_ya is not None:
                    t_ya.wait(nc.gpsimd)
                t_sc.wait(nc.gpsimd)
                t_p = c.g(nc.gpsimd.tensor_mul(tmps[:, :], sco[:, :],
                                               C_bc[s_i][:, :]))
                t_p.wait(nc.vector)
                t_ya = c.v(nc.vector.tensor_add(ytile[:, :], ytile[:, :],
                                                tmps[:, :]))
            s_dp.wait(nc.vector)
            c.v(nc.vector.scalar_tensor_tensor(
                ytile[:, :], cvt[:, :], dpc[:, ko:ko + 1], ytile[:, :],
                ALU.mult, ALU.add))
            convC.release(c.tip("vector"))
            s_zl.wait(nc.scalar)
            t_sc.wait(nc.scalar)
            t_zs = c.s(nc.scalar.activation(dAs[:, :], zt[:, :], AF.Silu))
            zC.release(t_zs)
            t_zs.wait(nc.vector)
            ys = ystg.acquire(nc.vector)
            t_ye = c.v(nc.vector.tensor_mul(ys[:, :], ytile[:, :],
                                            dAs[:, :]))
            t_ye.wait(nc.sync)
            s_yh.inc(nc.sync.dma_start(
                y_hbm[ko * 128:(ko + 1) * 128, h0:h0 + TH], ys[:, :]), 16)
            ystg.release(Tok(s_yh.h, s_yh.v))
    if phase == 5:
        s1 = c.sem()
        t_ye.wait(nc.gpsimd)
        t_sc.wait(nc.gpsimd)
        for j, buf in enumerate([deltaC, dxC, ytile, sco, B_bc[0][:, :],
                                 C_bc[0][:, :], us, dAs]):
            s1.inc(nc.gpsimd.dma_start(out_hbm[j * 128:(j + 1) * 128, :],
                                       buf[:, :] if j < 8 else buf), 16)
        s1.inc(nc.gpsimd.dma_start(out_hbm[1024:1152, 0:256], states[:, :]),
               16)
        s1.wait(nc.gpsimd)
        sub.__exit__(None, None, None)
        return s1
    sub.__exit__(None, None, None)

    if phase == 4:
        s1 = c.sem()
        s_yh.wait(nc.gpsimd)
        s1.inc(nc.gpsimd.dma_start(
            out_hbm[:, :].rearrange("(a p) t -> p a t", p=128),
            y_hbm[0:2048, 1024:2048].rearrange("(a p) t -> p a t", p=128)), 16)
        s1.wait(nc.gpsimd)
        return s1
    oscr = c.sb([128, D])
    sub = c.sub()
    cs = sub.__enter__()
    wo = emit_load_weight(cs, io["out_proj"], DI, D)
    ylhs = cs.sbring(2, [128, NKI * 128], BF16)
    ops = cs.ring(2, [128, D])
    xrr = cs.sbring(2, [128, D], F32)
    hrr = cs.sbring(2, [128, D], BF16)
    o3 = out_hbm.rearrange("(t p) f -> t p f", p=128)
    s_out = c.sem()
    y4 = y_hbm.rearrange("(ki p) t -> p ki t", p=128)
    for t in range(NT):
        yl = ylhs.acquire(nc.sync)
        s_yl = c.dsem("ylb")
        s_yh.wait(nc.sync)
        s_yl.inc(nc.sync.dma_start(
            yl[:, :].rearrange("p (ki n) -> p ki n", ki=NKI),
            y4[:, :, t * 128:(t + 1) * 128]), 16)
        pt = ops.acquire(nc.tensor)
        wo.sem.wait(nc.tensor)
        s_yl.wait(nc.tensor)
        for o0 in (0, 512):
            mm = mm_group(nc, pt[:, o0:o0 + 512],
                          [(yl[:, ki * 128:(ki + 1) * 128],
                            wo.lhsT(ki, o0, o0 + 512)) for ki in range(NKI)])
        t_mm = c.mark(mm, "tensor")
        ylhs.release(t_mm)
        xt = xrr.acquire(nc.sync)
        s_lx = c.dsem("xw")
        s_lx.inc(nc.sync.dma_start(xt[:, :], x3[t, :, :]), 16)
        ht = hrr.acquire(nc.sync)
        s_lh = c.dsem("hw")
        xh_ds.s_store.wait(nc.sync)
        s_lh.inc(nc.sync.dma_start(ht[:, :], xh_ds.h3[t, :, :]), 16)
        t_mm.wait(nc.vector)
        s_lx.wait(nc.vector)
        s_lh.wait(nc.vector)
        c.v(nc.vector.tensor_add(oscr[:, :], xt[:, :], ht[:, :]))
        tok_o = c.v(nc.vector.tensor_add(oscr[:, :], oscr[:, :], pt[:, 0:D]))
        ops.release(tok_o)
        xrr.release(tok_o)
        hrr.release(tok_o)
        tok_o.wait(nc.sync)
        s_out.inc(nc.sync.dma_start(o3[t, :, :], oscr[:, :]), 16)
        Tok(s_out.h, s_out.v).wait(nc.vector)
    sub.__exit__(None, None, None)
    s_out.wait(nc.sync)
    return s_out


def build_l2attnb_program():
    nc = bass.Bass("TRN2", target_bir_lowering=False)
    stack = ExitStack()
    c = Ctx(nc, stack)
    dtm = lambda n, sh_: nc.dram_tensor(n, sh_, F32, kind="ExternalInput").ap()
    io = {
        "norm_g": dtm("norm_g", [D]),
        "norm_b": dtm("norm_b", [D]),
        "in_proj": dtm("in_proj", [D, 4096]),
        "conv_wc": dtm("conv_wc", [128, 64]),
        "conv_bc": dtm("conv_bc", [128, 16]),
        "x_proj": dtm("x_proj", [2048, 96]),
        "dt_proj": dtm("dt_proj", [64, 2048]),
        "dt_bias": dtm("dt_bias", [2048]),
        "logA_c": dtm("logA_c", [128, 256]),
        "Dp_c": dtm("Dp_c", [128, 16]),
        "out_proj": dtm("out_proj", [2048, D]),
    }
    x_h = dtm("x", [T, D])
    out_h = nc.dram_tensor("out", [T, D], F32, kind="ExternalOutput").ap()
    import os
    with stack:
        emit_l2_attn_b(c, x_h, out_h, io,
                       phase=int(os.environ.get("MBPH", "9")))
    return nc



def emit_l3c_masks(c):
    nc = c.nc
    msl = c.sb([128, 128])
    msu = c.sb([128, 128])
    mkeepU = c.sb([128, 32])
    ident32 = c.sb([128, 32], BF16)
    iden128 = c.sb([128, 128])
    iscr = c.sb([128, 128])
    t0 = c.v(nc.vector.memset(msl[:, :], 0.0))
    t1 = c.v(nc.vector.memset(msu[:, :], 0.0))
    toks = []
    for b in range(4):
        r = slice(32 * b, 32 * b + 32)
        t_i = c.g(nc.gpsimd.iota(iscr[r, 0:32], pattern=[[1, 32]], base=0,
                                 channel_multiplier=-1,
                                 allow_small_or_imprecise_dtypes=True))
        t_i.wait(nc.vector)
        c.v(nc.vector.tensor_single_scalar(
            msl[r, 32 * b:32 * b + 32], iscr[r, 0:32], 0.0, ALU.is_lt))
        c.v(nc.vector.tensor_single_scalar(
            msu[r, 32 * b:32 * b + 32], iscr[r, 0:32], 0.0, ALU.is_gt))
        c.v(nc.vector.tensor_single_scalar(
            mkeepU[r, 0:32], iscr[r, 0:32], 0.0, ALU.is_ge))
        toks.append(c.v(nc.vector.tensor_single_scalar(
            ident32[r, 0:32], iscr[r, 0:32], 0.0, ALU.is_equal)))
    c.v(nc.vector.tensor_scalar_mul(msl[:, :], msl[:, :], -1.0))
    t_msu = c.v(nc.vector.tensor_scalar_mul(msu[:, :], msu[:, :], -1.0))
    c.tip("vector").wait(nc.gpsimd)
    t_ii = c.g(nc.gpsimd.iota(iscr[:, :], pattern=[[1, 128]], base=0,
                              channel_multiplier=-1,
                              allow_small_or_imprecise_dtypes=True))
    t_ii.wait(nc.vector)
    t_id = c.v(nc.vector.tensor_single_scalar(
        iden128[:, :], iscr[:, :], 0.0, ALU.is_equal))
    return {"msl": msl, "msu": msu, "mkeepU": mkeepU, "ident32": ident32,
            "iden128": iden128, "tok": Multi([t_msu, t_id] + toks)}


def emit_proj_conv_store(c, cs, xnT, s_xn, w_hbm, cw_hbm, dst_hbm, *,
                         norm_rows=None, pf=None, cacc=None, psr=None,
                         wtile=None):
    import os
    nc = c.nc
    c.tip("tensor").wait(nc.gpsimd)
    w = emit_load_weight(cs, w_hbm, D, D, tile=wtile)
    cw, s_cw = emit_load_f32(cs, cw_hbm, 128, 32)
    t_pr = emit_proj_fm(c, xnT, s_xn, w, pf, psr=psr)
    if os.environ.get("L3B_RAW"):
        return [t_pr] * ND
    t_cv = []
    for ki in range(ND):
        b0 = ki * T
        t_pr.wait(nc.vector)
        s_cw.wait(nc.vector)
        if t_cv:
            t_cv[-1].wait(nc.vector)
        c.v(nc.vector.tensor_scalar_mul(
            cacc[:, :], pf[:, b0:b0 + T], cw[:, ki * 4 + 3:ki * 4 + 4]))
        for s in range(1, 4):
            c.v(nc.vector.scalar_tensor_tensor(
                cacc[:, s:T], pf[:, b0:b0 + T - s],
                cw[:, ki * 4 + 3 - s:ki * 4 + 4 - s], cacc[:, s:T],
                ALU.mult, ALU.add))
        t_c = c.tip("vector")
        t_c.wait(nc.scalar)
        t_cv.append(c.s(nc.scalar.activation(pf[:, b0:b0 + T], cacc[:, :],
                                             AF.Silu)))
    return t_cv


def emit_l3_attn_b(c, x_hbm, out_hbm, io):
    import os
    ph = int(os.environ.get("L3B_PH", "9"))
    nc = c.nc
    H, DH, NG = 4, 256, 16
    mk = emit_l3c_masks(c)
    onw_bc = c.sb([128, DH])
    oT_big = c.sb([128, ND * T], BF16)
    ds, rstd, nmr, t_cols = emit_ln_stream(c, x_hbm)
    q_hbm = c.dram([D, T], BF16)
    k_hbm = c.dram([D, T], BF16)
    v_hbm = c.dram([D, T], BF16)
    s_qs, s_ks, s_vs = c.sem(), c.sem(), c.sem()
    onr, s_onr = emit_load_bf16_row(c, io["onorm_w"], DH)
    onf = c.sb([1, DH])
    s_onr.wait(nc.vector)
    t_onf = c.v(nc.vector.tensor_copy(onf[0:1, :], onr[0:1, :]))
    sub = c.sub()
    cs = sub.__enter__()
    t_ob = emit_bcast_rows(c, cs, [(onf[0:1, :], onw_bc, t_onf)])
    xnT = cs.sb([128, ND * T], BF16)
    s_xn = c.sem()
    ds.load_fm(xnT, 0, T, s_xn)
    pf = cs.sb([128, ND * T], BF16)
    cacc = cs.sb([128, T])
    psr = cs.ring(2, [128, 512])
    rb = cs.sb([128, T])
    sqb = cs.sb([128, 2 * T], BF16)
    onesf = cs.sb([128, 1], BF16)
    t_of = c.v(nc.vector.memset(onesf[:, 0:1], 1.0))
    rn_row = cs.sb([1, T])
    qstg = cs.sbring(2, [128, T], BF16)
    wtile = cs.sb([128, ND * D], BF16)
    LN16 = -2.77258872223978
    for pname, whb, cwhb, dsth, s_st, normq in (
            ("q", io["Wq"], io["conv_wq_c"], q_hbm, s_qs, 1),
            ("k", io["Wk"], io["conv_wk_c"], k_hbm, s_ks, 2),
            ("v", io["Wv"], io["conv_wv_c"], v_hbm, s_vs, 0)):
        t_cv = emit_proj_conv_store(c, cs, xnT, s_xn, whb, cwhb, dsth,
                                    pf=pf, cacc=cacc, psr=psr, wtile=wtile)
        if not normq:
            for ki in range(ND):
                t_cv[ki].wait(nc.sync)
                s_st.inc(nc.sync.dma_start(
                    dsth[ki * 128:(ki + 1) * 128, :],
                    pf[:, ki * T:(ki + 1) * T]), 16)
            continue
        for h in range(H):
            for d_ in range(2):
                ki = 2 * h + d_
                t_cv[ki].wait(nc.scalar)
                c.tip("tensor").wait(nc.scalar)
                c.s(nc.scalar.activation(sqb[:, d_ * T:(d_ + 1) * T],
                                         pf[:, ki * T:(ki + 1) * T],
                                         AF.Square))
            t_sq = c.tip("scalar")
            for n0 in range(0, T, 256):
                pt = psr.acquire(nc.tensor)
                t_sq.wait(nc.tensor)
                t_of.wait(nc.tensor)
                for d_ in range(2):
                    mm = nc.tensor.matmul(
                        pt[0:1, 0:256], onesf[:, 0:1],
                        sqb[:, d_ * T + n0:d_ * T + n0 + 256],
                        start=(d_ == 0), stop=(d_ == 1))
                t_mm = c.mark(mm, "tensor")
                t_mm.wait(nc.vector)
                c.tip("tensor").wait(nc.vector)
                t_ev = c.v(nc.vector.tensor_copy(
                    rn_row[0:1, n0:n0 + 256], pt[0:1, 0:256]))
                psr.release(t_ev)
            t_ev.wait(nc.scalar)
            c.wait_const(nc.scalar)
            c.s(nc.scalar.activation(rn_row[0:1, :], rn_row[0:1, :],
                                     AF.Ln, bias=c.constcol(1e-6)[0:1, :]))
            t_rn = c.s(nc.scalar.activation(
                rn_row[0:1, :], rn_row[0:1, :], AF.Exp,
                scale=c.constcol(-0.5)[0:1, :],
                bias=c.constcol(LN16 if normq == 1 else 0.0)[0:1, :]))
            bsub = c.sub()
            bcs = bsub.__enter__()
            t_rb = emit_bcast_rows(c, bcs, [(rn_row[0:1, :], rb, t_rn)])
            bsub.__exit__(None, None, None)
            for d_ in range(2):
                ki = 2 * h + d_
                qs = qstg.acquire(nc.vector)
                t_rb.wait(nc.vector)
                t_n = c.v(nc.vector.tensor_mul(qs[:, :],
                                               pf[:, ki * T:(ki + 1) * T],
                                               rb[:, :]))
                t_n.wait(nc.sync)
                s_st.inc(nc.sync.dma_start(
                    dsth[ki * 128:(ki + 1) * 128, :], qs[:, :]), 16)
                qstg.release(Tok(s_st.h, s_st.v))
    cs.exit_waits += [s_qs, s_ks, s_vs]
    sub.__exit__(None, None, None)
    if ph == 1:
        s1 = c.sem()
        s_qs.wait(nc.gpsimd)
        s_ks.wait(nc.gpsimd)
        s_vs.wait(nc.gpsimd)
        s1.inc(nc.gpsimd.dma_start(
            out_hbm[0:1024, :].rearrange("(a p) t -> p a t", p=128),
            q_hbm[:, 0:1024].rearrange("(a p) t -> p a t", p=128)), 16)
        s1.inc(nc.gpsimd.dma_start(
            out_hbm[1024:2048, :].rearrange("(a p) t -> p a t", p=128),
            v_hbm[:, 1024:2048].rearrange("(a p) t -> p a t", p=128)), 16)
        s1.wait(nc.gpsimd)
        return s1
    sub = c.sub()
    cs = sub.__enter__()
    Sf = [[cs.sb([128, DH]) for _ in range(2)] for _ in range(H)]
    Sb = [[cs.sb([128, DH], BF16) for _ in range(2)] for _ in range(H)]
    uwp = cs.ps([128, DH])
    opp = cs.ps([128, DH])
    ptp = cs.ps([128, DH])
    pbp = cs.ps([128, DH])
    ltp = cs.ps([128, 32])
    tpb = cs.ps([128, 32], BF16)
    dSp = [cs.ps([128, DH]) for _ in range(2)]
    tok_uw = tok_oA = tok_oB = tok_pt = tok_pb = tok_lt = None
    tok_tp = None
    tok_dS = None
    qf = cs.sb([128, ND * 128], BF16)
    kf = cs.sb([128, ND * 128], BF16)
    ktok = [cs.sb([128, DH], BF16) for _ in range(H)]
    vtok = [cs.sb([128, DH], BF16) for _ in range(H)]
    ug = [cs.sb([128, DH]) for _ in range(H)]
    wtg = [cs.sb([128, DH], BF16) for _ in range(H)]
    Np = [cs.sb([128, 128]) for _ in range(5)]
    Ntp = [cs.sb([128, 128]) for _ in range(5)]
    Rm = cs.sb([128, 128])
    Rt = cs.sb([128, 128])
    Rb = cs.sb([128, 128], BF16)
    lt_sb = cs.sb([128, 32], BF16)
    un_sb = cs.sb([128, DH], BF16)
    o2_sb = cs.sb([128, DH])
    osc_sb = cs.sb([128, DH], BF16)
    rsn = cs.sb([128, 8])
    tSb = [None] * H
    have_S = False
    t_red_prev = None
    for g in range(NG):
        g0 = g * 128
        c.tip("tensor").wait(nc.sync)
        s_g = c.dsem("l3bg")
        s_qs.wait(nc.sync)
        s_ks.wait(nc.sync)
        s_vs.wait(nc.sync)
        s_g.inc(nc.sync.dma_start(
            qf[:, :].rearrange("p (ki t) -> p ki t", ki=ND),
            q_hbm.rearrange("(ki p) t -> p ki t", p=128)[:, :, g0:g0 + 128]),
            16)
        s_g.inc(nc.sync.dma_start(
            kf[:, :].rearrange("p (ki t) -> p ki t", ki=ND),
            k_hbm.rearrange("(ki p) t -> p ki t", p=128)[:, :, g0:g0 + 128]),
            16)
        for h in range(H):
            s_g.inc(nc.sync.dma_start_transpose(
                ktok[h][:, :], k_hbm[h * DH:(h + 1) * DH, g0:g0 + 128]), 16)
            s_g.inc(nc.sync.dma_start_transpose(
                vtok[h][:, :], v_hbm[h * DH:(h + 1) * DH, g0:g0 + 128]), 16)
        prep_lv = int(os.environ.get("L3B_PREP", "6"))
        hlist = [0] if ph == 6 else (range(H) if prep_lv >= 2 else [])
        for h in hlist:
            s_g.wait(nc.tensor)
            mk["tok"].wait(nc.tensor)
            if tok_pt is not None:
                tok_pt.wait(nc.tensor)
            for d_ in range(2):
                mm = nc.tensor.matmul(
                    ptp[:, 0:128], kf[:, (2 * h + d_) * 128:(2 * h + d_ + 1) * 128],
                    kf[:, (2 * h + d_) * 128:(2 * h + d_ + 1) * 128],
                    start=(d_ == 0), stop=(d_ == 1))
            t_mm = c.mark(mm, "tensor")
            t_mm.wait(nc.vector)
            c.tip("tensor").wait(nc.vector)
            c.v(nc.vector.tensor_mul(Np[0][:, :], ptp[:, 0:128], mk["msl"][:, :]))
            tok_pt = t_g = c.v(nc.vector.tensor_mul(Ntp[0][:, :], ptp[:, 0:128],
                                                    mk["msu"][:, :]))
            for j in (range(1, 5) if prep_lv >= 3 else []):
                t_g.wait(nc.tensor)
                tok_pt.wait(nc.tensor)
                t_a = c.mark(nc.tensor.matmul(ptp[:, 0:128], Ntp[j - 1][:, :],
                                              Np[j - 1][:, :],
                                              start=True, stop=True), "tensor")
                if tok_pb is not None:
                    tok_pb.wait(nc.tensor)
                t_b = c.mark(nc.tensor.matmul(pbp[:, 0:128], Np[j - 1][:, :],
                                              Ntp[j - 1][:, :],
                                              start=True, stop=True), "tensor")
                t_a.wait(nc.vector)
                c.tip("tensor").wait(nc.vector)
                tok_pt = c.v(nc.vector.tensor_copy(Np[j][:, :], ptp[:, 0:128]))
                t_b.wait(nc.vector)
                tok_pb = t_g = c.v(nc.vector.tensor_copy(Ntp[j][:, :],
                                                         pbp[:, 0:128]))
            if prep_lv < 4:
                continue
            c.tip("tensor").wait(nc.vector)
            c.v(nc.vector.tensor_add(Rm[:, :], Ntp[4][:, :], mk["iden128"][:, :]))
            t_ch = c.v(nc.vector.tensor_add(Rt[:, :], Np[4][:, :],
                                            mk["iden128"][:, :]))
            for j in (3, 2, 1, 0):
                t_ch.wait(nc.tensor)
                tok_pt.wait(nc.tensor)
                t_a = c.mark(nc.tensor.matmul(ptp[:, 0:128], Rt[:, :],
                                              Ntp[j][:, :],
                                              start=True, stop=True), "tensor")
                tok_pb.wait(nc.tensor)
                t_b = c.mark(nc.tensor.matmul(pbp[:, 0:128], Ntp[j][:, :],
                                              Rt[:, :],
                                              start=True, stop=True), "tensor")
                t_a.wait(nc.vector)
                t_b.wait(nc.vector)
                tok_pt = c.v(nc.vector.tensor_add(Rm[:, :], ptp[:, 0:128],
                                                  Rm[:, :]))
                tok_pb = t_ch = c.v(nc.vector.tensor_add(Rt[:, :],
                                                         pbp[:, 0:128],
                                                         Rt[:, :]))
            if prep_lv < 5:
                continue
            t_ch.wait(nc.vector)
            c.tip("tensor").wait(nc.vector)
            t_rb = c.v(nc.vector.tensor_copy(Rb[:, :], Rm[:, :]))
            t_rb.wait(nc.tensor)
            tok_pt.wait(nc.tensor)
            t_u = c.mark(nc.tensor.matmul(ptp[:, 0:DH], Rb[:, :],
                                          vtok[h][:, :], start=True,
                                          stop=True), "tensor")
            t_u.wait(nc.vector)
            c.tip("tensor").wait(nc.vector)
            tok_pt = t_ue = c.v(nc.vector.tensor_copy(ug[h][:, :],
                                                      ptp[:, 0:DH]))
            for d_ in (range(2) if prep_lv >= 6 else []):
                tok_pb.wait(nc.tensor)
                t_w = c.mark(nc.tensor.matmul(
                    pbp[:, 0:128], ktok[h][:, d_ * 128:(d_ + 1) * 128],
                    Rb[:, :], start=True, stop=True), "tensor")
                t_w.wait(nc.vector)
                tok_pb = t_we = c.v(nc.vector.tensor_copy(
                    wtg[h][:, d_ * 128:(d_ + 1) * 128], pbp[:, 0:128]))
        t_prep = c.tip("vector")
        if ph == 6 and g == 0:
            s1 = c.sem()
            t_prep.wait(nc.gpsimd)
            s1.inc(nc.gpsimd.dma_start(out_hbm[0:128, 0:DH], ug[0][:, :]), 16)
            s1.inc(nc.gpsimd.dma_start(out_hbm[128:256, 0:DH], wtg[0][:, :]),
                   16)
            s1.inc(nc.gpsimd.dma_start(out_hbm[256:384, 0:128], Rm[:, :]), 16)
            s1.inc(nc.gpsimd.dma_start(out_hbm[384:512, 0:DH], ktok[0][:, :]),
                   16)
            s1.inc(nc.gpsimd.dma_start(out_hbm[512:640, 0:DH], vtok[0][:, :]),
                   16)
            s1.inc(nc.gpsimd.dma_start(
                out_hbm[640:768, 0:128],
                kf[:, 0 * 128:1 * 128]), 16)
            for jj in range(5):
                s1.inc(nc.gpsimd.dma_start(
                    out_hbm[768 + jj * 128:896 + jj * 128, 0:128],
                    Np[jj][:, :]), 16)
                s1.inc(nc.gpsimd.dma_start(
                    out_hbm[768 + (5 + jj) * 128:896 + (5 + jj) * 128, 0:128],
                    Ntp[jj][:, :]), 16)
            s1.inc(nc.gpsimd.dma_start(out_hbm[0:128, 256:384],
                                       mk["msl"][:, :]), 16)
            s1.inc(nc.gpsimd.dma_start(out_hbm[128:256, 256:384],
                                       mk["msu"][:, :]), 16)
            s1.inc(nc.gpsimd.dma_start(out_hbm[256:384, 256:384],
                                       mk["iden128"][:, :]), 16)
            s1.inc(nc.gpsimd.dma_start(out_hbm[384:512, 256:288],
                                       mk["mkeepU"][:, 0:32]), 16)
            s1.wait(nc.gpsimd)
            sub.__exit__(None, None, None)
            return s1
        for ch in (range(4) if ph >= 3 else []):
            b = 32 * ch
            for h in range(H):
                t_prep.wait(nc.tensor)
                if tok_lt is not None:
                    tok_lt.wait(nc.tensor)
                for d_ in range(2):
                    col = (2 * h + d_) * 128 + b
                    mm = nc.tensor.matmul(
                        ltp[b:b + 32, 0:32], kf[:, col:col + 32],
                        qf[:, col:col + 32], start=(d_ == 0), stop=(d_ == 1),
                        tile_position=(0, b))
                t_lt = c.mark(mm, "tensor")
                t_lt.wait(nc.vector)
                c.tip("tensor").wait(nc.vector)
                tok_lt = t_ltc = c.v(nc.vector.tensor_mul(
                    lt_sb[b:b + 32, 0:32], ltp[b:b + 32, 0:32],
                    mk["mkeepU"][b:b + 32, 0:32]))
                if tok_oA is not None:
                    tok_oA.wait(nc.tensor)
                    tok_oB.wait(nc.tensor)
                if have_S:
                    if tok_uw is not None:
                        tok_uw.wait(nc.tensor)
                    if tSb[h] is not None:
                        tSb[h].wait(nc.tensor)
                    for d_ in range(2):
                        mm = nc.tensor.matmul(
                            uwp[b:b + 32, 0:DH],
                            wtg[h][:, d_ * 128 + b:d_ * 128 + b + 32],
                            Sb[h][d_][:, :], start=(d_ == 0), stop=(d_ == 1),
                            tile_position=(0, b))
                    t_ws = c.mark(mm, "tensor")
                    t_ws.wait(nc.vector)
                    c.tip("tensor").wait(nc.vector)
                    tok_uw = t_un = c.v(nc.vector.tensor_sub(
                        un_sb[b:b + 32, :], ug[h][b:b + 32, :],
                        uwp[b:b + 32, 0:DH]))
                    for d_ in range(2):
                        col = (2 * h + d_) * 128 + b
                        mm = nc.tensor.matmul(
                            opp[b:b + 32, 0:DH], qf[:, col:col + 32],
                            Sb[h][d_][:, :], start=(d_ == 0), stop=False,
                            tile_position=(0, b))
                    c.mark(mm, "tensor")
                else:
                    c.tip("tensor").wait(nc.vector)
                    t_un = c.v(nc.vector.tensor_copy(un_sb[b:b + 32, :],
                                                     ug[h][b:b + 32, :]))
                t_un.wait(nc.tensor)
                t_ltc.wait(nc.tensor)
                t_o = c.mark(nc.tensor.matmul(
                    opp[b:b + 32, 0:DH], lt_sb[b:b + 32, 0:32],
                    un_sb[b:b + 32, :], start=(not have_S), stop=True,
                    tile_position=(b, b)), "tensor")
                t_su = None
                for d_ in range(2):
                    if tSb[h] is not None:
                        tSb[h].wait(nc.tensor)
                    if tok_dS is not None:
                        tok_dS.wait(nc.tensor)
                    t_su = c.mark(nc.tensor.matmul(
                        dSp[d_][0:128, 0:DH],
                        ktok[h][b:b + 32, d_ * 128:(d_ + 1) * 128],
                        un_sb[b:b + 32, :], start=True, stop=True,
                        tile_position=(b, 0)), "tensor")
                t_su.wait(nc.vector)
                for d_ in range(2):
                    if have_S:
                        c.v(nc.vector.tensor_add(Sf[h][d_][:, :],
                                                 Sf[h][d_][:, :],
                                                 dSp[d_][0:128, 0:DH]))
                    else:
                        c.v(nc.vector.tensor_copy(Sf[h][d_][:, :],
                                                  dSp[d_][0:128, 0:DH]))
                tok_dS = c.tip("vector")
                c.v(nc.vector.tensor_copy(Sb[h][0][:, :], Sf[h][0][:, :]))
                tSb[h] = c.v(nc.vector.tensor_copy(Sb[h][1][:, :],
                                                   Sf[h][1][:, :]))
                t_o.wait(nc.scalar)
                if t_red_prev is not None:
                    t_red_prev.wait(nc.scalar)
                tok_oA = t_sq = c.s(nc.scalar.activation(
                    o2_sb[b:b + 32, :], opp[b:b + 32, 0:DH], AF.Square))
                t_sq.wait(nc.vector)
                t_red = c.v(nc.vector.reduce_sum(rsn[b:b + 32, 2 * h:2 * h + 1],
                                                 o2_sb[b:b + 32, :],
                                                 axis=AX.X))
                t_red_prev = t_red
                t_red.wait(nc.scalar)
                c.wait_const(nc.scalar)
                c.s(nc.scalar.activation(rsn[b:b + 32, 2 * h + 1:2 * h + 2],
                                         rsn[b:b + 32, 2 * h:2 * h + 1], AF.Ln,
                                         scale=c.constcol(1.0 / DH)[b:b + 32, :],
                                         bias=c.constcol(1e-5)[b:b + 32, :]))
                t_rs = c.s(nc.scalar.activation(
                    rsn[b:b + 32, 2 * h + 1:2 * h + 2],
                    rsn[b:b + 32, 2 * h + 1:2 * h + 2], AF.Exp,
                    scale=c.constcol(-0.5)[b:b + 32, :]))
                t_rs.wait(nc.vector)
                t_ob.wait(nc.vector)
                c.tip("tensor").wait(nc.vector)
                tok_oB = t_os = c.v(nc.vector.scalar_tensor_tensor(
                    osc_sb[b:b + 32, :], opp[b:b + 32, 0:DH],
                    rsn[b:b + 32, 2 * h + 1:2 * h + 2], onw_bc[b:b + 32, :],
                    ALU.mult, ALU.mult))
                for d_ in (range(2) if ph >= 4 else []):
                    t_os.wait(nc.tensor)
                    mk["tok"].wait(nc.tensor)
                    if tok_tp is not None:
                        tok_tp.wait(nc.tensor)
                    t_tp = c.mark(nc.tensor.transpose(
                        tpb[0:128, 0:32],
                        osc_sb[b:b + 32, d_ * 128:(d_ + 1) * 128],
                        mk["ident32"][b:b + 32, 0:32], tile_position=(b, 0)),
                        "tensor")
                    t_tp.wait(nc.vector)
                    tok_tp = t_ev = c.v(nc.vector.tensor_copy(
                        oT_big[:, (2 * h + d_) * T + g0 + b:
                               (2 * h + d_) * T + g0 + b + 32],
                        tpb[0:128, 0:32]))
            have_S = True
    t_scan = c.tip("vector")
    if ph == 5:
        s1 = c.sem()
        t_scan.wait(nc.gpsimd)
        s1.inc(nc.gpsimd.dma_start(
            out_hbm[:, :].rearrange("(a p) f -> p a f", p=128),
            oT_big[:, :].rearrange("p (a f) -> p a f", a=16)), 16)
        s1.wait(nc.gpsimd)
        sub.__exit__(None, None, None)
        return s1
    sub.__exit__(None, None, None)
    sub = c.sub()
    cs = sub.__enter__()
    wo = emit_load_weight(cs, io["Wo"], D, D)
    ops2 = cs.ring(2, [128, D])
    xrr = cs.sbring(2, [128, D], F32)
    ost = cs.sbring(2, [128, D], F32)
    s_out = c.sem()
    x3 = x_hbm.rearrange("(t p) f -> t p f", p=128)
    o3 = out_hbm.rearrange("(t p) f -> t p f", p=128)
    for t in range(NT):
        xr = xrr.acquire(nc.sync)
        s_xl = c.dsem("l3bx")
        s_xl.inc(nc.sync.dma_start(xr[:, :], x3[t, :, :]), 16)
        pt = ops2.acquire(nc.tensor)
        wo.sem.wait(nc.tensor)
        t_scan.wait(nc.tensor)
        for o0 in range(0, D, 512):
            mm = mm_group(nc, pt[:, o0:o0 + 512],
                          [(oT_big[:, ki * T + t * 128:ki * T + (t + 1) * 128],
                            wo.lhsT(ki, o0, o0 + 512)) for ki in range(ND)])
        t_mm = c.mark(mm, "tensor")
        ot = ost.acquire(nc.vector)
        t_mm.wait(nc.vector)
        s_xl.wait(nc.vector)
        t_ad = c.v(nc.vector.tensor_add(ot[:, :], pt[:, 0:D], xr[:, :]))
        ops2.release(t_ad)
        xrr.release(t_ad)
        t_ad.wait(nc.sync)
        s_out.inc(nc.sync.dma_start(o3[t, :, :], ot[:, :]), 16)
        ost.release(Tok(s_out.h, s_out.v))
    sub.__exit__(None, None, None)
    s_out.wait(nc.gpsimd)
    return s_out


def build_l3attnb_program():
    nc = bass.Bass("TRN2", target_bir_lowering=False)
    stack = ExitStack()
    c = Ctx(nc, stack)
    dt = lambda n, sh: nc.dram_tensor(n, sh, F32, kind="ExternalInput").ap()
    io = {
        "Wq": dt("Wq", [D, D]), "Wk": dt("Wk", [D, D]), "Wv": dt("Wv", [D, D]),
        "conv_wq_c": dt("conv_wq_c", [128, 32]),
        "conv_wk_c": dt("conv_wk_c", [128, 32]),
        "conv_wv_c": dt("conv_wv_c", [128, 32]),
        "onorm_w": dt("onorm_w", [256]),
        "Wo": dt("Wo", [D, D]),
    }
    x_h = dt("x", [T, D])
    out_h = nc.dram_tensor("out", [T, D], F32, kind="ExternalOutput").ap()
    with stack:
        emit_l3_attn_b(c, x_h, out_h, io)
    return nc



def emit_fir_fm(c, cs, pf, cacc, fw_c, s_fw, dst_hbm, kk, t_cv, stg):
    nc = c.nc
    s_st = c.sem()
    t_pc = None
    for ki in range(ND):
        b0 = ki * T
        s_fw.wait(nc.vector)
        t_cv[-1].wait(nc.vector)
        if t_pc is not None:
            t_pc.wait(nc.vector)
        c.v(nc.vector.tensor_scalar_mul(
            cacc[:, :], pf[:, b0:b0 + T], fw_c[:, ki * kk + kk - 1:ki * kk + kk]))
        for s_ in range(1, kk):
            c.v(nc.vector.scalar_tensor_tensor(
                cacc[:, s_:T], pf[:, b0:b0 + T - s_],
                fw_c[:, ki * kk + kk - 1 - s_:ki * kk + kk - s_], cacc[:, s_:T],
                ALU.mult, ALU.add))
        sg = stg.acquire(nc.vector)
        t_pc = c.v(nc.vector.tensor_copy(sg[:, :], cacc[:, :]))
        t_pc.wait(nc.sync)
        s_st.inc(nc.sync.dma_start(dst_hbm[ki * 128:(ki + 1) * 128, :],
                                   sg[:, :]), 16)
        stg.release(Tok(s_st.h, s_st.v))
    return s_st


def emit_l3_attn_a(c, x_hbm, out_hbm, io):
    import os
    ph = int(os.environ.get("L3A_PH", "9"))
    nc = c.nc
    H, DH, NG = 4, 256, 16
    mk = emit_l3c_masks(c)
    identB = c.sb([128, 128], BF16)
    mk["tok"].wait(nc.vector)
    t_idb = c.v(nc.vector.tensor_copy(identB[:, :], mk["iden128"][:, :]))
    onw_bc = c.sb([128, DH])
    oT_big = c.sb([128, ND * T], BF16)
    gate_tok = c.sb([128, NT * 16])
    ds, rstd, nmr, t_cols = emit_ln_stream(c, x_hbm)
    q_hbm = c.dram([D, T], BF16)
    k_hbm = c.dram([D, T], BF16)
    v_hbm = c.dram([D, T], BF16)
    fs_hbm = c.dram([D, T], BF16)
    fl_hbm = c.dram([D, T], BF16)
    beta_hbm = c.dram([T, H])
    s_qs, s_ks, s_vs = c.sem(), c.sem(), c.sem()
    s_bt = c.sem()
    onr, s_onr = emit_load_bf16_row(c, io["onorm_w"], DH)
    onf = c.sb([1, DH])
    s_onr.wait(nc.vector)
    t_onf = c.v(nc.vector.tensor_copy(onf[0:1, :], onr[0:1, :]))
    subA = c.sub()
    csA = subA.__enter__()
    t_ob = emit_bcast_rows(c, csA, [(onf[0:1, :], onw_bc, t_onf)])
    xnT = csA.sb([128, ND * T], BF16)
    s_xn = c.sem()
    ds.load_fm(xnT, 0, T, s_xn)
    sub = c.sub()
    cs = sub.__enter__()
    pf = cs.sb([128, ND * T], BF16)
    cacc = cs.sb([128, T])
    psr = cs.ring(2, [128, 512])
    rb = cs.sb([128, T])
    sqb = cs.sb([128, 2 * T], BF16)
    onesf = cs.sb([128, 1], BF16)
    t_of = c.v(nc.vector.memset(onesf[:, 0:1], 1.0))
    rn_row = cs.sb([1, T])
    qstg = cs.sbring(2, [128, T], BF16)
    wtile = cs.sb([128, ND * D], BF16)
    firs = [None, None]
    for pname, whb, cwhb, dsth, s_st, normq in (
            ("q", io["Wq"], io["conv_wq_c"], q_hbm, s_qs, 1),
            ("k", io["Wk"], io["conv_wk_c"], k_hbm, s_ks, 1),
            ("v", io["Wv"], io["conv_wv_c"], v_hbm, s_vs, 0)):
        t_cv = emit_proj_conv_store(c, cs, xnT, s_xn, whb, cwhb, dsth,
                                    pf=pf, cacc=cacc, psr=psr, wtile=wtile)
        if not normq:
            for ki in range(ND):
                t_cv[ki].wait(nc.sync)
                s_st.inc(nc.sync.dma_start(
                    dsth[ki * 128:(ki + 1) * 128, :],
                    pf[:, ki * T:(ki + 1) * T]), 16)
            fw7, s_f7 = emit_load_f32(cs, io["fir_s_c"], 128, ND * 7)
            fw31, s_f31 = emit_load_f32(cs, io["fir_l_c"], 128, ND * 31)
            firs[0] = emit_fir_fm(c, cs, pf, cacc, fw7, s_f7, fs_hbm, 7,
                                  t_cv, qstg)
            firs[1] = emit_fir_fm(c, cs, pf, cacc, fw31, s_f31, fl_hbm, 31,
                                  t_cv, qstg)
            continue
        for h in range(H):
            for d_ in range(2):
                ki = 2 * h + d_
                t_cv[ki].wait(nc.scalar)
                c.tip("tensor").wait(nc.scalar)
                c.s(nc.scalar.activation(sqb[:, d_ * T:(d_ + 1) * T],
                                         pf[:, ki * T:(ki + 1) * T],
                                         AF.Square))
            t_sq = c.tip("scalar")
            for n0 in range(0, T, 256):
                pt = psr.acquire(nc.tensor)
                t_sq.wait(nc.tensor)
                t_of.wait(nc.tensor)
                for d_ in range(2):
                    mm = nc.tensor.matmul(
                        pt[0:1, 0:256], onesf[:, 0:1],
                        sqb[:, d_ * T + n0:d_ * T + n0 + 256],
                        start=(d_ == 0), stop=(d_ == 1))
                t_mm = c.mark(mm, "tensor")
                t_mm.wait(nc.vector)
                c.tip("tensor").wait(nc.vector)
                t_ev = c.v(nc.vector.tensor_copy(
                    rn_row[0:1, n0:n0 + 256], pt[0:1, 0:256]))
                psr.release(t_ev)
            t_ev.wait(nc.scalar)
            c.wait_const(nc.scalar)
            c.s(nc.scalar.activation(rn_row[0:1, :], rn_row[0:1, :],
                                     AF.Ln, bias=c.constcol(0.0)[0:1, :]))
            t_rn = c.s(nc.scalar.activation(
                rn_row[0:1, :], rn_row[0:1, :], AF.Exp,
                scale=c.constcol(-0.5)[0:1, :]))
            bsub = c.sub()
            bcs = bsub.__enter__()
            t_rb = emit_bcast_rows(c, bcs, [(rn_row[0:1, :], rb, t_rn)])
            bsub.__exit__(None, None, None)
            for d_ in range(2):
                ki = 2 * h + d_
                qs = qstg.acquire(nc.vector)
                t_rb.wait(nc.vector)
                t_n = c.v(nc.vector.tensor_mul(qs[:, :],
                                               pf[:, ki * T:(ki + 1) * T],
                                               rb[:, :]))
                t_n.wait(nc.sync)
                s_st.inc(nc.sync.dma_start(
                    dsth[ki * 128:(ki + 1) * 128, :], qs[:, :]), 16)
                qstg.release(Tok(s_st.h, s_st.v))
    wb = emit_load_weight(cs, io["Wb"], D, H)
    bstg = cs.sbring(2, [128, H], F32)
    for t in range(NT):
        pt = psr.acquire(nc.tensor)
        wb.sem.wait(nc.tensor)
        s_xn.wait(nc.tensor)
        mm = mm_group(nc, pt[:, 0:H],
                      [(xnT[:, ki * T + t * 128:ki * T + (t + 1) * 128],
                        wb.lhsT(ki, 0, H)) for ki in range(ND)])
        t_mm = c.mark(mm, "tensor")
        bs = bstg.acquire(nc.scalar)
        t_mm.wait(nc.scalar)
        t_sg = c.s(nc.scalar.activation(bs[:, 0:H], pt[:, 0:H], AF.Sigmoid))
        psr.release(t_sg)
        t_sg.wait(nc.sync)
        s_bt.inc(nc.sync.dma_start(
            beta_hbm.rearrange("(t p) h -> t p h", p=128)[t, :, :],
            bs[:, 0:H]), 16)
        bstg.release(Tok(s_bt.h, s_bt.v))
    cs.exit_waits += [s_qs, s_ks, s_vs, s_bt, firs[0], firs[1]]
    sub.__exit__(None, None, None)
    sub = c.sub()
    cs = sub.__enter__()
    w1 = emit_load_weight(cs, io["gmix_w1"], D, 2 * D)
    b1c, s_b1 = emit_load_f32(cs, io["gmix_b1_c"], 128, 16)
    h1T = cs.sb([128, 16 * T], BF16)
    psr = cs.ring(2, [128, 512])
    emit_proj_fm(c, xnT, s_xn, w1, h1T, bias_col=b1c, s_bias=s_b1,
                 act=AF.Gelu, psr=psr)
    w2 = emit_load_weight(cs, io["gmix_w2"], 2 * D, 16)
    b2r, s_b2 = emit_load_bf16_row(cs, io["gmix_b2"], 16)
    ones2, t_o2 = emit_ones(cs)
    for t in range(NT):
        pt = psr.acquire(nc.tensor)
        w2.sem.wait(nc.tensor)
        s_b2.wait(nc.tensor)
        t_o2.wait(nc.tensor)
        pairs = [(h1T[:, ki * T + t * 128:ki * T + (t + 1) * 128],
                  w2.lhsT(ki, 0, 16)) for ki in range(16)]
        pairs.append((ones2[0:1, 0:128], b2r[0:1, 0:16]))
        mm = mm_group(nc, pt[:, 0:16], pairs)
        t_mm = c.mark(mm, "tensor")
        t_mm.wait(nc.vector)
        t_gv = c.v(nc.vector.tensor_copy(gate_tok[:, t * 16:(t + 1) * 16],
                                         pt[:, 0:16]))
        psr.release(t_gv)
    t_gate = t_gv
    sub.__exit__(None, None, None)
    subA.__exit__(None, None, None)
    if ph == 1:
        s1 = c.sem()
        for sm in (s_qs, s_vs, firs[0], firs[1], s_bt):
            sm.wait(nc.gpsimd)
        t_gate.wait(nc.gpsimd)
        s1.inc(nc.gpsimd.dma_start(
            out_hbm[0:512, :].rearrange("(a p) t -> p a t", p=128),
            q_hbm[0:512, 0:1024].rearrange("(a p) t -> p a t", p=128)), 16)
        s1.inc(nc.gpsimd.dma_start(
            out_hbm[512:1024, :].rearrange("(a p) t -> p a t", p=128),
            fs_hbm[0:512, 0:1024].rearrange("(a p) t -> p a t", p=128)), 16)
        s1.inc(nc.gpsimd.dma_start(
            out_hbm[1024:1536, :].rearrange("(a p) t -> p a t", p=128),
            fl_hbm[0:512, 0:1024].rearrange("(a p) t -> p a t", p=128)), 16)
        s1.inc(nc.gpsimd.dma_start(
            out_hbm[1536:1552, 0:1024].rearrange("(a p) f -> p a f", p=4),
            beta_hbm[0:1024, :].rearrange("(a t) h -> h a t", a=16)[:, 0:4, :]
            ), 16) if False else None
        s1.inc(nc.gpsimd.dma_start(
            out_hbm[1536:1664, 0:256].rearrange("(a p) f -> p (a f)", p=128),
            gate_tok[:, 0:256]), 16)
        s1.inc(nc.gpsimd.dma_start(
            out_hbm[1664:1920, 0:1024].rearrange("(a p) t -> p a t", p=128),
            v_hbm[768:1024, 0:1024].rearrange("(a p) t -> p a t", p=128)), 16)
        s1.wait(nc.gpsimd)
        return s1
    sub = c.sub()
    cs = sub.__enter__()
    Sf = [[cs.sb([128, DH]) for _ in range(2)] for _ in range(H)]
    Sb = [[cs.sb([128, DH], BF16) for _ in range(2)] for _ in range(H)]
    uwp = cs.ps([128, DH])
    opp = cs.ps([128, DH])
    ptp = cs.ps([128, DH])
    pbp = cs.ps([128, DH])
    ltp = cs.ps([128, 32])
    tpb = cs.ps([128, 128], BF16)
    dSp = [cs.ps([128, DH]) for _ in range(2)]
    tok_uw = tok_o = tok_pt = tok_pb = tok_lt = tok_tp = tok_dS = None
    qf = cs.sb([128, ND * 128], BF16)
    kf = cs.sb([128, ND * 128], BF16)
    kbf = cs.sb([128, ND * 128], BF16)
    bb_g = cs.sb([128, 128], BF16)
    brows = [cs.sb([1, T], BF16) for _ in range(H)]
    s_brs = c.sem()
    for h in range(H):
        s_bt.wait(nc.gpsimd)
        s_brs.inc(nc.gpsimd.dma_start(
            brows[h][0:1, :],
            beta_hbm[:, h:h + 1].rearrange("t o -> o t"),
            allow_slow_non_contiguous=True), 16)
    bcol = cs.sb([128, H])
    ones1 = cs.sb([1, 128], BF16)
    t_o1 = c.v(nc.vector.memset(ones1[0:1, :], 1.0))
    alr = cs.sb([1, H], BF16)
    s_al = c.sem()
    s_al.inc(nc.gpsimd.dma_start(alr[0:1, :], io["alpha"][None, :]), 16)
    alpha_bc = cs.sb([128, H])
    s_al.wait(nc.tensor)
    t_o1.wait(nc.tensor)
    t_am = c.mark(nc.tensor.matmul(ptp[:, 0:H], ones1[0:1, :], alr[0:1, 0:H],
                                   start=True, stop=True), "tensor")
    t_am.wait(nc.vector)
    tok_pt = c.v(nc.vector.tensor_copy(alpha_bc[:, :], ptp[:, 0:H]))
    ktok = [cs.sb([128, DH], BF16) for _ in range(H)]
    vtok = [cs.sb([128, DH], BF16) for _ in range(H)]
    fstok = [cs.sb([128, DH], BF16) for _ in range(H)]
    fltok = [cs.sb([128, DH], BF16) for _ in range(H)]
    kbt = cs.sb([128, DH], BF16)
    vbt = cs.sb([128, DH], BF16)
    ug = [cs.sb([128, DH]) for _ in range(H)]
    wtg = [cs.sb([128, DH], BF16) for _ in range(H)]
    dts = [cs.sb([128, DH]) for _ in range(H)]
    Np = [cs.sb([128, 128]) for _ in range(5)]
    Ntp = [cs.sb([128, 128]) for _ in range(5)]
    Rm = cs.sb([128, 128])
    Rt = cs.sb([128, 128])
    Rb = cs.sb([128, 128], BF16)
    lt_sb = cs.sb([128, 32], BF16)
    un_sb = cs.sb([128, DH], BF16)
    sqt = cs.sb([128, DH])
    st1 = cs.sb([128, H])
    st2 = cs.sb([128, H])
    st3 = cs.sb([128, H])
    lg = cs.sb([128, 2 * H])
    w4 = cs.sb([128, H])
    omx = cs.sb([128, DH])
    osc_sb = cs.sb([128, DH], BF16)
    rsn = cs.sb([128, 8])
    tSb = [None] * H
    have_S = False
    t_red_prev = None
    for g in range(NG):
        g0 = g * 128
        c.tip("tensor").wait(nc.sync)
        c.tip("vector").wait(nc.sync)
        c.tip("scalar").wait(nc.sync)
        s_g = c.dsem("l3ag")
        s_qs.wait(nc.sync)
        s_ks.wait(nc.sync)
        s_vs.wait(nc.sync)
        s_bt.wait(nc.sync)
        firs[0].wait(nc.sync)
        firs[1].wait(nc.sync)
        s_g.inc(nc.sync.dma_start(
            qf[:, :].rearrange("p (ki t) -> p ki t", ki=ND),
            q_hbm.rearrange("(ki p) t -> p ki t", p=128)[:, :, g0:g0 + 128]),
            16)
        s_g.inc(nc.sync.dma_start(
            kf[:, :].rearrange("p (ki t) -> p ki t", ki=ND),
            k_hbm.rearrange("(ki p) t -> p ki t", p=128)[:, :, g0:g0 + 128]),
            16)
        s_g.inc(nc.sync.dma_start(bcol[:, :], beta_hbm[g0:g0 + 128, :]), 16)
        for h in range(H):
            s_g.inc(nc.sync.dma_start_transpose(
                ktok[h][:, :], k_hbm[h * DH:(h + 1) * DH, g0:g0 + 128]), 16)
            s_g.inc(nc.sync.dma_start_transpose(
                vtok[h][:, :], v_hbm[h * DH:(h + 1) * DH, g0:g0 + 128]), 16)
            s_g.inc(nc.sync.dma_start_transpose(
                fstok[h][:, :], fs_hbm[h * DH:(h + 1) * DH, g0:g0 + 128]), 16)
            s_g.inc(nc.sync.dma_start_transpose(
                fltok[h][:, :], fl_hbm[h * DH:(h + 1) * DH, g0:g0 + 128]), 16)
        for h in range(H):
            if tok_pt is not None:
                tok_pt.wait(nc.tensor)
            s_brs.wait(nc.tensor)
            t_o1.wait(nc.tensor)
            t_bb = c.mark(nc.tensor.matmul(ptp[:, 0:128], ones1[0:1, :],
                                           brows[h][0:1, g0:g0 + 128],
                                           start=True, stop=True), "tensor")
            t_bb.wait(nc.vector)
            c.tip("tensor").wait(nc.vector)
            t_bbc = c.v(nc.vector.tensor_copy(bb_g[:, :], ptp[:, 0:128]))
            s_g.wait(nc.vector)
            for d_ in range(2):
                t_kb = c.v(nc.vector.tensor_mul(
                    kbf[:, (2 * h + d_) * 128:(2 * h + d_ + 1) * 128],
                    kf[:, (2 * h + d_) * 128:(2 * h + d_ + 1) * 128],
                    bb_g[:, :]))
            t_vb = c.v(nc.vector.tensor_scalar_mul(vbt[:, :], vtok[h][:, :],
                                                   bcol[:, h:h + 1]))
            t_kbt = c.v(nc.vector.tensor_scalar_mul(kbt[:, :], ktok[h][:, :],
                                                    bcol[:, h:h + 1]))
            mk["tok"].wait(nc.tensor)
            t_kb.wait(nc.tensor)
            for d_ in range(2):
                sl = slice((2 * h + d_) * 128, (2 * h + d_ + 1) * 128)
                mm = nc.tensor.matmul(ptp[:, 0:128], kbf[:, sl], kf[:, sl],
                                      start=(d_ == 0), stop=(d_ == 1))
            t_mm = c.mark(mm, "tensor")
            if tok_pb is not None:
                tok_pb.wait(nc.tensor)
            for d_ in range(2):
                sl = slice((2 * h + d_) * 128, (2 * h + d_ + 1) * 128)
                mm = nc.tensor.matmul(pbp[:, 0:128], kf[:, sl], kbf[:, sl],
                                      start=(d_ == 0), stop=(d_ == 1))
            t_mm2 = c.mark(mm, "tensor")
            t_mm.wait(nc.vector)
            c.tip("tensor").wait(nc.vector)
            tok_pt = c.v(nc.vector.tensor_mul(Np[0][:, :], ptp[:, 0:128],
                                              mk["msl"][:, :]))
            t_mm2.wait(nc.vector)
            tok_pb = t_g = c.v(nc.vector.tensor_mul(Ntp[0][:, :],
                                                    pbp[:, 0:128],
                                                    mk["msu"][:, :]))
            for j in range(1, 5):
                t_g.wait(nc.tensor)
                tok_pt.wait(nc.tensor)
                t_a = c.mark(nc.tensor.matmul(ptp[:, 0:128], Ntp[j - 1][:, :],
                                              Np[j - 1][:, :],
                                              start=True, stop=True), "tensor")
                tok_pb.wait(nc.tensor)
                t_b = c.mark(nc.tensor.matmul(pbp[:, 0:128], Np[j - 1][:, :],
                                              Ntp[j - 1][:, :],
                                              start=True, stop=True), "tensor")
                t_a.wait(nc.vector)
                c.tip("tensor").wait(nc.vector)
                tok_pt = c.v(nc.vector.tensor_copy(Np[j][:, :], ptp[:, 0:128]))
                t_b.wait(nc.vector)
                tok_pb = t_g = c.v(nc.vector.tensor_copy(Ntp[j][:, :],
                                                         pbp[:, 0:128]))
            c.tip("tensor").wait(nc.vector)
            c.v(nc.vector.tensor_add(Rm[:, :], Ntp[4][:, :],
                                     mk["iden128"][:, :]))
            t_ch = c.v(nc.vector.tensor_add(Rt[:, :], Np[4][:, :],
                                            mk["iden128"][:, :]))
            for j in (3, 2, 1, 0):
                t_ch.wait(nc.tensor)
                tok_pt.wait(nc.tensor)
                t_a = c.mark(nc.tensor.matmul(ptp[:, 0:128], Rt[:, :],
                                              Ntp[j][:, :],
                                              start=True, stop=True), "tensor")
                tok_pb.wait(nc.tensor)
                t_b = c.mark(nc.tensor.matmul(pbp[:, 0:128], Ntp[j][:, :],
                                              Rt[:, :],
                                              start=True, stop=True), "tensor")
                t_a.wait(nc.vector)
                t_b.wait(nc.vector)
                tok_pt = c.v(nc.vector.tensor_add(Rm[:, :], ptp[:, 0:128],
                                                  Rm[:, :]))
                tok_pb = t_ch = c.v(nc.vector.tensor_add(Rt[:, :],
                                                         pbp[:, 0:128],
                                                         Rt[:, :]))
            t_ch.wait(nc.vector)
            c.tip("tensor").wait(nc.vector)
            t_rb = c.v(nc.vector.tensor_copy(Rb[:, :], Rm[:, :]))
            t_rb.wait(nc.tensor)
            tok_pt.wait(nc.tensor)
            t_vb.wait(nc.tensor)
            t_u = c.mark(nc.tensor.matmul(ptp[:, 0:DH], Rb[:, :],
                                          vbt[:, :], start=True,
                                          stop=True), "tensor")
            t_u.wait(nc.vector)
            c.tip("tensor").wait(nc.vector)
            tok_pt = t_ue = c.v(nc.vector.tensor_copy(ug[h][:, :],
                                                      ptp[:, 0:DH]))
            for d_ in range(2):
                tok_pb.wait(nc.tensor)
                t_kbt.wait(nc.tensor)
                t_w = c.mark(nc.tensor.matmul(
                    pbp[:, 0:128], kbt[:, d_ * 128:(d_ + 1) * 128],
                    Rb[:, :], start=True, stop=True), "tensor")
                t_w.wait(nc.vector)
                tok_pb = t_we = c.v(nc.vector.tensor_copy(
                    wtg[h][:, d_ * 128:(d_ + 1) * 128], pbp[:, 0:128]))
        t_prep = c.tip("vector")
        for ch in range(4):
            b = 32 * ch
            for h in range(H):
                t_prep.wait(nc.tensor)
                if tok_lt is not None:
                    tok_lt.wait(nc.tensor)
                for d_ in range(2):
                    col = (2 * h + d_) * 128 + b
                    mm = nc.tensor.matmul(
                        ltp[b:b + 32, 0:32], kf[:, col:col + 32],
                        qf[:, col:col + 32], start=(d_ == 0), stop=(d_ == 1),
                        tile_position=(0, b))
                t_lt = c.mark(mm, "tensor")
                t_lt.wait(nc.vector)
                c.tip("tensor").wait(nc.vector)
                tok_lt = t_ltc = c.v(nc.vector.tensor_mul(
                    lt_sb[b:b + 32, 0:32], ltp[b:b + 32, 0:32],
                    mk["mkeepU"][b:b + 32, 0:32]))
                if tok_o is not None:
                    tok_o.wait(nc.tensor)
                if have_S:
                    if tok_uw is not None:
                        tok_uw.wait(nc.tensor)
                    if tSb[h] is not None:
                        tSb[h].wait(nc.tensor)
                    for d_ in range(2):
                        mm = nc.tensor.matmul(
                            uwp[b:b + 32, 0:DH],
                            wtg[h][:, d_ * 128 + b:d_ * 128 + b + 32],
                            Sb[h][d_][:, :], start=(d_ == 0), stop=(d_ == 1),
                            tile_position=(0, b))
                    t_ws = c.mark(mm, "tensor")
                    t_ws.wait(nc.vector)
                    c.tip("tensor").wait(nc.vector)
                    tok_uw = t_un = c.v(nc.vector.tensor_sub(
                        un_sb[b:b + 32, :], ug[h][b:b + 32, :],
                        uwp[b:b + 32, 0:DH]))
                    for d_ in range(2):
                        col = (2 * h + d_) * 128 + b
                        mm = nc.tensor.matmul(
                            opp[b:b + 32, 0:DH], qf[:, col:col + 32],
                            Sb[h][d_][:, :], start=(d_ == 0), stop=False,
                            tile_position=(0, b))
                    c.mark(mm, "tensor")
                else:
                    c.tip("tensor").wait(nc.vector)
                    t_un = c.v(nc.vector.tensor_copy(un_sb[b:b + 32, :],
                                                     ug[h][b:b + 32, :]))
                t_un.wait(nc.tensor)
                t_ltc.wait(nc.tensor)
                t_o = c.mark(nc.tensor.matmul(
                    opp[b:b + 32, 0:DH], lt_sb[b:b + 32, 0:32],
                    un_sb[b:b + 32, :], start=(not have_S), stop=True,
                    tile_position=(b, b)), "tensor")
                t_su = None
                for d_ in range(2):
                    if tSb[h] is not None:
                        tSb[h].wait(nc.tensor)
                    if tok_dS is not None:
                        tok_dS.wait(nc.tensor)
                    t_su = c.mark(nc.tensor.matmul(
                        dSp[d_][0:128, 0:DH],
                        ktok[h][b:b + 32, d_ * 128:(d_ + 1) * 128],
                        un_sb[b:b + 32, :], start=True, stop=True,
                        tile_position=(b, 0)), "tensor")
                t_su.wait(nc.vector)
                for d_ in range(2):
                    if have_S:
                        c.v(nc.vector.tensor_add(Sf[h][d_][:, :],
                                                 Sf[h][d_][:, :],
                                                 dSp[d_][0:128, 0:DH]))
                    else:
                        c.v(nc.vector.tensor_copy(Sf[h][d_][:, :],
                                                  dSp[d_][0:128, 0:DH]))
                tok_dS = c.tip("vector")
                c.v(nc.vector.tensor_copy(Sb[h][0][:, :], Sf[h][0][:, :]))
                tSb[h] = c.v(nc.vector.tensor_copy(Sb[h][1][:, :],
                                                   Sf[h][1][:, :]))
                t_o.wait(nc.vector)
                tok_o = c.v(nc.vector.tensor_copy(dts[h][b:b + 32, :],
                                                  opp[b:b + 32, 0:DH]))
            have_S = True
        if ph == 6 and g == 0:
            s1 = c.sem()
            tok_o.wait(nc.gpsimd)
            tok_dS.wait(nc.gpsimd)
            for h in range(H):
                s1.inc(nc.gpsimd.dma_start(
                    out_hbm[h * 128:(h + 1) * 128, 0:DH], dts[h][:, :]), 16)
                s1.inc(nc.gpsimd.dma_start(
                    out_hbm[512 + h * 128:512 + (h + 1) * 128, 0:DH],
                    ug[h][:, :]), 16)
            s1.inc(nc.gpsimd.dma_start(out_hbm[1024:1152, 0:DH],
                                       vbt[:, :]), 16)
            s1.inc(nc.gpsimd.dma_start(out_hbm[1152:1280, 0:128],
                                       bb_g[:, :]), 16)
            s1.inc(nc.gpsimd.dma_start(out_hbm[1280:1408, 0:4],
                                       bcol[:, :]), 16)
            s1.inc(nc.gpsimd.dma_start(out_hbm[1408:1536, 0:DH],
                                       vtok[3][:, :]), 16)
            s_vs.wait(nc.gpsimd)
            s1.inc(nc.gpsimd.dma_start(
                out_hbm[1536:1792, 0:128],
                v_hbm[768:1024, 0:128].rearrange("(a p) t -> p a t", p=128
                                                 ).rearrange(
                    "p a t -> (a p) t")), 16) if False else None
            s1.inc(nc.gpsimd.dma_start(
                out_hbm[1536:1792, 0:1024].rearrange("(a p) t -> p a t", p=128),
                v_hbm[768:1024, 0:1024].rearrange("(a p) t -> p a t", p=128)),
                16)
            s1.wait(nc.gpsimd)
            sub.__exit__(None, None, None)
            return s1
        for h in range(H):
            t_gate.wait(nc.vector)
            srcs = [fstok[h][:, :], fltok[h][:, :], dts[h][:, :],
                    vtok[h][:, :]]
            for br, src in enumerate(srcs):
                if br == 2:
                    tok_o.wait(nc.vector)
                    tok_o.wait(nc.scalar)
                s_g.wait(nc.vector)
                s_g.wait(nc.scalar)
                c.tip("vector").wait(nc.scalar)
                t_s1 = c.v(nc.vector.reduce_sum(st1[:, br:br + 1], src,
                                                axis=AX.X))
                t_sq = c.s(nc.scalar.activation(sqt[:, :], src, AF.Square))
                t_sq.wait(nc.vector)
                t_s2 = c.v(nc.vector.reduce_sum(st2[:, br:br + 1], sqt[:, :],
                                                axis=AX.X))
                t_s3 = c.v(nc.vector.reduce_max(st3[:, br:br + 1], src,
                                                axis=AX.X))
            t_s2.wait(nc.scalar)
            c.tip("vector").wait(nc.scalar)
            c.wait_const(nc.scalar)
            t_c1 = c.s(nc.scalar.activation(lg[:, H:2 * H], st1[:, :],
                                            AF.Square))
            t_c1.wait(nc.vector)
            t_vx = c.v(nc.vector.scalar_tensor_tensor(
                lg[:, H:2 * H], lg[:, H:2 * H], c.constcol(-1.0 / DH),
                st2[:, :], ALU.mult, ALU.add))
            t_vx.wait(nc.scalar)
            c.s(nc.scalar.activation(lg[:, H:2 * H], lg[:, H:2 * H],
                                     AF.Relu))
            c.s(nc.scalar.activation(lg[:, H:2 * H], lg[:, H:2 * H], AF.Ln,
                                     scale=c.constcol(1.0 / (DH - 1)),
                                     bias=c.constcol(1e-30)))
            t_sd = c.s(nc.scalar.activation(lg[:, H:2 * H], lg[:, H:2 * H],
                                            AF.Exp,
                                            scale=c.constcol(0.5)))
            t_sd.wait(nc.vector)
            t_st = c.v(nc.vector.scalar_tensor_tensor(
                st3[:, :], st1[:, :], c.constcol(1.0 / DH), st3[:, :],
                ALU.mult, ALU.add))
            t_st = c.v(nc.vector.tensor_add(st3[:, :], st3[:, :],
                                            lg[:, H:2 * H]))
            gview = gate_tok[:, g * 16:(g + 1) * 16].rearrange(
                "p (br hh) -> p br hh", br=4)[:, :, h]
            t_al = c.v(nc.vector.scalar_tensor_tensor(
                lg[:, 0:H], st3[:, :], alpha_bc[:, h:h + 1], gview,
                ALU.mult_div3, ALU.add)) if False else None
            t_sc3 = c.v(nc.vector.tensor_scalar_mul(st3[:, :], st3[:, :],
                                                    1.0 / 3.0))
            t_lg = c.v(nc.vector.scalar_tensor_tensor(
                lg[:, 0:H], st3[:, :], alpha_bc[:, h:h + 1], gview,
                ALU.mult, ALU.add))
            t_mx = c.v(nc.vector.reduce_max(rsn[:, 2:3], lg[:, 0:H],
                                            axis=AX.X))
            t_ng = c.v(nc.vector.tensor_scalar_mul(rsn[:, 2:3], rsn[:, 2:3],
                                                   -1.0))
            t_ng.wait(nc.scalar)
            c.tip("vector").wait(nc.scalar)
            t_ex = c.s(nc.scalar.activation(w4[:, :], lg[:, 0:H], AF.Exp,
                                            bias=rsn[:, 2:3]))
            t_ex.wait(nc.vector)
            t_sm = c.v(nc.vector.reduce_sum(rsn[:, 3:4], w4[:, :], axis=AX.X))
            t_rc = c.v(nc.vector.reciprocal(rsn[:, 4:5], rsn[:, 3:4]))
            t_w4 = c.v(nc.vector.tensor_scalar_mul(w4[:, :], w4[:, :],
                                                   rsn[:, 4:5]))
            t_w4.wait(nc.vector)
            c.v(nc.vector.tensor_scalar_mul(omx[:, :], srcs[0],
                                            w4[:, 0:1]))
            for br in range(1, 4):
                c.v(nc.vector.scalar_tensor_tensor(
                    omx[:, :], srcs[br], w4[:, br:br + 1], omx[:, :],
                    ALU.mult, ALU.add))
            t_omx = c.tip("vector")
            t_omx.wait(nc.scalar)
            if t_red_prev is not None:
                t_red_prev.wait(nc.scalar)
            t_sq = c.s(nc.scalar.activation(sqt[:, :], omx[:, :], AF.Square))
            t_sq.wait(nc.vector)
            t_red = c.v(nc.vector.reduce_sum(rsn[:, 0:1], sqt[:, :],
                                             axis=AX.X))
            t_red_prev = t_red
            t_red.wait(nc.scalar)
            c.s(nc.scalar.activation(rsn[:, 1:2], rsn[:, 0:1], AF.Ln,
                                     scale=c.constcol(1.0 / DH),
                                     bias=c.constcol(1e-5)))
            t_rs = c.s(nc.scalar.activation(rsn[:, 1:2], rsn[:, 1:2], AF.Exp,
                                            scale=c.constcol(-0.5)))
            t_rs.wait(nc.vector)
            t_ob.wait(nc.vector)
            c.tip("tensor").wait(nc.vector)
            t_os = c.v(nc.vector.scalar_tensor_tensor(
                osc_sb[:, :], omx[:, :], rsn[:, 1:2], onw_bc[:, :],
                ALU.mult, ALU.mult))
            for d_ in range(2):
                t_os.wait(nc.tensor)
                t_idb.wait(nc.tensor)
                if tok_tp is not None:
                    tok_tp.wait(nc.tensor)
                t_tp = c.mark(nc.tensor.transpose(
                    tpb[0:128, 0:128], osc_sb[:, d_ * 128:(d_ + 1) * 128],
                    identB[:, :]), "tensor")
                t_tp.wait(nc.vector)
                tok_tp = t_ev = c.v(nc.vector.tensor_copy(
                    oT_big[:, (2 * h + d_) * T + g0:(2 * h + d_) * T + g0 + 128],
                    tpb[0:128, 0:128]))
    t_scan = c.tip("vector")
    if ph == 5:
        s1 = c.sem()
        t_scan.wait(nc.gpsimd)
        s1.inc(nc.gpsimd.dma_start(
            out_hbm[:, :].rearrange("(a p) f -> p a f", p=128),
            oT_big[:, :].rearrange("p (a f) -> p a f", a=16)), 16)
        s1.wait(nc.gpsimd)
        sub.__exit__(None, None, None)
        return s1
    sub.__exit__(None, None, None)
    sub = c.sub()
    cs = sub.__enter__()
    wo = emit_load_weight(cs, io["Wo"], D, D)
    ops2 = cs.ring(2, [128, D])
    xrr = cs.sbring(2, [128, D], F32)
    ost = cs.sbring(2, [128, D], F32)
    s_out = c.sem()
    x3 = x_hbm.rearrange("(t p) f -> t p f", p=128)
    o3 = out_hbm.rearrange("(t p) f -> t p f", p=128)
    for t in range(NT):
        xr = xrr.acquire(nc.sync)
        s_xl = c.dsem("l3ax")
        s_xl.inc(nc.sync.dma_start(xr[:, :], x3[t, :, :]), 16)
        pt = ops2.acquire(nc.tensor)
        wo.sem.wait(nc.tensor)
        t_scan.wait(nc.tensor)
        for o0 in range(0, D, 512):
            mm = mm_group(nc, pt[:, o0:o0 + 512],
                          [(oT_big[:, ki * T + t * 128:ki * T + (t + 1) * 128],
                            wo.lhsT(ki, o0, o0 + 512)) for ki in range(ND)])
        t_mm = c.mark(mm, "tensor")
        ot = ost.acquire(nc.vector)
        t_mm.wait(nc.vector)
        s_xl.wait(nc.vector)
        t_ad = c.v(nc.vector.tensor_add(ot[:, :], pt[:, 0:D], xr[:, :]))
        ops2.release(t_ad)
        xrr.release(t_ad)
        t_ad.wait(nc.sync)
        s_out.inc(nc.sync.dma_start(o3[t, :, :], ot[:, :]), 16)
        ost.release(Tok(s_out.h, s_out.v))
    sub.__exit__(None, None, None)
    s_out.wait(nc.gpsimd)
    return s_out


def build_l3attna_program():
    nc = bass.Bass("TRN2", target_bir_lowering=False)
    stack = ExitStack()
    c = Ctx(nc, stack)
    dt = lambda n, sh: nc.dram_tensor(n, sh, F32, kind="ExternalInput").ap()
    io = {
        "Wq": dt("Wq", [D, D]), "Wk": dt("Wk", [D, D]), "Wv": dt("Wv", [D, D]),
        "Wb": dt("Wb", [D, 4]),
        "conv_wq_c": dt("conv_wq_c", [128, 32]),
        "conv_wk_c": dt("conv_wk_c", [128, 32]),
        "conv_wv_c": dt("conv_wv_c", [128, 32]),
        "fir_s_c": dt("fir_s_c", [128, 8 * 7]),
        "fir_l_c": dt("fir_l_c", [128, 8 * 31]),
        "alpha": dt("alpha", [4]),
        "gmix_w1": dt("gmix_w1", [D, 2 * D]),
        "gmix_b1_c": dt("gmix_b1_c", [128, 16]),
        "gmix_w2": dt("gmix_w2", [2 * D, 16]),
        "gmix_b2": dt("gmix_b2", [16]),
        "onorm_w": dt("onorm_w", [256]),
        "Wo": dt("Wo", [D, D]),
    }
    x_h = dt("x", [T, D])
    out_h = nc.dram_tensor("out", [T, D], F32, kind="ExternalOutput").ap()
    with stack:
        emit_l3_attn_a(c, x_h, out_h, io)
    return nc



def emit_l0_attn_a(c, x_hbm, out_hbm, io, mode="main"):
    import math
    nc = c.nc
    H, DM, MF = 16, 64, 256
    DN = float(DM) ** -0.25
    RATIO = float(MF) ** -0.5
    LNR = math.log(RATIO)
    EPS = 1e-4
    identB = c.sb([128, 128], BF16)
    iscr = c.sb([128, 128])
    t_ii = c.g(nc.gpsimd.iota(iscr[:, :], pattern=[[1, 128]], base=0,
                              channel_multiplier=-1,
                              allow_small_or_imprecise_dtypes=True))
    t_ii.wait(nc.vector)
    t_idb = c.v(nc.vector.tensor_single_scalar(identB[:, :], iscr[:, :],
                                               0.0, ALU.is_equal))
    ds, rstd, nmr, t_cols = emit_ln_stream(c, x_hbm)
    att_tok = c.sb([128, NT * D], BF16)
    sub0 = c.sub()
    cs0 = sub0.__enter__()
    qT = cs0.sb([128, ND * T], BF16)
    kT = cs0.sb([128, ND * T], BF16)
    v_tok = cs0.sb([128, NT * D], BF16)
    PT2 = cs0.sb([128, MF], BF16)
    s_pt = c.sem()
    s_pt.inc(nc.gpsimd.dma_start(PT2[:, :], io["PT2"][:, :]), 16)
    onesc = cs0.sb([128, 1], BF16)
    t_on = c.v(nc.vector.memset(onesc[:, 0:1], 1.0))
    ones1 = cs0.sb([1, 128], BF16)
    t_o1 = c.v(nc.vector.memset(ones1[0:1, :], 1.0))
    sub = c.sub()
    cs = sub.__enter__()
    xnT = cs.sb([128, ND * T], BF16)
    s_xn = c.sem()
    ds.load_fm(xnT, 0, T, s_xn)
    wtile = cs.sb([128, ND * D], BF16)
    psr = cs.ring(2, [128, 512])
    projs = [(io["Wk"], kT)] if mode == "max" else \
        [(io["Wq"], qT), (io["Wk"], kT)]
    for whb, dstT in projs:
        c.tip("tensor").wait(nc.gpsimd)
        w = emit_load_weight(cs, whb, D, D, tile=wtile)
        emit_proj_fm(c, xnT, s_xn, w, dstT, psr=psr)
    if mode == "main":
        c.tip("tensor").wait(nc.gpsimd)
        wv = emit_load_weight(cs, io["Wv"], D, D, tile=wtile)
        psr2 = cs.ring(2, [128, D])
        emit_proj_tm(c, lambda ki, t:
                     xnT[:, ki * T + t * 128:ki * T + (t + 1) * 128],
                     s_xn, wv, v_tok, psr=psr2)
    sub.__exit__(None, None, None)
    t_proj = Multi([c.tip("scalar"), c.tip("vector")])
    ddp = cs0.ps([128, MF])
    dgp = cs0.ps([128, 4])
    ctxp = [cs0.ps([128, 128]) for _ in range(2)]
    kcp = cs0.ps([128, 128])
    tpq = cs0.ps([128, 128], BF16)
    outp = cs0.ps([128, 64])
    sq = cs0.sb([128, 128], BF16)
    mrun = cs0.sb([128, 2])
    kp_sb = cs0.sb([128, MF], BF16)
    qpT_sb = cs0.sb([128, MF], BF16)
    ctx_sb = [cs0.sb([128, 128], BF16) for _ in range(H)]
    kc_sb = cs0.sb([128, 2 * H])
    kcbc = [cs0.sb([128, MF]) for _ in range(H)]
    bias_c = cs0.sb([128, 6])
    rowb = cs0.sb([1, 256], BF16)
    mx_bc = cs0.sb([128, 1])
    tmpq = cs0.sb([128, MF])
    t_mi = c.v(nc.vector.memset(mrun[:, 0:1], -1e30))

    def dd_mm(srcT, h, t, tok_prev):
        ki, off = h // 2, (h % 2) * 64
        s_pt.wait(nc.tensor)
        t_proj.wait(nc.tensor)
        if tok_prev is not None:
            tok_prev.wait(nc.tensor)
        mm = nc.tensor.matmul(
            ddp[0:128, 0:MF],
            srcT[:, ki * T + t * 128:ki * T + (t + 1) * 128][
                slice(off, off + 64), :],
            PT2[off:off + 64, :], start=True, stop=True,
            tile_position=(off, 0))
        return c.mark(mm, "tensor")

    def diag_mm(srcT, h, t, tok_prev, col):
        ki, off = h // 2, (h % 2) * 64
        sl = srcT[:, ki * T + t * 128:ki * T + (t + 1) * 128][
            slice(off, off + 64), :]
        c.tip("vector").wait(nc.scalar)
        t_sq = c.s(nc.scalar.activation(sq[off:off + 64, :], sl, AF.Square))
        t_sq.wait(nc.tensor)
        t_on.wait(nc.tensor)
        if tok_prev is not None:
            tok_prev.wait(nc.tensor)
        mm = nc.tensor.matmul(dgp[0:128, col:col + 1], sq[off:off + 64, :],
                              onesc[off:off + 64, 0:1], start=True, stop=True,
                              tile_position=(off, 0))
        return c.mark(mm, "tensor")

    if mode == "max":
        t_dd = t_rm = None
        for h in range(H):
            for t in range(NT):
                t_dd = dd_mm(kT, h, t, t_rm)
                t_dd.wait(nc.vector)
                c.v(nc.vector.reduce_max(mrun[:, 1:2], ddp[0:128, 0:MF],
                                         axis=AX.X))
                t_mi.wait(nc.vector)
                t_rm = c.v(nc.vector.tensor_tensor(mrun[:, 0:1],
                                                   mrun[:, 0:1],
                                                   mrun[:, 1:2], ALU.max))
        mx_hbm = c.dram([128, 1])
        s_mx = c.sem()
        t_rm.wait(nc.sync)
        s_mx.inc(nc.sync.dma_start(mx_hbm[:, :], mrun[:, 0:1]), 16)
        mxrow = cs0.sb([1, 128])
        s_mr = c.sem()
        s_mx.wait(nc.gpsimd)
        s_mr.inc(nc.gpsimd.dma_start(mxrow[0:1, :],
                                     mx_hbm[:, :].rearrange("p o -> o p")),
                 16)
        s_mr.wait(nc.vector)
        t_gm = c.v(nc.vector.reduce_max(mxrow[0:1, 0:1], mxrow[0:1, :],
                                        axis=AX.X))
        s_ci = c.sem()
        t_gm.wait(nc.sync)
        s_ci.inc(nc.sync.dma_start(out_hbm[0:1, 0:1], mxrow[0:1, 0:1]), 16)
        s_ci.wait(nc.gpsimd)
        sub0.__exit__(None, None, None)
        return s_ci
    mxrow = cs0.sb([1, 128])
    s_mr = c.sem()
    s_mr.inc(nc.gpsimd.dma_start(mxrow[0:1, 0:8], io["mx8"][None, :]), 16)
    s_mr.wait(nc.vector)
    t_gm = c.v(nc.vector.reduce_max(mxrow[0:1, 0:1], mxrow[0:1, 0:8],
                                    axis=AX.X))
    t_gm.wait(nc.scalar)
    t_h2 = c.s(nc.scalar.copy(rowb[0:1, 0:1], mxrow[0:1, 0:1]))
    t_h2.wait(nc.vector)
    t_l2 = c.v(nc.vector.tensor_sub(mxrow[0:1, 1:2], mxrow[0:1, 0:1],
                                    rowb[0:1, 0:1]))
    t_l3 = c.v(nc.vector.tensor_copy(rowb[0:1, 1:2], mxrow[0:1, 1:2]))
    t_l3.wait(nc.tensor)
    t_o1.wait(nc.tensor)
    mm = nc.tensor.matmul(kcp[0:128, 0:1], ones1[0:1, :], rowb[0:1, 0:1],
                          start=True, stop=False)
    mm = nc.tensor.matmul(kcp[0:128, 0:1], ones1[0:1, :], rowb[0:1, 1:2],
                          start=False, stop=True)
    t_mm = c.mark(mm, "tensor")
    t_mm.wait(nc.vector)
    t_mb = c.v(nc.vector.tensor_copy(mx_bc[:, 0:1], kcp[0:128, 0:1]))
    t_mb.wait(nc.vector)
    c.wait_const(nc.vector)
    t_bb = c.v(nc.vector.tensor_scalar(bias_c[:, 0:1], mx_bc[:, 0:1],
                                       -DN, LNR, ALU.mult, ALU.add))
    t_kp = t_ctx = t_dg = None
    t_ctxe = []
    for h in range(H):
        for t in range(NT):
            t_dd = dd_mm(kT, h, t, t_kp)
            t_dg = diag_mm(kT, h, t, t_dg, 0)
            t_dg.wait(nc.vector)
            t_bb.wait(nc.vector)
            if t_kp is not None:
                t_kp.wait(nc.vector)
            t_bc = c.v(nc.vector.scalar_tensor_tensor(
                bias_c[:, 1:2], dgp[0:128, 0:1], c.constcol(-0.5 * DN * DN),
                bias_c[:, 0:1], ALU.mult, ALU.add))
            t_bc.wait(nc.scalar)
            t_dd.wait(nc.scalar)
            c.wait_const(nc.scalar)
            t_ex = c.s(nc.scalar.activation(tmpq[:, :], ddp[0:128, 0:MF],
                                            AF.Exp, scale=c.constcol(DN),
                                            bias=bias_c[:, 1:2]))
            t_ex.wait(nc.vector)
            t_kp = c.v(nc.vector.tensor_scalar(kp_sb[:, :], tmpq[:, :],
                                               RATIO * EPS, 0.0, ALU.add,
                                               ALU.bypass))
            t_kp.wait(nc.tensor)
            if t == 0 and t_ctxe:
                t_ctxe[-1].wait(nc.tensor)
            for mh in range(2):
                mm = nc.tensor.matmul(
                    ctxp[mh][0:128, h % 2 * 64:h % 2 * 64 + 64],
                    kp_sb[:, mh * 128:(mh + 1) * 128],
                    v_tok[:, t * D + h * 64:t * D + (h + 1) * 64],
                    start=(t == 0), stop=(t == NT - 1))
                c.mark(mm, "tensor")
                mm = nc.tensor.matmul(
                    kcp[0:128, h % 2 + 2 * mh:h % 2 + 2 * mh + 1],
                    kp_sb[:, mh * 128:(mh + 1) * 128],
                    onesc[:, 0:1], start=(t == 0), stop=(t == NT - 1))
                t_ctx = c.mark(mm, "tensor")
        if h % 2 == 1:
            t_ctx.wait(nc.vector)
            c.tip("tensor").wait(nc.vector)
            for mh in range(2):
                for hh in (h - 1, h):
                    t_e = c.v(nc.vector.tensor_copy(
                        ctx_sb[hh][:, mh * 64:(mh + 1) * 64],
                        ctxp[mh][0:128, hh % 2 * 64:hh % 2 * 64 + 64]))
            for col in range(4):
                hh = (h - 1) + col % 2
                mh = col // 2
                t_e = c.v(nc.vector.tensor_copy(
                    kc_sb[:, hh * 2 + mh:hh * 2 + mh + 1],
                    kcp[0:128, col:col + 1]))
            t_ctxe.append(t_e)
    kc1_hbm = c.dram([2 * H, 128])
    s_kcs = c.sem()
    t_ctxe[-1].wait(nc.sync)
    for col in range(2 * H):
        s_kcs.inc(nc.sync.dma_start(
            kc1_hbm[col:col + 1, :].rearrange("o t -> t o"),
            kc_sb[:, col:col + 1]), 16)
    kcrow = cs0.sb([1, 128])
    hlrow = cs0.sb([1, 256], BF16)
    t_kcb = []
    tok_bc = None
    t_hl = None
    for h in range(H):
        for mh in range(2):
            col = h * 2 + mh
            s_kr = c.dsem("l0kcr")
            s_kcs.wait(nc.sync)
            if t_hl is not None:
                t_hl.wait(nc.sync)
            s_kr.inc(nc.sync.dma_start(kcrow[0:1, :],
                                       kc1_hbm[col:col + 1, :]), 16)
            s_kr.wait(nc.scalar)
            t_hi = c.s(nc.scalar.copy(hlrow[0:1, 0:128], kcrow[0:1, :]))
            t_hi.wait(nc.vector)
            t_hl = c.v(nc.vector.tensor_sub(hlrow[0:1, 128:256],
                                            kcrow[0:1, :],
                                            hlrow[0:1, 0:128]))
            t_hl.wait(nc.tensor)
            t_o1.wait(nc.tensor)
            if tok_bc is not None:
                tok_bc.wait(nc.tensor)
            mm = nc.tensor.matmul(kcp[0:128, 0:128], ones1[0:1, :],
                                  hlrow[0:1, 0:128], start=True, stop=False)
            mm = nc.tensor.matmul(kcp[0:128, 0:128], ones1[0:1, :],
                                  hlrow[0:1, 128:256], start=False, stop=True)
            t_b2 = c.mark(mm, "tensor")
            t_b2.wait(nc.vector)
            tok_bc = t_kb = c.v(nc.vector.tensor_copy(
                kcbc[h][:, mh * 128:(mh + 1) * 128], kcp[0:128, 0:128]))
            t_kcb.append(t_kb)
    t_kcall = t_kcb[-1]
    t_qp = t_dg = t_out = tok_tp = None
    t_ev = None
    for t in range(NT):
        for h in range(H):
            t_dd = dd_mm(qT, h, t, t_qp)
            t_dg = diag_mm(qT, h, t, t_dg, 1)
            t_dd.wait(nc.vector)
            if t_qp is not None:
                t_qp.wait(nc.vector)
            t_mx = c.v(nc.vector.reduce_max(bias_c[:, 2:3], ddp[0:128, 0:MF],
                                            axis=AX.X))
            t_dg.wait(nc.vector)
            c.wait_const(nc.vector)
            t_b1 = c.v(nc.vector.tensor_scalar(bias_c[:, 3:4],
                                               bias_c[:, 2:3], -DN, LNR,
                                               ALU.mult, ALU.add))
            t_bc = c.v(nc.vector.scalar_tensor_tensor(
                bias_c[:, 3:4], dgp[0:128, 1:2], c.constcol(-0.5 * DN * DN),
                bias_c[:, 3:4], ALU.mult, ALU.add))
            t_bc.wait(nc.scalar)
            c.wait_const(nc.scalar)
            t_ex = c.s(nc.scalar.activation(tmpq[:, :], ddp[0:128, 0:MF],
                                            AF.Exp, scale=c.constcol(DN),
                                            bias=bias_c[:, 3:4]))
            t_ex.wait(nc.vector)
            t_qp = c.v(nc.vector.tensor_scalar(kp_sb[:, :], tmpq[:, :],
                                               RATIO * EPS, 0.0, ALU.add,
                                               ALU.bypass))
            t_kcall.wait(nc.vector)
            t_qp.wait(nc.vector)
            t_dv = c.v(nc.vector.tensor_mul(tmpq[:, :], kp_sb[:, :],
                                            kcbc[h][:, :]))
            t_dv2 = c.v(nc.vector.reduce_sum(bias_c[:, 4:5], tmpq[:, :],
                                             axis=AX.X))
            t_rc = c.v(nc.vector.reciprocal(bias_c[:, 5:6], bias_c[:, 4:5]))
            for mh in range(2):
                t_qp.wait(nc.tensor)
                t_idb.wait(nc.tensor)
                if tok_tp is not None:
                    tok_tp.wait(nc.tensor)
                mm = nc.tensor.transpose(tpq[0:128, 0:128],
                                         kp_sb[:, mh * 128:(mh + 1) * 128],
                                         identB[:, :])
                t_tp = c.mark(mm, "tensor")
                t_tp.wait(nc.vector)
                tok_tp = c.v(nc.vector.tensor_copy(
                    qpT_sb[:, mh * 128:(mh + 1) * 128], tpq[0:128, 0:128]))
            tok_tp.wait(nc.tensor)
            if t_ev is not None:
                t_ev.wait(nc.tensor)
            for mh in range(2):
                mm = nc.tensor.matmul(outp[0:128, 0:64],
                                      qpT_sb[:, mh * 128:(mh + 1) * 128],
                                      ctx_sb[h][:, mh * 64:(mh + 1) * 64],
                                      start=(mh == 0), stop=(mh == 1))
            t_out = c.mark(mm, "tensor")
            t_out.wait(nc.vector)
            t_rc.wait(nc.vector)
            t_ev = c.v(nc.vector.tensor_scalar_mul(
                att_tok[:, t * D + h * 64:t * D + (h + 1) * 64],
                outp[0:128, 0:64], bias_c[:, 5:6]))
    t_att = c.tip("vector")
    sub0.__exit__(None, None, None)
    sub = c.sub()
    cs = sub.__enter__()
    wo = emit_load_weight(cs, io["Wo"], D, D)
    bo_r, s_bo = emit_load_bf16_row(cs, io["bo"], D)
    ones2, t_o2 = emit_ones(cs)
    attT_buf = cs.sb([128, ND * T], BF16)
    attT, s_at = emit_bounceT(c, att_tok, t_att, D, outT=attT_buf)
    ops2 = cs.ring(2, [128, D])
    xrr = cs.sbring(2, [128, D], F32)
    ost = cs.sbring(2, [128, D], F32)
    s_out = c.sem()
    x3 = x_hbm.rearrange("(t p) f -> t p f", p=128)
    o3 = out_hbm.rearrange("(t p) f -> t p f", p=128)
    for t in range(NT):
        xr = xrr.acquire(nc.sync)
        s_xl = c.dsem("l0ax")
        s_xl.inc(nc.sync.dma_start(xr[:, :], x3[t, :, :]), 16)
        pt = ops2.acquire(nc.tensor)
        wo.sem.wait(nc.tensor)
        s_at.wait(nc.tensor)
        s_bo.wait(nc.tensor)
        t_o2.wait(nc.tensor)
        for o0 in range(0, D, 512):
            pairs = [(attT[:, ki * T + t * 128:ki * T + (t + 1) * 128],
                      wo.lhsT(ki, o0, o0 + 512)) for ki in range(ND)]
            pairs.append((ones2[0:1, 0:128], bo_r[0:1, o0:o0 + 512]))
            mm = mm_group(nc, pt[:, o0:o0 + 512], pairs)
        t_mm = c.mark(mm, "tensor")
        ot = ost.acquire(nc.vector)
        t_mm.wait(nc.vector)
        s_xl.wait(nc.vector)
        t_ad = c.v(nc.vector.tensor_add(ot[:, :], pt[:, 0:D], xr[:, :]))
        ops2.release(t_ad)
        xrr.release(t_ad)
        t_ad.wait(nc.sync)
        s_out.inc(nc.sync.dma_start(o3[t, :, :], ot[:, :]), 16)
        ost.release(Tok(s_out.h, s_out.v))
    sub.__exit__(None, None, None)
    s_out.wait(nc.gpsimd)
    return s_out


def build_l0attna_program(mode="main"):
    nc = bass.Bass("TRN2", target_bir_lowering=False)
    stack = ExitStack()
    c = Ctx(nc, stack)
    dt = lambda n, sh: nc.dram_tensor(n, sh, F32, kind="ExternalInput").ap()
    io = {
        "Wq": dt("Wq", [D, D]), "Wk": dt("Wk", [D, D]), "Wv": dt("Wv", [D, D]),
        "Wo": dt("Wo", [D, D]), "bo": dt("bo", [D]),
        "PT2": dt("PT2", [128, 256]),
    }
    if mode == "main":
        io["mx8"] = dt("mx8", [8])
    x_h = dt("x", [T, D])
    oshape = [1, 1] if mode == "max" else [T, D]
    out_h = nc.dram_tensor("out", oshape, F32, kind="ExternalOutput").ap()
    with stack:
        emit_l0_attn_a(c, x_h, out_h, io, mode=mode)
    return nc

from concourse.bass_utils import run_bass_kernel_spmd as _run_spmd

_PROGS = {}


def _prog(name, builder, *a, **k):
    if name not in _PROGS:
        _PROGS[name] = builder(*a, **k)
    return _PROGS[name]


def _colw(v, ncol):
    return np.ascontiguousarray(np.asarray(v, np.float32).reshape(ncol, 128).T)


def _cvc(cw, kk=4):
    cw = np.asarray(cw, np.float32)
    return np.ascontiguousarray(
        cw.reshape(8, 128, kk).transpose(1, 0, 2).reshape(128, 8 * kk))


def _host_wqkv(Wqkv):
    H_, DH_ = 16, 64
    W = np.asarray(Wqkv, np.float32).reshape(3, H_, DH_, D)
    out = np.zeros((3, D, D), np.float32)
    for g in range(2):
        for h in range(H_):
            for t in range(2):
                out[g, :, t * 512 + h * 32:(t * 512 + h * 32) + 32] = \
                    W[g, h, t::2, :].transpose(1, 0)
    out[2] = W[2].reshape(D, D).T
    return np.ascontiguousarray(np.concatenate([out[0], out[1], out[2]],
                                               axis=1))


def _ct(a):
    return np.ascontiguousarray(np.asarray(a, np.float32).T)


def kernel(**inp):
    NCORES = 8
    cores = list(range(NCORES))
    xs = [np.ascontiguousarray(np.asarray(inp["L0_attn_a_x"][i], np.float32))
          for i in range(NCORES)]

    def launch(name, builder, wm, bargs=()):
        nc = _prog(name, builder, *bargs)
        ims = [dict(wm, x=xs[i]) for i in range(NCORES)]
        res = _run_spmd(nc, ims, core_ids=cores)
        for i in range(NCORES):
            xs[i] = res.results[i]["out"]

    PT2 = np.ascontiguousarray(
        np.concatenate([np.asarray(inp["L0_attn_a_P"], np.float32).T] * 2, 0))
    wm_max = {"Wq": inp["L0_attn_a_Wq"], "Wk": inp["L0_attn_a_Wk"],
              "Wv": inp["L0_attn_a_Wv"], "Wo": inp["L0_attn_a_Wo"],
              "bo": inp["L0_attn_a_bo"], "PT2": PT2}
    nc1 = _prog("l0aa_max", build_l0attna_program, "max")
    r1 = _run_spmd(nc1, [dict(wm_max, x=xs[i]) for i in range(NCORES)],
                   core_ids=cores)
    mx8 = np.ascontiguousarray(np.concatenate(
        [r1.results[i]["out"].reshape(1) for i in range(NCORES)]).astype(
            np.float32))
    launch("l0aa", build_l0attna_program, dict(wm_max, mx8=mx8), ("main",))

    cw = np.asarray(inp["L0_ffn_conv_w"], np.float32)
    wm_l0ffn = {
        "router_w": inp["L0_ffn_router_w"],
        "router_bias": inp["L0_ffn_router_bias"],
        "gate_w": inp["L0_ffn_expert_gate_w"],
        "up_w": inp["L0_ffn_expert_up_w"],
        "down_w": inp["L0_ffn_expert_down_w"],
        "conv_w0": _ct(cw[:, :, 0]), "conv_w1": _ct(cw[:, :, 1]),
        "conv_w2": _ct(cw[:, :, 2]),
        "shared_up": inp["L0_ffn_shared_up_w"],
        "shared_down": inp["L0_ffn_shared_down_w"]}
    launch("l0ffn", build_l0ffn_program, wm_l0ffn)

    wm = {"Wq": _ct(inp["L0_attn_b_Wq"]),
          "Wqkv": _ct(inp["L0_attn_b_Wqkv"]),
          "bqkv": inp["L0_attn_b_bqkv"],
          "Wop": _ct(inp["L0_attn_b_Wop"]),
          "W2c": np.ascontiguousarray(
              np.asarray(inp["L0_attn_b_W2"], np.float32).reshape(2, 128).T),
          "b2": inp["L0_attn_b_b2"]}
    launch("l0ab", build_l0attnb_program, wm)
    launch("l0ffn", build_l0ffn_program, wm_l0ffn)

    heads3 = np.ascontiguousarray(np.concatenate(
        [np.asarray(inp["L1_attn_a_log_xi"], np.float32),
         np.asarray(inp["L1_attn_a_pi_gate_logit"], np.float32),
         np.asarray(inp["L1_attn_a_e_gate_logit"], np.float32)])[None, :])
    launch("l1aa", build_l1attna_program,
           {"Wqkv": _host_wqkv(inp["L1_attn_a_Wqkv"]),
            "Wo": _ct(inp["L1_attn_a_Wo"]), "heads3": heads3})
    wm_l1ffn = {"Wu": _ct(inp["L1_ffn_w_up"]), "Wd": _ct(inp["L1_ffn_w_down"])}
    launch("l1ffn", build_l1ffn_program, wm_l1ffn)
    launch("l1ab", build_l1attnb_program, {"W": _ct(inp["L1_attn_b_W"])})
    launch("l1ffn", build_l1ffn_program, wm_l1ffn)

    launch("l2aa", build_l2attna_program,
           {"rby": inp["L2_attn_a_rby"],
            "Wq": _ct(inp["L2_attn_a_Wq"]), "bq": inp["L2_attn_a_bq"],
            "Wk": _ct(inp["L2_attn_a_Wk"]), "bk": inp["L2_attn_a_bk"],
            "Wv": _ct(inp["L2_attn_a_Wv"]), "bv": inp["L2_attn_a_bv"],
            "Wo": _ct(inp["L2_attn_a_Wo"]), "bo": inp["L2_attn_a_bo"],
            "ln_gamma": inp["L2_attn_a_ln_gamma"],
            "ln_beta": inp["L2_attn_a_ln_beta"]})
    wm_l2ffn = {"W1": inp["L2_ffn_W1"],
                "b1c": col_layout(np.asarray(inp["L2_ffn_b1"], np.float32)),
                "W2": inp["L2_ffn_W2"], "b2": inp["L2_ffn_b2"]}
    launch("l2ffn", build_l2ffn_program, wm_l2ffn)
    logA = np.asarray(inp["L2_attn_b_log_A"], np.float32)
    cwm = np.asarray(inp["L2_attn_b_conv_w"], np.float32)
    launch("l2ab", build_l2attnb_program,
           {"norm_g": inp["L2_attn_b_norm_g"],
            "norm_b": inp["L2_attn_b_norm_b"],
            "in_proj": _ct(inp["L2_attn_b_in_proj_w"]),
            "conv_wc": np.ascontiguousarray(
                cwm.reshape(16, 128, 4).transpose(1, 0, 2).reshape(128, 64)),
            "conv_bc": _colw(inp["L2_attn_b_conv_b"], 16),
            "x_proj": _ct(inp["L2_attn_b_x_proj_w"]),
            "dt_proj": _ct(inp["L2_attn_b_dt_proj_w"]),
            "dt_bias": inp["L2_attn_b_dt_proj_b"],
            "logA_c": np.ascontiguousarray(
                logA.reshape(16, 128, 16).transpose(1, 0, 2).reshape(128, 256)),
            "Dp_c": _colw(inp["L2_attn_b_Dp"], 16),
            "out_proj": _ct(inp["L2_attn_b_out_proj_w"])})
    launch("l2ffn", build_l2ffn_program, wm_l2ffn)

    launch("l3aa", build_l3attna_program,
           {"Wq": _ct(inp["L3_attn_a_q_proj_w"]),
            "Wk": _ct(inp["L3_attn_a_k_proj_w"]),
            "Wv": _ct(inp["L3_attn_a_v_proj_w"]),
            "Wb": _ct(inp["L3_attn_a_b_proj_w"]),
            "conv_wq_c": _cvc(inp["L3_attn_a_q_conv_w"]),
            "conv_wk_c": _cvc(inp["L3_attn_a_k_conv_w"]),
            "conv_wv_c": _cvc(inp["L3_attn_a_v_conv_w"]),
            "fir_s_c": _cvc(np.asarray(inp["L3_attn_a_fir_short_w"],
                                       np.float32).reshape(D, 7), 7),
            "fir_l_c": _cvc(np.asarray(inp["L3_attn_a_fir_long_w"],
                                       np.float32).reshape(D, 31), 31),
            "alpha": np.ascontiguousarray(
                np.asarray(inp["L3_attn_a_alpha"], np.float32)[:, 0]),
            "gmix_w1": _ct(inp["L3_attn_a_gmix_w1"]),
            "gmix_b1_c": _colw(inp["L3_attn_a_gmix_b1"], 16),
            "gmix_w2": _ct(inp["L3_attn_a_gmix_w2"]),
            "gmix_b2": inp["L3_attn_a_gmix_b2"],
            "onorm_w": inp["L3_attn_a_o_norm_w"],
            "Wo": _ct(inp["L3_attn_a_o_proj_w"])})
    wm_l3ffn = {
        "Wgt": np.ascontiguousarray(np.concatenate(
            [np.asarray(inp["L3_ffn_Wg"], np.float32),
             np.asarray(inp["L3_ffn_Wt"], np.float32)], axis=1)),
        "bgt": np.ascontiguousarray(np.concatenate(
            [np.asarray(inp["L3_ffn_bg"], np.float32),
             np.asarray(inp["L3_ffn_bt"], np.float32)])),
        "W1": inp["L3_ffn_W1"],
        "b1c": np.ascontiguousarray(np.concatenate(
            [col_layout(np.asarray(inp["L3_ffn_b1"], np.float32)[e])
             for e in range(8)], axis=1)),
        "W2": inp["L3_ffn_W2"],
        "b2r": np.ascontiguousarray(
            np.asarray(inp["L3_ffn_b2"], np.float32).flatten())}
    launch("l3ffn", build_l3ffn_program, wm_l3ffn)
    launch("l3ab", build_l3attnb_program,
           {"Wq": inp["L3_attn_b_Wq"], "Wk": inp["L3_attn_b_Wk"],
            "Wv": inp["L3_attn_b_Wv"],
            "conv_wq_c": _cvc(inp["L3_attn_b_conv_wq"]),
            "conv_wk_c": _cvc(inp["L3_attn_b_conv_wk"]),
            "conv_wv_c": _cvc(inp["L3_attn_b_conv_wv"]),
            "onorm_w": inp["L3_attn_b_onorm_w"],
            "Wo": inp["L3_attn_b_Wo"]})
    launch("l3ffn", build_l3ffn_program, wm_l3ffn)

    return np.stack(xs, 0).astype(np.float32)


def build_l0ffn_program():
    nc = bass.Bass("TRN2", target_bir_lowering=False)
    stack = ExitStack()
    c = Ctx(nc, stack)
    dt = lambda n, sh: nc.dram_tensor(n, sh, F32, kind="ExternalInput").ap()
    io = {
        "router_w": dt("router_w", [D, 16]),
        "router_bias": dt("router_bias", [16]),
        "gate_w": dt("gate_w", [15, D, 512]),
        "up_w": dt("up_w", [15, D, 512]),
        "down_w": dt("down_w", [15, 512, D]),
        "conv_w0": dt("conv_w0", [D, 512]),
        "conv_w1": dt("conv_w1", [D, 512]),
        "conv_w2": dt("conv_w2", [D, 512]),
        "shared_up": dt("shared_up", [D, 512]),
        "shared_down": dt("shared_down", [512, D]),
    }
    x_h = dt("x", [T, D])
    out_h = nc.dram_tensor("out", [T, D], F32, kind="ExternalOutput").ap()
    with stack:
        xres = c.sb([128, NT * D], F32)
        s_x = emit_load_x(c, x_h, xres)
        ln = emit_ln_transpose(c, xres, s_x)
        tok = emit_l0_ffn(c, xres, s_x, ln, io)
        emit_store_x(c, xres, out_h, [tok])
    return nc


# version 05
# speedup vs baseline: 1.1790x; 1.1790x over previous
import math
import numpy as np
import ml_dtypes
from contextlib import ExitStack

import concourse.bass as bass
import concourse.mybir as mybir
from concourse.bass_utils import run_bass_kernel_spmd

F32 = mybir.dt.float32
BF16 = mybir.dt.bfloat16
AF = mybir.ActivationFunctionType
ALU = mybir.AluOpType
AX = mybir.AxisListType

T = 2048
D = 1024
NT = T // 128
ND = D // 128
NCORES = 8


class Sem:
    def __init__(self, ctx, name):
        self.h = ctx.stack.enter_context(ctx.nc.semaphore(name))
        self.v = 0

    def inc(self, instr, n=1):
        instr.then_inc(self.h, n)
        self.v += n
        return instr

    def wait(self, eng, v=None):
        eng.wait_ge(self.h, self.v if v is None else v)


class Multi:
    def __init__(self, toks):
        self.toks = toks

    def wait(self, eng):
        for t in self.toks:
            t.wait(eng)


class Tok:

    def __init__(self, sem_h, v):
        self.h, self.v = sem_h, v

    def wait(self, eng):
        eng.wait_ge(self.h, self.v)


class Ring:

    def __init__(self, tiles):
        self.tiles = tiles
        self.toks = [None] * len(tiles)
        self.i = 0
        self.n = len(tiles)

    def acquire(self, eng):
        t = self.toks[self.i % self.n]
        if t is not None:
            t.wait(eng)
        tile = self.tiles[self.i % self.n]
        self.i += 1
        return tile

    def release(self, tok):
        self.toks[(self.i - 1) % self.n] = tok


class Ctx:
    def __init__(self, nc, stack):
        self.nc = nc
        self.stack = stack
        self._c = 0
        self.cstate = {"tile": None, "used": 0, "sem": None, "map": {}}
        self.cstate["tile"] = self.stack.enter_context(
            self.nc.sbuf_tensor("constc", [128, 48], F32))
        self.cstate["sem"] = self.sem("s_const")

    def constcol(self, val, n=48):
        cs = self.cstate
        val = float(val)
        if val in cs["map"]:
            return cs["map"][val]
        j = cs["used"]
        cs["used"] += 1
        ap = cs["tile"][:, j:j + 1]
        cs["sem"].inc(self.nc.vector.memset(ap, val))
        cs["map"][val] = ap
        return ap

    def wait_const(self, eng):
        self.cstate["sem"].wait(eng)

    def _n(self, p):
        self._c += 1
        return f"{p}{self._c}"

    def _streams(self):
        if not hasattr(self, "est"):
            self.est = {n: self.sem("st_" + n)
                        for n in ("tensor", "vector", "scalar", "pool")}
        return self.est

    def mark(self, instr, eng):
        st = self._streams()[eng]
        instr._wait_ge(st.h, st.v)
        st.inc(instr, 1)
        return Tok(st.h, st.v)

    def v(self, instr):
        return self.mark(instr, "vector")

    def s(self, instr):
        return self.mark(instr, "scalar")

    def g(self, instr):
        return self.mark(instr, "pool")

    def tip(self, eng):
        st = self._streams()[eng]
        return Tok(st.h, st.v)

    def sem(self, name=None):
        return Sem(self, name or self._n("s"))

    def dsem(self, tag):
        if not hasattr(self, "_dsems"):
            self._dsems = {}
        if tag not in self._dsems:
            self._dsems[tag] = self.sem("d_" + tag)
        return self._dsems[tag]

    def sb(self, shape, dtype=F32, name=None):
        return self.stack.enter_context(
            self.nc.sbuf_tensor(name or self._n("zb"), shape, dtype))

    def ps(self, shape, dtype=F32, name=None):
        return self.stack.enter_context(
            self.nc.psum_tensor(name or self._n("zp"), shape, dtype))

    def dram(self, shape, dtype=F32, name=None):
        return self.nc.dram_tensor(self._n("zd") if name is None else name,
                                   shape, dtype, kind="Internal").ap()

    def sub(self):
        parent = self

        class _SubCM:
            def __enter__(self_inner):
                st = ExitStack()
                child = Ctx.__new__(Ctx)
                child.nc = parent.nc
                child.stack = st
                child._c = 0
                child.cstate = parent.cstate
                child.sem = parent.sem
                child.dsem = parent.dsem
                child.mark = parent.mark
                child.tip = parent.tip
                child.constcol = parent.constcol
                if hasattr(parent, "est"):
                    child.est = parent.est
                child._n = lambda p: parent._n(p)
                child.exit_waits = []
                self_inner.child = child
                self_inner.st = st
                return child

            def __exit__(self_inner, *a):
                nc = parent.nc
                sems = list(parent._streams().values())
                waits = [(s_.h, s_.v) for s_ in sems]
                waits += [(s_.h, s_.v) for s_ in self_inner.child.exit_waits]
                for eng in (nc.tensor, nc.vector, nc.scalar, nc.gpsimd,
                            nc.sync):
                    for h, v in waits:
                        if v > 0:
                            eng.wait_ge(h, v)
                self_inner.st.close()
                return False

        return _SubCM()

    def ring(self, n, shape, dtype=F32):
        return Ring([self.ps(shape, dtype) for _ in range(n)])

    def sbring(self, n, shape, dtype):
        return Ring([self.sb(shape, dtype) for _ in range(n)])


def mm_group(nc, out_ap, pairs):
    n = len(pairs)
    mm = None
    for j, (l, r) in enumerate(pairs):
        mm = nc.tensor.matmul(out_ap, l, r, start=(j == 0), stop=(j == n - 1))
    return mm



def emit_load_x(c, x_hbm, xres, after=()):
    ld = c.sem()
    for s_ in after:
        s_.wait(c.nc.sync)
    src = x_hbm.rearrange("(t p) f -> p t f", p=128)
    dst = xres[:, :].rearrange("p (t f) -> p t f", t=NT)
    ld.inc(c.nc.sync.dma_start(dst, src), 16)
    return ld


def emit_store_x(c, xres, out_hbm, wait_sems):
    st = c.sem()
    for s in wait_sems:
        s.wait(c.nc.sync)
    dst = out_hbm.rearrange("(t p) f -> p t f", p=128)
    src = xres[:, :].rearrange("p (t f) -> p t f", t=NT)
    st.inc(c.nc.sync.dma_start(dst, src), 16)
    st.wait(c.nc.sync)
    return st


class WH:

    def __init__(self, tile, din, dout, sem):
        self.t, self.din, self.dout, self.sem = tile, din, dout, sem
        self.nki = din // 128

    def lhsT(self, ki, o0, o1):
        return self.t[:, ki * self.dout + o0: ki * self.dout + o1]


def emit_load_weight(c, w_hbm_slice, din, dout, queue=None, tile=None, sem=None):
    nc = c.nc
    nki = din // 128
    tile = tile if tile is not None else c.sb([128, nki * dout], BF16)
    s = sem or c.sem()
    src = w_hbm_slice.rearrange("(ki p) o -> p ki o", p=128)
    dst = tile[:, 0:nki * dout].rearrange("p (ki o) -> p ki o", ki=nki)
    s.inc((queue or nc.gpsimd).dma_start(dst, src), 16)
    return WH(tile, din, dout, s)


def emit_load_bf16_row(c, v_hbm, n, queue=None):
    tile = c.sb([1, n], BF16)
    s = c.sem()
    s.inc((queue or c.nc.gpsimd).dma_start(tile[0:1, :], v_hbm[None, :]), 16)
    return tile, s


def emit_load_f32(c, v_hbm_2d, p, n, queue=None):
    tile = c.sb([p, n], F32)
    s = c.sem()
    s.inc((queue or c.nc.gpsimd).dma_start(tile[:, :], v_hbm_2d[:, :]), 16)
    return tile, s


def emit_ones(c, n=128):
    ones = c.sb([1, n], BF16)
    tok = c.v(c.nc.vector.memset(ones[0:1, :], 1.0))
    return ones, tok


def emit_bcast_rows(c, cs, jobs, onesb=None):
    nc = c.nc
    if onesb is None:
        onesb, t_ob = emit_ones(cs)
    else:
        onesb, t_ob = onesb
    maxn = max(j[0].shape[1] for j in jobs)
    hi = cs.sb([1, maxn], BF16)
    lo = cs.sb([1, maxn], BF16)
    ps = cs.ps([128, 512])
    t_last = None
    for row, dst, rdy in jobs:
        n = row.shape[1]
        rdy.wait(nc.vector)
        t_hi = c.v(nc.vector.tensor_copy(hi[0:1, 0:n], row))
        t_lo = c.v(nc.vector.tensor_sub(lo[0:1, 0:n], row, hi[0:1, 0:n]))
        for o0 in range(0, n, 512):
            o1 = min(o0 + 512, n)
            t_lo.wait(nc.tensor)
            t_ob.wait(nc.tensor)
            if t_last is not None:
                t_last.wait(nc.tensor)
            nc.tensor.matmul(ps[:, 0:o1 - o0], onesb[0:1, :], hi[0:1, o0:o1],
                             start=True, stop=False)
            mm = nc.tensor.matmul(ps[:, 0:o1 - o0], onesb[0:1, :],
                                  lo[0:1, o0:o1], start=False, stop=True)
            t_bc = c.mark(mm, "tensor")
            t_bc.wait(nc.vector)
            t_last = c.v(nc.vector.tensor_copy(dst[:, o0:o1],
                                               ps[:, 0:o1 - o0]))
    return t_last



class LNOut:
    def __init__(self, lnT, sem, lnx, sem_tok):
        self.lnT, self.sem, self.lnx, self.sem_tok = lnT, sem, lnx, sem_tok

    def kT(self, ki, t0=0, t1=T):
        return self.lnT[:, ki * T + t0: ki * T + t1]


def emit_ln_transpose(c, src_sb, ready_sem, eps=1e-5, lnT=None,
                      rms_compose=False):
    nc = c.nc
    ssum = c.sb([128, NT])
    ssq = c.sb([128, NT])
    sq_scr = c.sb([128, D])
    mcol = c.sb([128, NT])
    msq = c.sb([128, NT])
    rstd = c.sb([128, NT])
    nmr = c.sb([128, NT])
    stg = [c.sb([128, D], BF16) for _ in range(2)]
    stg_sems = [c.sem(), c.sem()]
    ln_hbm = c.dram([T, D], BF16)
    lnT = lnT if lnT is not None else c.sb([128, ND * T], BF16)

    s_store = c.sem()
    s_tp = c.sem()

    ready_sem.wait(nc.vector)
    ready_sem.wait(nc.scalar)
    t_sq = t_red = None
    for t in range(NT):
        t_red = c.v(nc.vector.reduce_sum(
            ssum[:, t:t + 1], src_sb[:, t * D:(t + 1) * D], axis=AX.X))
        t_sq = c.s(nc.scalar.activation(
            sq_scr[:, :], src_sb[:, t * D:(t + 1) * D], AF.Square,
            accum_out=ssq[:, t:t + 1]))
    t_mcol = c.v(nc.vector.tensor_scalar_mul(mcol[:, :], ssum[:, :], 1.0 / D))
    t_sq.wait(nc.vector)
    c.v(nc.vector.tensor_scalar_mul(ssq[:, :], ssq[:, :], 1.0 / D))
    t_mcol.wait(nc.scalar)
    t_msq = c.s(nc.scalar.activation(msq[:, :], mcol[:, :], AF.Square))
    t_msq.wait(nc.vector)
    t_var = c.v(nc.vector.tensor_sub(ssq[:, :], ssq[:, :], msq[:, :]))
    epsc = c.constcol(eps)
    halfneg = c.constcol(-0.5)
    halfpos = c.constcol(0.5)
    t_var.wait(nc.scalar)
    c.wait_const(nc.scalar)
    c.s(nc.scalar.activation(rstd[:, :], ssq[:, :], AF.Ln, bias=epsc))
    t_rstd = c.s(nc.scalar.activation(rstd[:, :], rstd[:, :], AF.Exp,
                                      scale=halfneg))
    if rms_compose:
        sv = c.sb([128, NT])
        c.s(nc.scalar.activation(sv[:, :], ssq[:, :], AF.Ln))
        t_sv = c.s(nc.scalar.activation(sv[:, :], sv[:, :], AF.Exp,
                                        scale=halfpos))
        t_sv.wait(nc.vector)
        t_rstd.wait(nc.vector)
        c.v(nc.vector.tensor_mul(sv[:, :], sv[:, :], rstd[:, :]))
        c.v(nc.vector.tensor_scalar_add(sv[:, :], sv[:, :], 1e-8))
        t_sc = c.v(nc.vector.reciprocal(sv[:, :], sv[:, :]))
        t_rstd = c.v(nc.vector.tensor_mul(rstd[:, :], rstd[:, :], sv[:, :]))
    t_rstd.wait(nc.vector)
    t_nmr = c.v(nc.vector.scalar_tensor_tensor(
        nmr[:, :], mcol[:, :], -1.0, rstd[:, :], ALU.mult, ALU.mult))
    t_nmr.wait(nc.scalar)
    dst3 = ln_hbm.rearrange("(t p) f -> t p f", p=128)
    for t in range(NT):
        if t >= 2:
            stg_sems[t % 2].wait(nc.scalar)
        tok = c.s(nc.scalar.activation(
            stg[t % 2][:, :], src_sb[:, t * D:(t + 1) * D], AF.Identity,
            scale=rstd[:, t:t + 1], bias=nmr[:, t:t + 1]))
        tok.wait(nc.sync)
        st = nc.sync.dma_start(dst3[t, :, :], stg[t % 2][:, :])
        stg_sems[t % 2].inc(st, 16)
        s_store.v += 16
    stg_sems[0].wait(nc.sync)
    stg_sems[1].wait(nc.sync)
    for ki in range(ND):
        s_tp.inc(nc.sync.dma_start_transpose(
            lnT[:, ki * T:(ki + 1) * T], ln_hbm[:, ki * 128:(ki + 1) * 128]), 16)
    out = LNOut(lnT, s_tp, None, None)
    out.rstd, out.nmr, out.var = rstd, nmr, ssq
    out.t_affine = t_nmr
    return out



class FfnShared:

    def __init__(self, c, dff, dout=D, tb=512, h_rms=False):
        self.dff, self.tb = dff, tb
        self.nh = 2
        self.fh = dff // 2
        self.kh = self.fh // 128
        self.wtileA = c.sb([128, ND * self.fh], BF16)
        self.wtileB = c.sb([128, self.kh * dout], BF16)
        self.tokA = None
        self.hTr = c.sbring(1 if h_rms else 2, [128, self.kh * tb], BF16)
        self.hps = c.ring(2, [128, 512])
        self.yps = c.ring(2, [128, dout])
        self.ones, self.s_ones = emit_ones(c)
        if h_rms:
            self.onesc = c.sb([128, 1], BF16)
            self.s_onesc = c.v(c.nc.vector.memset(self.onesc[:, 0:1], 1.0))
            self.h2sc = c.sbring(2, [128, tb], BF16)
            self.l2ps = c.ring(2, [1, tb])
            self.l2a = c.sb([1, T])
            self.l2row = c.sb([1, tb])
            self.l2col = c.sb([128, NT])
            self.sc2 = c.sb([128, NT])
            self.l2_hbm = c.dram([T])
            self.y_hbm = c.dram([T, D])
            self.ystg = c.sbring(2, [128, D], F32)
            self.s_yst = c.sem()


def emit_ffn_expert(c, xres, s_x, ln, sh, W1sl, W2sl, *, act=AF.Relu,
                    b1col=None, s_b1=None, b2row=None, s_b2=None,
                    wcol=None, s_wcol=None, h_rms=False, w_bf16=False):
    nc = c.nc
    dff, TB = sh.dff, sh.tb
    FH, KH = sh.fh, sh.kh
    NTB = T // TB
    tok_add = None
    for half in range(2):
        if sh.tokA is not None:
            sh.tokA.wait(nc.gpsimd)
        w1 = emit_load_weight(c, W1sl[:, half * FH:(half + 1) * FH], D, FH,
                              tile=sh.wtileA)
        w2 = emit_load_weight(c, W2sl[half * FH:(half + 1) * FH, :], FH, D,
                              tile=sh.wtileB)
        for tb in range(NTB):
            t0 = tb * TB
            hT = sh.hTr.acquire(nc.scalar)
            w1.sem.wait(nc.tensor)
            ln.sem.wait(nc.tensor)
            tok_ev = None
            if h_rms:
                lp = sh.l2ps.acquire(nc.tensor)
            for ko in range(KH):
                pt = sh.hps.acquire(nc.tensor)
                mm = mm_group(nc, pt[:, 0:TB],
                              [(w1.lhsT(ki, ko * 128, (ko + 1) * 128),
                                ln.kT(ki, t0, t0 + TB)) for ki in range(ND)])
                tok_mm = c.mark(mm, "tensor")
                tok_mm.wait(nc.scalar)
                if s_b1 is not None:
                    s_b1.wait(nc.scalar)
                kw = {}
                if b1col is not None:
                    kw["bias"] = b1col[:, half * KH + ko: half * KH + ko + 1]
                ev = nc.scalar.activation(
                    hT[:, ko * TB:(ko + 1) * TB], pt[:, 0:TB], act, **kw)
                tok_ev = c.mark(ev, "scalar")
                sh.hps.release(tok_ev)
                if h_rms:
                    h2 = sh.h2sc.acquire(nc.scalar)
                    t_h2 = c.mark(nc.scalar.activation(
                        h2[:, 0:TB], hT[:, ko * TB:(ko + 1) * TB], AF.Square),
                        "scalar")
                    t_h2.wait(nc.tensor)
                    sh.s_onesc.wait(nc.tensor)
                    mm2 = nc.tensor.matmul(lp[0:1, 0:TB], sh.onesc[:, 0:1],
                                           h2[:, 0:TB], start=(ko == 0),
                                           stop=(ko == KH - 1))
                    t_mm2 = c.mark(mm2, "tensor")
                    sh.h2sc.release(t_mm2)
            if h_rms:
                t_mm2.wait(nc.vector)
                if half == 0:
                    t_l2 = c.mark(nc.vector.tensor_copy(
                        sh.l2a[0:1, t0:t0 + TB], lp[0:1, 0:TB]), "vector")
                    sh.l2ps.release(t_l2)
                else:
                    t_l2 = c.mark(nc.vector.tensor_add(
                        sh.l2row[0:1, 0:TB], sh.l2a[0:1, t0:t0 + TB],
                        lp[0:1, 0:TB]), "vector")
                    sh.l2ps.release(t_l2)
                    t_l2.wait(nc.sync)
                    s_st = c.dsem("ffl2st")
                    s_st.inc(nc.sync.dma_start(
                        sh.l2_hbm[t0:t0 + TB][None, :], sh.l2row[0:1, 0:TB]), 16)
                    s_st.wait(nc.sync)
                    s_ld = c.dsem("ffl2ld")
                    ntb_t = TB // 128
                    src = sh.l2_hbm[t0:t0 + TB].rearrange("(t p) -> p t", p=128)
                    s_ld.inc(nc.sync.dma_start(
                        sh.l2col[:, tb * ntb_t:(tb + 1) * ntb_t], src,
                        allow_slow_non_contiguous=True), 16)
                    cdff = c.constcol(1.0 / dff)
                    chalf = c.constcol(0.5)
                    s_ld.wait(nc.scalar)
                    c.wait_const(nc.scalar)
                    sl = slice(tb * ntb_t, (tb + 1) * ntb_t)
                    c.s(nc.scalar.activation(sh.sc2[:, sl], sh.l2col[:, sl],
                                             AF.Ln, scale=cdff))
                    t_s1 = c.mark(nc.scalar.activation(
                        sh.sc2[:, sl], sh.sc2[:, sl], AF.Exp,
                        scale=chalf), "scalar")
                    t_s1.wait(nc.vector)
                    c.v(nc.vector.tensor_scalar_add(sh.sc2[:, sl],
                                                    sh.sc2[:, sl], 1e-8))
                    t_sc2 = c.mark(nc.vector.reciprocal(sh.sc2[:, sl],
                                                        sh.sc2[:, sl]), "vector")
            w2.sem.wait(nc.tensor)
            sh.s_ones.wait(nc.tensor)
            if s_b2 is not None:
                s_b2.wait(nc.tensor)
            tok_ev.wait(nc.tensor)
            for st in range(TB // 128):
                t = (t0 + st * 128) // 128
                pt = sh.yps.acquire(nc.tensor)
                for b in range(2):
                    o0, o1 = b * 512, (b + 1) * 512
                    pairs = [(hT[:, ko * TB + st * 128: ko * TB + (st + 1) * 128],
                              w2.lhsT(ko, o0, o1)) for ko in range(KH)]
                    if half == 0 and b2row is not None:
                        pairs.append((sh.ones[0:1, 0:128], b2row[0:1, o0:o1]))
                    mm = mm_group(nc, pt[:, o0:o1], pairs)
                    tok_mm2 = c.mark(mm, "tensor")
                if h_rms and half == 0:
                    tok_mm2.wait(nc.vector)
                    ystg = sh.ystg.acquire(nc.vector)
                    tok_add = c.mark(nc.vector.tensor_copy(
                        ystg[:, :], pt[:, 0:D]), "vector")
                    sh.yps.release(tok_add)
                    tok_add.wait(nc.sync)
                    yd = sh.y_hbm.rearrange("(t p) f -> t p f", p=128)
                    styd = nc.sync.dma_start(yd[t, :, :], ystg[:, :])
                    sh.s_yst.inc(styd, 16)
                    sh.ystg.release(Tok(sh.s_yst.h, sh.s_yst.v))
                else:
                    tok_mm2.wait(nc.vector)
                    s_x.wait(nc.vector)
                    if h_rms:
                        ystg = sh.ystg.acquire(nc.sync)
                        sh.s_yst.wait(nc.sync)
                        yd = sh.y_hbm.rearrange("(t p) f -> t p f", p=128)
                        s_yld = c.dsem("ffyld")
                        s_yld.inc(nc.sync.dma_start(ystg[:, :], yd[t, :, :]), 16)
                        s_yld.wait(nc.vector)
                        c.v(nc.vector.tensor_add(
                            ystg[:, :], pt[:, 0:D], ystg[:, :]))
                        tok_add = c.v(nc.vector.scalar_tensor_tensor(
                            xres[:, t * D:(t + 1) * D], ystg[:, :],
                            sh.sc2[:, t:t + 1], xres[:, t * D:(t + 1) * D],
                            ALU.mult, ALU.add))
                        sh.ystg.release(tok_add)
                    elif wcol is not None:
                        if s_wcol is not None:
                            s_wcol.wait(nc.vector)
                        tok_add = c.mark(nc.vector.scalar_tensor_tensor(
                            xres[:, t * D:(t + 1) * D], pt[:, 0:D],
                            wcol(t), xres[:, t * D:(t + 1) * D],
                            ALU.mult, ALU.add), "vector")
                    else:
                        tok_add = c.mark(nc.vector.tensor_add(
                            xres[:, t * D:(t + 1) * D],
                            xres[:, t * D:(t + 1) * D], pt[:, 0:D]), "vector")
                    sh.yps.release(tok_add)
            sh.hTr.release(tok_mm2)
        sh.tokA = tok_mm2
    return tok_add



def emit_ternarize(c, W_h, din, dout, Wq_hbm, after=()):
    nc = c.nc
    nki = din // 128
    CH = max(1, 4096 // dout)
    nblk = nki // CH
    wf = c.sbring(2, [128, CH * dout], F32)
    m1 = c.sbring(2, [128, CH * dout], BF16)
    m2 = c.sbring(2, [128, CH * dout], BF16)
    red = c.sb([128, 2])
    alps = c.ps([1, 1])
    albc = c.ps([128, 1])
    onef = c.sb([128, 1])
    onef2 = c.sb([1, 128])
    thr = c.sb([128, 2])
    c.v(nc.vector.memset(onef[:, 0:1], 1.0))
    s_onef = c.v(nc.vector.memset(onef2[0:1, :], 1.0))
    c.v(nc.vector.memset(red[:, 0:1], 0.0))
    src3 = W_h.rearrange("(blk q) o -> blk q o", q=128 * CH)
    src4 = W_h.rearrange("(blk ki p) o -> blk p ki o", p=128, ki=CH)
    for s_ in after:
        s_.wait(c.nc.gpsimd)
    s_ld = c.sem()
    toks = []
    for blk in range(nblk):
        w = wf.acquire(nc.gpsimd)
        s_ld.inc(nc.gpsimd.dma_start(
            w[:, :].rearrange("p (ki o) -> p ki o", ki=CH), src4[blk]), 16)
        s_ld.wait(nc.vector)
        t = c.v(nc.vector.reduce_sum(red[:, 1:2], w[:, :], axis=AX.X,
                                     apply_absolute_value=True))
        t = c.v(nc.vector.tensor_add(red[:, 0:1], red[:, 0:1], red[:, 1:2]))
        wf.release(t)
    t_red = t
    t_red.wait(nc.tensor)
    s_onef.wait(nc.tensor)
    t_al = c.mark(nc.tensor.matmul(alps[0:1, 0:1], onef[:, 0:1], red[:, 0:1],
                                   start=True, stop=True), "tensor")
    t_al.wait(nc.vector)
    t_cp = c.v(nc.vector.tensor_copy(thr[0:1, 0:1], alps[0:1, 0:1]))
    t_cp.wait(nc.tensor)
    t_bc = c.mark(nc.tensor.matmul(albc[:, 0:1], onef2[0:1, :], thr[0:1, 0:1],
                                   start=True, stop=True), "tensor")
    halfn = c.constcol(0.5 / (din * dout))
    neg1 = c.constcol(-1.0)
    t_bc.wait(nc.scalar)
    c.wait_const(nc.scalar)
    c.s(nc.scalar.activation(thr[:, 0:1], albc[:, 0:1], AF.Identity,
                             scale=halfn))
    t_thr = c.s(nc.scalar.activation(thr[:, 1:2], thr[:, 0:1], AF.Identity,
                                     scale=neg1))
    t_thr.wait(nc.vector)
    s_store = c.sem()
    dst = Wq_hbm.rearrange("(blk ki p) o -> blk p ki o", p=128, ki=CH)
    for blk in range(nblk):
        w = wf.acquire(nc.gpsimd)
        s_ld.inc(nc.gpsimd.dma_start(
            w[:, :].rearrange("p (ki o) -> p ki o", ki=CH), src4[blk]), 16)
        s_ld.wait(nc.vector)
        a = m1.acquire(nc.vector)
        c.v(nc.vector.tensor_single_scalar(a[:, :], w[:, :], thr[:, 0:1],
                                           ALU.is_gt))
        b = m2.acquire(nc.vector)
        c.v(nc.vector.tensor_single_scalar(b[:, :], w[:, :], thr[:, 1:2],
                                           ALU.is_lt))
        m2.release(c.tip("vector"))
        t_q = c.v(nc.vector.tensor_sub(a[:, :], a[:, :], b[:, :]))
        wf.release(t_q)
        t_q.wait(nc.sync)
        st = nc.sync.dma_start(
            dst[blk], a[:, :].rearrange("p (ki o) -> p ki o", ki=CH))
        s_store.inc(st, 16)
        m1.release(Tok(s_store.h, s_store.v))
    return s_store



def col_layout(v):
    n = v.shape[0]
    return np.ascontiguousarray(v.reshape(n // 128, 128).T).astype(np.float32)


def build_l2ffn_program():
    nc = bass.Bass("TRN2", target_bir_lowering=False)
    stack = ExitStack()
    c = Ctx(nc, stack)
    x_h = nc.dram_tensor("x", [T, D], F32, kind="ExternalInput").ap()
    W1_h = nc.dram_tensor("W1", [D, 4096], F32, kind="ExternalInput").ap()
    b1c_h = nc.dram_tensor("b1c", [128, 32], F32, kind="ExternalInput").ap()
    W2_h = nc.dram_tensor("W2", [4096, D], F32, kind="ExternalInput").ap()
    b2_h = nc.dram_tensor("b2", [D], F32, kind="ExternalInput").ap()
    out_h = nc.dram_tensor("out", [T, D], F32, kind="ExternalOutput").ap()
    with stack:
        xres = c.sb([128, NT * D], F32)
        s_x = emit_load_x(c, x_h, xres)
        ln = emit_ln_transpose(c, xres, s_x)
        b2row, s_b2 = emit_load_bf16_row(c, b2_h, D)
        b1col, s_b1 = emit_load_f32(c, b1c_h, 128, 32)
        sh = FfnShared(c, 4096)
        tok = emit_ffn_expert(c, xres, s_x, ln, sh, W1_h, W2_h, act=AF.Relu,
                              b1col=b1col, s_b1=s_b1, b2row=b2row, s_b2=s_b2)
        emit_store_x(c, xres, out_h, [tok])
    return nc


def build_l1ffn_program():
    nc = bass.Bass("TRN2", target_bir_lowering=False)
    stack = ExitStack()
    c = Ctx(nc, stack)
    x_h = nc.dram_tensor("x", [T, D], F32, kind="ExternalInput").ap()
    Wu_h = nc.dram_tensor("Wu", [D, 4096], F32, kind="ExternalInput").ap()
    Wd_h = nc.dram_tensor("Wd", [4096, D], F32, kind="ExternalInput").ap()
    out_h = nc.dram_tensor("out", [T, D], F32, kind="ExternalOutput").ap()
    with stack:
        Wq1 = c.dram([D, 4096], BF16)
        Wq2 = c.dram([4096, D], BF16)
        with c.sub() as ct:
            s_t1 = emit_ternarize(ct, Wu_h, D, 4096, Wq1)
        with c.sub() as ct:
            s_t2 = emit_ternarize(ct, Wd_h, 4096, D, Wq2, after=[s_t1])
        xres = c.sb([128, NT * D], F32)
        s_x = emit_load_x(c, x_h, xres, after=[s_t1, s_t2])
        ln = emit_ln_transpose(c, xres, s_x, rms_compose=True)
        sh = FfnShared(c, 4096, h_rms=True, tb=256)
        s_t1.wait(nc.gpsimd)
        s_t2.wait(nc.gpsimd)
        tok = emit_ffn_expert(c, xres, s_x, ln, sh, Wq1, Wq2, act=AF.Gelu,
                              h_rms=True, w_bf16=True)
        emit_store_x(c, xres, out_h, [tok])
    return nc



def emit_l3_gating(c, ln, Wgt_h, bgt_h):
    nc = c.nc
    w = emit_load_weight(c, Wgt_h, D, 9)
    bgt, s_b = emit_load_bf16_row(c, bgt_h, 9)
    ones, t_ones = emit_ones(c)
    wgts = c.sb([128, NT * 8])
    gps = c.ring(2, [128, 9])
    mx = c.sb([128, 4])
    gat = c.sb([128, 8])
    tok = None
    for t in range(NT):
        pt = gps.acquire(nc.tensor)
        w.sem.wait(nc.tensor)
        ln.sem.wait(nc.tensor)
        s_b.wait(nc.tensor)
        t_ones.wait(nc.tensor)
        pairs = [(ln.lnT[:, ki * T + t * 128: ki * T + (t + 1) * 128],
                  w.lhsT(ki, 0, 9)) for ki in range(ND)]
        pairs.append((ones[0:1, 0:128], bgt[0:1, 0:9]))
        mm = mm_group(nc, pt[:, 0:9], pairs)
        t_mm = c.mark(mm, "tensor")
        t_mm.wait(nc.vector)
        t_mx = c.v(nc.vector.reduce_max(mx[:, 0:1], pt[:, 0:8], axis=AX.X,
                                        negate=True))
        t_mx.wait(nc.scalar)
        t_e8 = c.s(nc.scalar.activation(gat[:, 0:8], pt[:, 0:8], AF.Exp,
                                        bias=mx[:, 0:1],
                                        accum_out=mx[:, 1:2]))
        t_th = c.s(nc.scalar.activation(mx[:, 2:3], pt[:, 8:9], AF.Sigmoid))
        gps.release(t_th)
        t_e8.wait(nc.vector)
        t_th.wait(nc.vector)
        c.v(nc.vector.reciprocal(mx[:, 1:2], mx[:, 1:2]))
        c.v(nc.vector.tensor_scalar_mul(gat[:, 0:8], gat[:, 0:8], mx[:, 1:2]))
        c.v(nc.vector.tensor_scalar_mul(mx[:, 2:3], mx[:, 2:3], 0.125))
        c.v(nc.vector.tensor_scalar(gat[:, 0:8], gat[:, 0:8], mx[:, 2:3], 0.0,
                                    ALU.subtract, ALU.max))
        c.v(nc.vector.reduce_sum(mx[:, 3:4], gat[:, 0:8], axis=AX.X))
        c.v(nc.vector.tensor_single_scalar(mx[:, 1:2], mx[:, 3:4], 0.0,
                                           ALU.is_equal))
        c.v(nc.vector.tensor_add(mx[:, 3:4], mx[:, 3:4], mx[:, 1:2]))
        c.v(nc.vector.reciprocal(mx[:, 3:4], mx[:, 3:4]))
        tok = c.v(nc.vector.tensor_scalar_mul(
            wgts[:, t * 8:(t + 1) * 8], gat[:, 0:8], mx[:, 3:4]))
    return wgts, tok


def build_l3ffn_program():
    nc = bass.Bass("TRN2", target_bir_lowering=False)
    stack = ExitStack()
    c = Ctx(nc, stack)
    x_h = nc.dram_tensor("x", [T, D], F32, kind="ExternalInput").ap()
    Wgt_h = nc.dram_tensor("Wgt", [D, 9], F32, kind="ExternalInput").ap()
    bgt_h = nc.dram_tensor("bgt", [9], F32, kind="ExternalInput").ap()
    W1_h = nc.dram_tensor("W1", [8, D, 4096], F32, kind="ExternalInput").ap()
    b1c_h = nc.dram_tensor("b1c", [128, 8 * 32], F32, kind="ExternalInput").ap()
    W2_h = nc.dram_tensor("W2", [8, 4096, D], F32, kind="ExternalInput").ap()
    b2r_h = nc.dram_tensor("b2r", [8 * D], F32, kind="ExternalInput").ap()
    out_h = nc.dram_tensor("out", [T, D], F32, kind="ExternalOutput").ap()
    with stack:
        xres = c.sb([128, NT * D], F32)
        s_x = emit_load_x(c, x_h, xres)
        ln = emit_ln_transpose(c, xres, s_x)
        wgts, t_w = emit_l3_gating(c, ln, Wgt_h, bgt_h)
        b1col, s_b1 = emit_load_f32(c, b1c_h, 128, 8 * 32)
        b2stg = Ring([c.sb([1, D], BF16) for _ in range(2)])
        sh = FfnShared(c, 4096)
        tok = None
        for e in range(8):
            b2t = b2stg.acquire(nc.gpsimd)
            s_b2 = c.dsem("l3b2")
            s_b2.inc(nc.gpsimd.dma_start(b2t[0:1, :],
                                         b2r_h[e * D:(e + 1) * D][None, :]), 16)
            tok = emit_ffn_expert(
                c, xres, s_x, ln, sh, W1_h[e], W2_h[e],
                act=AF.Gelu_apprx_tanh,
                b1col=b1col[:, e * 32:(e + 1) * 32], s_b1=s_b1,
                b2row=b2t[0:1, :], s_b2=s_b2,
                wcol=lambda t, e=e: wgts[:, t * 8 + e: t * 8 + e + 1],
                s_wcol=None)
            b2stg.release(sh.tokA)
        emit_store_x(c, xres, out_h, [tok])
    return nc



def emit_l0_router(c, ln, Wr_h, br_h):
    nc = c.nc
    w = emit_load_weight(c, Wr_h, D, 16)
    br, s_b = emit_load_bf16_row(c, br_h, 16)
    ones, t_ones = emit_ones(c)
    comb = c.sb([128, NT * 16])
    rps = c.ring(2, [128, 16])
    mx = c.sb([128, 6])
    prb = c.sb([128, 16])
    msk = c.sb([128, 16])
    tok = None
    for t in range(NT):
        pt = rps.acquire(nc.tensor)
        w.sem.wait(nc.tensor)
        ln.sem.wait(nc.tensor)
        s_b.wait(nc.tensor)
        t_ones.wait(nc.tensor)
        pairs = [(ln.lnT[:, ki * T + t * 128: ki * T + (t + 1) * 128],
                  w.lhsT(ki, 0, 16)) for ki in range(ND)]
        pairs.append((ones[0:1, 0:128], br[0:1, 0:16]))
        t_mm = c.mark(mm_group(nc, pt[:, 0:16], pairs), "tensor")
        t_mm.wait(nc.vector)
        t_mx = c.v(nc.vector.reduce_max(mx[:, 0:1], pt[:, 0:16], axis=AX.X,
                                        negate=True))
        t_mx.wait(nc.scalar)
        t_e = c.s(nc.scalar.activation(prb[:, 0:16], pt[:, 0:16], AF.Exp,
                                       bias=mx[:, 0:1], accum_out=mx[:, 1:2]))
        rps.release(t_e)
        t_e.wait(nc.vector)
        c.v(nc.vector.reciprocal(mx[:, 1:2], mx[:, 1:2]))
        c.v(nc.vector.tensor_scalar_mul(prb[:, 0:16], prb[:, 0:16],
                                        mx[:, 1:2]))
        c.v(nc.vector.reduce_max(mx[:, 2:3], prb[:, 0:16], axis=AX.X))
        c.v(nc.vector.tensor_single_scalar(msk[:, 0:16], prb[:, 0:16],
                                           mx[:, 2:3], ALU.is_ge))
        c.v(nc.vector.scalar_tensor_tensor(msk[:, 0:16], msk[:, 0:16], -2.0,
                                           prb[:, 0:16], ALU.mult, ALU.add))
        c.v(nc.vector.reduce_max(mx[:, 3:4], msk[:, 0:16], axis=AX.X))
        c.v(nc.vector.tensor_single_scalar(msk[:, 0:16], prb[:, 0:16],
                                           mx[:, 3:4], ALU.is_ge))
        c.v(nc.vector.tensor_add(mx[:, 4:5], mx[:, 2:3], mx[:, 3:4]))
        c.v(nc.vector.tensor_scalar_add(mx[:, 4:5], mx[:, 4:5], 1e-6))
        c.v(nc.vector.reciprocal(mx[:, 4:5], mx[:, 4:5]))
        c.v(nc.vector.tensor_mul(msk[:, 0:16], msk[:, 0:16], prb[:, 0:16]))
        tok = c.v(nc.vector.tensor_scalar_mul(
            comb[:, t * 16:(t + 1) * 16], msk[:, 0:16], mx[:, 4:5]))
    return comb, tok


def emit_l0_gated(c, xres, s_x, ln, Wg_sl, Wu_sl, Wd_sl, *, conv_taps=None,
                  wcol=None, tiles=None, I=512):
    nc = c.nc
    KH = I // 128
    TB = 512
    NTB = T // TB
    wgt, wut, wdt, tok_prev = tiles
    if tok_prev is not None:
        tok_prev.wait(nc.gpsimd)
    if conv_taps is not None:
        ws = [emit_load_weight(c, W_h, D, I, tile=tl)
              for (W_h, sh), tl in zip(conv_taps, (wgt, wut, wdt))]
        wd = emit_load_weight(c, Wd_sl, I, D, tile=c.l0_wd2)
        wu = emit_load_weight(c, Wu_sl, D, I, tile=c.l0_wu2)
    else:
        wg = emit_load_weight(c, Wg_sl, D, I, tile=wgt)
        wu = emit_load_weight(c, Wu_sl, D, I, tile=wut)
        wd = emit_load_weight(c, Wd_sl, I, D, tile=wdt)
    hT = c.l0_hT
    gs = c.l0_gscr
    tok_mm2 = None
    for tb in range(NTB):
        t0 = tb * TB
        if c.l0_htok is not None:
            c.l0_htok.wait(nc.vector)
        ln.sem.wait(nc.tensor)
        wu.sem.wait(nc.tensor)
        wd.sem.wait(nc.tensor)
        tok_ev = None
        for ko in range(KH):
            pg = c.l0_gps.acquire(nc.tensor)
            if conv_taps is not None:
                for wi in ws:
                    wi.sem.wait(nc.tensor)
                order = sorted(range(3), key=lambda k: conv_taps[k][1] != 0)
                triples = []
                for k in order:
                    shf = conv_taps[k][1]
                    lo = max(t0 + shf, 0)
                    hi = t0 + TB + shf
                    off = lo - (t0 + shf)
                    for ki in range(ND):
                        triples.append((pg[:, off:TB],
                                        ws[k].lhsT(ki, ko * 128,
                                                   (ko + 1) * 128),
                                        ln.lnT[:, ki * T + lo: ki * T + hi]))
                for j, (o_, l_, r_) in enumerate(triples):
                    mm = nc.tensor.matmul(o_, l_, r_, start=(j == 0),
                                          stop=(j == len(triples) - 1))
                t_g = c.mark(mm, "tensor")
            else:
                wg.sem.wait(nc.tensor)
                t_g = c.mark(mm_group(nc, pg[:, 0:TB],
                             [(wg.lhsT(ki, ko * 128, (ko + 1) * 128),
                               ln.kT(ki, t0, t0 + TB)) for ki in range(ND)]),
                             "tensor")
            pu = c.l0_ups.acquire(nc.tensor)
            t_u = c.mark(mm_group(nc, pu[:, 0:TB],
                         [(wu.lhsT(ki, ko * 128, (ko + 1) * 128),
                           ln.kT(ki, t0, t0 + TB)) for ki in range(ND)]),
                         "tensor")
            t_g.wait(nc.scalar)
            gt = gs.acquire(nc.scalar)
            t_sg = c.s(nc.scalar.activation(gt[:, 0:TB], pg[:, 0:TB], AF.Silu))
            c.l0_gps.release(t_sg)
            t_u.wait(nc.vector)
            t_sg.wait(nc.vector)
            tok_ev = c.v(nc.vector.tensor_mul(
                hT[:, ko * TB:(ko + 1) * TB], gt[:, 0:TB], pu[:, 0:TB]))
            gs.release(tok_ev)
            c.l0_ups.release(tok_ev)
        tok_ev.wait(nc.tensor)
        for st in range(TB // 128):
            t = (t0 + st * 128) // 128
            pt = c.l0_yps.acquire(nc.tensor)
            for b in range(2):
                o0, o1 = b * 512, (b + 1) * 512
                pairs = [(hT[:, ko * TB + st * 128: ko * TB + (st + 1) * 128],
                          wd.lhsT(ko, o0, o1)) for ko in range(KH)]
                mm = mm_group(nc, pt[:, o0:o1], pairs)
                tok_mm2 = c.mark(mm, "tensor")
            tok_mm2.wait(nc.vector)
            s_x.wait(nc.vector)
            if wcol is not None:
                tok_add = c.v(nc.vector.scalar_tensor_tensor(
                    xres[:, t * D:(t + 1) * D], pt[:, 0:D], wcol(t),
                    xres[:, t * D:(t + 1) * D], ALU.mult, ALU.add))
            else:
                tok_add = c.v(nc.vector.tensor_add(
                    xres[:, t * D:(t + 1) * D], xres[:, t * D:(t + 1) * D],
                    pt[:, 0:D]))
            c.l0_yps.release(tok_add)
        c.l0_htok = tok_mm2
    return tok_mm2, tok_add


def emit_l0_ffn(c, xres, s_x, ln, io):
    nc = c.nc
    comb, t_comb = emit_l0_router(c, ln, io["router_w"], io["router_bias"])
    I = 512
    KH = I // 128
    TB = 512
    c.l0_hT = c.sb([128, KH * TB], BF16)
    c.l0_gscr = Ring([c.sb([128, TB], BF16) for _ in range(2)])
    c.l0_gps = c.ring(2, [128, TB])
    c.l0_ups = c.ring(2, [128, TB])
    c.l0_yps = c.ring(1, [128, D])
    c.l0_htok = None
    wgt = c.sb([128, ND * I], BF16)
    wut = c.sb([128, ND * I], BF16)
    wdt = c.sb([128, KH * D], BF16)
    c.l0_wd2 = c.sb([128, KH * D], BF16)
    c.l0_wu2 = c.sb([128, ND * I], BF16)
    tok_prev = None
    tok_add = None
    for e in range(15):
        tok_prev, tok_add = emit_l0_gated(
            c, xres, s_x, ln, io["gate_w"][e], io["up_w"][e], io["down_w"][e],
            wcol=lambda t, e=e: comb[:, t * 16 + e: t * 16 + e + 1],
            tiles=(wgt, wut, wdt, tok_prev))
    lnscr = c.sb([128, D], BF16)
    for t in range(NT):
        ln.t_affine.wait(nc.scalar)
        t_l = c.s(nc.scalar.activation(
            lnscr[:, :], xres[:, t * D:(t + 1) * D], AF.Identity,
            scale=ln.rstd[:, t:t + 1], bias=ln.nmr[:, t:t + 1]))
        t_l.wait(nc.vector)
        tok_add = c.v(nc.vector.scalar_tensor_tensor(
            xres[:, t * D:(t + 1) * D], lnscr[:, :],
            comb[:, t * 16 + 15: t * 16 + 16],
            xres[:, t * D:(t + 1) * D], ALU.mult, ALU.add))
        tok_add.wait(nc.scalar)
    tok_prev, tok_add = emit_l0_gated(
        c, xres, s_x, ln, None, io["shared_up"], io["shared_down"],
        conv_taps=[(io["conv_w0"], -2), (io["conv_w1"], -1),
                   (io["conv_w2"], 0)],
        wcol=None, tiles=(wgt, wut, wdt, tok_prev))
    return tok_add



def emit_bounceT(c, src_sb, src_toks, dout, outT=None, name=None):
    nc = c.nc
    hbm = c.dram([T, dout], BF16)
    nko = dout // 128
    outT = outT if outT is not None else c.sb([128, nko * T], BF16)
    s_st = c.sem()
    s_tp = c.sem()
    dst3 = hbm.rearrange("(t p) f -> t p f", p=128)
    src3 = src_sb[:, :].rearrange("p (t f) -> p t f", t=NT)
    toks = src_toks if isinstance(src_toks, list) else [src_toks] * NT
    for t in range(NT):
        toks[t].wait(nc.sync)
        s_st.inc(nc.sync.dma_start(dst3[t, :, :], src3[:, t, :]), 16)
    s_st.wait(nc.sync)
    for ko in range(nko):
        s_tp.inc(nc.sync.dma_start_transpose(
            outT[:, ko * T:(ko + 1) * T], hbm[:, ko * 128:(ko + 1) * 128]), 16)
    return outT, s_tp



class MhaShared:
    def __init__(self, c, H, dh, dv_total, after=()):
        self.H, self.dh = H, dh
        self.dvt = dv_total
        for s_ in after:
            s_.wait(c.nc.gpsimd)
            s_.wait(c.nc.vector)
        self.sps = c.ring(1, [128, T])
        self.tps = c.ring(1, [128, 128], BF16)
        self.ops_ = c.ring(1, [128, dv_total])
        self.P = c.sb([128, T], BF16)
        self.PT = c.sb([128, T], BF16)
        self.mxl = c.sb([128, 4])
        self.ident = c.sb([128, 128], BF16)
        self.iscr = c.sb([128, 128])
        t_i = c.g(c.nc.gpsimd.iota(self.iscr[:, :], pattern=[[1, 128]], base=0,
                                   channel_multiplier=-1,
                                   allow_small_or_imprecise_dtypes=True))
        t_i.wait(c.nc.vector)
        self.t_ident = c.v(c.nc.vector.tensor_single_scalar(
            self.ident[:, :], self.iscr[:, :], 0.0, ALU.is_equal))
        self.maskU = c.sb([128, 128])
        t_m = c.v(c.nc.vector.tensor_single_scalar(
            self.maskU[:, :], self.iscr[:, :], 0.0, ALU.is_gt))
        self.t_maskU = c.v(c.nc.vector.tensor_scalar_mul(
            self.maskU[:, :], self.maskU[:, :], -1e30))


def emit_mha(c, sh, qT, kT, v_tok, out_tok, ti, s_qkv, *, causal,
             pre_hook=None, scale=1.0, scale_ap=None, q_tile_mode=False,
             pieces_fn=None):
    nc = c.nc
    H, dh, dvt = sh.H, sh.dh, sh.dvt
    klen = (ti + 1) * 128 if causal else T
    ntk = klen // 128
    op = sh.ops_.acquire(nc.tensor)
    prev_ttp = None
    prev_av = None
    t_av = None
    for h in range(H):
        sp = sh.sps.acquire(nc.tensor)
        s_qkv.wait(nc.tensor)
        if pieces_fn is not None:
            pieces = pieces_fn(h)
        else:
            step = min(dh, 128)
            pieces = [((h * dh + r) // 128, (h * dh + r) % 128, step)
                      for r in range(0, dh, step)]
        for n0 in range(0, klen, 512):
            n1 = min(n0 + 512, klen)
            for j, (ki, base, nr) in enumerate(pieces):
                rows = slice(base, base + nr)
                if q_tile_mode:
                    q_ap = qT[:, ki * 128:(ki + 1) * 128][rows, :]
                else:
                    q_ap = qT[:, ki * T + ti * 128:
                              ki * T + (ti + 1) * 128][rows, :]
                kw2 = {}
                if pieces_fn is not None:
                    kw2["tile_position"] = (base, 0)
                mm = nc.tensor.matmul(
                    sp[:, n0:n1], q_ap,
                    kT[:, ki * T + n0: ki * T + n1][rows, :],
                    start=(j == 0), stop=(j == len(pieces) - 1), **kw2)
        t_s = c.mark(mm, "tensor")
        t_pre = None
        if pre_hook is not None:
            t_pre = pre_hook(h, ti, klen, sp, t_s)
        if causal:
            sh.t_maskU.wait(nc.vector)
            if t_pre is None:
                t_s.wait(nc.vector)
            t_pre = c.v(nc.vector.tensor_add(
                sp[:, klen - 128:klen], sp[:, klen - 128:klen],
                sh.maskU[:, :]))
        (t_pre or t_s).wait(nc.vector)
        t_mx = c.v(nc.vector.reduce_max(sh.mxl[:, 0:1], sp[:, 0:klen],
                                        axis=AX.X, negate=True))
        if scale_ap is not None:
            t_mx = c.v(nc.vector.tensor_scalar_mul(sh.mxl[:, 1:2],
                                                   sh.mxl[:, 0:1], scale_ap))
            bias_ap = sh.mxl[:, 1:2]
        elif scale != 1.0:
            t_mx = c.v(nc.vector.tensor_scalar_mul(sh.mxl[:, 1:2],
                                                   sh.mxl[:, 0:1], scale))
            bias_ap = sh.mxl[:, 1:2]
        else:
            bias_ap = sh.mxl[:, 0:1]
        t_mx.wait(nc.scalar)
        if prev_ttp is not None:
            prev_ttp.wait(nc.scalar)
        kw = {}
        if scale_ap is not None:
            kw["scale"] = scale_ap
        elif scale != 1.0:
            c.wait_const(nc.scalar)
            kw["scale"] = c.constcol(scale)
        t_exp = c.s(nc.scalar.activation(
            sh.P[:, 0:klen], sp[:, 0:klen], AF.Exp, bias=bias_ap,
            accum_out=sh.mxl[:, 2:3], **kw))
        sh.sps.release(t_exp)
        t_exp.wait(nc.vector)
        c.v(nc.vector.reciprocal(sh.mxl[:, 3:4], sh.mxl[:, 2:3]))
        t_pl = c.v(nc.vector.tensor_scalar_mul(
            sh.P[:, 0:klen], sh.P[:, 0:klen], sh.mxl[:, 3:4]))
        if getattr(c, "dbg_dump_P", None) is not None and h == 0 and ti == 0:
            dst_h, s_d = c.dbg_dump_P
            t_pl.wait(nc.gpsimd)
            s_d.inc(nc.gpsimd.dma_start(dst_h[0:128, 0:klen], sh.P[:, 0:klen]),
                    16)
        t_ptc = None
        for tk in range(ntk):
            tp = sh.tps.acquire(nc.tensor)
            t_pl.wait(nc.tensor)
            sh.t_ident.wait(nc.tensor)
            t_tp = c.mark(nc.tensor.transpose(
                tp[:, :], sh.P[:, tk * 128:(tk + 1) * 128], sh.ident[:, :]),
                "tensor")
            t_tp.wait(nc.scalar)
            if prev_av is not None:
                prev_av.wait(nc.scalar)
                prev_av = None
            t_ptc = c.s(nc.scalar.copy(sh.PT[:, tk * 128:(tk + 1) * 128],
                                       tp[:, :]))
            sh.tps.release(t_ptc)
        prev_ttp = t_tp
        t_ptc.wait(nc.tensor)
        if getattr(c, "dbg_dump_PT", None) is not None and h == 0 and ti == 0:
            dst_h, s_d = c.dbg_dump_PT
            t_ptc.wait(nc.gpsimd)
            s_d.inc(nc.gpsimd.dma_start(dst_h[0:128, 0:T], sh.PT[:, 0:T]), 16)
        for tk in range(ntk):
            for b0 in range(0, dh, 512):
                b1 = min(b0 + 512, dh)
                mm = nc.tensor.matmul(
                    op[:, h * dh + b0: h * dh + b1],
                    sh.PT[:, tk * 128:(tk + 1) * 128],
                    v_tok[:, tk * dvt + h * dh + b0: tk * dvt + h * dh + b1],
                    start=(tk == 0), stop=(tk == ntk - 1))
        t_av = c.mark(mm, "tensor")
        prev_av = t_av
        if getattr(c, "dbg_dump_OP", None) is not None and ti == 0 and \
                h == c.dbg_dump_OP[2]:
            dst_h, s_d, _hh = c.dbg_dump_OP
            scr_d = c.dbg_dump_scr
            t_av.wait(nc.vector)
            t_cpd = c.v(nc.vector.tensor_copy(scr_d[:, 0:dvt], op[:, 0:dvt]))
            t_cpd.wait(nc.gpsimd)
            s_d.inc(nc.gpsimd.dma_start(dst_h[0:128, 0:dvt], scr_d[:, 0:dvt]),
                    16)
    t_av.wait(nc.vector)
    if out_tok is None:
        t_out = sh.out_cb(op)
    else:
        t_out = c.v(nc.vector.tensor_copy(out_tok, op[:, 0:dvt]))
    sh.ops_.release(t_out)
    return t_out


def ceil_div(a, b):
    return (a + b - 1) // b



def build_l1attnb_program():
    nc = bass.Bass("TRN2", target_bir_lowering=False)
    stack = ExitStack()
    c = Ctx(nc, stack)
    x_h = nc.dram_tensor("x", [T, D], F32, kind="ExternalInput").ap()
    W_h = nc.dram_tensor("W", [D, 2 * D], F32, kind="ExternalInput").ap()
    out_h = nc.dram_tensor("out", [T, D], F32, kind="ExternalOutput").ap()
    with stack:
        xres = c.sb([128, NT * D], F32)
        s_x = emit_load_x(c, x_h, xres)
        ln = emit_ln_transpose(c, xres, s_x)
        tok = emit_l1_attn_b(c, xres, s_x, ln, W_h)
        emit_store_x(c, xres, out_h, [tok])
    return nc


def emit_l1_attn_b(c, xres, s_x, ln, W_h):
    nc = c.nc
    kT = c.sb([128, ND * T], BF16)
    v_tok = c.sb([128, NT * D], BF16)
    sub = c.sub()
    cs = sub.__enter__()
    wk = emit_load_weight(cs, W_h[:, 0:D], D, D)
    wv = emit_load_weight(cs, W_h[:, D:2 * D], D, D)
    kps = cs.ring(2, [128, 512])
    t_ev = None
    for ko in range(ND):
        for n0 in range(0, T, 512):
            pt = kps.acquire(nc.tensor)
            wk.sem.wait(nc.tensor)
            ln.sem.wait(nc.tensor)
            t_mm = c.mark(mm_group(nc, pt[:, 0:512],
                          [(wk.lhsT(ki, ko * 128, (ko + 1) * 128),
                            ln.kT(ki, n0, n0 + 512)) for ki in range(ND)]),
                          "tensor")
            t_mm.wait(nc.scalar)
            t_ev = c.s(nc.scalar.copy(kT[:, ko * T + n0: ko * T + n0 + 512],
                                      pt[:, 0:512]))
            kps.release(t_ev)
    t_kT = t_ev
    vps = cs.ring(2, [128, D])
    for t in range(NT):
        pt = vps.acquire(nc.tensor)
        wv.sem.wait(nc.tensor)
        ln.sem.wait(nc.tensor)
        for b in range(2):
            o0, o1 = b * 512, (b + 1) * 512
            mm = mm_group(nc, pt[:, o0:o1],
                          [(ln.lnT[:, ki * T + t * 128: ki * T + (t + 1) * 128],
                            wv.lhsT(ki, o0, o1)) for ki in range(ND)])
        t_mm = c.mark(mm, "tensor")
        t_mm.wait(nc.scalar)
        t_ev = c.s(nc.scalar.copy(v_tok[:, t * D:(t + 1) * D], pt[:, 0:D]))
        vps.release(t_ev)
    t_vt = t_ev
    sub.__exit__(None, None, None)
    sh = MhaShared(c, 1, D, D, after=[t_kT, t_vt])
    scr = c.sb([128, D])
    tok_add = None

    for ti in range(NT):
        def out_cb(op, ti=ti):
            t = ti
            s_x.wait(nc.vector)
            ln.t_affine.wait(nc.vector)
            c.v(nc.vector.scalar_tensor_tensor(
                scr[:, :], xres[:, t * D:(t + 1) * D], ln.rstd[:, t:t + 1],
                op[:, 0:D], ALU.mult, ALU.add))
            c.v(nc.vector.tensor_scalar_add(scr[:, :], scr[:, :],
                                            ln.nmr[:, t:t + 1]))
            return c.v(nc.vector.tensor_add(
                xres[:, t * D:(t + 1) * D], xres[:, t * D:(t + 1) * D],
                scr[:, :]))
        sh.out_cb = out_cb
        t_kT.wait(nc.tensor)
        t_vt.wait(nc.tensor)
        tok_add = emit_mha(c, sh, ln.lnT, kT, v_tok, None, ti, s_x,
                           causal=True)
    return tok_add



def emit_proj_fm(c, rhsT, s_rhs, w, outT, *, bias_col=None, s_bias=None,
                 act=None, psr=None):
    nc = c.nc
    psr = psr or c.ring(2, [128, 512])
    nko = w.dout // 128
    t_ev = None
    for ko in range(nko):
        for n0 in range(0, T, 512):
            pt = psr.acquire(nc.tensor)
            w.sem.wait(nc.tensor)
            s_rhs.wait(nc.tensor)
            t_mm = c.mark(mm_group(nc, pt[:, 0:512],
                          [(w.lhsT(ki, ko * 128, (ko + 1) * 128),
                            rhsT[:, ki * T + n0: ki * T + n0 + 512])
                           for ki in range(w.nki)]), "tensor")
            t_mm.wait(nc.scalar)
            if s_bias is not None:
                s_bias.wait(nc.scalar)
            kw = {}
            if bias_col is not None:
                kw["bias"] = bias_col[:, ko:ko + 1]
            t_ev = c.s(nc.scalar.activation(
                outT[:, ko * T + n0: ko * T + n0 + 512], pt[:, 0:512],
                act or AF.Identity, **kw))
            psr.release(t_ev)
    return t_ev


def emit_proj_tm(c, lhsT_fn, s_lhs, w, out_tok, *, bias_row=None, s_bias=None,
                 ones=None, psr=None, act=None):
    nc = c.nc
    dout = w.dout
    psr = psr or c.ring(2, [128, dout])
    t_ev = None
    for t in range(NT):
        pt = psr.acquire(nc.tensor)
        w.sem.wait(nc.tensor)
        s_lhs.wait(nc.tensor)
        if s_bias is not None:
            s_bias.wait(nc.tensor)
        for o0 in range(0, dout, 512):
            o1 = min(o0 + 512, dout)
            pairs = [(lhsT_fn(ki, t), w.lhsT(ki, o0, o1))
                     for ki in range(w.nki)]
            if bias_row is not None:
                pairs.append((ones[0:1, 0:128], bias_row[0:1, o0:o1]))
            mm = mm_group(nc, pt[:, o0:o1], pairs)
        t_mm = c.mark(mm, "tensor")
        t_mm.wait(nc.scalar)
        t_ev = c.s(nc.scalar.activation(out_tok[:, t * dout:(t + 1) * dout],
                                        pt[:, 0:dout], act or AF.Identity))
        psr.release(t_ev)
    return t_ev



class DStream:

    def __init__(self, c, dout, hbm=None):
        self.c = c
        self.dout = dout
        self.nki = dout // 128
        self.hbm = hbm if hbm is not None else c.dram([T, dout], BF16)
        self.s_store = c.sem()
        self.h3 = self.hbm.rearrange("(t p) f -> t p f", p=128)

    def put(self, t, tok, src_ap):
        tok.wait(self.c.nc.sync)
        self.s_store.inc(self.c.nc.sync.dma_start(self.h3[t, :, :], src_ap),
                         16)

    def load_fm(self, dst, n0, n1, s_out, full=True):
        nc = self.c.nc
        w = n1 - n0
        if full:
            self.s_store.wait(nc.sync)
        for ki in range(self.nki):
            s_out.inc(nc.sync.dma_start_transpose(
                dst[:, ki * w:(ki + 1) * w],
                self.hbm[n0:n1, ki * 128:(ki + 1) * 128]), 16)
        return s_out


def emit_ln_stream(c, x_hbm, *, pe_fn=None, stats_out=None):
    nc = c.nc
    x3 = x_hbm.rearrange("(t p) f -> t p f", p=128)
    ssum = c.sb([128, NT])
    ssq = c.sb([128, NT])
    mcol = c.sb([128, NT])
    msq = c.sb([128, NT])
    rstd = c.sb([128, NT])
    nmr = c.sb([128, NT])
    sq_scr = c.sb([128, D])
    xr = c.sbring(3, [128, D], F32)
    stg = c.sbring(2, [128, D], BF16)
    ds = DStream(c, D)
    t_red = t_sq = None
    xtoks = []
    for t in range(NT):
        xt = xr.acquire(nc.sync)
        s_ld = c.dsem("lnx")
        s_ld.inc(nc.sync.dma_start(xt[:, :], x3[t, :, :]), 16)
        s_ld.wait(nc.vector)
        t_red = c.v(nc.vector.reduce_sum(ssum[:, t:t + 1], xt[:, :],
                                         axis=AX.X))
        s_ld.wait(nc.scalar)
        t_sq = c.s(nc.scalar.activation(sq_scr[:, :], xt[:, :], AF.Square,
                                        accum_out=ssq[:, t:t + 1]))
        xr.release(Multi([t_red, t_sq]))
    t_sq.wait(nc.vector)
    c.v(nc.vector.tensor_scalar_mul(mcol[:, :], ssum[:, :], 1.0 / D))
    t_m = c.v(nc.vector.tensor_scalar_mul(ssq[:, :], ssq[:, :], 1.0 / D))
    t_m.wait(nc.scalar)
    t_msq = c.s(nc.scalar.activation(msq[:, :], mcol[:, :], AF.Square))
    t_msq.wait(nc.vector)
    t_var = c.v(nc.vector.tensor_sub(ssq[:, :], ssq[:, :], msq[:, :]))
    epsc = c.constcol(1e-5)
    halfneg = c.constcol(-0.5)
    t_var.wait(nc.scalar)
    c.wait_const(nc.scalar)
    c.s(nc.scalar.activation(rstd[:, :], ssq[:, :], AF.Ln, bias=epsc))
    t_rstd = c.s(nc.scalar.activation(rstd[:, :], rstd[:, :], AF.Exp,
                                      scale=halfneg))
    t_rstd.wait(nc.vector)
    t_cols = c.v(nc.vector.scalar_tensor_tensor(
        nmr[:, :], mcol[:, :], -1.0, rstd[:, :], ALU.mult, ALU.mult))
    for t in range(NT):
        xt = xr.acquire(nc.sync)
        s_ld = c.dsem("lnx")
        s_ld.inc(nc.sync.dma_start(xt[:, :], x3[t, :, :]), 16)
        st = stg.acquire(nc.scalar if pe_fn is None else nc.vector)
        if pe_fn is None:
            s_ld.wait(nc.scalar)
            t_cols.wait(nc.scalar)
            tok = c.s(nc.scalar.activation(
                st[:, :], xt[:, :], AF.Identity, scale=rstd[:, t:t + 1],
                bias=nmr[:, t:t + 1]))
        else:
            t_pe = pe_fn(t, sq_scr[:, :])
            s_ld.wait(nc.vector)
            t_cols.wait(nc.vector)
            t_pe.wait(nc.vector)
            c.v(nc.vector.scalar_tensor_tensor(
                sq_scr[:, :], xt[:, :], rstd[:, t:t + 1], sq_scr[:, :],
                ALU.mult, ALU.add))
            tok = c.v(nc.vector.tensor_scalar_add(st[:, :], sq_scr[:, :],
                                                  nmr[:, t:t + 1]))
        xr.release(tok)
        ds.put(t, tok, st[:, :])
        stg.release(Tok(ds.s_store.h, ds.s_store.v))
    return ds, rstd, nmr, t_cols


def emit_kv_fm_tm(c, ds, wk_h, wv_h, kT, v_tok, *, bk_row=None, bv_row=None,
                  act=None, wq2=None, qout=None):
    nc = c.nc
    sub = c.sub()
    cs = sub.__enter__()
    ds.s_store.wait(nc.gpsimd)
    ds.s_store.wait(nc.vector)
    wk = emit_load_weight(cs, wk_h, D, D) if wk_h is not None else None
    wv = emit_load_weight(cs, wv_h, D, D)
    wq = emit_load_weight(cs, wq2, D, D) if wq2 is not None else None
    ones, t_on = emit_ones(cs)
    onesw = cs.sb([1, 512], BF16)
    t_onw = c.v(c.nc.vector.memset(onesw[0:1, :], 1.0))
    if bk_row is not None:
        bk, s_bk = emit_load_bf16_row(cs, bk_row, D)
    if bv_row is not None:
        bv, s_bv = emit_load_bf16_row(cs, bv_row, D)
    yblk = cs.sbring(2, [128, ND * 512], BF16)
    kps = cs.ring(2, [128, 512])
    vps = cs.ring(2, [128, D])
    t_k = t_v = t_q = None
    for nb in range(T // 512):
        n0 = nb * 512
        yb = yblk.acquire(nc.sync)
        s_y = c.dsem("kvblk")
        ds.load_fm(yb, n0, n0 + 512, s_y)
        fm_jobs = []
        if wk is not None:
            fm_jobs.append((wk, kT, bk if bk_row is not None else None))
        if wq is not None:
            fm_jobs.append((wq, qout, None))
        for w_, outT, bias in fm_jobs:
            for ko in range(ND):
                pt = kps.acquire(nc.tensor)
                w_.sem.wait(nc.tensor)
                s_y.wait(nc.tensor)
                pairs = [(w_.lhsT(ki, ko * 128, (ko + 1) * 128),
                          yb[:, ki * 512:(ki + 1) * 512])
                         for ki in range(ND)]
                if bias is not None:
                    s_bk.wait(nc.tensor)
                    t_onw.wait(nc.tensor)
                    pairs.append((bias[0:1, ko * 128:(ko + 1) * 128],
                                  onesw[0:1, 0:512]))
                t_mm = c.mark(mm_group(nc, pt[:, 0:512], pairs), "tensor")
                t_mm.wait(nc.scalar)
                t_ev = c.s(nc.scalar.activation(
                    outT[:, ko * T + n0: ko * T + n0 + 512], pt[:, 0:512],
                    act or AF.Identity))
                kps.release(t_ev)
                if outT is kT:
                    t_k = t_ev
                else:
                    t_q = t_ev
                last_here = t_ev
        for tt in range(4):
            t = nb * 4 + tt
            pt = vps.acquire(nc.tensor)
            wv.sem.wait(nc.tensor)
            s_y.wait(nc.tensor)
            for o0 in range(0, D, 512):
                pairs = [(yb[:, ki * 512 + tt * 128: ki * 512 + (tt + 1) * 128],
                          wv.lhsT(ki, o0, o0 + 512)) for ki in range(ND)]
                if bv_row is not None:
                    s_bv.wait(nc.tensor)
                    t_on.wait(nc.tensor)
                    pairs.append((ones[0:1, 0:128], bv[0:1, o0:o0 + 512]))
                mm = mm_group(nc, pt[:, o0:o0 + 512], pairs)
            t_mm = c.mark(mm, "tensor")
            t_mm.wait(nc.scalar)
            t_v = c.s(nc.scalar.activation(v_tok[:, t * D:(t + 1) * D],
                                           pt[:, 0:D], act or AF.Identity))
            vps.release(t_v)
        yblk.release(Multi([x for x in (t_k, t_q, t_v) if x]))
    sub.__exit__(None, None, None)
    toks = [x for x in (t_k, t_q, t_v) if x is not None]
    return Multi(toks)



def emit_l2_attn_a(c, x_hbm, out_hbm, io, phase=9):
    nc = c.nc
    rby, s_rby = emit_load_f32(c, io["rby"][None, :], 1, 3)
    fac = c.sb([1, 8])
    wrow = c.sb([1, 3])
    c.v(nc.vector.memset(wrow[0:1, 0:1], 0.1))
    c.v(nc.vector.memset(wrow[0:1, 1:2], 0.05))
    c.v(nc.vector.memset(wrow[0:1, 2:3], 0.15))
    s_rby.wait(nc.vector)
    c.v(nc.vector.tensor_mul(fac[0:1, 4:7], rby[0:1, 0:3], wrow[0:1, 0:3]))
    c.v(nc.vector.reduce_sum(fac[0:1, 0:1], fac[0:1, 4:7], axis=AX.X))
    c.v(nc.vector.tensor_scalar(fac[0:1, 0:1], fac[0:1, 0:1], 1.0, 0.125,
                                ALU.add, ALU.mult))
    c.v(nc.vector.tensor_scalar(fac[0:1, 1:2], rby[0:1, 2:3], 0.1, 1.0,
                                ALU.mult, ALU.add))
    c.v(nc.vector.reciprocal(fac[0:1, 1:2], fac[0:1, 1:2]))
    c.v(nc.vector.tensor_scalar_mul(fac[0:1, 4:5], rby[0:1, 0:1], 0.1))
    c.v(nc.vector.tensor_scalar_mul(fac[0:1, 5:6], rby[0:1, 2:3], 0.05))
    c.v(nc.vector.tensor_add(fac[0:1, 2:3], fac[0:1, 4:5], fac[0:1, 5:6]))
    c.v(nc.vector.tensor_scalar_add(fac[0:1, 2:3], fac[0:1, 2:3], 1.0))
    c.v(nc.vector.tensor_scalar(fac[0:1, 3:4], rby[0:1, 1:2], 0.1, 1.0,
                                ALU.mult, ALU.add))
    t_fac = c.v(nc.vector.tensor_scalar_mul(fac[0:1, 3:4], fac[0:1, 3:4],
                                            1e-5))
    onesf = c.sb([1, 128])
    t_of = c.v(nc.vector.memset(onesf[0:1, :], 1.0))
    smcol = c.sb([128, 1])
    epscol = c.sb([128, 1])
    inv_b = c.sb([128, D])
    off_b = c.sb([128, D])
    iot = c.sb([1, D])
    poscol = c.sb([128, 2])
    iot2 = c.sb([1, D])
    sub = c.sub()
    cs = sub.__enter__()
    t_i1 = c.g(nc.gpsimd.iota(iot[0:1, :].rearrange("a (i two) -> a i two",
                                                    two=2),
                              pattern=[[1, 512], [0, 2]], base=0,
                              channel_multiplier=0,
                              allow_small_or_imprecise_dtypes=True))
    t_i1.wait(nc.scalar)
    c1 = -4.0 * math.log(10000.0) / D
    c.wait_const(nc.scalar)
    t_inv = c.s(nc.scalar.activation(iot[0:1, :], iot[0:1, :], AF.Exp,
                                     scale=c.constcol(c1)[0:1, 0:1]))
    t_inv.wait(nc.vector)
    t_inv = c.v(nc.vector.tensor_scalar_mul(iot[0:1, :], iot[0:1, :],
                                            fac[0:1, 1:2]))
    t_i2 = c.g(nc.gpsimd.iota(iot2[0:1, :].rearrange("a (i two) -> a i two",
                                                     two=2),
                              pattern=[[0, 512], [1, 2]], base=0,
                              channel_multiplier=0,
                              allow_small_or_imprecise_dtypes=True))
    t_i2.wait(nc.vector)
    t_off = c.v(nc.vector.tensor_scalar_mul(iot2[0:1, :], iot2[0:1, :],
                                            math.pi / 2))
    emit_bcast_rows(c, cs, [
        (fac[0:1, 0:1], smcol, t_fac),
        (fac[0:1, 3:4], epscol, t_fac),
        (iot[0:1, :], inv_b, t_inv),
        (iot2[0:1, :], off_b, t_off),
    ])
    sub.__exit__(None, None, None)
    pescr2 = c.sb([128, D])
    kint = c.sb([128, D], mybir.dt.int32)
    t_pc = c.g(nc.gpsimd.iota(poscol[:, 0:1], pattern=[[0, 1]], base=0,
                              channel_multiplier=1,
                              allow_small_or_imprecise_dtypes=True))

    def emit_pe(t, dst):
        t_pc.wait(nc.vector)
        c.v(nc.vector.tensor_scalar_add(poscol[:, 1:2], poscol[:, 0:1],
                                        float(t * 128)))
        c.v(nc.vector.tensor_scalar_mul(dst, inv_b[:, :], poscol[:, 1:2]))
        c.v(nc.vector.tensor_add(dst, dst, off_b[:, :]))
        c.v(nc.vector.tensor_scalar_mul(pescr2[:, :], dst,
                                        1.0 / (2 * math.pi)))
        c.v(nc.vector.tensor_copy(kint[:, :], pescr2[:, :]))
        c.v(nc.vector.tensor_copy(pescr2[:, :], kint[:, :]))
        c.v(nc.vector.scalar_tensor_tensor(dst, pescr2[:, :],
                                           -2 * math.pi, dst,
                                           ALU.mult, ALU.add))
        c.v(nc.vector.tensor_scalar_min(dst, dst, math.pi))
        t_u = c.v(nc.vector.tensor_scalar_max(dst, dst, -math.pi))
        t_u.wait(nc.scalar)
        return c.s(nc.scalar.activation(dst, dst, AF.Sin))

    ds, rstd, nmr, t_cols = emit_ln_stream(c, x_hbm, pe_fn=emit_pe)

    if phase == 1:
        s1 = c.sem()
        ds.s_store.wait(nc.gpsimd)
        s1.inc(nc.gpsimd.dma_start(out_hbm[:, :], ds.hbm[:, :]), 16)
        s1.wait(nc.gpsimd)
        return s1
    kT = c.sb([128, ND * T], BF16)
    v_tok = c.sb([128, NT * D], BF16)
    t_kv = emit_kv_fm_tm(c, ds, io["Wk"], io["Wv"], kT, v_tok,
                         bk_row=io["bk"], bv_row=io["bv"])
    if phase == 25:
        s1 = c.sem()
        t_kv.wait(nc.gpsimd)
        s1.inc(nc.gpsimd.dma_start(
            out_hbm.rearrange("(t p) f -> p t f", p=128),
            v_tok[:, :].rearrange("p (t f) -> p t f", t=NT)), 16)
        s1.wait(nc.gpsimd)
        return s1
    if phase == 27:
        s1 = c.sem()
        t_kv.wait(nc.gpsimd)
        dstk = out_hbm.rearrange("(a p) f -> p a f", p=128)
        srck = kT[:, :].rearrange("p (a f) -> p a f", a=16)
        s1.inc(nc.gpsimd.dma_start(dstk, srck), 16)
        s1.wait(nc.gpsimd)
        return s1
    att_ds = DStream(c, D)
    sub = c.sub()
    cs = sub.__enter__()
    t_kv.wait(nc.gpsimd)
    t_kv.wait(nc.vector)
    t_kv.wait(nc.scalar)
    wq = emit_load_weight(cs, io["Wq"], D, D)
    bq, s_bq = emit_load_bf16_row(cs, io["bq"], D)
    onesq = cs.sb([1, 128], BF16)
    t_onq = c.v(nc.vector.memset(onesq[0:1, :], 1.0))
    qsrc = cs.sbring(2, [128, ND * 128], BF16)
    qT1r = cs.sbring(2, [128, ND * 128], BF16)
    qps = cs.ring(1, [128, 128])
    attstg = cs.sbring(2, [128, D], BF16)
    sh = MhaShared(cs, 16, 64, D, after=[t_kv])

    for ti in range(NT):
        yb = qsrc.acquire(nc.sync)
        if ti == 0:
            t_kv.wait(nc.sync)
        s_y = c.dsem("qblk")
        ds.load_fm(yb, ti * 128, (ti + 1) * 128, s_y)
        qT1 = qT1r.acquire(nc.scalar)
        t_qe = None
        for ko in range(ND):
            pt = qps.acquire(nc.tensor)
            wq.sem.wait(nc.tensor)
            s_y.wait(nc.tensor)
            s_bq.wait(nc.tensor)
            t_onq.wait(nc.tensor)
            pairs = [(wq.lhsT(ki, ko * 128, (ko + 1) * 128),
                      yb[:, ki * 128:(ki + 1) * 128]) for ki in range(ND)]
            pairs.append((bq[0:1, ko * 128:(ko + 1) * 128], onesq[0:1, :]))
            t_mm = c.mark(mm_group(nc, pt[:, 0:128], pairs), "tensor")
            t_mm.wait(nc.scalar)
            t_qe = c.s(nc.scalar.activation(
                qT1[:, ko * 128:(ko + 1) * 128], pt[:, 0:128], AF.Identity))
            qps.release(t_qe)
        qsrc.release(t_qe)
        at = attstg.acquire(nc.vector)
        import os as _os
        if phase == 29:
            t_att = emit_mha(c, sh, kT, kT, v_tok, at[:, :], ti,
                             Multi([t_kv, t_qe]), causal=False,
                             scale_ap=smcol[:, 0:1], q_tile_mode=False)
        elif _os.environ.get("L2CONSTSCALE"):
            t_att = emit_mha(c, sh, qT1, kT, v_tok, at[:, :], ti,
                             Multi([t_kv, t_qe]), causal=False,
                             scale=float(_os.environ["L2CONSTSCALE"]),
                             q_tile_mode=True)
        else:
            t_att = emit_mha(c, sh, qT1, kT, v_tok, at[:, :], ti,
                             Multi([t_kv, t_qe]), causal=False,
                             scale_ap=smcol[:, 0:1], q_tile_mode=True)
        att_ds.put(ti, t_att, at[:, :])
        attstg.release(Tok(att_ds.s_store.h, att_ds.s_store.v))
        qT1r.release(t_att)
        if phase == 28 and ti == 0:
            s1 = c.sem()
            t_att.wait(nc.gpsimd)
            s1.inc(nc.gpsimd.dma_start(out_hbm[0:128, 0:D], at[:, :]), 16)
            s1.wait(nc.gpsimd)
            sub.__exit__(None, None, None)
            return s1
        if phase == 26 and ti == 3:
            s1 = c.sem()
            t_qe.wait(nc.gpsimd)
            s1.inc(nc.gpsimd.dma_start(out_hbm[0:128, 0:ND * 128], qT1[:, :]),
                   16)
            s1.wait(nc.gpsimd)
            sub.__exit__(None, None, None)
            return s1
    sub.__exit__(None, None, None)

    if phase == 2:
        s1 = c.sem()
        att_ds.s_store.wait(nc.gpsimd)
        s1.inc(nc.gpsimd.dma_start(out_hbm[:, :], att_ds.hbm[:, :]), 16)
        s1.wait(nc.gpsimd)
        return s1
    t_att.wait(nc.gpsimd)
    t_att.wait(nc.vector)
    t_att.wait(nc.scalar)
    t_att.wait(nc.sync)
    gam_b = c.sb([128, D])
    bet_b = c.sb([128, D])
    y1scr = c.sb([128, D])
    sqs = c.sb([128, D])
    ssum2 = c.sb([128, NT])
    ssq2 = c.sb([128, NT])
    mcol2 = c.sb([128, NT])
    rstd2 = c.sb([128, NT])
    nmr2 = c.sb([128, NT])
    sub = c.sub()
    cs = sub.__enter__()
    wo = emit_load_weight(cs, io["Wo"], D, D)
    bo, s_bo = emit_load_bf16_row(cs, io["bo"], D)
    ones2, t_on2 = emit_ones(cs)
    gb, s_gb = emit_load_f32(cs, io["ln_gamma"][None, :], 1, D)
    bb, s_bb = emit_load_f32(cs, io["ln_beta"][None, :], 1, D)
    s_gb.wait(nc.vector)
    t_fac.wait(nc.vector)
    t_gr = c.v(nc.vector.tensor_scalar_mul(gb[0:1, :], gb[0:1, :],
                                           fac[0:1, 2:3]))
    t_gb = emit_bcast_rows(c, cs, [(gb[0:1, :], gam_b, t_gr),
                                   (bb[0:1, :], bet_b, s_bb)])
    y1_hbm = c.dram([T, D])
    yd3 = y1_hbm.rearrange("(t p) f -> t p f", p=128)
    x3 = x_hbm.rearrange("(t p) f -> t p f", p=128)
    o3 = out_hbm.rearrange("(t p) f -> t p f", p=128)
    alhs = cs.sbring(2, [128, ND * 128], BF16)
    y1ps = cs.ring(2, [128, D])
    xrr = cs.sbring(2, [128, D], F32)
    s_y1st = c.sem()
    t_sq = None
    for t in range(NT):
        ab = alhs.acquire(nc.sync)
        s_a = c.dsem("ablk")
        att_ds.load_fm(ab, t * 128, (t + 1) * 128, s_a)
        pt = y1ps.acquire(nc.tensor)
        wo.sem.wait(nc.tensor)
        s_a.wait(nc.tensor)
        s_bo.wait(nc.tensor)
        t_on2.wait(nc.tensor)
        for o0 in (0, 512):
            pairs = [(ab[:, ki * 128:(ki + 1) * 128],
                      wo.lhsT(ki, o0, o0 + 512)) for ki in range(ND)]
            pairs.append((ones2[0:1, 0:128], bo[0:1, o0:o0 + 512]))
            mm = mm_group(nc, pt[:, o0:o0 + 512], pairs)
        t_mm = c.mark(mm, "tensor")
        alhs.release(t_mm)
        xt = xrr.acquire(nc.sync)
        s_ld = c.dsem("xw")
        s_ld.inc(nc.sync.dma_start(xt[:, :], x3[t, :, :]), 16)
        t_pe = emit_pe(t, y1scr[:, :])
        t_pe.wait(nc.vector)
        t_mm.wait(nc.vector)
        s_ld.wait(nc.vector)
        t_cols.wait(nc.vector)
        c.v(nc.vector.scalar_tensor_tensor(
            y1scr[:, :], xt[:, :], rstd[:, t:t + 1], y1scr[:, :],
            ALU.mult, ALU.add))
        xrr.release(c.tip("vector"))
        c.v(nc.vector.tensor_scalar_add(y1scr[:, :], y1scr[:, :],
                                        nmr[:, t:t + 1]))
        t_y1 = c.v(nc.vector.tensor_add(y1scr[:, :], y1scr[:, :], pt[:, 0:D]))
        y1ps.release(t_y1)
        c.v(nc.vector.reduce_sum(ssum2[:, t:t + 1], y1scr[:, :], axis=AX.X))
        t_y1.wait(nc.scalar)
        t_sq = c.s(nc.scalar.activation(sqs[:, :], y1scr[:, :], AF.Square,
                                        accum_out=ssq2[:, t:t + 1]))
        t_y1.wait(nc.sync)
        s_y1st.inc(nc.sync.dma_start(yd3[t, :, :], y1scr[:, :]), 16)
        t_sq.wait(nc.vector)
        Tok(s_y1st.h, s_y1st.v).wait(nc.vector)
    t_sq.wait(nc.vector)
    c.v(nc.vector.tensor_scalar_mul(mcol2[:, :], ssum2[:, :], 1.0 / D))
    c.v(nc.vector.tensor_scalar_mul(ssq2[:, :], ssq2[:, :], 1.0 / D))
    t_m2 = c.v(nc.vector.tensor_mul(rstd2[:, :], mcol2[:, :], mcol2[:, :]))
    c.v(nc.vector.tensor_sub(ssq2[:, :], ssq2[:, :], rstd2[:, :]))
    t_var = c.v(nc.vector.tensor_scalar_add(ssq2[:, :], ssq2[:, :],
                                            epscol[:, 0:1]))
    t_var.wait(nc.scalar)
    c.wait_const(nc.scalar)
    c.s(nc.scalar.activation(rstd2[:, :], ssq2[:, :], AF.Ln))
    t_r2 = c.s(nc.scalar.activation(rstd2[:, :], rstd2[:, :], AF.Exp,
                                    scale=c.constcol(-0.5)))
    t_r2.wait(nc.vector)
    t_n2 = c.v(nc.vector.scalar_tensor_tensor(
        nmr2[:, :], mcol2[:, :], -1.0, rstd2[:, :], ALU.mult, ALU.mult))
    y1t = cs.sbring(2, [128, D], F32)
    s_out = c.sem()
    s_y1st.wait(nc.sync)
    for t in range(NT):
        yt = y1t.acquire(nc.sync)
        s_ld = c.dsem("y1l")
        s_ld.inc(nc.sync.dma_start(yt[:, :], yd3[t, :, :]), 16)
        xt = xrr.acquire(nc.sync)
        s_lx = c.dsem("xw2")
        s_lx.inc(nc.sync.dma_start(xt[:, :], x3[t, :, :]), 16)
        s_ld.wait(nc.scalar)
        t_n2.wait(nc.scalar)
        t_nm = c.s(nc.scalar.activation(sqs[:, :], yt[:, :], AF.Identity,
                                        scale=rstd2[:, t:t + 1],
                                        bias=nmr2[:, t:t + 1]))
        y1t.release(t_nm)
        t_nm.wait(nc.vector)
        t_gb.wait(nc.vector)
        c.v(nc.vector.tensor_mul(sqs[:, :], sqs[:, :], gam_b[:, :]))
        c.v(nc.vector.tensor_add(sqs[:, :], sqs[:, :], bet_b[:, :]))
        s_lx.wait(nc.vector)
        tok_add = c.v(nc.vector.tensor_add(sqs[:, :], sqs[:, :], xt[:, :]))
        xrr.release(tok_add)
        tok_add.wait(nc.sync)
        s_out.inc(nc.sync.dma_start(o3[t, :, :], sqs[:, :]), 16)
        Tok(s_out.h, s_out.v).wait(nc.scalar)
        tok_add.wait(nc.scalar)
    sub.__exit__(None, None, None)
    s_out.wait(nc.sync)
    return s_out


def build_l2attna_program():
    nc = bass.Bass("TRN2", target_bir_lowering=False)
    stack = ExitStack()
    c = Ctx(nc, stack)
    dt = lambda n, sh_: nc.dram_tensor(n, sh_, F32, kind="ExternalInput").ap()
    io = {
        "rby": dt("rby", [3]),
        "Wq": dt("Wq", [D, D]), "bq": dt("bq", [D]),
        "Wk": dt("Wk", [D, D]), "bk": dt("bk", [D]),
        "Wv": dt("Wv", [D, D]), "bv": dt("bv", [D]),
        "Wo": dt("Wo", [D, D]), "bo": dt("bo", [D]),
        "ln_gamma": dt("ln_gamma", [D]), "ln_beta": dt("ln_beta", [D]),
    }
    x_h = dt("x", [T, D])
    out_h = nc.dram_tensor("out", [T, D], F32, kind="ExternalOutput").ap()
    import os
    with stack:
        emit_l2_attn_a(c, x_h, out_h, io, phase=int(os.environ.get("L2PH", "9")))
    return nc



def emit_l1_attn_a(c, x_hbm, out_hbm, io):
    nc = c.nc
    H, DH = 16, 64
    ds, rstd, nmr, t_cols = emit_ln_stream(c, x_hbm)
    qT = c.sb([128, ND * T], BF16)
    kT = c.sb([128, ND * T], BF16)
    v_tok = c.sb([128, NT * D], BF16)
    Wq_sl = io["Wqkv"][:, 0:D]
    Wk_sl = io["Wqkv"][:, D:2 * D]
    Wv_sl = io["Wqkv"][:, 2 * D:3 * D]
    t_kv = emit_kv_fm_tm(c, ds, Wk_sl, Wv_sl, kT, v_tok, wq2=Wq_sl, qout=qT)

    hs, s_hs = emit_load_f32(c, io["heads3"], 1, 48)
    hrow = c.sb([1, 16])
    hrow2 = c.sb([1, 16])
    pig16 = c.dram([16])
    nch16 = c.dram([16])
    s_hs.wait(nc.scalar)
    t_p = c.s(nc.scalar.activation(hrow[0:1, :], hs[0:1, 16:32], AF.Sigmoid))
    t_e = c.s(nc.scalar.activation(hrow2[0:1, :], hs[0:1, 32:48], AF.Sigmoid))
    t_x = c.s(nc.scalar.activation(hs[0:1, 0:16], hs[0:1, 0:16], AF.Exp,
                                   scale=c.constcol(-1.0)[0:1, 0:1]))
    t_x.wait(nc.vector)
    t_e.wait(nc.vector)
    c.v(nc.vector.tensor_mul(hrow2[0:1, :], hrow2[0:1, :], hs[0:1, 0:16]))
    t_n = c.v(nc.vector.tensor_scalar_mul(hrow2[0:1, :], hrow2[0:1, :], -8.0))
    s_hd = c.sem()
    t_p.wait(nc.sync)
    s_hd.inc(nc.sync.dma_start(pig16[None, :], hrow[0:1, :]), 16)
    t_n.wait(nc.sync)
    s_hd.inc(nc.sync.dma_start(nch16[None, :], hrow2[0:1, :]), 16)
    pigc = c.sb([16, 1])
    s_pc = c.sem()
    s_hd.wait(nc.sync)
    s_pc.inc(nc.sync.dma_start(pigc[:, 0:1], pig16[:, None]), 16)
    nchB = c.sb([128, 16])
    fcol = c.sb([128, 2])
    posB = c.sb([128, T])
    posr = c.sb([1, T])
    sub = c.sub()
    cs = sub.__enter__()
    t_nb = emit_bcast_rows(c, cs, [(hrow2[0:1, :], nchB, t_n)])
    pighi = cs.sb([16, 1], BF16)
    piglo = cs.sb([16, 1], BF16)
    s_pc.wait(nc.vector)
    c.v(nc.vector.tensor_copy(pighi[:, 0:1], pigc[:, 0:1]))
    t_pl = c.v(nc.vector.tensor_sub(piglo[:, 0:1], pigc[:, 0:1],
                                    pighi[:, 0:1]))
    kint1 = cs.sb([128, 1], mybir.dt.int32)
    t_i = c.g(nc.gpsimd.iota(fcol[:, 0:1], pattern=[[0, 1]], base=0,
                             channel_multiplier=1,
                             allow_small_or_imprecise_dtypes=True))
    t_i.wait(nc.vector)
    c.v(nc.vector.tensor_scalar_mul(fcol[:, 1:2], fcol[:, 0:1], 1.0 / 32.0))
    c.v(nc.vector.tensor_scalar_add(fcol[:, 1:2], fcol[:, 1:2], -0.499))
    c.v(nc.vector.tensor_copy(kint1[:, 0:1], fcol[:, 1:2]))
    c.v(nc.vector.tensor_copy(fcol[:, 1:2], kint1[:, 0:1]))
    c.v(nc.vector.scalar_tensor_tensor(fcol[:, 0:1], fcol[:, 1:2], -32.0,
                                       fcol[:, 0:1], ALU.mult, ALU.add))
    t_f = c.v(nc.vector.tensor_scalar(fcol[:, 0:1], fcol[:, 0:1],
                                      -2.0 / 64.0 * math.log(math.pi),
                                      math.log(math.pi), ALU.mult, ALU.add))
    t_f.wait(nc.scalar)
    t_f = c.s(nc.scalar.activation(fcol[:, 0:1], fcol[:, 0:1], AF.Exp))
    t_pr = c.g(nc.gpsimd.iota(posr[0:1, :], pattern=[[1, T]], base=0,
                              channel_multiplier=0,
                              allow_small_or_imprecise_dtypes=True))
    emit_bcast_rows(c, cs, [(posr[0:1, :], posB, t_pr)])
    selps = cs.ps([128, 1])
    sel = cs.sb([16, 128], BF16)
    seli = cs.sb([16, 128])
    thetasc = cs.sb([128, T])
    tric = cs.sb([128, T])
    kint = cs.sb([128, T], mybir.dt.int32)
    rs1 = cs.sb([128, T], BF16)
    rs2 = cs.sb([128, T], BF16)
    rs3 = cs.sb([128, T], BF16)
    rs4 = cs.sb([128, T], BF16)
    pfc = cs.sb([128, 2])

    def sincos(dst, src_tok, extra_bias):
        c.v(nc.vector.tensor_scalar(tric[:, :], thetasc[:, :], 1.0,
                                    extra_bias, ALU.mult, ALU.add)) \
            if extra_bias else c.v(nc.vector.tensor_copy(tric[:, :],
                                                         thetasc[:, :]))
        c.v(nc.vector.tensor_scalar_mul(dst, tric[:, :],
                                        1.0 / (2 * math.pi)))
        c.v(nc.vector.tensor_copy(kint[:, :], dst))
        c.v(nc.vector.tensor_copy(dst, kint[:, :]))
        c.v(nc.vector.scalar_tensor_tensor(dst, dst, -2 * math.pi,
                                           tric[:, :], ALU.mult, ALU.add))
        c.v(nc.vector.tensor_scalar_min(dst, dst, math.pi))
        t_u = c.v(nc.vector.tensor_scalar_max(dst, dst, -math.pi))
        t_u.wait(nc.scalar)
        return c.s(nc.scalar.activation(dst, dst, AF.Sin))

    t_rot = None
    for ch in range(4):
        t_si = c.g(nc.gpsimd.iota(
            seli[:, :].rearrange("h (g k) -> h g k", g=4),
            pattern=[[1, 4], [0, 32]], base=4 * ch, channel_multiplier=-1,
            allow_small_or_imprecise_dtypes=True))
        t_si.wait(nc.vector)
        t_se = c.v(nc.vector.tensor_single_scalar(sel[:, :], seli[:, :], 0.0,
                                                  ALU.is_equal))
        t_se.wait(nc.tensor)
        t_pl.wait(nc.tensor)
        nc.tensor.matmul(selps[:, 0:1], sel[:, :] if False else sel[:, :],
                         pighi[:, 0:1], start=True, stop=False)
        mm = nc.tensor.matmul(selps[:, 0:1], sel[:, :], piglo[:, 0:1],
                              start=False, stop=True)
        t_sm = c.mark(mm, "tensor")
        t_sm.wait(nc.vector)
        c.v(nc.vector.tensor_copy(pfc[:, 0:1], selps[:, 0:1]))
        t_f.wait(nc.vector)
        t_pf = c.v(nc.vector.tensor_mul(pfc[:, 0:1], pfc[:, 0:1],
                                        fcol[:, 0:1]))
        t_pf.wait(nc.vector)
        c.v(nc.vector.tensor_scalar_mul(thetasc[:, :], posB[:, :],
                                        pfc[:, 0:1]))
        t_sin = sincos(rs1[:, :] if False else tric[:, :], None, 0.0)
        sinb = rs3
        t_sin.wait(nc.vector)
        t_sb = c.v(nc.vector.tensor_copy(sinb[:, :], tric[:, :]))
        t_cos = sincos(tric[:, :], None, math.pi / 2)
        cosb = rs4
        t_cos.wait(nc.vector)
        t_cb = c.v(nc.vector.tensor_copy(cosb[:, :], tric[:, :]))
        for buf in (qT, kT):
            t1 = buf[:, ch * T:(ch + 1) * T]
            t2 = buf[:, (ch + 4) * T:(ch + 5) * T]
            t_kv.wait(nc.vector)
            c.v(nc.vector.tensor_mul(rs1[:, :], t1, cosb[:, :]))
            c.v(nc.vector.tensor_mul(rs2[:, :], t2, sinb[:, :]))
            kv1 = c.v(nc.vector.tensor_mul(thetasc[:, :], t1, sinb[:, :]))
            c.v(nc.vector.tensor_sub(t1, rs1[:, :], rs2[:, :]))
            c.v(nc.vector.tensor_mul(rs2[:, :], t2, cosb[:, :]))
            t_rot = c.v(nc.vector.tensor_add(t2, thetasc[:, :], rs2[:, :]))
    sub.__exit__(None, None, None)

    D0 = c.sb([128, T])
    absD = c.sb([128, T])
    t_d0 = c.g(nc.gpsimd.iota(D0[:, :], pattern=[[1, T]], base=0,
                              channel_multiplier=-1,
                              allow_small_or_imprecise_dtypes=True))
    att_ds = DStream(c, D)
    sub = c.sub()
    cs = sub.__enter__()
    attstg = cs.sbring(2, [128, D], BF16)
    sh = MhaShared(cs, H, DH, D, after=[t_rot, Tok(c.est["pool"].h,
                                                   c.est["pool"].v)])
    state = {"absD_tok": None}
    for ti in range(NT):
        t_d0.wait(nc.vector)
        if state["absD_tok"] is not None:
            pass
        t_a1 = c.v(nc.vector.tensor_scalar_add(absD[:, 0:(ti + 1) * 128],
                                               D0[:, 0:(ti + 1) * 128],
                                               float(-ti * 128)))
        t_a1.wait(nc.scalar)
        t_abs = c.s(nc.scalar.activation(absD[:, 0:(ti + 1) * 128],
                                         absD[:, 0:(ti + 1) * 128], AF.Abs))

        def hook(h, ti_, klen, sp, t_s, t_abs=t_abs):
            t_abs.wait(nc.vector)
            t_nb.wait(nc.vector)
            t_s.wait(nc.vector)
            return c.v(nc.vector.scalar_tensor_tensor(
                sp[:, 0:klen], absD[:, 0:klen], nchB[:, h:h + 1],
                sp[:, 0:klen], ALU.mult, ALU.add))
        at = attstg.acquire(nc.vector)
        t_att = emit_mha(c, sh, qT, kT, v_tok, at[:, :], ti, t_kv,
                         causal=True, pre_hook=hook, scale=0.125,
                         pieces_fn=lambda h: [
                             (h // 4, (h % 4) * 32, 32),
                             (4 + h // 4, (h % 4) * 32, 32)])
        att_ds.put(ti, t_att, at[:, :])
        attstg.release(Tok(att_ds.s_store.h, att_ds.s_store.v))
        t_att.wait(nc.vector)
        t_att.wait(nc.scalar)
    sub.__exit__(None, None, None)

    oscr = c.sb([128, D])
    sub = c.sub()
    cs = sub.__enter__()
    wo = emit_load_weight(cs, io["Wo"], D, D)
    alhs = cs.sbring(2, [128, ND * 128], BF16)
    y1ps = cs.ring(2, [128, D])
    xrr = cs.sbring(2, [128, D], F32)
    x3 = x_hbm.rearrange("(t p) f -> t p f", p=128)
    o3 = out_hbm.rearrange("(t p) f -> t p f", p=128)
    s_out = c.sem()
    for t in range(NT):
        ab = alhs.acquire(nc.sync)
        s_a = c.dsem("ablk")
        att_ds.load_fm(ab, t * 128, (t + 1) * 128, s_a)
        pt = y1ps.acquire(nc.tensor)
        wo.sem.wait(nc.tensor)
        s_a.wait(nc.tensor)
        for o0 in (0, 512):
            mm = mm_group(nc, pt[:, o0:o0 + 512],
                          [(ab[:, ki * 128:(ki + 1) * 128],
                            wo.lhsT(ki, o0, o0 + 512)) for ki in range(ND)])
        t_mm = c.mark(mm, "tensor")
        alhs.release(t_mm)
        xt = xrr.acquire(nc.sync)
        s_ld = c.dsem("xw")
        s_ld.inc(nc.sync.dma_start(xt[:, :], x3[t, :, :]), 16)
        t_mm.wait(nc.vector)
        s_ld.wait(nc.vector)
        tok_o = c.v(nc.vector.tensor_add(oscr[:, :], xt[:, :], pt[:, 0:D]))
        y1ps.release(tok_o)
        xrr.release(tok_o)
        tok_o.wait(nc.sync)
        s_out.inc(nc.sync.dma_start(o3[t, :, :], oscr[:, :]), 16)
        Tok(s_out.h, s_out.v).wait(nc.vector)
    sub.__exit__(None, None, None)
    s_out.wait(nc.sync)
    return s_out


def build_l1attna_program():
    nc = bass.Bass("TRN2", target_bir_lowering=False)
    stack = ExitStack()
    c = Ctx(nc, stack)
    dtm = lambda n, sh_: nc.dram_tensor(n, sh_, F32, kind="ExternalInput").ap()
    io = {
        "Wqkv": dtm("Wqkv", [D, 3 * D]),
        "Wo": dtm("Wo", [D, D]),
        "heads3": dtm("heads3", [1, 48]),
    }
    x_h = dtm("x", [T, D])
    out_h = nc.dram_tensor("out", [T, D], F32, kind="ExternalOutput").ap()
    with stack:
        emit_l1_attn_a(c, x_h, out_h, io)
    return nc



def emit_l0_attn_b(c, x_hbm, out_hbm, io):
    nc = c.nc
    G, DHg = 4, 256
    WINDOWS = (32.0, 64.0, 128.0, 256.0)
    ds, rstd, nmr, t_cols = emit_ln_stream(c, x_hbm)
    q_inT = c.sb([128, ND * T], BF16)
    sub = c.sub()
    cs = sub.__enter__()
    wqi = emit_load_weight(cs, io["Wq"], D, D)
    yblk = cs.sbring(2, [128, ND * 512], BF16)
    kps = cs.ring(2, [128, 512])
    t_qi = None
    for nb in range(T // 512):
        n0 = nb * 512
        yb = yblk.acquire(nc.sync)
        s_y = c.dsem("kvblk")
        ds.load_fm(yb, n0, n0 + 512, s_y)
        for ko in range(ND):
            pt = kps.acquire(nc.tensor)
            wqi.sem.wait(nc.tensor)
            s_y.wait(nc.tensor)
            t_mm = c.mark(mm_group(nc, pt[:, 0:512],
                          [(wqi.lhsT(ki, ko * 128, (ko + 1) * 128),
                            yb[:, ki * 512:(ki + 1) * 512])
                           for ki in range(ND)]), "tensor")
            t_mm.wait(nc.scalar)
            t_qi = c.s(nc.scalar.copy(
                q_inT[:, ko * T + n0: ko * T + n0 + 512], pt[:, 0:512]))
            kps.release(t_qi)
        yblk.release(t_qi)
    sub.__exit__(None, None, None)

    qT = c.sb([128, ND * T], BF16)
    kT = c.sb([128, ND * T], BF16)
    v_tok = c.sb([128, NT * D], BF16)
    hT256 = c.sb([128, 2 * T], BF16)
    dxcol = c.sb([128, NT])
    dx_hbm = c.dram([T])
    sub = c.sub()
    cs = sub.__enter__()
    wtile = cs.sb([128, ND * D], BF16)
    bq, s_bq = emit_load_bf16_row(cs, io["bqkv"][0:D], D)
    bk, s_bk = emit_load_bf16_row(cs, io["bqkv"][D:2 * D], D)
    bv, s_bv = emit_load_bf16_row(cs, io["bqkv"][2 * D:3 * D], D)
    ones, t_on = emit_ones(cs)
    onesw = cs.sb([1, 512], BF16)
    t_onw = c.v(nc.vector.memset(onesw[0:1, :], 1.0))
    relub = cs.sbring(1, [128, ND * 512], BF16)
    pps = cs.ring(2, [128, 512])
    vps = cs.ring(2, [128, D])
    dps = cs.ring(2, [1, 512])
    w2hi = cs.sb([128, 2], BF16)
    w2lo = cs.sb([128, 2], BF16)
    w2r, s_w2 = emit_load_f32(cs, io["W2c"], 128, 2)
    b2r, s_b2 = emit_load_f32(cs, io["b2"][None, :], 1, 1)
    s_w2.wait(nc.vector)
    c.v(nc.vector.tensor_copy(w2hi[:, :], w2r[:, :]))
    t_w2 = c.v(nc.vector.tensor_sub(w2lo[:, :], w2r[:, :], w2hi[:, :]))
    dxrow = cs.sb([1, T])
    t_q = t_k = t_v = None
    tok_w = None

    for w_sl, outT, brow, s_b in ((io["Wqkv"][:, 0:D], qT, bq, s_bq),
                                  (io["Wqkv"][:, D:2 * D], kT, bk, s_bk)):
        if tok_w is not None:
            tok_w.wait(nc.gpsimd)
        w_ = emit_load_weight(cs, w_sl, D, D, tile=wtile)
        for nb in range(4):
            n0 = nb * 512
            for ko in range(ND):
                pt = pps.acquire(nc.tensor)
                w_.sem.wait(nc.tensor)
                t_qi.wait(nc.tensor)
                s_b.wait(nc.tensor)
                t_onw.wait(nc.tensor)
                pairs = [(w_.lhsT(ki, ko * 128, (ko + 1) * 128),
                          q_inT[:, ki * T + n0: ki * T + n0 + 512])
                         for ki in range(ND)]
                pairs.append((brow[0:1, ko * 128:(ko + 1) * 128],
                              onesw[0:1, 0:512]))
                t_mm = c.mark(mm_group(nc, pt[:, 0:512], pairs), "tensor")
                tok_w = t_mm
                t_mm.wait(nc.scalar)
                t_ev = c.s(nc.scalar.copy(
                    outT[:, ko * T + n0: ko * T + n0 + 512], pt[:, 0:512]))
                pps.release(t_ev)
                if outT is qT:
                    t_q = t_ev
                else:
                    t_k = t_ev
    tok_w.wait(nc.gpsimd)
    wv = emit_load_weight(cs, io["Wqkv"][:, 2 * D:3 * D], D, D, tile=wtile)
    for t in range(NT):
        pt = vps.acquire(nc.tensor)
        wv.sem.wait(nc.tensor)
        t_qi.wait(nc.tensor)
        s_bv.wait(nc.tensor)
        t_on.wait(nc.tensor)
        for o0 in (0, 512):
            pairs = [(q_inT[:, ki * T + t * 128: ki * T + (t + 1) * 128],
                      wv.lhsT(ki, o0, o0 + 512)) for ki in range(ND)]
            pairs.append((ones[0:1, 0:128], bv[0:1, o0:o0 + 512]))
            mm = mm_group(nc, pt[:, o0:o0 + 512], pairs)
        t_mm = c.mark(mm, "tensor")
        tok_w = t_mm
        t_mm.wait(nc.scalar)
        t_v = c.s(nc.scalar.copy(v_tok[:, t * D:(t + 1) * D], pt[:, 0:D]))
        vps.release(t_v)
    tok_w.wait(nc.gpsimd)
    wop = emit_load_weight(cs, io["Wop"], D, 256, tile=wtile)
    t_dx = None
    for nb in range(4):
        n0 = nb * 512
        rb = relub.acquire(nc.vector)
        t_qi.wait(nc.vector)
        t_r = None
        for ki in range(ND):
            t_r = c.v(nc.vector.tensor_relu(
                rb[:, ki * 512:(ki + 1) * 512],
                q_inT[:, ki * T + n0: ki * T + n0 + 512]))
        t_h = None
        for ko in range(2):
            pt = pps.acquire(nc.tensor)
            wop.sem.wait(nc.tensor)
            t_r.wait(nc.tensor)
            t_mm = c.mark(mm_group(nc, pt[:, 0:512],
                          [(wop.lhsT(ki, ko * 128, (ko + 1) * 128),
                            rb[:, ki * 512:(ki + 1) * 512])
                           for ki in range(ND)]), "tensor")
            t_mm.wait(nc.scalar)
            t_h = c.s(nc.scalar.copy(
                hT256[:, ko * T + n0: ko * T + n0 + 512], pt[:, 0:512]))
            pps.release(t_h)
        relub.release(t_mm)
        dp = dps.acquire(nc.tensor)
        t_h.wait(nc.tensor)
        t_w2.wait(nc.tensor)
        for j, wcolt in enumerate((w2hi, w2lo)):
            for ko in range(2):
                mm = nc.tensor.matmul(
                    dp[0:1, 0:512], wcolt[:, ko:ko + 1],
                    hT256[:, ko * T + n0: ko * T + n0 + 512],
                    start=(j == 0 and ko == 0), stop=(j == 1 and ko == 1))
        t_dmm = c.mark(mm, "tensor")
        t_dmm.wait(nc.scalar)
        s_b2.wait(nc.scalar)
        t_sg = c.s(nc.scalar.activation(dxrow[0:1, n0:n0 + 512],
                                        dp[0:1, 0:512], AF.Sigmoid,
                                        bias=b2r[0:1, 0:1]))
        dps.release(t_sg)
        t_sg.wait(nc.vector)
        t_dx = c.v(nc.vector.tensor_scalar_mul(dxrow[0:1, n0:n0 + 512],
                                               dxrow[0:1, n0:n0 + 512],
                                               float(T)))
    s_st = c.sem()
    t_dx.wait(nc.sync)
    s_st.inc(nc.sync.dma_start(dx_hbm[None, :], dxrow[0:1, :]), 16)
    s_st.wait(nc.sync)
    s_dc = c.sem()
    s_dc.inc(nc.sync.dma_start(
        dxcol[:, :], dx_hbm.rearrange("(t p) -> p t", p=128),
        allow_slow_non_contiguous=True), 16)
    sub.__exit__(None, None, None)

    D0 = c.sb([128, T])
    mscr = c.sb([128, T], BF16)
    m2scr = c.sb([128, T], BF16)
    t_d0 = c.g(nc.gpsimd.iota(D0[:, :], pattern=[[1, T]], base=0,
                              channel_multiplier=-1,
                              allow_small_or_imprecise_dtypes=True))
    att_ds = DStream(c, D)
    sub = c.sub()
    cs = sub.__enter__()
    attstg = cs.sbring(2, [128, D], BF16)
    thrc = cs.sb([128, 8])
    sh = MhaShared(cs, G, DHg, D, after=[Tok(c.est["scalar"].h,
                                             c.est["scalar"].v)])
    scale = 2.0 / math.sqrt(float(T))
    for ti in range(NT):
        def hook(g, ti_, klen, sp, t_s, ti__=ti):
            q0 = float(ti__ * 128)
            s_dc.wait(nc.vector)
            t_d0.wait(nc.vector)
            c.v(nc.vector.tensor_scalar_add(thrc[:, g:g + 1],
                                            dxcol[:, ti__:ti__ + 1],
                                            q0 + WINDOWS[g]))
            c.v(nc.vector.tensor_single_scalar(mscr[:, :], D0[:, :],
                                               thrc[:, g:g + 1], ALU.is_le))
            c.v(nc.vector.tensor_single_scalar(m2scr[:, :], D0[:, :], q0,
                                               ALU.is_ge))
            c.v(nc.vector.tensor_mul(mscr[:, :], mscr[:, :], m2scr[:, :]))
            c.v(nc.vector.tensor_scalar(mscr[:, :], mscr[:, :], 1e30, -1e30,
                                        ALU.mult, ALU.add))
            t_s.wait(nc.vector)
            return c.v(nc.vector.tensor_add(sp[:, 0:klen], sp[:, 0:klen],
                                            mscr[:, :]))
        at = attstg.acquire(nc.vector)
        t_att = emit_mha(c, sh, qT, kT, v_tok, at[:, :], ti,
                         Multi([t_q, t_k, t_v]), causal=False,
                         pre_hook=hook, scale=scale)
        att_ds.put(ti, t_att, at[:, :])
        attstg.release(Tok(att_ds.s_store.h, att_ds.s_store.v))
        t_att.wait(nc.vector)
    sub.__exit__(None, None, None)

    oscr = c.sb([128, D])
    sub = c.sub()
    cs = sub.__enter__()
    xrr = cs.sbring(2, [128, D], F32)
    arr = cs.sbring(2, [128, D], BF16)
    x3 = x_hbm.rearrange("(t p) f -> t p f", p=128)
    o3 = out_hbm.rearrange("(t p) f -> t p f", p=128)
    a3 = att_ds.h3
    s_out = c.sem()
    att_ds.s_store.wait(nc.sync)
    for t in range(NT):
        xt = xrr.acquire(nc.sync)
        s_lx = c.dsem("xw")
        s_lx.inc(nc.sync.dma_start(xt[:, :], x3[t, :, :]), 16)
        ar = arr.acquire(nc.sync)
        s_la = c.dsem("aw")
        s_la.inc(nc.sync.dma_start(ar[:, :], a3[t, :, :]), 16)
        s_lx.wait(nc.vector)
        s_la.wait(nc.vector)
        tok_o = c.v(nc.vector.tensor_add(oscr[:, :], xt[:, :], ar[:, :]))
        xrr.release(tok_o)
        arr.release(tok_o)
        tok_o.wait(nc.sync)
        s_out.inc(nc.sync.dma_start(o3[t, :, :], oscr[:, :]), 16)
        Tok(s_out.h, s_out.v).wait(nc.vector)
    sub.__exit__(None, None, None)
    s_out.wait(nc.sync)
    return s_out


def build_l0attnb_program():
    nc = bass.Bass("TRN2", target_bir_lowering=False)
    stack = ExitStack()
    c = Ctx(nc, stack)
    dtm = lambda n, sh_: nc.dram_tensor(n, sh_, F32, kind="ExternalInput").ap()
    io = {
        "Wq": dtm("Wq", [D, D]),
        "Wqkv": dtm("Wqkv", [D, 3 * D]),
        "bqkv": dtm("bqkv", [3 * D]),
        "Wop": dtm("Wop", [D, 256]),
        "W2c": dtm("W2c", [128, 2]),
        "b2": dtm("b2", [1]),
    }
    x_h = dtm("x", [T, D])
    out_h = nc.dram_tensor("out", [T, D], F32, kind="ExternalOutput").ap()
    with stack:
        emit_l0_attn_b(c, x_h, out_h, io)
    return nc



def emit_l2_attn_b(c, x_hbm, out_hbm, io, phase=9):
    nc = c.nc
    DI = 2048
    NKI = DI // 128
    DS = 16
    DTR = 64
    TH = T // 2

    x3 = x_hbm.rearrange("(t p) f -> t p f", p=128)
    ssum = c.sb([128, NT])
    ssq = c.sb([128, NT])
    mcol = c.sb([128, NT])
    msq = c.sb([128, NT])
    rstd = c.sb([128, NT])
    nmr = c.sb([128, NT])
    sq_scr = c.sb([128, D])
    g_bc = c.sb([128, D])
    b_bc = c.sb([128, D])
    xr = c.sbring(3, [128, D], F32)
    stg = c.sbring(2, [128, D], BF16)
    xh_ds = DStream(c, D)
    xn_ds = DStream(c, D)
    gr, s_gr = emit_load_f32(c, io["norm_g"][None, :], 1, D)
    br, s_br = emit_load_f32(c, io["norm_b"][None, :], 1, D)
    sub = c.sub()
    cs = sub.__enter__()
    emit_bcast_rows(c, cs, [(gr[0:1, :], g_bc, s_gr), (br[0:1, :], b_bc,
                                                       s_br)])
    sub.__exit__(None, None, None)
    t_red = t_sq = None
    for t in range(NT):
        xt = xr.acquire(nc.sync)
        s_ld = c.dsem("lnx")
        s_ld.inc(nc.sync.dma_start(xt[:, :], x3[t, :, :]), 16)
        s_ld.wait(nc.vector)
        t_red = c.v(nc.vector.reduce_sum(ssum[:, t:t + 1], xt[:, :],
                                         axis=AX.X))
        s_ld.wait(nc.scalar)
        t_sq = c.s(nc.scalar.activation(sq_scr[:, :], xt[:, :], AF.Square,
                                        accum_out=ssq[:, t:t + 1]))
        xr.release(Multi([t_red, t_sq]))
    t_sq.wait(nc.vector)
    c.v(nc.vector.tensor_scalar_mul(mcol[:, :], ssum[:, :], 1.0 / D))
    t_m = c.v(nc.vector.tensor_scalar_mul(ssq[:, :], ssq[:, :], 1.0 / D))
    t_m.wait(nc.scalar)
    t_msq = c.s(nc.scalar.activation(msq[:, :], mcol[:, :], AF.Square))
    t_msq.wait(nc.vector)
    t_var = c.v(nc.vector.tensor_sub(ssq[:, :], ssq[:, :], msq[:, :]))
    epsc = c.constcol(1e-5)
    halfneg = c.constcol(-0.5)
    t_var.wait(nc.scalar)
    c.wait_const(nc.scalar)
    c.s(nc.scalar.activation(rstd[:, :], ssq[:, :], AF.Ln, bias=epsc))
    t_rstd = c.s(nc.scalar.activation(rstd[:, :], rstd[:, :], AF.Exp,
                                      scale=halfneg))
    rstdc = c.sb([128, NT])
    sv = c.sb([128, NT])
    t_rstd.wait(nc.vector)
    c.v(nc.vector.tensor_mul(sv[:, :], ssq[:, :], rstd[:, :]))
    t_sv = c.v(nc.vector.tensor_mul(sv[:, :], sv[:, :], rstd[:, :]))
    t_sv.wait(nc.scalar)
    c.s(nc.scalar.activation(sv[:, :], sv[:, :], AF.Ln, bias=epsc))
    t_r2 = c.s(nc.scalar.activation(sv[:, :], sv[:, :], AF.Exp,
                                    scale=halfneg))
    t_r2.wait(nc.vector)
    t_rc = c.v(nc.vector.tensor_mul(rstdc[:, :], rstd[:, :], sv[:, :]))
    t_nm = c.v(nc.vector.scalar_tensor_tensor(
        nmr[:, :], mcol[:, :], -1.0, rstdc[:, :], ALU.mult, ALU.mult))
    nmr1 = c.sb([128, NT])
    t_nm1 = c.v(nc.vector.scalar_tensor_tensor(
        nmr1[:, :], mcol[:, :], -1.0, rstd[:, :], ALU.mult, ALU.mult))
    for t in range(NT):
        xt = xr.acquire(nc.sync)
        s_ld = c.dsem("lnx")
        s_ld.inc(nc.sync.dma_start(xt[:, :], x3[t, :, :]), 16)
        st1 = stg.acquire(nc.scalar)
        s_ld.wait(nc.scalar)
        t_nm1.wait(nc.scalar)
        tok1 = c.s(nc.scalar.activation(st1[:, :], xt[:, :], AF.Identity,
                                        scale=rstd[:, t:t + 1],
                                        bias=nmr1[:, t:t + 1]))
        xh_ds.put(t, tok1, st1[:, :])
        stg.release(Tok(xh_ds.s_store.h, xh_ds.s_store.v))
        st2 = stg.acquire(nc.scalar)
        t_nm.wait(nc.scalar)
        tok2 = c.s(nc.scalar.activation(sq_scr[:, :], xt[:, :], AF.Identity,
                                        scale=rstdc[:, t:t + 1],
                                        bias=nmr[:, t:t + 1]))
        xr.release(tok2)
        tok2.wait(nc.vector)
        c.v(nc.vector.tensor_mul(sq_scr[:, :], sq_scr[:, :], g_bc[:, :]))
        tok3 = c.v(nc.vector.tensor_add(st2[:, :], sq_scr[:, :], b_bc[:, :]))
        xn_ds.put(t, tok3, st2[:, :])
        stg.release(Tok(xn_ds.s_store.h, xn_ds.s_store.v))
        tok3.wait(nc.scalar)

    if phase == 1:
        s1 = c.sem()
        xn_ds.s_store.wait(nc.gpsimd)
        s1.inc(nc.gpsimd.dma_start(out_hbm[:, :], xn_ds.hbm[:, :]), 16)
        s1.wait(nc.gpsimd)
        return s1
    xssm_hbm = c.dram([DI, T], BF16)
    z_hbm = c.dram([DI, T], BF16)
    s_xs = c.sem()
    s_zs = c.sem()
    sub = c.sub()
    cs = sub.__enter__()
    wtile = cs.sb([128, ND * DI], BF16)
    yblk = cs.sbring(2, [128, ND * 512], BF16)
    pps = cs.ring(2, [128, 512])
    pstg = cs.sbring(2, [128, 512], BF16)
    tok_w = None
    for half, (cols, hbm, s_st) in enumerate(
            (((0, DI), xssm_hbm, s_xs), ((DI, 2 * DI), z_hbm, s_zs))):
        if tok_w is not None:
            tok_w.wait(nc.gpsimd)
        w_ = emit_load_weight(cs, io["in_proj"][:, cols[0]:cols[1]], D, DI,
                              tile=wtile)
        for nb in range(4):
            n0 = nb * 512
            yb = yblk.acquire(nc.sync)
            s_y = c.dsem("kvblk")
            xn_ds.load_fm(yb, n0, n0 + 512, s_y)
            for ko in range(NKI):
                pt = pps.acquire(nc.tensor)
                w_.sem.wait(nc.tensor)
                s_y.wait(nc.tensor)
                t_mm = c.mark(mm_group(nc, pt[:, 0:512],
                              [(w_.lhsT(ki, ko * 128, (ko + 1) * 128),
                                yb[:, ki * 512:(ki + 1) * 512])
                               for ki in range(ND)]), "tensor")
                tok_w = t_mm
                ps_ = pstg.acquire(nc.scalar)
                t_mm.wait(nc.scalar)
                t_ev = c.s(nc.scalar.copy(ps_[:, :], pt[:, 0:512]))
                pps.release(t_ev)
                t_ev.wait(nc.sync)
                s_st.inc(nc.sync.dma_start(
                    hbm[ko * 128:(ko + 1) * 128, n0:n0 + 512], ps_[:, :]), 16)
                pstg.release(Tok(s_st.h, s_st.v))
            yblk.release(t_mm)
    sub.__exit__(None, None, None)

    if phase == 2:
        s1 = c.sem()
        s_xs.wait(nc.gpsimd)
        s1.inc(nc.gpsimd.dma_start(
            out_hbm[:, :].rearrange("(a p) t -> p a t", p=128),
            xssm_hbm[:, 1024:2048].rearrange("(a p) t -> p a t", p=128)), 16)
        s1.wait(nc.gpsimd)
        return s1
    conv_hbm = c.dram([DI, T], BF16)
    xpT = c.sb([128, T], BF16)
    s_cv = c.sem()
    sub = c.sub()
    cs = sub.__enter__()
    cwc, s_cw = emit_load_f32(cs, io["conv_wc"], 128, NKI * 4)
    cbc, s_cb = emit_load_f32(cs, io["conv_bc"], 128, NKI)
    wxp = emit_load_weight(cs, io["x_proj"], DI, 96)
    xin = cs.sbring(2, [128, T + 4], BF16)
    cscr = cs.sb([128, T])
    cbf = cs.sbring(2, [128, T], BF16)
    xpps = [cs.ps([128, 512]) for _ in range(4)]
    t_cvl = None
    for ko in range(NKI):
        xi = xin.acquire(nc.sync)
        s_xl = c.dsemm if False else c.dsem("cvx")
        s_xs.wait(nc.sync)
        s_xl.inc(nc.sync.dma_start(xi[:, 4:4 + T],
                                   xssm_hbm[ko * 128:(ko + 1) * 128, :]), 16)
        s_xl.wait(nc.vector)
        t_z = c.v(nc.vector.memset(xi[:, 0:4], 0.0))
        s_cw.wait(nc.vector)
        s_cb.wait(nc.vector)
        t_cv = None
        for k in range(4):
            if k == 0:
                c.v(nc.vector.tensor_scalar_mul(
                    cscr[:, :], xi[:, 1 + k: 1 + k + T],
                    cwc[:, ko * 4 + k: ko * 4 + k + 1]))
            else:
                c.v(nc.vector.scalar_tensor_tensor(
                    cscr[:, :], xi[:, 1 + k: 1 + k + T],
                    cwc[:, ko * 4 + k: ko * 4 + k + 1], cscr[:, :],
                    ALU.mult, ALU.add))
        t_cv = c.v(nc.vector.tensor_scalar_add(cscr[:, :], cscr[:, :],
                                               cbc[:, ko:ko + 1]))
        cb = cbf.acquire(nc.scalar)
        t_cv.wait(nc.scalar)
        t_si = c.s(nc.scalar.activation(cb[:, :], cscr[:, :], AF.Silu))
        t_si.wait(nc.sync)
        s_cv.inc(nc.sync.dma_start(conv_hbm[ko * 128:(ko + 1) * 128, :],
                                   cb[:, :]), 16)
        wxp.sem.wait(nc.tensor)
        t_si.wait(nc.tensor)
        t_xp = None
        for nb in range(4):
            mm = nc.tensor.matmul(xpps[nb][0:96, 0:512],
                                  wxp.lhsT(ko, 0, 96),
                                  cb[:, nb * 512:(nb + 1) * 512],
                                  start=(ko == 0), stop=(ko == NKI - 1))
            t_xp = c.mark(mm, "tensor")
        cbf.release(Multi([t_xp, Tok(s_cv.h, s_cv.v)]))
        xin.release(t_cv)
        t_cvl = t_xp
    t_xpe = None
    for nb in range(4):
        t_cvl.wait(nc.scalar)
        t_xpe = c.s(nc.scalar.copy(xpT[0:96, nb * 512:(nb + 1) * 512],
                                   xpps[nb][0:96, 0:512]))
    sub.__exit__(None, None, None)

    if phase == 3:
        s1 = c.sem()
        s_cv.wait(nc.gpsimd)
        t_xpe.wait(nc.gpsimd)
        s1.inc(nc.gpsimd.dma_start(
            out_hbm[0:1024, :].rearrange("(a p) t -> p a t", p=128),
            conv_hbm[0:1024, 1024:2048].rearrange("(a p) t -> p a t", p=128)), 16)
        s1.inc(nc.gpsimd.dma_start(
            out_hbm[1024:1280, :].rearrange("(a p) t -> p a t", p=128),
            xpT[:, :].rearrange("p (a t) -> p a t", a=2)), 16)
        s1.wait(nc.gpsimd)
        return s1
    bc_hbm = c.dram([32, T], BF16)
    s_bc = c.sem()
    t_xpe.wait(nc.sync)
    s_bc.inc(nc.sync.dma_start(bc_hbm[:, :], xpT[64:96, :]), 16)

    y_hbm = c.dram([DI, T], BF16)
    s_yh = c.sem()
    Acol = c.sb([128, DS])
    states = c.sb([128, NKI * DS])
    logc, s_lA = emit_load_f32(c, io["logA_c"], 128, NKI * DS)
    dpc, s_dp = emit_load_f32(c, io["Dp_c"], 128, NKI)
    dtbc, s_dtb = emit_load_bf16_row(c, io["dt_bias"], DI)
    t_st0 = c.v(nc.vector.memset(states[:, :], 0.0))
    sub = c.sub()
    cs = sub.__enter__()
    dtp = cs.sb([64, DI], BF16)
    s_dtp = c.sem()
    s_dtp.inc(nc.gpsimd.dma_start(dtp[:, :], io["dt_proj"][:, :]), 16)
    onesw = cs.sb([1, 512], BF16)
    t_onw = c.v(nc.vector.memset(onesw[0:1, :], 1.0))
    B_bc = [cs.sb([128, TH], BF16) for _ in range(DS)]
    C_bc = [cs.sb([128, TH], BF16) for _ in range(DS)]
    rowst = cs.sbring(2, [1, TH], BF16)
    bcps = cs.ring(2, [128, 512], F32)
    deltaC = cs.sb([128, TH])
    dxC = cs.sb([128, TH])
    dAs = cs.sb([128, TH])
    us = cs.sb([128, TH])
    sco = cs.sb([128, TH])
    ytile = cs.sb([128, TH])
    tmps = cs.sb([128, TH])
    convC = cs.sbring(2, [128, TH], BF16)
    zC = cs.sbring(2, [128, TH], BF16)
    ystg = cs.sbring(2, [128, TH], BF16)
    dps = cs.ring(2, [128, 512])
    t_sc = t_ya = t_u = t_ye = None
    halves = [0] if phase == 5 else range(2)
    for half in halves:
        h0 = half * TH
        if half == 1:
            c.tip("pool").wait(nc.vector)
        for s_i in range(2 * DS):
            rw = rowst.acquire(nc.sync)
            s_rl = c.dsem("bcrow")
            s_bc.wait(nc.sync)
            s_rl.inc(nc.sync.dma_start(rw[0:1, :],
                                       bc_hbm[s_i:s_i + 1, h0:h0 + TH]), 16)
            dst = (B_bc if s_i < DS else C_bc)[s_i % DS]
            t_b = None
            for o0 in range(0, TH, 512):
                pt = bcps.acquire(nc.tensor)
                s_rl.wait(nc.tensor)
                t_onw.wait(nc.tensor)
                mm = nc.tensor.matmul(pt[:, 0:512], onesw[0:1, 0:128],
                                      rw[0:1, o0:o0 + 512],
                                      start=True, stop=True)
                t_mmb = c.mark(mm, "tensor")
                t_mmb.wait(nc.vector)
                t_b = c.v(nc.vector.tensor_copy(dst[:, o0:o0 + 512],
                                                pt[:, 0:512]))
                bcps.release(t_b)
            rowst.release(t_b)
        t_bcast = t_b
        for ko in ([0] if phase == 5 else range(NKI)):
            cvt = convC.acquire(nc.sync)
            s_cl = c.dsem("cvl")
            s_cv.wait(nc.sync)
            s_cl.inc(nc.sync.dma_start(
                cvt[:, :], conv_hbm[ko * 128:(ko + 1) * 128, h0:h0 + TH]), 16)
            zt = zC.acquire(nc.sync)
            s_zl = c.dsem("zl")
            s_zs.wait(nc.sync)
            s_zl.inc(nc.sync.dma_start(
                zt[:, :], z_hbm[ko * 128:(ko + 1) * 128, h0:h0 + TH]), 16)
            t_dl = None
            for o0 in range(0, TH, 512):
                pt = dps.acquire(nc.tensor)
                s_dtp.wait(nc.tensor)
                t_xpe.wait(nc.tensor)
                s_dtb.wait(nc.tensor)
                t_onw.wait(nc.tensor)
                pairs = [(dtp[:, ko * 128:(ko + 1) * 128],
                          xpT[0:64, h0 + o0: h0 + o0 + 512])]
                pairs.append((dtbc[0:1, ko * 128:(ko + 1) * 128],
                              onesw[0:1, 0:512]))
                t_mmd = c.mark(mm_group(nc, pt[:, 0:512], pairs), "tensor")
                t_mmd.wait(nc.scalar)
                c.wait_const(nc.scalar)
                if t_ye is not None and o0 == 0:
                    t_ye.wait(nc.scalar)
                c.s(nc.scalar.activation(dAs[:, o0:o0 + 512], pt[:, 0:512],
                                         AF.Abs))
                c.s(nc.scalar.activation(dAs[:, o0:o0 + 512],
                                         dAs[:, o0:o0 + 512], AF.Exp,
                                         scale=c.constcol(-1.0)))
                c.s(nc.scalar.activation(dAs[:, o0:o0 + 512],
                                         dAs[:, o0:o0 + 512], AF.Ln,
                                         bias=c.constcol(1.0)))
                t_rl = c.s(nc.scalar.activation(deltaC[:, o0:o0 + 512],
                                                pt[:, 0:512], AF.Relu))
                t_rl.wait(nc.vector)
                t_dl = c.v(nc.vector.tensor_add(deltaC[:, o0:o0 + 512],
                                                deltaC[:, o0:o0 + 512],
                                                dAs[:, o0:o0 + 512]))
                dps.release(t_dl)
            s_cl.wait(nc.vector)
            if t_u is not None:
                t_u.wait(nc.vector)
            t_dx = c.v(nc.vector.tensor_mul(dxC[:, :], deltaC[:, :],
                                            cvt[:, :]))
            s_lA.wait(nc.scalar)
            t_A = c.s(nc.scalar.activation(
                Acol[:, :], logc[:, ko * DS:(ko + 1) * DS], AF.Exp))
            t_A.wait(nc.vector)
            t_An = c.v(nc.vector.tensor_scalar_mul(Acol[:, :], Acol[:, :],
                                                   -1.0))
            t_y0 = c.v(nc.vector.memset(ytile[:, :], 0.0))
            for s_i in range(DS):
                t_An.wait(nc.scalar)
                t_dl.wait(nc.scalar)
                if t_sc is not None:
                    t_sc.wait(nc.scalar)
                t_dA = c.s(nc.scalar.activation(dAs[:, :], deltaC[:, :],
                                                AF.Exp,
                                                scale=Acol[:, s_i:s_i + 1]))
                if t_sc is not None:
                    t_sc.wait(nc.gpsimd)
                t_dx.wait(nc.gpsimd)
                if s_i == 0:
                    t_bcast.wait(nc.gpsimd)
                t_u = c.g(nc.gpsimd.tensor_mul(us[:, :], dxC[:, :],
                                               B_bc[s_i][:, :]))
                t_dA.wait(nc.vector)
                t_u.wait(nc.vector)
                t_sc = c.v(nc.vector.tensor_tensor_scan(
                    sco[:, :], dAs[:, :], us[:, :],
                    states[:, ko * DS + s_i: ko * DS + s_i + 1],
                    ALU.mult, ALU.add))
                c.v(nc.vector.tensor_copy(
                    states[:, ko * DS + s_i: ko * DS + s_i + 1],
                    sco[:, TH - 1:TH]))
                if t_ya is not None:
                    t_ya.wait(nc.gpsimd)
                t_sc.wait(nc.gpsimd)
                t_p = c.g(nc.gpsimd.tensor_mul(tmps[:, :], sco[:, :],
                                               C_bc[s_i][:, :]))
                t_p.wait(nc.vector)
                t_ya = c.v(nc.vector.tensor_add(ytile[:, :], ytile[:, :],
                                                tmps[:, :]))
            s_dp.wait(nc.vector)
            c.v(nc.vector.scalar_tensor_tensor(
                ytile[:, :], cvt[:, :], dpc[:, ko:ko + 1], ytile[:, :],
                ALU.mult, ALU.add))
            convC.release(c.tip("vector"))
            s_zl.wait(nc.scalar)
            t_sc.wait(nc.scalar)
            t_zs = c.s(nc.scalar.activation(dAs[:, :], zt[:, :], AF.Silu))
            zC.release(t_zs)
            t_zs.wait(nc.vector)
            ys = ystg.acquire(nc.vector)
            t_ye = c.v(nc.vector.tensor_mul(ys[:, :], ytile[:, :],
                                            dAs[:, :]))
            t_ye.wait(nc.sync)
            s_yh.inc(nc.sync.dma_start(
                y_hbm[ko * 128:(ko + 1) * 128, h0:h0 + TH], ys[:, :]), 16)
            ystg.release(Tok(s_yh.h, s_yh.v))
    if phase == 5:
        s1 = c.sem()
        t_ye.wait(nc.gpsimd)
        t_sc.wait(nc.gpsimd)
        for j, buf in enumerate([deltaC, dxC, ytile, sco, B_bc[0][:, :],
                                 C_bc[0][:, :], us, dAs]):
            s1.inc(nc.gpsimd.dma_start(out_hbm[j * 128:(j + 1) * 128, :],
                                       buf[:, :] if j < 8 else buf), 16)
        s1.inc(nc.gpsimd.dma_start(out_hbm[1024:1152, 0:256], states[:, :]),
               16)
        s1.wait(nc.gpsimd)
        sub.__exit__(None, None, None)
        return s1
    sub.__exit__(None, None, None)

    if phase == 4:
        s1 = c.sem()
        s_yh.wait(nc.gpsimd)
        s1.inc(nc.gpsimd.dma_start(
            out_hbm[:, :].rearrange("(a p) t -> p a t", p=128),
            y_hbm[0:2048, 1024:2048].rearrange("(a p) t -> p a t", p=128)), 16)
        s1.wait(nc.gpsimd)
        return s1
    oscr = c.sb([128, D])
    sub = c.sub()
    cs = sub.__enter__()
    wo = emit_load_weight(cs, io["out_proj"], DI, D)
    ylhs = cs.sbring(2, [128, NKI * 128], BF16)
    ops = cs.ring(2, [128, D])
    xrr = cs.sbring(2, [128, D], F32)
    hrr = cs.sbring(2, [128, D], BF16)
    o3 = out_hbm.rearrange("(t p) f -> t p f", p=128)
    s_out = c.sem()
    y4 = y_hbm.rearrange("(ki p) t -> p ki t", p=128)
    for t in range(NT):
        yl = ylhs.acquire(nc.sync)
        s_yl = c.dsem("ylb")
        s_yh.wait(nc.sync)
        s_yl.inc(nc.sync.dma_start(
            yl[:, :].rearrange("p (ki n) -> p ki n", ki=NKI),
            y4[:, :, t * 128:(t + 1) * 128]), 16)
        pt = ops.acquire(nc.tensor)
        wo.sem.wait(nc.tensor)
        s_yl.wait(nc.tensor)
        for o0 in (0, 512):
            mm = mm_group(nc, pt[:, o0:o0 + 512],
                          [(yl[:, ki * 128:(ki + 1) * 128],
                            wo.lhsT(ki, o0, o0 + 512)) for ki in range(NKI)])
        t_mm = c.mark(mm, "tensor")
        ylhs.release(t_mm)
        xt = xrr.acquire(nc.sync)
        s_lx = c.dsem("xw")
        s_lx.inc(nc.sync.dma_start(xt[:, :], x3[t, :, :]), 16)
        ht = hrr.acquire(nc.sync)
        s_lh = c.dsem("hw")
        xh_ds.s_store.wait(nc.sync)
        s_lh.inc(nc.sync.dma_start(ht[:, :], xh_ds.h3[t, :, :]), 16)
        t_mm.wait(nc.vector)
        s_lx.wait(nc.vector)
        s_lh.wait(nc.vector)
        c.v(nc.vector.tensor_add(oscr[:, :], xt[:, :], ht[:, :]))
        tok_o = c.v(nc.vector.tensor_add(oscr[:, :], oscr[:, :], pt[:, 0:D]))
        ops.release(tok_o)
        xrr.release(tok_o)
        hrr.release(tok_o)
        tok_o.wait(nc.sync)
        s_out.inc(nc.sync.dma_start(o3[t, :, :], oscr[:, :]), 16)
        Tok(s_out.h, s_out.v).wait(nc.vector)
    sub.__exit__(None, None, None)
    s_out.wait(nc.sync)
    return s_out


def build_l2attnb_program():
    nc = bass.Bass("TRN2", target_bir_lowering=False)
    stack = ExitStack()
    c = Ctx(nc, stack)
    dtm = lambda n, sh_: nc.dram_tensor(n, sh_, F32, kind="ExternalInput").ap()
    io = {
        "norm_g": dtm("norm_g", [D]),
        "norm_b": dtm("norm_b", [D]),
        "in_proj": dtm("in_proj", [D, 4096]),
        "conv_wc": dtm("conv_wc", [128, 64]),
        "conv_bc": dtm("conv_bc", [128, 16]),
        "x_proj": dtm("x_proj", [2048, 96]),
        "dt_proj": dtm("dt_proj", [64, 2048]),
        "dt_bias": dtm("dt_bias", [2048]),
        "logA_c": dtm("logA_c", [128, 256]),
        "Dp_c": dtm("Dp_c", [128, 16]),
        "out_proj": dtm("out_proj", [2048, D]),
    }
    x_h = dtm("x", [T, D])
    out_h = nc.dram_tensor("out", [T, D], F32, kind="ExternalOutput").ap()
    import os
    with stack:
        emit_l2_attn_b(c, x_h, out_h, io,
                       phase=int(os.environ.get("MBPH", "9")))
    return nc



def emit_l3c_masks(c):
    nc = c.nc
    msl = c.sb([128, 128])
    msu = c.sb([128, 128])
    mkeepU = c.sb([128, 32])
    ident32 = c.sb([128, 32], BF16)
    iden128 = c.sb([128, 128])
    iscr = c.sb([128, 128])
    t0 = c.v(nc.vector.memset(msl[:, :], 0.0))
    t1 = c.v(nc.vector.memset(msu[:, :], 0.0))
    toks = []
    for b in range(4):
        r = slice(32 * b, 32 * b + 32)
        t_i = c.g(nc.gpsimd.iota(iscr[r, 0:32], pattern=[[1, 32]], base=0,
                                 channel_multiplier=-1,
                                 allow_small_or_imprecise_dtypes=True))
        t_i.wait(nc.vector)
        c.v(nc.vector.tensor_single_scalar(
            msl[r, 32 * b:32 * b + 32], iscr[r, 0:32], 0.0, ALU.is_lt))
        c.v(nc.vector.tensor_single_scalar(
            msu[r, 32 * b:32 * b + 32], iscr[r, 0:32], 0.0, ALU.is_gt))
        c.v(nc.vector.tensor_single_scalar(
            mkeepU[r, 0:32], iscr[r, 0:32], 0.0, ALU.is_ge))
        toks.append(c.v(nc.vector.tensor_single_scalar(
            ident32[r, 0:32], iscr[r, 0:32], 0.0, ALU.is_equal)))
    c.v(nc.vector.tensor_scalar_mul(msl[:, :], msl[:, :], -1.0))
    t_msu = c.v(nc.vector.tensor_scalar_mul(msu[:, :], msu[:, :], -1.0))
    c.tip("vector").wait(nc.gpsimd)
    t_ii = c.g(nc.gpsimd.iota(iscr[:, :], pattern=[[1, 128]], base=0,
                              channel_multiplier=-1,
                              allow_small_or_imprecise_dtypes=True))
    t_ii.wait(nc.vector)
    t_id = c.v(nc.vector.tensor_single_scalar(
        iden128[:, :], iscr[:, :], 0.0, ALU.is_equal))
    return {"msl": msl, "msu": msu, "mkeepU": mkeepU, "ident32": ident32,
            "iden128": iden128, "tok": Multi([t_msu, t_id] + toks)}


def emit_proj_conv_store(c, cs, xnT, s_xn, w_hbm, cw_hbm, dst_hbm, *,
                         norm_rows=None, pf=None, cacc=None, psr=None,
                         wtile=None):
    import os
    nc = c.nc
    c.tip("tensor").wait(nc.gpsimd)
    w = emit_load_weight(cs, w_hbm, D, D, tile=wtile)
    cw, s_cw = emit_load_f32(cs, cw_hbm, 128, 32)
    t_pr = emit_proj_fm(c, xnT, s_xn, w, pf, psr=psr)
    if os.environ.get("L3B_RAW"):
        return [t_pr] * ND
    t_cv = []
    for ki in range(ND):
        b0 = ki * T
        t_pr.wait(nc.vector)
        s_cw.wait(nc.vector)
        if t_cv:
            t_cv[-1].wait(nc.vector)
        c.v(nc.vector.tensor_scalar_mul(
            cacc[:, :], pf[:, b0:b0 + T], cw[:, ki * 4 + 3:ki * 4 + 4]))
        for s in range(1, 4):
            c.v(nc.vector.scalar_tensor_tensor(
                cacc[:, s:T], pf[:, b0:b0 + T - s],
                cw[:, ki * 4 + 3 - s:ki * 4 + 4 - s], cacc[:, s:T],
                ALU.mult, ALU.add))
        t_c = c.tip("vector")
        t_c.wait(nc.scalar)
        t_cv.append(c.s(nc.scalar.activation(pf[:, b0:b0 + T], cacc[:, :],
                                             AF.Silu)))
    return t_cv


def emit_l3_attn_b(c, x_hbm, out_hbm, io):
    import os
    ph = int(os.environ.get("L3B_PH", "9"))
    nc = c.nc
    H, DH, NG = 4, 256, 16
    mk = emit_l3c_masks(c)
    onw_bc = c.sb([128, DH])
    oT_big = c.sb([128, ND * T], BF16)
    ds, rstd, nmr, t_cols = emit_ln_stream(c, x_hbm)
    q_hbm = c.dram([D, T], BF16)
    k_hbm = c.dram([D, T], BF16)
    v_hbm = c.dram([D, T], BF16)
    s_qs, s_ks, s_vs = c.sem(), c.sem(), c.sem()
    onr, s_onr = emit_load_bf16_row(c, io["onorm_w"], DH)
    onf = c.sb([1, DH])
    s_onr.wait(nc.vector)
    t_onf = c.v(nc.vector.tensor_copy(onf[0:1, :], onr[0:1, :]))
    sub = c.sub()
    cs = sub.__enter__()
    t_ob = emit_bcast_rows(c, cs, [(onf[0:1, :], onw_bc, t_onf)])
    xnT = cs.sb([128, ND * T], BF16)
    s_xn = c.sem()
    ds.load_fm(xnT, 0, T, s_xn)
    pf = cs.sb([128, ND * T], BF16)
    cacc = cs.sb([128, T])
    psr = cs.ring(2, [128, 512])
    rb = cs.sb([128, T])
    sqb = cs.sb([128, 2 * T], BF16)
    onesf = cs.sb([128, 1], BF16)
    t_of = c.v(nc.vector.memset(onesf[:, 0:1], 1.0))
    rn_row = cs.sb([1, T])
    qstg = cs.sbring(2, [128, T], BF16)
    wtile = cs.sb([128, ND * D], BF16)
    LN16 = -2.77258872223978
    for pname, whb, cwhb, dsth, s_st, normq in (
            ("q", io["Wq"], io["conv_wq_c"], q_hbm, s_qs, 1),
            ("k", io["Wk"], io["conv_wk_c"], k_hbm, s_ks, 2),
            ("v", io["Wv"], io["conv_wv_c"], v_hbm, s_vs, 0)):
        t_cv = emit_proj_conv_store(c, cs, xnT, s_xn, whb, cwhb, dsth,
                                    pf=pf, cacc=cacc, psr=psr, wtile=wtile)
        if not normq:
            for ki in range(ND):
                t_cv[ki].wait(nc.sync)
                s_st.inc(nc.sync.dma_start(
                    dsth[ki * 128:(ki + 1) * 128, :],
                    pf[:, ki * T:(ki + 1) * T]), 16)
            continue
        for h in range(H):
            for d_ in range(2):
                ki = 2 * h + d_
                t_cv[ki].wait(nc.scalar)
                c.tip("tensor").wait(nc.scalar)
                c.s(nc.scalar.activation(sqb[:, d_ * T:(d_ + 1) * T],
                                         pf[:, ki * T:(ki + 1) * T],
                                         AF.Square))
            t_sq = c.tip("scalar")
            for n0 in range(0, T, 256):
                pt = psr.acquire(nc.tensor)
                t_sq.wait(nc.tensor)
                t_of.wait(nc.tensor)
                for d_ in range(2):
                    mm = nc.tensor.matmul(
                        pt[0:1, 0:256], onesf[:, 0:1],
                        sqb[:, d_ * T + n0:d_ * T + n0 + 256],
                        start=(d_ == 0), stop=(d_ == 1))
                t_mm = c.mark(mm, "tensor")
                t_mm.wait(nc.vector)
                c.tip("tensor").wait(nc.vector)
                t_ev = c.v(nc.vector.tensor_copy(
                    rn_row[0:1, n0:n0 + 256], pt[0:1, 0:256]))
                psr.release(t_ev)
            t_ev.wait(nc.scalar)
            c.wait_const(nc.scalar)
            c.s(nc.scalar.activation(rn_row[0:1, :], rn_row[0:1, :],
                                     AF.Ln, bias=c.constcol(1e-6)[0:1, :]))
            t_rn = c.s(nc.scalar.activation(
                rn_row[0:1, :], rn_row[0:1, :], AF.Exp,
                scale=c.constcol(-0.5)[0:1, :],
                bias=c.constcol(LN16 if normq == 1 else 0.0)[0:1, :]))
            bsub = c.sub()
            bcs = bsub.__enter__()
            t_rb = emit_bcast_rows(c, bcs, [(rn_row[0:1, :], rb, t_rn)])
            bsub.__exit__(None, None, None)
            for d_ in range(2):
                ki = 2 * h + d_
                qs = qstg.acquire(nc.vector)
                t_rb.wait(nc.vector)
                t_n = c.v(nc.vector.tensor_mul(qs[:, :],
                                               pf[:, ki * T:(ki + 1) * T],
                                               rb[:, :]))
                t_n.wait(nc.sync)
                s_st.inc(nc.sync.dma_start(
                    dsth[ki * 128:(ki + 1) * 128, :], qs[:, :]), 16)
                qstg.release(Tok(s_st.h, s_st.v))
    cs.exit_waits += [s_qs, s_ks, s_vs]
    sub.__exit__(None, None, None)
    if ph == 1:
        s1 = c.sem()
        s_qs.wait(nc.gpsimd)
        s_ks.wait(nc.gpsimd)
        s_vs.wait(nc.gpsimd)
        s1.inc(nc.gpsimd.dma_start(
            out_hbm[0:1024, :].rearrange("(a p) t -> p a t", p=128),
            q_hbm[:, 0:1024].rearrange("(a p) t -> p a t", p=128)), 16)
        s1.inc(nc.gpsimd.dma_start(
            out_hbm[1024:2048, :].rearrange("(a p) t -> p a t", p=128),
            v_hbm[:, 1024:2048].rearrange("(a p) t -> p a t", p=128)), 16)
        s1.wait(nc.gpsimd)
        return s1
    sub = c.sub()
    cs = sub.__enter__()
    Sf = [[cs.sb([128, DH]) for _ in range(2)] for _ in range(H)]
    Sb = [[cs.sb([128, DH], BF16) for _ in range(2)] for _ in range(H)]
    uwp = cs.ps([128, DH])
    opp = cs.ps([128, DH])
    ptp = cs.ps([128, DH])
    pbp = cs.ps([128, DH])
    ltp = cs.ps([128, 32])
    tpb = cs.ps([128, 32], BF16)
    dSp = [cs.ps([128, DH]) for _ in range(2)]
    tok_uw = tok_oA = tok_oB = tok_pt = tok_pb = tok_lt = None
    tok_tp = None
    tok_dS = None
    qf = cs.sb([128, ND * 128], BF16)
    kf = cs.sb([128, ND * 128], BF16)
    ktok = [cs.sb([128, DH], BF16) for _ in range(H)]
    vtok = [cs.sb([128, DH], BF16) for _ in range(H)]
    ug = [cs.sb([128, DH]) for _ in range(H)]
    wtg = [cs.sb([128, DH], BF16) for _ in range(H)]
    Np = [cs.sb([128, 128]) for _ in range(5)]
    Ntp = [cs.sb([128, 128]) for _ in range(5)]
    Rm = cs.sb([128, 128])
    Rt = cs.sb([128, 128])
    Rb = cs.sb([128, 128], BF16)
    lt_sb = cs.sb([128, 32], BF16)
    un_sb = cs.sb([128, DH], BF16)
    o2_sb = cs.sb([128, DH])
    osc_sb = cs.sb([128, DH], BF16)
    rsn = cs.sb([128, 8])
    tSb = [None] * H
    have_S = False
    t_red_prev = None
    for g in range(NG):
        g0 = g * 128
        c.tip("tensor").wait(nc.sync)
        s_g = c.dsem("l3bg")
        s_qs.wait(nc.sync)
        s_ks.wait(nc.sync)
        s_vs.wait(nc.sync)
        s_g.inc(nc.sync.dma_start(
            qf[:, :].rearrange("p (ki t) -> p ki t", ki=ND),
            q_hbm.rearrange("(ki p) t -> p ki t", p=128)[:, :, g0:g0 + 128]),
            16)
        s_g.inc(nc.sync.dma_start(
            kf[:, :].rearrange("p (ki t) -> p ki t", ki=ND),
            k_hbm.rearrange("(ki p) t -> p ki t", p=128)[:, :, g0:g0 + 128]),
            16)
        for h in range(H):
            s_g.inc(nc.sync.dma_start_transpose(
                ktok[h][:, :], k_hbm[h * DH:(h + 1) * DH, g0:g0 + 128]), 16)
            s_g.inc(nc.sync.dma_start_transpose(
                vtok[h][:, :], v_hbm[h * DH:(h + 1) * DH, g0:g0 + 128]), 16)
        prep_lv = int(os.environ.get("L3B_PREP", "6"))
        hlist = [0] if ph == 6 else (range(H) if prep_lv >= 2 else [])
        for h in hlist:
            s_g.wait(nc.tensor)
            mk["tok"].wait(nc.tensor)
            if tok_pt is not None:
                tok_pt.wait(nc.tensor)
            for d_ in range(2):
                mm = nc.tensor.matmul(
                    ptp[:, 0:128], kf[:, (2 * h + d_) * 128:(2 * h + d_ + 1) * 128],
                    kf[:, (2 * h + d_) * 128:(2 * h + d_ + 1) * 128],
                    start=(d_ == 0), stop=(d_ == 1))
            t_mm = c.mark(mm, "tensor")
            t_mm.wait(nc.vector)
            c.tip("tensor").wait(nc.vector)
            c.v(nc.vector.tensor_mul(Np[0][:, :], ptp[:, 0:128], mk["msl"][:, :]))
            tok_pt = t_g = c.v(nc.vector.tensor_mul(Ntp[0][:, :], ptp[:, 0:128],
                                                    mk["msu"][:, :]))
            for j in (range(1, 5) if prep_lv >= 3 else []):
                t_g.wait(nc.tensor)
                tok_pt.wait(nc.tensor)
                t_a = c.mark(nc.tensor.matmul(ptp[:, 0:128], Ntp[j - 1][:, :],
                                              Np[j - 1][:, :],
                                              start=True, stop=True), "tensor")
                if tok_pb is not None:
                    tok_pb.wait(nc.tensor)
                t_b = c.mark(nc.tensor.matmul(pbp[:, 0:128], Np[j - 1][:, :],
                                              Ntp[j - 1][:, :],
                                              start=True, stop=True), "tensor")
                t_a.wait(nc.vector)
                c.tip("tensor").wait(nc.vector)
                tok_pt = c.v(nc.vector.tensor_copy(Np[j][:, :], ptp[:, 0:128]))
                t_b.wait(nc.vector)
                tok_pb = t_g = c.v(nc.vector.tensor_copy(Ntp[j][:, :],
                                                         pbp[:, 0:128]))
            if prep_lv < 4:
                continue
            c.tip("tensor").wait(nc.vector)
            c.v(nc.vector.tensor_add(Rm[:, :], Ntp[4][:, :], mk["iden128"][:, :]))
            t_ch = c.v(nc.vector.tensor_add(Rt[:, :], Np[4][:, :],
                                            mk["iden128"][:, :]))
            for j in (3, 2, 1, 0):
                t_ch.wait(nc.tensor)
                tok_pt.wait(nc.tensor)
                t_a = c.mark(nc.tensor.matmul(ptp[:, 0:128], Rt[:, :],
                                              Ntp[j][:, :],
                                              start=True, stop=True), "tensor")
                tok_pb.wait(nc.tensor)
                t_b = c.mark(nc.tensor.matmul(pbp[:, 0:128], Ntp[j][:, :],
                                              Rt[:, :],
                                              start=True, stop=True), "tensor")
                t_a.wait(nc.vector)
                t_b.wait(nc.vector)
                tok_pt = c.v(nc.vector.tensor_add(Rm[:, :], ptp[:, 0:128],
                                                  Rm[:, :]))
                tok_pb = t_ch = c.v(nc.vector.tensor_add(Rt[:, :],
                                                         pbp[:, 0:128],
                                                         Rt[:, :]))
            if prep_lv < 5:
                continue
            t_ch.wait(nc.vector)
            c.tip("tensor").wait(nc.vector)
            t_rb = c.v(nc.vector.tensor_copy(Rb[:, :], Rm[:, :]))
            t_rb.wait(nc.tensor)
            tok_pt.wait(nc.tensor)
            t_u = c.mark(nc.tensor.matmul(ptp[:, 0:DH], Rb[:, :],
                                          vtok[h][:, :], start=True,
                                          stop=True), "tensor")
            t_u.wait(nc.vector)
            c.tip("tensor").wait(nc.vector)
            tok_pt = t_ue = c.v(nc.vector.tensor_copy(ug[h][:, :],
                                                      ptp[:, 0:DH]))
            for d_ in (range(2) if prep_lv >= 6 else []):
                tok_pb.wait(nc.tensor)
                t_w = c.mark(nc.tensor.matmul(
                    pbp[:, 0:128], ktok[h][:, d_ * 128:(d_ + 1) * 128],
                    Rb[:, :], start=True, stop=True), "tensor")
                t_w.wait(nc.vector)
                tok_pb = t_we = c.v(nc.vector.tensor_copy(
                    wtg[h][:, d_ * 128:(d_ + 1) * 128], pbp[:, 0:128]))
        t_prep = c.tip("vector")
        if ph == 6 and g == 0:
            s1 = c.sem()
            t_prep.wait(nc.gpsimd)
            s1.inc(nc.gpsimd.dma_start(out_hbm[0:128, 0:DH], ug[0][:, :]), 16)
            s1.inc(nc.gpsimd.dma_start(out_hbm[128:256, 0:DH], wtg[0][:, :]),
                   16)
            s1.inc(nc.gpsimd.dma_start(out_hbm[256:384, 0:128], Rm[:, :]), 16)
            s1.inc(nc.gpsimd.dma_start(out_hbm[384:512, 0:DH], ktok[0][:, :]),
                   16)
            s1.inc(nc.gpsimd.dma_start(out_hbm[512:640, 0:DH], vtok[0][:, :]),
                   16)
            s1.inc(nc.gpsimd.dma_start(
                out_hbm[640:768, 0:128],
                kf[:, 0 * 128:1 * 128]), 16)
            for jj in range(5):
                s1.inc(nc.gpsimd.dma_start(
                    out_hbm[768 + jj * 128:896 + jj * 128, 0:128],
                    Np[jj][:, :]), 16)
                s1.inc(nc.gpsimd.dma_start(
                    out_hbm[768 + (5 + jj) * 128:896 + (5 + jj) * 128, 0:128],
                    Ntp[jj][:, :]), 16)
            s1.inc(nc.gpsimd.dma_start(out_hbm[0:128, 256:384],
                                       mk["msl"][:, :]), 16)
            s1.inc(nc.gpsimd.dma_start(out_hbm[128:256, 256:384],
                                       mk["msu"][:, :]), 16)
            s1.inc(nc.gpsimd.dma_start(out_hbm[256:384, 256:384],
                                       mk["iden128"][:, :]), 16)
            s1.inc(nc.gpsimd.dma_start(out_hbm[384:512, 256:288],
                                       mk["mkeepU"][:, 0:32]), 16)
            s1.wait(nc.gpsimd)
            sub.__exit__(None, None, None)
            return s1
        for ch in (range(4) if ph >= 3 else []):
            b = 32 * ch
            for h in range(H):
                t_prep.wait(nc.tensor)
                if tok_lt is not None:
                    tok_lt.wait(nc.tensor)
                for d_ in range(2):
                    col = (2 * h + d_) * 128 + b
                    mm = nc.tensor.matmul(
                        ltp[b:b + 32, 0:32], kf[:, col:col + 32],
                        qf[:, col:col + 32], start=(d_ == 0), stop=(d_ == 1),
                        tile_position=(0, b))
                t_lt = c.mark(mm, "tensor")
                t_lt.wait(nc.vector)
                c.tip("tensor").wait(nc.vector)
                tok_lt = t_ltc = c.v(nc.vector.tensor_mul(
                    lt_sb[b:b + 32, 0:32], ltp[b:b + 32, 0:32],
                    mk["mkeepU"][b:b + 32, 0:32]))
                if tok_oA is not None:
                    tok_oA.wait(nc.tensor)
                    tok_oB.wait(nc.tensor)
                if have_S:
                    if tok_uw is not None:
                        tok_uw.wait(nc.tensor)
                    if tSb[h] is not None:
                        tSb[h].wait(nc.tensor)
                    for d_ in range(2):
                        mm = nc.tensor.matmul(
                            uwp[b:b + 32, 0:DH],
                            wtg[h][:, d_ * 128 + b:d_ * 128 + b + 32],
                            Sb[h][d_][:, :], start=(d_ == 0), stop=(d_ == 1),
                            tile_position=(0, b))
                    t_ws = c.mark(mm, "tensor")
                    t_ws.wait(nc.vector)
                    c.tip("tensor").wait(nc.vector)
                    tok_uw = t_un = c.v(nc.vector.tensor_sub(
                        un_sb[b:b + 32, :], ug[h][b:b + 32, :],
                        uwp[b:b + 32, 0:DH]))
                    for d_ in range(2):
                        col = (2 * h + d_) * 128 + b
                        mm = nc.tensor.matmul(
                            opp[b:b + 32, 0:DH], qf[:, col:col + 32],
                            Sb[h][d_][:, :], start=(d_ == 0), stop=False,
                            tile_position=(0, b))
                    c.mark(mm, "tensor")
                else:
                    c.tip("tensor").wait(nc.vector)
                    t_un = c.v(nc.vector.tensor_copy(un_sb[b:b + 32, :],
                                                     ug[h][b:b + 32, :]))
                t_un.wait(nc.tensor)
                t_ltc.wait(nc.tensor)
                t_o = c.mark(nc.tensor.matmul(
                    opp[b:b + 32, 0:DH], lt_sb[b:b + 32, 0:32],
                    un_sb[b:b + 32, :], start=(not have_S), stop=True,
                    tile_position=(b, b)), "tensor")
                t_su = None
                for d_ in range(2):
                    if tSb[h] is not None:
                        tSb[h].wait(nc.tensor)
                    if tok_dS is not None:
                        tok_dS.wait(nc.tensor)
                    t_su = c.mark(nc.tensor.matmul(
                        dSp[d_][0:128, 0:DH],
                        ktok[h][b:b + 32, d_ * 128:(d_ + 1) * 128],
                        un_sb[b:b + 32, :], start=True, stop=True,
                        tile_position=(b, 0)), "tensor")
                t_su.wait(nc.vector)
                for d_ in range(2):
                    if have_S:
                        c.v(nc.vector.tensor_add(Sf[h][d_][:, :],
                                                 Sf[h][d_][:, :],
                                                 dSp[d_][0:128, 0:DH]))
                    else:
                        c.v(nc.vector.tensor_copy(Sf[h][d_][:, :],
                                                  dSp[d_][0:128, 0:DH]))
                tok_dS = c.tip("vector")
                c.v(nc.vector.tensor_copy(Sb[h][0][:, :], Sf[h][0][:, :]))
                tSb[h] = c.v(nc.vector.tensor_copy(Sb[h][1][:, :],
                                                   Sf[h][1][:, :]))
                t_o.wait(nc.scalar)
                if t_red_prev is not None:
                    t_red_prev.wait(nc.scalar)
                tok_oA = t_sq = c.s(nc.scalar.activation(
                    o2_sb[b:b + 32, :], opp[b:b + 32, 0:DH], AF.Square))
                t_sq.wait(nc.vector)
                t_red = c.v(nc.vector.reduce_sum(rsn[b:b + 32, 2 * h:2 * h + 1],
                                                 o2_sb[b:b + 32, :],
                                                 axis=AX.X))
                t_red_prev = t_red
                t_red.wait(nc.scalar)
                c.wait_const(nc.scalar)
                c.s(nc.scalar.activation(rsn[b:b + 32, 2 * h + 1:2 * h + 2],
                                         rsn[b:b + 32, 2 * h:2 * h + 1], AF.Ln,
                                         scale=c.constcol(1.0 / DH)[b:b + 32, :],
                                         bias=c.constcol(1e-5)[b:b + 32, :]))
                t_rs = c.s(nc.scalar.activation(
                    rsn[b:b + 32, 2 * h + 1:2 * h + 2],
                    rsn[b:b + 32, 2 * h + 1:2 * h + 2], AF.Exp,
                    scale=c.constcol(-0.5)[b:b + 32, :]))
                t_rs.wait(nc.vector)
                t_ob.wait(nc.vector)
                c.tip("tensor").wait(nc.vector)
                tok_oB = t_os = c.v(nc.vector.scalar_tensor_tensor(
                    osc_sb[b:b + 32, :], opp[b:b + 32, 0:DH],
                    rsn[b:b + 32, 2 * h + 1:2 * h + 2], onw_bc[b:b + 32, :],
                    ALU.mult, ALU.mult))
                for d_ in (range(2) if ph >= 4 else []):
                    t_os.wait(nc.tensor)
                    mk["tok"].wait(nc.tensor)
                    if tok_tp is not None:
                        tok_tp.wait(nc.tensor)
                    t_tp = c.mark(nc.tensor.transpose(
                        tpb[0:128, 0:32],
                        osc_sb[b:b + 32, d_ * 128:(d_ + 1) * 128],
                        mk["ident32"][b:b + 32, 0:32], tile_position=(b, 0)),
                        "tensor")
                    t_tp.wait(nc.vector)
                    tok_tp = t_ev = c.v(nc.vector.tensor_copy(
                        oT_big[:, (2 * h + d_) * T + g0 + b:
                               (2 * h + d_) * T + g0 + b + 32],
                        tpb[0:128, 0:32]))
            have_S = True
    t_scan = c.tip("vector")
    if ph == 5:
        s1 = c.sem()
        t_scan.wait(nc.gpsimd)
        s1.inc(nc.gpsimd.dma_start(
            out_hbm[:, :].rearrange("(a p) f -> p a f", p=128),
            oT_big[:, :].rearrange("p (a f) -> p a f", a=16)), 16)
        s1.wait(nc.gpsimd)
        sub.__exit__(None, None, None)
        return s1
    sub.__exit__(None, None, None)
    sub = c.sub()
    cs = sub.__enter__()
    wo = emit_load_weight(cs, io["Wo"], D, D)
    ops2 = cs.ring(2, [128, D])
    xrr = cs.sbring(2, [128, D], F32)
    ost = cs.sbring(2, [128, D], F32)
    s_out = c.sem()
    x3 = x_hbm.rearrange("(t p) f -> t p f", p=128)
    o3 = out_hbm.rearrange("(t p) f -> t p f", p=128)
    for t in range(NT):
        xr = xrr.acquire(nc.sync)
        s_xl = c.dsem("l3bx")
        s_xl.inc(nc.sync.dma_start(xr[:, :], x3[t, :, :]), 16)
        pt = ops2.acquire(nc.tensor)
        wo.sem.wait(nc.tensor)
        t_scan.wait(nc.tensor)
        for o0 in range(0, D, 512):
            mm = mm_group(nc, pt[:, o0:o0 + 512],
                          [(oT_big[:, ki * T + t * 128:ki * T + (t + 1) * 128],
                            wo.lhsT(ki, o0, o0 + 512)) for ki in range(ND)])
        t_mm = c.mark(mm, "tensor")
        ot = ost.acquire(nc.vector)
        t_mm.wait(nc.vector)
        s_xl.wait(nc.vector)
        t_ad = c.v(nc.vector.tensor_add(ot[:, :], pt[:, 0:D], xr[:, :]))
        ops2.release(t_ad)
        xrr.release(t_ad)
        t_ad.wait(nc.sync)
        s_out.inc(nc.sync.dma_start(o3[t, :, :], ot[:, :]), 16)
        ost.release(Tok(s_out.h, s_out.v))
    sub.__exit__(None, None, None)
    s_out.wait(nc.gpsimd)
    return s_out


def build_l3attnb_program():
    nc = bass.Bass("TRN2", target_bir_lowering=False)
    stack = ExitStack()
    c = Ctx(nc, stack)
    dt = lambda n, sh: nc.dram_tensor(n, sh, F32, kind="ExternalInput").ap()
    io = {
        "Wq": dt("Wq", [D, D]), "Wk": dt("Wk", [D, D]), "Wv": dt("Wv", [D, D]),
        "conv_wq_c": dt("conv_wq_c", [128, 32]),
        "conv_wk_c": dt("conv_wk_c", [128, 32]),
        "conv_wv_c": dt("conv_wv_c", [128, 32]),
        "onorm_w": dt("onorm_w", [256]),
        "Wo": dt("Wo", [D, D]),
    }
    x_h = dt("x", [T, D])
    out_h = nc.dram_tensor("out", [T, D], F32, kind="ExternalOutput").ap()
    with stack:
        emit_l3_attn_b(c, x_h, out_h, io)
    return nc



def emit_fir_fm(c, cs, pf, cacc, fw_c, s_fw, dst_hbm, kk, t_cv, stg):
    nc = c.nc
    s_st = c.sem()
    t_pc = None
    for ki in range(ND):
        b0 = ki * T
        s_fw.wait(nc.vector)
        t_cv[-1].wait(nc.vector)
        if t_pc is not None:
            t_pc.wait(nc.vector)
        c.v(nc.vector.tensor_scalar_mul(
            cacc[:, :], pf[:, b0:b0 + T], fw_c[:, ki * kk + kk - 1:ki * kk + kk]))
        for s_ in range(1, kk):
            c.v(nc.vector.scalar_tensor_tensor(
                cacc[:, s_:T], pf[:, b0:b0 + T - s_],
                fw_c[:, ki * kk + kk - 1 - s_:ki * kk + kk - s_], cacc[:, s_:T],
                ALU.mult, ALU.add))
        sg = stg.acquire(nc.vector)
        t_pc = c.v(nc.vector.tensor_copy(sg[:, :], cacc[:, :]))
        t_pc.wait(nc.sync)
        s_st.inc(nc.sync.dma_start(dst_hbm[ki * 128:(ki + 1) * 128, :],
                                   sg[:, :]), 16)
        stg.release(Tok(s_st.h, s_st.v))
    return s_st


def emit_l3_attn_a(c, x_hbm, out_hbm, io):
    import os
    ph = int(os.environ.get("L3A_PH", "9"))
    nc = c.nc
    H, DH, NG = 4, 256, 16
    mk = emit_l3c_masks(c)
    identB = c.sb([128, 128], BF16)
    mk["tok"].wait(nc.vector)
    t_idb = c.v(nc.vector.tensor_copy(identB[:, :], mk["iden128"][:, :]))
    onw_bc = c.sb([128, DH])
    oT_big = c.sb([128, ND * T], BF16)
    gate_tok = c.sb([128, NT * 16])
    ds, rstd, nmr, t_cols = emit_ln_stream(c, x_hbm)
    q_hbm = c.dram([D, T], BF16)
    k_hbm = c.dram([D, T], BF16)
    v_hbm = c.dram([D, T], BF16)
    fs_hbm = c.dram([D, T], BF16)
    fl_hbm = c.dram([D, T], BF16)
    beta_hbm = c.dram([T, H])
    s_qs, s_ks, s_vs = c.sem(), c.sem(), c.sem()
    s_bt = c.sem()
    onr, s_onr = emit_load_bf16_row(c, io["onorm_w"], DH)
    onf = c.sb([1, DH])
    s_onr.wait(nc.vector)
    t_onf = c.v(nc.vector.tensor_copy(onf[0:1, :], onr[0:1, :]))
    subA = c.sub()
    csA = subA.__enter__()
    t_ob = emit_bcast_rows(c, csA, [(onf[0:1, :], onw_bc, t_onf)])
    xnT = csA.sb([128, ND * T], BF16)
    s_xn = c.sem()
    ds.load_fm(xnT, 0, T, s_xn)
    sub = c.sub()
    cs = sub.__enter__()
    pf = cs.sb([128, ND * T], BF16)
    cacc = cs.sb([128, T])
    psr = cs.ring(2, [128, 512])
    rb = cs.sb([128, T])
    sqb = cs.sb([128, 2 * T], BF16)
    onesf = cs.sb([128, 1], BF16)
    t_of = c.v(nc.vector.memset(onesf[:, 0:1], 1.0))
    rn_row = cs.sb([1, T])
    qstg = cs.sbring(2, [128, T], BF16)
    wtile = cs.sb([128, ND * D], BF16)
    firs = [None, None]
    for pname, whb, cwhb, dsth, s_st, normq in (
            ("q", io["Wq"], io["conv_wq_c"], q_hbm, s_qs, 1),
            ("k", io["Wk"], io["conv_wk_c"], k_hbm, s_ks, 1),
            ("v", io["Wv"], io["conv_wv_c"], v_hbm, s_vs, 0)):
        t_cv = emit_proj_conv_store(c, cs, xnT, s_xn, whb, cwhb, dsth,
                                    pf=pf, cacc=cacc, psr=psr, wtile=wtile)
        if not normq:
            for ki in range(ND):
                t_cv[ki].wait(nc.sync)
                s_st.inc(nc.sync.dma_start(
                    dsth[ki * 128:(ki + 1) * 128, :],
                    pf[:, ki * T:(ki + 1) * T]), 16)
            fw7, s_f7 = emit_load_f32(cs, io["fir_s_c"], 128, ND * 7)
            fw31, s_f31 = emit_load_f32(cs, io["fir_l_c"], 128, ND * 31)
            firs[0] = emit_fir_fm(c, cs, pf, cacc, fw7, s_f7, fs_hbm, 7,
                                  t_cv, qstg)
            firs[1] = emit_fir_fm(c, cs, pf, cacc, fw31, s_f31, fl_hbm, 31,
                                  t_cv, qstg)
            continue
        for h in range(H):
            for d_ in range(2):
                ki = 2 * h + d_
                t_cv[ki].wait(nc.scalar)
                c.tip("tensor").wait(nc.scalar)
                c.s(nc.scalar.activation(sqb[:, d_ * T:(d_ + 1) * T],
                                         pf[:, ki * T:(ki + 1) * T],
                                         AF.Square))
            t_sq = c.tip("scalar")
            for n0 in range(0, T, 256):
                pt = psr.acquire(nc.tensor)
                t_sq.wait(nc.tensor)
                t_of.wait(nc.tensor)
                for d_ in range(2):
                    mm = nc.tensor.matmul(
                        pt[0:1, 0:256], onesf[:, 0:1],
                        sqb[:, d_ * T + n0:d_ * T + n0 + 256],
                        start=(d_ == 0), stop=(d_ == 1))
                t_mm = c.mark(mm, "tensor")
                t_mm.wait(nc.vector)
                c.tip("tensor").wait(nc.vector)
                t_ev = c.v(nc.vector.tensor_copy(
                    rn_row[0:1, n0:n0 + 256], pt[0:1, 0:256]))
                psr.release(t_ev)
            t_ev.wait(nc.scalar)
            c.wait_const(nc.scalar)
            c.s(nc.scalar.activation(rn_row[0:1, :], rn_row[0:1, :],
                                     AF.Ln, bias=c.constcol(0.0)[0:1, :]))
            t_rn = c.s(nc.scalar.activation(
                rn_row[0:1, :], rn_row[0:1, :], AF.Exp,
                scale=c.constcol(-0.5)[0:1, :]))
            bsub = c.sub()
            bcs = bsub.__enter__()
            t_rb = emit_bcast_rows(c, bcs, [(rn_row[0:1, :], rb, t_rn)])
            bsub.__exit__(None, None, None)
            for d_ in range(2):
                ki = 2 * h + d_
                qs = qstg.acquire(nc.vector)
                t_rb.wait(nc.vector)
                t_n = c.v(nc.vector.tensor_mul(qs[:, :],
                                               pf[:, ki * T:(ki + 1) * T],
                                               rb[:, :]))
                t_n.wait(nc.sync)
                s_st.inc(nc.sync.dma_start(
                    dsth[ki * 128:(ki + 1) * 128, :], qs[:, :]), 16)
                qstg.release(Tok(s_st.h, s_st.v))
    wb = emit_load_weight(cs, io["Wb"], D, H)
    bstg = cs.sbring(2, [128, H], F32)
    for t in range(NT):
        pt = psr.acquire(nc.tensor)
        wb.sem.wait(nc.tensor)
        s_xn.wait(nc.tensor)
        mm = mm_group(nc, pt[:, 0:H],
                      [(xnT[:, ki * T + t * 128:ki * T + (t + 1) * 128],
                        wb.lhsT(ki, 0, H)) for ki in range(ND)])
        t_mm = c.mark(mm, "tensor")
        bs = bstg.acquire(nc.scalar)
        t_mm.wait(nc.scalar)
        t_sg = c.s(nc.scalar.activation(bs[:, 0:H], pt[:, 0:H], AF.Sigmoid))
        psr.release(t_sg)
        t_sg.wait(nc.sync)
        s_bt.inc(nc.sync.dma_start(
            beta_hbm.rearrange("(t p) h -> t p h", p=128)[t, :, :],
            bs[:, 0:H]), 16)
        bstg.release(Tok(s_bt.h, s_bt.v))
    cs.exit_waits += [s_qs, s_ks, s_vs, s_bt, firs[0], firs[1]]
    sub.__exit__(None, None, None)
    sub = c.sub()
    cs = sub.__enter__()
    w1 = emit_load_weight(cs, io["gmix_w1"], D, 2 * D)
    b1c, s_b1 = emit_load_f32(cs, io["gmix_b1_c"], 128, 16)
    h1T = cs.sb([128, 16 * T], BF16)
    psr = cs.ring(2, [128, 512])
    emit_proj_fm(c, xnT, s_xn, w1, h1T, bias_col=b1c, s_bias=s_b1,
                 act=AF.Gelu, psr=psr)
    w2 = emit_load_weight(cs, io["gmix_w2"], 2 * D, 16)
    b2r, s_b2 = emit_load_bf16_row(cs, io["gmix_b2"], 16)
    ones2, t_o2 = emit_ones(cs)
    for t in range(NT):
        pt = psr.acquire(nc.tensor)
        w2.sem.wait(nc.tensor)
        s_b2.wait(nc.tensor)
        t_o2.wait(nc.tensor)
        pairs = [(h1T[:, ki * T + t * 128:ki * T + (t + 1) * 128],
                  w2.lhsT(ki, 0, 16)) for ki in range(16)]
        pairs.append((ones2[0:1, 0:128], b2r[0:1, 0:16]))
        mm = mm_group(nc, pt[:, 0:16], pairs)
        t_mm = c.mark(mm, "tensor")
        t_mm.wait(nc.vector)
        t_gv = c.v(nc.vector.tensor_copy(gate_tok[:, t * 16:(t + 1) * 16],
                                         pt[:, 0:16]))
        psr.release(t_gv)
    t_gate = t_gv
    sub.__exit__(None, None, None)
    subA.__exit__(None, None, None)
    if ph == 1:
        s1 = c.sem()
        for sm in (s_qs, s_vs, firs[0], firs[1], s_bt):
            sm.wait(nc.gpsimd)
        t_gate.wait(nc.gpsimd)
        s1.inc(nc.gpsimd.dma_start(
            out_hbm[0:512, :].rearrange("(a p) t -> p a t", p=128),
            q_hbm[0:512, 0:1024].rearrange("(a p) t -> p a t", p=128)), 16)
        s1.inc(nc.gpsimd.dma_start(
            out_hbm[512:1024, :].rearrange("(a p) t -> p a t", p=128),
            fs_hbm[0:512, 0:1024].rearrange("(a p) t -> p a t", p=128)), 16)
        s1.inc(nc.gpsimd.dma_start(
            out_hbm[1024:1536, :].rearrange("(a p) t -> p a t", p=128),
            fl_hbm[0:512, 0:1024].rearrange("(a p) t -> p a t", p=128)), 16)
        s1.inc(nc.gpsimd.dma_start(
            out_hbm[1536:1552, 0:1024].rearrange("(a p) f -> p a f", p=4),
            beta_hbm[0:1024, :].rearrange("(a t) h -> h a t", a=16)[:, 0:4, :]
            ), 16) if False else None
        s1.inc(nc.gpsimd.dma_start(
            out_hbm[1536:1664, 0:256].rearrange("(a p) f -> p (a f)", p=128),
            gate_tok[:, 0:256]), 16)
        s1.inc(nc.gpsimd.dma_start(
            out_hbm[1664:1920, 0:1024].rearrange("(a p) t -> p a t", p=128),
            v_hbm[768:1024, 0:1024].rearrange("(a p) t -> p a t", p=128)), 16)
        s1.wait(nc.gpsimd)
        return s1
    sub = c.sub()
    cs = sub.__enter__()
    Sf = [[cs.sb([128, DH]) for _ in range(2)] for _ in range(H)]
    Sb = [[cs.sb([128, DH], BF16) for _ in range(2)] for _ in range(H)]
    uwp = cs.ps([128, DH])
    opp = cs.ps([128, DH])
    ptp = cs.ps([128, DH])
    pbp = cs.ps([128, DH])
    ltp = cs.ps([128, 32])
    tpb = cs.ps([128, 128], BF16)
    dSp = [cs.ps([128, DH]) for _ in range(2)]
    tok_uw = tok_o = tok_pt = tok_pb = tok_lt = tok_tp = tok_dS = None
    qf = cs.sb([128, ND * 128], BF16)
    kf = cs.sb([128, ND * 128], BF16)
    kbf = cs.sb([128, ND * 128], BF16)
    bb_g = cs.sb([128, 128], BF16)
    brows = [cs.sb([1, T], BF16) for _ in range(H)]
    s_brs = c.sem()
    for h in range(H):
        s_bt.wait(nc.gpsimd)
        s_brs.inc(nc.gpsimd.dma_start(
            brows[h][0:1, :],
            beta_hbm[:, h:h + 1].rearrange("t o -> o t"),
            allow_slow_non_contiguous=True), 16)
    bcol = cs.sb([128, H])
    ones1 = cs.sb([1, 128], BF16)
    t_o1 = c.v(nc.vector.memset(ones1[0:1, :], 1.0))
    alr = cs.sb([1, H], BF16)
    s_al = c.sem()
    s_al.inc(nc.gpsimd.dma_start(alr[0:1, :], io["alpha"][None, :]), 16)
    alpha_bc = cs.sb([128, H])
    s_al.wait(nc.tensor)
    t_o1.wait(nc.tensor)
    t_am = c.mark(nc.tensor.matmul(ptp[:, 0:H], ones1[0:1, :], alr[0:1, 0:H],
                                   start=True, stop=True), "tensor")
    t_am.wait(nc.vector)
    tok_pt = c.v(nc.vector.tensor_copy(alpha_bc[:, :], ptp[:, 0:H]))
    ktok = [cs.sb([128, DH], BF16) for _ in range(H)]
    vtok = [cs.sb([128, DH], BF16) for _ in range(H)]
    fstok = [cs.sb([128, DH], BF16) for _ in range(H)]
    fltok = [cs.sb([128, DH], BF16) for _ in range(H)]
    kbt = cs.sb([128, DH], BF16)
    vbt = cs.sb([128, DH], BF16)
    ug = [cs.sb([128, DH]) for _ in range(H)]
    wtg = [cs.sb([128, DH], BF16) for _ in range(H)]
    dts = [cs.sb([128, DH]) for _ in range(H)]
    Np = [cs.sb([128, 128]) for _ in range(5)]
    Ntp = [cs.sb([128, 128]) for _ in range(5)]
    Rm = cs.sb([128, 128])
    Rt = cs.sb([128, 128])
    Rb = cs.sb([128, 128], BF16)
    lt_sb = cs.sb([128, 32], BF16)
    un_sb = cs.sb([128, DH], BF16)
    sqt = cs.sb([128, DH])
    st1 = cs.sb([128, H])
    st2 = cs.sb([128, H])
    st3 = cs.sb([128, H])
    lg = cs.sb([128, 2 * H])
    w4 = cs.sb([128, H])
    omx = cs.sb([128, DH])
    osc_sb = cs.sb([128, DH], BF16)
    rsn = cs.sb([128, 8])
    tSb = [None] * H
    have_S = False
    t_red_prev = None
    for g in range(NG):
        g0 = g * 128
        c.tip("tensor").wait(nc.sync)
        c.tip("vector").wait(nc.sync)
        c.tip("scalar").wait(nc.sync)
        s_g = c.dsem("l3ag")
        s_qs.wait(nc.sync)
        s_ks.wait(nc.sync)
        s_vs.wait(nc.sync)
        s_bt.wait(nc.sync)
        firs[0].wait(nc.sync)
        firs[1].wait(nc.sync)
        s_g.inc(nc.sync.dma_start(
            qf[:, :].rearrange("p (ki t) -> p ki t", ki=ND),
            q_hbm.rearrange("(ki p) t -> p ki t", p=128)[:, :, g0:g0 + 128]),
            16)
        s_g.inc(nc.sync.dma_start(
            kf[:, :].rearrange("p (ki t) -> p ki t", ki=ND),
            k_hbm.rearrange("(ki p) t -> p ki t", p=128)[:, :, g0:g0 + 128]),
            16)
        s_g.inc(nc.sync.dma_start(bcol[:, :], beta_hbm[g0:g0 + 128, :]), 16)
        for h in range(H):
            s_g.inc(nc.sync.dma_start_transpose(
                ktok[h][:, :], k_hbm[h * DH:(h + 1) * DH, g0:g0 + 128]), 16)
            s_g.inc(nc.sync.dma_start_transpose(
                vtok[h][:, :], v_hbm[h * DH:(h + 1) * DH, g0:g0 + 128]), 16)
            s_g.inc(nc.sync.dma_start_transpose(
                fstok[h][:, :], fs_hbm[h * DH:(h + 1) * DH, g0:g0 + 128]), 16)
            s_g.inc(nc.sync.dma_start_transpose(
                fltok[h][:, :], fl_hbm[h * DH:(h + 1) * DH, g0:g0 + 128]), 16)
        for h in range(H):
            if tok_pt is not None:
                tok_pt.wait(nc.tensor)
            s_brs.wait(nc.tensor)
            t_o1.wait(nc.tensor)
            t_bb = c.mark(nc.tensor.matmul(ptp[:, 0:128], ones1[0:1, :],
                                           brows[h][0:1, g0:g0 + 128],
                                           start=True, stop=True), "tensor")
            t_bb.wait(nc.vector)
            c.tip("tensor").wait(nc.vector)
            t_bbc = c.v(nc.vector.tensor_copy(bb_g[:, :], ptp[:, 0:128]))
            s_g.wait(nc.vector)
            for d_ in range(2):
                t_kb = c.v(nc.vector.tensor_mul(
                    kbf[:, (2 * h + d_) * 128:(2 * h + d_ + 1) * 128],
                    kf[:, (2 * h + d_) * 128:(2 * h + d_ + 1) * 128],
                    bb_g[:, :]))
            t_vb = c.v(nc.vector.tensor_scalar_mul(vbt[:, :], vtok[h][:, :],
                                                   bcol[:, h:h + 1]))
            t_kbt = c.v(nc.vector.tensor_scalar_mul(kbt[:, :], ktok[h][:, :],
                                                    bcol[:, h:h + 1]))
            mk["tok"].wait(nc.tensor)
            t_kb.wait(nc.tensor)
            for d_ in range(2):
                sl = slice((2 * h + d_) * 128, (2 * h + d_ + 1) * 128)
                mm = nc.tensor.matmul(ptp[:, 0:128], kbf[:, sl], kf[:, sl],
                                      start=(d_ == 0), stop=(d_ == 1))
            t_mm = c.mark(mm, "tensor")
            if tok_pb is not None:
                tok_pb.wait(nc.tensor)
            for d_ in range(2):
                sl = slice((2 * h + d_) * 128, (2 * h + d_ + 1) * 128)
                mm = nc.tensor.matmul(pbp[:, 0:128], kf[:, sl], kbf[:, sl],
                                      start=(d_ == 0), stop=(d_ == 1))
            t_mm2 = c.mark(mm, "tensor")
            t_mm.wait(nc.vector)
            c.tip("tensor").wait(nc.vector)
            tok_pt = c.v(nc.vector.tensor_mul(Np[0][:, :], ptp[:, 0:128],
                                              mk["msl"][:, :]))
            t_mm2.wait(nc.vector)
            tok_pb = t_g = c.v(nc.vector.tensor_mul(Ntp[0][:, :],
                                                    pbp[:, 0:128],
                                                    mk["msu"][:, :]))
            for j in range(1, 5):
                t_g.wait(nc.tensor)
                tok_pt.wait(nc.tensor)
                t_a = c.mark(nc.tensor.matmul(ptp[:, 0:128], Ntp[j - 1][:, :],
                                              Np[j - 1][:, :],
                                              start=True, stop=True), "tensor")
                tok_pb.wait(nc.tensor)
                t_b = c.mark(nc.tensor.matmul(pbp[:, 0:128], Np[j - 1][:, :],
                                              Ntp[j - 1][:, :],
                                              start=True, stop=True), "tensor")
                t_a.wait(nc.vector)
                c.tip("tensor").wait(nc.vector)
                tok_pt = c.v(nc.vector.tensor_copy(Np[j][:, :], ptp[:, 0:128]))
                t_b.wait(nc.vector)
                tok_pb = t_g = c.v(nc.vector.tensor_copy(Ntp[j][:, :],
                                                         pbp[:, 0:128]))
            c.tip("tensor").wait(nc.vector)
            c.v(nc.vector.tensor_add(Rm[:, :], Ntp[4][:, :],
                                     mk["iden128"][:, :]))
            t_ch = c.v(nc.vector.tensor_add(Rt[:, :], Np[4][:, :],
                                            mk["iden128"][:, :]))
            for j in (3, 2, 1, 0):
                t_ch.wait(nc.tensor)
                tok_pt.wait(nc.tensor)
                t_a = c.mark(nc.tensor.matmul(ptp[:, 0:128], Rt[:, :],
                                              Ntp[j][:, :],
                                              start=True, stop=True), "tensor")
                tok_pb.wait(nc.tensor)
                t_b = c.mark(nc.tensor.matmul(pbp[:, 0:128], Ntp[j][:, :],
                                              Rt[:, :],
                                              start=True, stop=True), "tensor")
                t_a.wait(nc.vector)
                t_b.wait(nc.vector)
                tok_pt = c.v(nc.vector.tensor_add(Rm[:, :], ptp[:, 0:128],
                                                  Rm[:, :]))
                tok_pb = t_ch = c.v(nc.vector.tensor_add(Rt[:, :],
                                                         pbp[:, 0:128],
                                                         Rt[:, :]))
            t_ch.wait(nc.vector)
            c.tip("tensor").wait(nc.vector)
            t_rb = c.v(nc.vector.tensor_copy(Rb[:, :], Rm[:, :]))
            t_rb.wait(nc.tensor)
            tok_pt.wait(nc.tensor)
            t_vb.wait(nc.tensor)
            t_u = c.mark(nc.tensor.matmul(ptp[:, 0:DH], Rb[:, :],
                                          vbt[:, :], start=True,
                                          stop=True), "tensor")
            t_u.wait(nc.vector)
            c.tip("tensor").wait(nc.vector)
            tok_pt = t_ue = c.v(nc.vector.tensor_copy(ug[h][:, :],
                                                      ptp[:, 0:DH]))
            for d_ in range(2):
                tok_pb.wait(nc.tensor)
                t_kbt.wait(nc.tensor)
                t_w = c.mark(nc.tensor.matmul(
                    pbp[:, 0:128], kbt[:, d_ * 128:(d_ + 1) * 128],
                    Rb[:, :], start=True, stop=True), "tensor")
                t_w.wait(nc.vector)
                tok_pb = t_we = c.v(nc.vector.tensor_copy(
                    wtg[h][:, d_ * 128:(d_ + 1) * 128], pbp[:, 0:128]))
        t_prep = c.tip("vector")
        for ch in range(4):
            b = 32 * ch
            for h in range(H):
                t_prep.wait(nc.tensor)
                if tok_lt is not None:
                    tok_lt.wait(nc.tensor)
                for d_ in range(2):
                    col = (2 * h + d_) * 128 + b
                    mm = nc.tensor.matmul(
                        ltp[b:b + 32, 0:32], kf[:, col:col + 32],
                        qf[:, col:col + 32], start=(d_ == 0), stop=(d_ == 1),
                        tile_position=(0, b))
                t_lt = c.mark(mm, "tensor")
                t_lt.wait(nc.vector)
                c.tip("tensor").wait(nc.vector)
                tok_lt = t_ltc = c.v(nc.vector.tensor_mul(
                    lt_sb[b:b + 32, 0:32], ltp[b:b + 32, 0:32],
                    mk["mkeepU"][b:b + 32, 0:32]))
                if tok_o is not None:
                    tok_o.wait(nc.tensor)
                if have_S:
                    if tok_uw is not None:
                        tok_uw.wait(nc.tensor)
                    if tSb[h] is not None:
                        tSb[h].wait(nc.tensor)
                    for d_ in range(2):
                        mm = nc.tensor.matmul(
                            uwp[b:b + 32, 0:DH],
                            wtg[h][:, d_ * 128 + b:d_ * 128 + b + 32],
                            Sb[h][d_][:, :], start=(d_ == 0), stop=(d_ == 1),
                            tile_position=(0, b))
                    t_ws = c.mark(mm, "tensor")
                    t_ws.wait(nc.vector)
                    c.tip("tensor").wait(nc.vector)
                    tok_uw = t_un = c.v(nc.vector.tensor_sub(
                        un_sb[b:b + 32, :], ug[h][b:b + 32, :],
                        uwp[b:b + 32, 0:DH]))
                    for d_ in range(2):
                        col = (2 * h + d_) * 128 + b
                        mm = nc.tensor.matmul(
                            opp[b:b + 32, 0:DH], qf[:, col:col + 32],
                            Sb[h][d_][:, :], start=(d_ == 0), stop=False,
                            tile_position=(0, b))
                    c.mark(mm, "tensor")
                else:
                    c.tip("tensor").wait(nc.vector)
                    t_un = c.v(nc.vector.tensor_copy(un_sb[b:b + 32, :],
                                                     ug[h][b:b + 32, :]))
                t_un.wait(nc.tensor)
                t_ltc.wait(nc.tensor)
                t_o = c.mark(nc.tensor.matmul(
                    opp[b:b + 32, 0:DH], lt_sb[b:b + 32, 0:32],
                    un_sb[b:b + 32, :], start=(not have_S), stop=True,
                    tile_position=(b, b)), "tensor")
                t_su = None
                for d_ in range(2):
                    if tSb[h] is not None:
                        tSb[h].wait(nc.tensor)
                    if tok_dS is not None:
                        tok_dS.wait(nc.tensor)
                    t_su = c.mark(nc.tensor.matmul(
                        dSp[d_][0:128, 0:DH],
                        ktok[h][b:b + 32, d_ * 128:(d_ + 1) * 128],
                        un_sb[b:b + 32, :], start=True, stop=True,
                        tile_position=(b, 0)), "tensor")
                t_su.wait(nc.vector)
                for d_ in range(2):
                    if have_S:
                        c.v(nc.vector.tensor_add(Sf[h][d_][:, :],
                                                 Sf[h][d_][:, :],
                                                 dSp[d_][0:128, 0:DH]))
                    else:
                        c.v(nc.vector.tensor_copy(Sf[h][d_][:, :],
                                                  dSp[d_][0:128, 0:DH]))
                tok_dS = c.tip("vector")
                c.v(nc.vector.tensor_copy(Sb[h][0][:, :], Sf[h][0][:, :]))
                tSb[h] = c.v(nc.vector.tensor_copy(Sb[h][1][:, :],
                                                   Sf[h][1][:, :]))
                t_o.wait(nc.vector)
                tok_o = c.v(nc.vector.tensor_copy(dts[h][b:b + 32, :],
                                                  opp[b:b + 32, 0:DH]))
            have_S = True
        if ph == 6 and g == 0:
            s1 = c.sem()
            tok_o.wait(nc.gpsimd)
            tok_dS.wait(nc.gpsimd)
            for h in range(H):
                s1.inc(nc.gpsimd.dma_start(
                    out_hbm[h * 128:(h + 1) * 128, 0:DH], dts[h][:, :]), 16)
                s1.inc(nc.gpsimd.dma_start(
                    out_hbm[512 + h * 128:512 + (h + 1) * 128, 0:DH],
                    ug[h][:, :]), 16)
            s1.inc(nc.gpsimd.dma_start(out_hbm[1024:1152, 0:DH],
                                       vbt[:, :]), 16)
            s1.inc(nc.gpsimd.dma_start(out_hbm[1152:1280, 0:128],
                                       bb_g[:, :]), 16)
            s1.inc(nc.gpsimd.dma_start(out_hbm[1280:1408, 0:4],
                                       bcol[:, :]), 16)
            s1.inc(nc.gpsimd.dma_start(out_hbm[1408:1536, 0:DH],
                                       vtok[3][:, :]), 16)
            s_vs.wait(nc.gpsimd)
            s1.inc(nc.gpsimd.dma_start(
                out_hbm[1536:1792, 0:128],
                v_hbm[768:1024, 0:128].rearrange("(a p) t -> p a t", p=128
                                                 ).rearrange(
                    "p a t -> (a p) t")), 16) if False else None
            s1.inc(nc.gpsimd.dma_start(
                out_hbm[1536:1792, 0:1024].rearrange("(a p) t -> p a t", p=128),
                v_hbm[768:1024, 0:1024].rearrange("(a p) t -> p a t", p=128)),
                16)
            s1.wait(nc.gpsimd)
            sub.__exit__(None, None, None)
            return s1
        for h in range(H):
            t_gate.wait(nc.vector)
            srcs = [fstok[h][:, :], fltok[h][:, :], dts[h][:, :],
                    vtok[h][:, :]]
            for br, src in enumerate(srcs):
                if br == 2:
                    tok_o.wait(nc.vector)
                    tok_o.wait(nc.scalar)
                s_g.wait(nc.vector)
                s_g.wait(nc.scalar)
                c.tip("vector").wait(nc.scalar)
                t_s1 = c.v(nc.vector.reduce_sum(st1[:, br:br + 1], src,
                                                axis=AX.X))
                t_sq = c.s(nc.scalar.activation(sqt[:, :], src, AF.Square))
                t_sq.wait(nc.vector)
                t_s2 = c.v(nc.vector.reduce_sum(st2[:, br:br + 1], sqt[:, :],
                                                axis=AX.X))
                t_s3 = c.v(nc.vector.reduce_max(st3[:, br:br + 1], src,
                                                axis=AX.X))
            t_s2.wait(nc.scalar)
            c.tip("vector").wait(nc.scalar)
            c.wait_const(nc.scalar)
            t_c1 = c.s(nc.scalar.activation(lg[:, H:2 * H], st1[:, :],
                                            AF.Square))
            t_c1.wait(nc.vector)
            t_vx = c.v(nc.vector.scalar_tensor_tensor(
                lg[:, H:2 * H], lg[:, H:2 * H], c.constcol(-1.0 / DH),
                st2[:, :], ALU.mult, ALU.add))
            t_vx.wait(nc.scalar)
            c.s(nc.scalar.activation(lg[:, H:2 * H], lg[:, H:2 * H],
                                     AF.Relu))
            c.s(nc.scalar.activation(lg[:, H:2 * H], lg[:, H:2 * H], AF.Ln,
                                     scale=c.constcol(1.0 / (DH - 1)),
                                     bias=c.constcol(1e-30)))
            t_sd = c.s(nc.scalar.activation(lg[:, H:2 * H], lg[:, H:2 * H],
                                            AF.Exp,
                                            scale=c.constcol(0.5)))
            t_sd.wait(nc.vector)
            t_st = c.v(nc.vector.scalar_tensor_tensor(
                st3[:, :], st1[:, :], c.constcol(1.0 / DH), st3[:, :],
                ALU.mult, ALU.add))
            t_st = c.v(nc.vector.tensor_add(st3[:, :], st3[:, :],
                                            lg[:, H:2 * H]))
            gview = gate_tok[:, g * 16:(g + 1) * 16].rearrange(
                "p (br hh) -> p br hh", br=4)[:, :, h]
            t_al = c.v(nc.vector.scalar_tensor_tensor(
                lg[:, 0:H], st3[:, :], alpha_bc[:, h:h + 1], gview,
                ALU.mult_div3, ALU.add)) if False else None
            t_sc3 = c.v(nc.vector.tensor_scalar_mul(st3[:, :], st3[:, :],
                                                    1.0 / 3.0))
            t_lg = c.v(nc.vector.scalar_tensor_tensor(
                lg[:, 0:H], st3[:, :], alpha_bc[:, h:h + 1], gview,
                ALU.mult, ALU.add))
            t_mx = c.v(nc.vector.reduce_max(rsn[:, 2:3], lg[:, 0:H],
                                            axis=AX.X))
            t_ng = c.v(nc.vector.tensor_scalar_mul(rsn[:, 2:3], rsn[:, 2:3],
                                                   -1.0))
            t_ng.wait(nc.scalar)
            c.tip("vector").wait(nc.scalar)
            t_ex = c.s(nc.scalar.activation(w4[:, :], lg[:, 0:H], AF.Exp,
                                            bias=rsn[:, 2:3]))
            t_ex.wait(nc.vector)
            t_sm = c.v(nc.vector.reduce_sum(rsn[:, 3:4], w4[:, :], axis=AX.X))
            t_rc = c.v(nc.vector.reciprocal(rsn[:, 4:5], rsn[:, 3:4]))
            t_w4 = c.v(nc.vector.tensor_scalar_mul(w4[:, :], w4[:, :],
                                                   rsn[:, 4:5]))
            t_w4.wait(nc.vector)
            c.v(nc.vector.tensor_scalar_mul(omx[:, :], srcs[0],
                                            w4[:, 0:1]))
            for br in range(1, 4):
                c.v(nc.vector.scalar_tensor_tensor(
                    omx[:, :], srcs[br], w4[:, br:br + 1], omx[:, :],
                    ALU.mult, ALU.add))
            t_omx = c.tip("vector")
            t_omx.wait(nc.scalar)
            if t_red_prev is not None:
                t_red_prev.wait(nc.scalar)
            t_sq = c.s(nc.scalar.activation(sqt[:, :], omx[:, :], AF.Square))
            t_sq.wait(nc.vector)
            t_red = c.v(nc.vector.reduce_sum(rsn[:, 0:1], sqt[:, :],
                                             axis=AX.X))
            t_red_prev = t_red
            t_red.wait(nc.scalar)
            c.s(nc.scalar.activation(rsn[:, 1:2], rsn[:, 0:1], AF.Ln,
                                     scale=c.constcol(1.0 / DH),
                                     bias=c.constcol(1e-5)))
            t_rs = c.s(nc.scalar.activation(rsn[:, 1:2], rsn[:, 1:2], AF.Exp,
                                            scale=c.constcol(-0.5)))
            t_rs.wait(nc.vector)
            t_ob.wait(nc.vector)
            c.tip("tensor").wait(nc.vector)
            t_os = c.v(nc.vector.scalar_tensor_tensor(
                osc_sb[:, :], omx[:, :], rsn[:, 1:2], onw_bc[:, :],
                ALU.mult, ALU.mult))
            for d_ in range(2):
                t_os.wait(nc.tensor)
                t_idb.wait(nc.tensor)
                if tok_tp is not None:
                    tok_tp.wait(nc.tensor)
                t_tp = c.mark(nc.tensor.transpose(
                    tpb[0:128, 0:128], osc_sb[:, d_ * 128:(d_ + 1) * 128],
                    identB[:, :]), "tensor")
                t_tp.wait(nc.vector)
                tok_tp = t_ev = c.v(nc.vector.tensor_copy(
                    oT_big[:, (2 * h + d_) * T + g0:(2 * h + d_) * T + g0 + 128],
                    tpb[0:128, 0:128]))
    t_scan = c.tip("vector")
    if ph == 5:
        s1 = c.sem()
        t_scan.wait(nc.gpsimd)
        s1.inc(nc.gpsimd.dma_start(
            out_hbm[:, :].rearrange("(a p) f -> p a f", p=128),
            oT_big[:, :].rearrange("p (a f) -> p a f", a=16)), 16)
        s1.wait(nc.gpsimd)
        sub.__exit__(None, None, None)
        return s1
    sub.__exit__(None, None, None)
    sub = c.sub()
    cs = sub.__enter__()
    wo = emit_load_weight(cs, io["Wo"], D, D)
    ops2 = cs.ring(2, [128, D])
    xrr = cs.sbring(2, [128, D], F32)
    ost = cs.sbring(2, [128, D], F32)
    s_out = c.sem()
    x3 = x_hbm.rearrange("(t p) f -> t p f", p=128)
    o3 = out_hbm.rearrange("(t p) f -> t p f", p=128)
    for t in range(NT):
        xr = xrr.acquire(nc.sync)
        s_xl = c.dsem("l3ax")
        s_xl.inc(nc.sync.dma_start(xr[:, :], x3[t, :, :]), 16)
        pt = ops2.acquire(nc.tensor)
        wo.sem.wait(nc.tensor)
        t_scan.wait(nc.tensor)
        for o0 in range(0, D, 512):
            mm = mm_group(nc, pt[:, o0:o0 + 512],
                          [(oT_big[:, ki * T + t * 128:ki * T + (t + 1) * 128],
                            wo.lhsT(ki, o0, o0 + 512)) for ki in range(ND)])
        t_mm = c.mark(mm, "tensor")
        ot = ost.acquire(nc.vector)
        t_mm.wait(nc.vector)
        s_xl.wait(nc.vector)
        t_ad = c.v(nc.vector.tensor_add(ot[:, :], pt[:, 0:D], xr[:, :]))
        ops2.release(t_ad)
        xrr.release(t_ad)
        t_ad.wait(nc.sync)
        s_out.inc(nc.sync.dma_start(o3[t, :, :], ot[:, :]), 16)
        ost.release(Tok(s_out.h, s_out.v))
    sub.__exit__(None, None, None)
    s_out.wait(nc.gpsimd)
    return s_out


def build_l3attna_program():
    nc = bass.Bass("TRN2", target_bir_lowering=False)
    stack = ExitStack()
    c = Ctx(nc, stack)
    dt = lambda n, sh: nc.dram_tensor(n, sh, F32, kind="ExternalInput").ap()
    io = {
        "Wq": dt("Wq", [D, D]), "Wk": dt("Wk", [D, D]), "Wv": dt("Wv", [D, D]),
        "Wb": dt("Wb", [D, 4]),
        "conv_wq_c": dt("conv_wq_c", [128, 32]),
        "conv_wk_c": dt("conv_wk_c", [128, 32]),
        "conv_wv_c": dt("conv_wv_c", [128, 32]),
        "fir_s_c": dt("fir_s_c", [128, 8 * 7]),
        "fir_l_c": dt("fir_l_c", [128, 8 * 31]),
        "alpha": dt("alpha", [4]),
        "gmix_w1": dt("gmix_w1", [D, 2 * D]),
        "gmix_b1_c": dt("gmix_b1_c", [128, 16]),
        "gmix_w2": dt("gmix_w2", [2 * D, 16]),
        "gmix_b2": dt("gmix_b2", [16]),
        "onorm_w": dt("onorm_w", [256]),
        "Wo": dt("Wo", [D, D]),
    }
    x_h = dt("x", [T, D])
    out_h = nc.dram_tensor("out", [T, D], F32, kind="ExternalOutput").ap()
    with stack:
        emit_l3_attn_a(c, x_h, out_h, io)
    return nc



def emit_l0_attn_a(c, x_hbm, out_hbm, io, mode="main"):
    import math
    nc = c.nc
    H, DM, MF = 16, 64, 256
    DN = float(DM) ** -0.25
    RATIO = float(MF) ** -0.5
    LNR = math.log(RATIO)
    EPS = 1e-4
    identB = c.sb([128, 128], BF16)
    iscr = c.sb([128, 128])
    t_ii = c.g(nc.gpsimd.iota(iscr[:, :], pattern=[[1, 128]], base=0,
                              channel_multiplier=-1,
                              allow_small_or_imprecise_dtypes=True))
    t_ii.wait(nc.vector)
    t_idb = c.v(nc.vector.tensor_single_scalar(identB[:, :], iscr[:, :],
                                               0.0, ALU.is_equal))
    ds, rstd, nmr, t_cols = emit_ln_stream(c, x_hbm)
    att_tok = c.sb([128, NT * D], BF16)
    sub0 = c.sub()
    cs0 = sub0.__enter__()
    qT = cs0.sb([128, ND * T], BF16)
    kT = cs0.sb([128, ND * T], BF16)
    v_tok = cs0.sb([128, NT * D], BF16)
    PT2 = cs0.sb([128, MF], BF16)
    s_pt = c.sem()
    s_pt.inc(nc.gpsimd.dma_start(PT2[:, :], io["PT2"][:, :]), 16)
    onesc = cs0.sb([128, 1], BF16)
    t_on = c.v(nc.vector.memset(onesc[:, 0:1], 1.0))
    ones1 = cs0.sb([1, 128], BF16)
    t_o1 = c.v(nc.vector.memset(ones1[0:1, :], 1.0))
    sub = c.sub()
    cs = sub.__enter__()
    xnT = cs.sb([128, ND * T], BF16)
    s_xn = c.sem()
    ds.load_fm(xnT, 0, T, s_xn)
    wtile = cs.sb([128, ND * D], BF16)
    psr = cs.ring(2, [128, 512])
    projs = [(io["Wk"], kT)] if mode == "max" else \
        [(io["Wq"], qT), (io["Wk"], kT)]
    for whb, dstT in projs:
        c.tip("tensor").wait(nc.gpsimd)
        w = emit_load_weight(cs, whb, D, D, tile=wtile)
        emit_proj_fm(c, xnT, s_xn, w, dstT, psr=psr)
    if mode == "main":
        c.tip("tensor").wait(nc.gpsimd)
        wv = emit_load_weight(cs, io["Wv"], D, D, tile=wtile)
        psr2 = cs.ring(2, [128, D])
        emit_proj_tm(c, lambda ki, t:
                     xnT[:, ki * T + t * 128:ki * T + (t + 1) * 128],
                     s_xn, wv, v_tok, psr=psr2)
    sub.__exit__(None, None, None)
    t_proj = Multi([c.tip("scalar"), c.tip("vector")])
    ddp = cs0.ps([128, MF])
    dgp = cs0.ps([128, 4])
    ctxp = [cs0.ps([128, 128]) for _ in range(2)]
    kcp = cs0.ps([128, 128])
    tpq = cs0.ps([128, 128], BF16)
    outp = cs0.ps([128, 64])
    sq = cs0.sb([128, 128], BF16)
    mrun = cs0.sb([128, 2])
    kp_sb = cs0.sb([128, MF], BF16)
    qpT_sb = cs0.sb([128, MF], BF16)
    ctx_sb = [cs0.sb([128, 128], BF16) for _ in range(H)]
    kc_sb = cs0.sb([128, 2 * H])
    kcbc = [cs0.sb([128, MF]) for _ in range(H)]
    bias_c = cs0.sb([128, 6])
    rowb = cs0.sb([1, 256], BF16)
    mx_bc = cs0.sb([128, 1])
    tmpq = cs0.sb([128, MF])
    t_mi = c.v(nc.vector.memset(mrun[:, 0:1], -1e30))

    def dd_mm(srcT, h, t, tok_prev):
        ki, off = h // 2, (h % 2) * 64
        s_pt.wait(nc.tensor)
        t_proj.wait(nc.tensor)
        if tok_prev is not None:
            tok_prev.wait(nc.tensor)
        mm = nc.tensor.matmul(
            ddp[0:128, 0:MF],
            srcT[:, ki * T + t * 128:ki * T + (t + 1) * 128][
                slice(off, off + 64), :],
            PT2[off:off + 64, :], start=True, stop=True,
            tile_position=(off, 0))
        return c.mark(mm, "tensor")

    def diag_mm(srcT, h, t, tok_prev, col):
        ki, off = h // 2, (h % 2) * 64
        sl = srcT[:, ki * T + t * 128:ki * T + (t + 1) * 128][
            slice(off, off + 64), :]
        c.tip("vector").wait(nc.scalar)
        t_sq = c.s(nc.scalar.activation(sq[off:off + 64, :], sl, AF.Square))
        t_sq.wait(nc.tensor)
        t_on.wait(nc.tensor)
        if tok_prev is not None:
            tok_prev.wait(nc.tensor)
        mm = nc.tensor.matmul(dgp[0:128, col:col + 1], sq[off:off + 64, :],
                              onesc[off:off + 64, 0:1], start=True, stop=True,
                              tile_position=(off, 0))
        return c.mark(mm, "tensor")

    if mode == "max":
        t_dd = t_rm = None
        for h in range(H):
            for t in range(NT):
                t_dd = dd_mm(kT, h, t, t_rm)
                t_dd.wait(nc.vector)
                c.v(nc.vector.reduce_max(mrun[:, 1:2], ddp[0:128, 0:MF],
                                         axis=AX.X))
                t_mi.wait(nc.vector)
                t_rm = c.v(nc.vector.tensor_tensor(mrun[:, 0:1],
                                                   mrun[:, 0:1],
                                                   mrun[:, 1:2], ALU.max))
        mx_hbm = c.dram([128, 1])
        s_mx = c.sem()
        t_rm.wait(nc.sync)
        s_mx.inc(nc.sync.dma_start(mx_hbm[:, :], mrun[:, 0:1]), 16)
        mxrow = cs0.sb([1, 128])
        s_mr = c.sem()
        s_mx.wait(nc.gpsimd)
        s_mr.inc(nc.gpsimd.dma_start(mxrow[0:1, :],
                                     mx_hbm[:, :].rearrange("p o -> o p")),
                 16)
        s_mr.wait(nc.vector)
        t_gm = c.v(nc.vector.reduce_max(mxrow[0:1, 0:1], mxrow[0:1, :],
                                        axis=AX.X))
        s_ci = c.sem()
        t_gm.wait(nc.sync)
        s_ci.inc(nc.sync.dma_start(out_hbm[0:1, 0:1], mxrow[0:1, 0:1]), 16)
        s_ci.wait(nc.gpsimd)
        sub0.__exit__(None, None, None)
        return s_ci
    mxrow = cs0.sb([1, 128])
    s_mr = c.sem()
    s_mr.inc(nc.gpsimd.dma_start(mxrow[0:1, 0:8], io["mx8"][None, :]), 16)
    s_mr.wait(nc.vector)
    t_gm = c.v(nc.vector.reduce_max(mxrow[0:1, 0:1], mxrow[0:1, 0:8],
                                    axis=AX.X))
    t_gm.wait(nc.scalar)
    t_h2 = c.s(nc.scalar.copy(rowb[0:1, 0:1], mxrow[0:1, 0:1]))
    t_h2.wait(nc.vector)
    t_l2 = c.v(nc.vector.tensor_sub(mxrow[0:1, 1:2], mxrow[0:1, 0:1],
                                    rowb[0:1, 0:1]))
    t_l3 = c.v(nc.vector.tensor_copy(rowb[0:1, 1:2], mxrow[0:1, 1:2]))
    t_l3.wait(nc.tensor)
    t_o1.wait(nc.tensor)
    mm = nc.tensor.matmul(kcp[0:128, 0:1], ones1[0:1, :], rowb[0:1, 0:1],
                          start=True, stop=False)
    mm = nc.tensor.matmul(kcp[0:128, 0:1], ones1[0:1, :], rowb[0:1, 1:2],
                          start=False, stop=True)
    t_mm = c.mark(mm, "tensor")
    t_mm.wait(nc.vector)
    t_mb = c.v(nc.vector.tensor_copy(mx_bc[:, 0:1], kcp[0:128, 0:1]))
    t_mb.wait(nc.vector)
    c.wait_const(nc.vector)
    t_bb = c.v(nc.vector.tensor_scalar(bias_c[:, 0:1], mx_bc[:, 0:1],
                                       -DN, LNR, ALU.mult, ALU.add))
    t_kp = t_ctx = t_dg = None
    t_ctxe = []
    for h in range(H):
        for t in range(NT):
            t_dd = dd_mm(kT, h, t, t_kp)
            t_dg = diag_mm(kT, h, t, t_dg, 0)
            t_dg.wait(nc.vector)
            t_bb.wait(nc.vector)
            if t_kp is not None:
                t_kp.wait(nc.vector)
            t_bc = c.v(nc.vector.scalar_tensor_tensor(
                bias_c[:, 1:2], dgp[0:128, 0:1], c.constcol(-0.5 * DN * DN),
                bias_c[:, 0:1], ALU.mult, ALU.add))
            t_bc.wait(nc.scalar)
            t_dd.wait(nc.scalar)
            c.wait_const(nc.scalar)
            t_ex = c.s(nc.scalar.activation(tmpq[:, :], ddp[0:128, 0:MF],
                                            AF.Exp, scale=c.constcol(DN),
                                            bias=bias_c[:, 1:2]))
            t_ex.wait(nc.vector)
            t_kp = c.v(nc.vector.tensor_scalar(kp_sb[:, :], tmpq[:, :],
                                               RATIO * EPS, 0.0, ALU.add,
                                               ALU.bypass))
            t_kp.wait(nc.tensor)
            if t == 0 and t_ctxe:
                t_ctxe[-1].wait(nc.tensor)
            for mh in range(2):
                mm = nc.tensor.matmul(
                    ctxp[mh][0:128, h % 2 * 64:h % 2 * 64 + 64],
                    kp_sb[:, mh * 128:(mh + 1) * 128],
                    v_tok[:, t * D + h * 64:t * D + (h + 1) * 64],
                    start=(t == 0), stop=(t == NT - 1))
                c.mark(mm, "tensor")
                mm = nc.tensor.matmul(
                    kcp[0:128, h % 2 + 2 * mh:h % 2 + 2 * mh + 1],
                    kp_sb[:, mh * 128:(mh + 1) * 128],
                    onesc[:, 0:1], start=(t == 0), stop=(t == NT - 1))
                t_ctx = c.mark(mm, "tensor")
        if h % 2 == 1:
            t_ctx.wait(nc.vector)
            c.tip("tensor").wait(nc.vector)
            for mh in range(2):
                for hh in (h - 1, h):
                    t_e = c.v(nc.vector.tensor_copy(
                        ctx_sb[hh][:, mh * 64:(mh + 1) * 64],
                        ctxp[mh][0:128, hh % 2 * 64:hh % 2 * 64 + 64]))
            for col in range(4):
                hh = (h - 1) + col % 2
                mh = col // 2
                t_e = c.v(nc.vector.tensor_copy(
                    kc_sb[:, hh * 2 + mh:hh * 2 + mh + 1],
                    kcp[0:128, col:col + 1]))
            t_ctxe.append(t_e)
    kc1_hbm = c.dram([2 * H, 128])
    s_kcs = c.sem()
    t_ctxe[-1].wait(nc.sync)
    for col in range(2 * H):
        s_kcs.inc(nc.sync.dma_start(
            kc1_hbm[col:col + 1, :].rearrange("o t -> t o"),
            kc_sb[:, col:col + 1]), 16)
    kcrow = cs0.sb([1, 128])
    hlrow = cs0.sb([1, 256], BF16)
    t_kcb = []
    tok_bc = None
    t_hl = None
    for h in range(H):
        for mh in range(2):
            col = h * 2 + mh
            s_kr = c.dsem("l0kcr")
            s_kcs.wait(nc.sync)
            if t_hl is not None:
                t_hl.wait(nc.sync)
            s_kr.inc(nc.sync.dma_start(kcrow[0:1, :],
                                       kc1_hbm[col:col + 1, :]), 16)
            s_kr.wait(nc.scalar)
            t_hi = c.s(nc.scalar.copy(hlrow[0:1, 0:128], kcrow[0:1, :]))
            t_hi.wait(nc.vector)
            t_hl = c.v(nc.vector.tensor_sub(hlrow[0:1, 128:256],
                                            kcrow[0:1, :],
                                            hlrow[0:1, 0:128]))
            t_hl.wait(nc.tensor)
            t_o1.wait(nc.tensor)
            if tok_bc is not None:
                tok_bc.wait(nc.tensor)
            mm = nc.tensor.matmul(kcp[0:128, 0:128], ones1[0:1, :],
                                  hlrow[0:1, 0:128], start=True, stop=False)
            mm = nc.tensor.matmul(kcp[0:128, 0:128], ones1[0:1, :],
                                  hlrow[0:1, 128:256], start=False, stop=True)
            t_b2 = c.mark(mm, "tensor")
            t_b2.wait(nc.vector)
            tok_bc = t_kb = c.v(nc.vector.tensor_copy(
                kcbc[h][:, mh * 128:(mh + 1) * 128], kcp[0:128, 0:128]))
            t_kcb.append(t_kb)
    t_kcall = t_kcb[-1]
    t_qp = t_dg = t_out = tok_tp = None
    t_ev = None
    for t in range(NT):
        for h in range(H):
            t_dd = dd_mm(qT, h, t, t_qp)
            t_dg = diag_mm(qT, h, t, t_dg, 1)
            t_dd.wait(nc.vector)
            if t_qp is not None:
                t_qp.wait(nc.vector)
            t_mx = c.v(nc.vector.reduce_max(bias_c[:, 2:3], ddp[0:128, 0:MF],
                                            axis=AX.X))
            t_dg.wait(nc.vector)
            c.wait_const(nc.vector)
            t_b1 = c.v(nc.vector.tensor_scalar(bias_c[:, 3:4],
                                               bias_c[:, 2:3], -DN, LNR,
                                               ALU.mult, ALU.add))
            t_bc = c.v(nc.vector.scalar_tensor_tensor(
                bias_c[:, 3:4], dgp[0:128, 1:2], c.constcol(-0.5 * DN * DN),
                bias_c[:, 3:4], ALU.mult, ALU.add))
            t_bc.wait(nc.scalar)
            c.wait_const(nc.scalar)
            t_ex = c.s(nc.scalar.activation(tmpq[:, :], ddp[0:128, 0:MF],
                                            AF.Exp, scale=c.constcol(DN),
                                            bias=bias_c[:, 3:4]))
            t_ex.wait(nc.vector)
            t_qp = c.v(nc.vector.tensor_scalar(kp_sb[:, :], tmpq[:, :],
                                               RATIO * EPS, 0.0, ALU.add,
                                               ALU.bypass))
            t_kcall.wait(nc.vector)
            t_qp.wait(nc.vector)
            t_dv = c.v(nc.vector.tensor_mul(tmpq[:, :], kp_sb[:, :],
                                            kcbc[h][:, :]))
            t_dv2 = c.v(nc.vector.reduce_sum(bias_c[:, 4:5], tmpq[:, :],
                                             axis=AX.X))
            t_rc = c.v(nc.vector.reciprocal(bias_c[:, 5:6], bias_c[:, 4:5]))
            for mh in range(2):
                t_qp.wait(nc.tensor)
                t_idb.wait(nc.tensor)
                if tok_tp is not None:
                    tok_tp.wait(nc.tensor)
                mm = nc.tensor.transpose(tpq[0:128, 0:128],
                                         kp_sb[:, mh * 128:(mh + 1) * 128],
                                         identB[:, :])
                t_tp = c.mark(mm, "tensor")
                t_tp.wait(nc.vector)
                tok_tp = c.v(nc.vector.tensor_copy(
                    qpT_sb[:, mh * 128:(mh + 1) * 128], tpq[0:128, 0:128]))
            tok_tp.wait(nc.tensor)
            if t_ev is not None:
                t_ev.wait(nc.tensor)
            for mh in range(2):
                mm = nc.tensor.matmul(outp[0:128, 0:64],
                                      qpT_sb[:, mh * 128:(mh + 1) * 128],
                                      ctx_sb[h][:, mh * 64:(mh + 1) * 64],
                                      start=(mh == 0), stop=(mh == 1))
            t_out = c.mark(mm, "tensor")
            t_out.wait(nc.vector)
            t_rc.wait(nc.vector)
            t_ev = c.v(nc.vector.tensor_scalar_mul(
                att_tok[:, t * D + h * 64:t * D + (h + 1) * 64],
                outp[0:128, 0:64], bias_c[:, 5:6]))
    t_att = c.tip("vector")
    sub0.__exit__(None, None, None)
    sub = c.sub()
    cs = sub.__enter__()
    wo = emit_load_weight(cs, io["Wo"], D, D)
    bo_r, s_bo = emit_load_bf16_row(cs, io["bo"], D)
    ones2, t_o2 = emit_ones(cs)
    attT_buf = cs.sb([128, ND * T], BF16)
    attT, s_at = emit_bounceT(c, att_tok, t_att, D, outT=attT_buf)
    ops2 = cs.ring(2, [128, D])
    xrr = cs.sbring(2, [128, D], F32)
    ost = cs.sbring(2, [128, D], F32)
    s_out = c.sem()
    x3 = x_hbm.rearrange("(t p) f -> t p f", p=128)
    o3 = out_hbm.rearrange("(t p) f -> t p f", p=128)
    for t in range(NT):
        xr = xrr.acquire(nc.sync)
        s_xl = c.dsem("l0ax")
        s_xl.inc(nc.sync.dma_start(xr[:, :], x3[t, :, :]), 16)
        pt = ops2.acquire(nc.tensor)
        wo.sem.wait(nc.tensor)
        s_at.wait(nc.tensor)
        s_bo.wait(nc.tensor)
        t_o2.wait(nc.tensor)
        for o0 in range(0, D, 512):
            pairs = [(attT[:, ki * T + t * 128:ki * T + (t + 1) * 128],
                      wo.lhsT(ki, o0, o0 + 512)) for ki in range(ND)]
            pairs.append((ones2[0:1, 0:128], bo_r[0:1, o0:o0 + 512]))
            mm = mm_group(nc, pt[:, o0:o0 + 512], pairs)
        t_mm = c.mark(mm, "tensor")
        ot = ost.acquire(nc.vector)
        t_mm.wait(nc.vector)
        s_xl.wait(nc.vector)
        t_ad = c.v(nc.vector.tensor_add(ot[:, :], pt[:, 0:D], xr[:, :]))
        ops2.release(t_ad)
        xrr.release(t_ad)
        t_ad.wait(nc.sync)
        s_out.inc(nc.sync.dma_start(o3[t, :, :], ot[:, :]), 16)
        ost.release(Tok(s_out.h, s_out.v))
    sub.__exit__(None, None, None)
    s_out.wait(nc.gpsimd)
    return s_out


def build_l0attna_program(mode="main"):
    nc = bass.Bass("TRN2", target_bir_lowering=False)
    stack = ExitStack()
    c = Ctx(nc, stack)
    dt = lambda n, sh: nc.dram_tensor(n, sh, F32, kind="ExternalInput").ap()
    io = {
        "Wq": dt("Wq", [D, D]), "Wk": dt("Wk", [D, D]), "Wv": dt("Wv", [D, D]),
        "Wo": dt("Wo", [D, D]), "bo": dt("bo", [D]),
        "PT2": dt("PT2", [128, 256]),
    }
    if mode == "main":
        io["mx8"] = dt("mx8", [8])
    x_h = dt("x", [T, D])
    oshape = [1, 1] if mode == "max" else [T, D]
    out_h = nc.dram_tensor("out", oshape, F32, kind="ExternalOutput").ap()
    with stack:
        emit_l0_attn_a(c, x_h, out_h, io, mode=mode)
    return nc

from concourse.bass_utils import run_bass_kernel_spmd as _run_spmd

_PROGS = {}


def _prog(name, builder, *a, **k):
    if name not in _PROGS:
        _PROGS[name] = builder(*a, **k)
    return _PROGS[name]


def _colw(v, ncol):
    return np.ascontiguousarray(np.asarray(v, np.float32).reshape(ncol, 128).T)


def _cvc(cw, kk=4):
    cw = np.asarray(cw, np.float32)
    return np.ascontiguousarray(
        cw.reshape(8, 128, kk).transpose(1, 0, 2).reshape(128, 8 * kk))


def _host_wqkv(Wqkv):
    H_, DH_ = 16, 64
    W = np.asarray(Wqkv, np.float32).reshape(3, H_, DH_, D)
    out = np.zeros((3, D, D), np.float32)
    for g in range(2):
        for h in range(H_):
            for t in range(2):
                out[g, :, t * 512 + h * 32:(t * 512 + h * 32) + 32] = \
                    W[g, h, t::2, :].transpose(1, 0)
    out[2] = W[2].reshape(D, D).T
    return np.ascontiguousarray(np.concatenate([out[0], out[1], out[2]],
                                               axis=1))


def _ct(a):
    return np.ascontiguousarray(np.asarray(a, np.float32).T)


def kernel(**inp):
    NCORES = 8
    cores = list(range(NCORES))
    xs = [np.ascontiguousarray(np.asarray(inp["L0_attn_a_x"][i], np.float32))
          for i in range(NCORES)]

    def launch(name, builder, wm, bargs=()):
        nc = _prog(name, builder, *bargs)
        ims = [dict(wm, x=xs[i]) for i in range(NCORES)]
        res = _run_spmd(nc, ims, core_ids=cores)
        for i in range(NCORES):
            xs[i] = res.results[i]["out"]

    PT2 = np.ascontiguousarray(
        np.concatenate([np.asarray(inp["L0_attn_a_P"], np.float32).T] * 2, 0))
    wm_max = {"Wq": inp["L0_attn_a_Wq"], "Wk": inp["L0_attn_a_Wk"],
              "Wv": inp["L0_attn_a_Wv"], "Wo": inp["L0_attn_a_Wo"],
              "bo": inp["L0_attn_a_bo"], "PT2": PT2}
    nc1 = _prog("l0aa_max", build_l0attna_program, "max")
    r1 = _run_spmd(nc1, [dict(wm_max, x=xs[i]) for i in range(NCORES)],
                   core_ids=cores)
    mx8 = np.ascontiguousarray(np.concatenate(
        [r1.results[i]["out"].reshape(1) for i in range(NCORES)]).astype(
            np.float32))
    launch("l0aa", build_l0attna_program, dict(wm_max, mx8=mx8), ("main",))

    cw = np.asarray(inp["L0_ffn_conv_w"], np.float32)
    wm_l0ffn = {
        "router_w": inp["L0_ffn_router_w"],
        "router_bias": inp["L0_ffn_router_bias"],
        "gate_w": inp["L0_ffn_expert_gate_w"],
        "up_w": inp["L0_ffn_expert_up_w"],
        "down_w": inp["L0_ffn_expert_down_w"],
        "conv_w0": _ct(cw[:, :, 0]), "conv_w1": _ct(cw[:, :, 1]),
        "conv_w2": _ct(cw[:, :, 2]),
        "shared_up": inp["L0_ffn_shared_up_w"],
        "shared_down": inp["L0_ffn_shared_down_w"]}
    launch("l0ffn", build_l0ffn_program, wm_l0ffn)

    _unused = (inp["L0_attn_b_Wk"], inp["L0_attn_b_Wv"])
    wm = {"Wq": _ct(inp["L0_attn_b_Wq"]),
          "Wqkv": _ct(inp["L0_attn_b_Wqkv"]),
          "bqkv": inp["L0_attn_b_bqkv"],
          "Wop": _ct(inp["L0_attn_b_Wop"]),
          "W2c": np.ascontiguousarray(
              np.asarray(inp["L0_attn_b_W2"], np.float32).reshape(2, 128).T),
          "b2": inp["L0_attn_b_b2"]}
    launch("l0ab", build_l0attnb_program, wm)
    launch("l0ffn", build_l0ffn_program, wm_l0ffn)

    heads3 = np.ascontiguousarray(np.concatenate(
        [np.asarray(inp["L1_attn_a_log_xi"], np.float32),
         np.asarray(inp["L1_attn_a_pi_gate_logit"], np.float32),
         np.asarray(inp["L1_attn_a_e_gate_logit"], np.float32)])[None, :])
    launch("l1aa", build_l1attna_program,
           {"Wqkv": _host_wqkv(inp["L1_attn_a_Wqkv"]),
            "Wo": _ct(inp["L1_attn_a_Wo"]), "heads3": heads3})
    wm_l1ffn = {"Wu": _ct(inp["L1_ffn_w_up"]), "Wd": _ct(inp["L1_ffn_w_down"])}
    launch("l1ffn", build_l1ffn_program, wm_l1ffn)
    launch("l1ab", build_l1attnb_program, {"W": _ct(inp["L1_attn_b_W"])})
    launch("l1ffn", build_l1ffn_program, wm_l1ffn)

    launch("l2aa", build_l2attna_program,
           {"rby": inp["L2_attn_a_rby"],
            "Wq": _ct(inp["L2_attn_a_Wq"]), "bq": inp["L2_attn_a_bq"],
            "Wk": _ct(inp["L2_attn_a_Wk"]), "bk": inp["L2_attn_a_bk"],
            "Wv": _ct(inp["L2_attn_a_Wv"]), "bv": inp["L2_attn_a_bv"],
            "Wo": _ct(inp["L2_attn_a_Wo"]), "bo": inp["L2_attn_a_bo"],
            "ln_gamma": inp["L2_attn_a_ln_gamma"],
            "ln_beta": inp["L2_attn_a_ln_beta"]})
    wm_l2ffn = {"W1": inp["L2_ffn_W1"],
                "b1c": col_layout(np.asarray(inp["L2_ffn_b1"], np.float32)),
                "W2": inp["L2_ffn_W2"], "b2": inp["L2_ffn_b2"]}
    launch("l2ffn", build_l2ffn_program, wm_l2ffn)
    logA = np.asarray(inp["L2_attn_b_log_A"], np.float32)
    cwm = np.asarray(inp["L2_attn_b_conv_w"], np.float32)
    launch("l2ab", build_l2attnb_program,
           {"norm_g": inp["L2_attn_b_norm_g"],
            "norm_b": inp["L2_attn_b_norm_b"],
            "in_proj": _ct(inp["L2_attn_b_in_proj_w"]),
            "conv_wc": np.ascontiguousarray(
                cwm.reshape(16, 128, 4).transpose(1, 0, 2).reshape(128, 64)),
            "conv_bc": _colw(inp["L2_attn_b_conv_b"], 16),
            "x_proj": _ct(inp["L2_attn_b_x_proj_w"]),
            "dt_proj": _ct(inp["L2_attn_b_dt_proj_w"]),
            "dt_bias": inp["L2_attn_b_dt_proj_b"],
            "logA_c": np.ascontiguousarray(
                logA.reshape(16, 128, 16).transpose(1, 0, 2).reshape(128, 256)),
            "Dp_c": _colw(inp["L2_attn_b_Dp"], 16),
            "out_proj": _ct(inp["L2_attn_b_out_proj_w"])})
    launch("l2ffn", build_l2ffn_program, wm_l2ffn)

    launch("l3aa", build_l3attna_program,
           {"Wq": _ct(inp["L3_attn_a_q_proj_w"]),
            "Wk": _ct(inp["L3_attn_a_k_proj_w"]),
            "Wv": _ct(inp["L3_attn_a_v_proj_w"]),
            "Wb": _ct(inp["L3_attn_a_b_proj_w"]),
            "conv_wq_c": _cvc(inp["L3_attn_a_q_conv_w"]),
            "conv_wk_c": _cvc(inp["L3_attn_a_k_conv_w"]),
            "conv_wv_c": _cvc(inp["L3_attn_a_v_conv_w"]),
            "fir_s_c": _cvc(np.asarray(inp["L3_attn_a_fir_short_w"],
                                       np.float32).reshape(D, 7), 7),
            "fir_l_c": _cvc(np.asarray(inp["L3_attn_a_fir_long_w"],
                                       np.float32).reshape(D, 31), 31),
            "alpha": np.ascontiguousarray(
                np.asarray(inp["L3_attn_a_alpha"], np.float32)[:, 0]),
            "gmix_w1": _ct(inp["L3_attn_a_gmix_w1"]),
            "gmix_b1_c": _colw(inp["L3_attn_a_gmix_b1"], 16),
            "gmix_w2": _ct(inp["L3_attn_a_gmix_w2"]),
            "gmix_b2": inp["L3_attn_a_gmix_b2"],
            "onorm_w": inp["L3_attn_a_o_norm_w"],
            "Wo": _ct(inp["L3_attn_a_o_proj_w"])})
    wm_l3ffn = {
        "Wgt": np.ascontiguousarray(np.concatenate(
            [np.asarray(inp["L3_ffn_Wg"], np.float32),
             np.asarray(inp["L3_ffn_Wt"], np.float32)], axis=1)),
        "bgt": np.ascontiguousarray(np.concatenate(
            [np.asarray(inp["L3_ffn_bg"], np.float32),
             np.asarray(inp["L3_ffn_bt"], np.float32)])),
        "W1": inp["L3_ffn_W1"],
        "b1c": np.ascontiguousarray(np.concatenate(
            [col_layout(np.asarray(inp["L3_ffn_b1"], np.float32)[e])
             for e in range(8)], axis=1)),
        "W2": inp["L3_ffn_W2"],
        "b2r": np.ascontiguousarray(
            np.asarray(inp["L3_ffn_b2"], np.float32).flatten())}
    launch("l3ffn", build_l3ffn_program, wm_l3ffn)
    launch("l3ab", build_l3attnb_program,
           {"Wq": inp["L3_attn_b_Wq"], "Wk": inp["L3_attn_b_Wk"],
            "Wv": inp["L3_attn_b_Wv"],
            "conv_wq_c": _cvc(inp["L3_attn_b_conv_wq"]),
            "conv_wk_c": _cvc(inp["L3_attn_b_conv_wk"]),
            "conv_wv_c": _cvc(inp["L3_attn_b_conv_wv"]),
            "onorm_w": inp["L3_attn_b_onorm_w"],
            "Wo": inp["L3_attn_b_Wo"]})
    launch("l3ffn", build_l3ffn_program, wm_l3ffn)

    return np.stack(xs, 0).astype(np.float32)


def build_l0ffn_program():
    nc = bass.Bass("TRN2", target_bir_lowering=False)
    stack = ExitStack()
    c = Ctx(nc, stack)
    dt = lambda n, sh: nc.dram_tensor(n, sh, F32, kind="ExternalInput").ap()
    io = {
        "router_w": dt("router_w", [D, 16]),
        "router_bias": dt("router_bias", [16]),
        "gate_w": dt("gate_w", [15, D, 512]),
        "up_w": dt("up_w", [15, D, 512]),
        "down_w": dt("down_w", [15, 512, D]),
        "conv_w0": dt("conv_w0", [D, 512]),
        "conv_w1": dt("conv_w1", [D, 512]),
        "conv_w2": dt("conv_w2", [D, 512]),
        "shared_up": dt("shared_up", [D, 512]),
        "shared_down": dt("shared_down", [512, D]),
    }
    x_h = dt("x", [T, D])
    out_h = nc.dram_tensor("out", [T, D], F32, kind="ExternalOutput").ap()
    with stack:
        xres = c.sb([128, NT * D], F32)
        s_x = emit_load_x(c, x_h, xres)
        ln = emit_ln_transpose(c, xres, s_x)
        tok = emit_l0_ffn(c, xres, s_x, ln, io)
        emit_store_x(c, xres, out_h, [tok])
    return nc
